# version 09
# baseline (speedup 1.0000x reference)
import numpy as np
import ml_dtypes
import concourse.bass as bass
import concourse.mybir as mybir
from concourse.bass_utils import run_bass_kernel_spmd

F32 = mybir.dt.float32
BF16 = mybir.dt.bfloat16
U32 = mybir.dt.uint32
AF = mybir.ActivationFunctionType
ALU = mybir.AluOpType
AX = mybir.AxisListType
BF16NP = ml_dtypes.bfloat16

T = 2048
D = 1024
NT = T // 128
ND = D // 128
MAGIC = float(np.float32(1.5 * 2 ** 23))
LN2 = float(np.log(2.0))

DEBUG = False
DIAG3 = False
STAGE = 3
L0_PARTS = "AdBCDr"

ENGINES = ("sync", "tensor", "vector", "scalar", "gpsimd")


class KB:

    def __init__(self):
        self.nc = bass.Bass("TRN2", target_bir_lowering=False)
        self.H = {}
        self.q = {e: [] for e in ENGINES}
        self.counts = {}
        self.sb = {}
        self._ctx = []
        self.semnames = []

    def din(self, name, shape, dt=F32):
        self.H[name] = self.nc.dram_tensor(name, list(shape), dt,
                                           kind="ExternalInput").ap()
        return self.H[name]

    def dout(self, name, shape, dt=F32):
        self.H[name] = self.nc.dram_tensor(name, list(shape), dt,
                                           kind="ExternalOutput").ap()
        return self.H[name]

    def dtmp(self, name, shape, dt=F32):
        self.H[name] = self.nc.dram_tensor(name, list(shape), dt,
                                           kind="Internal").ap()
        return self.H[name]

    def S(self, name, shape, dtype):
        cm = self.nc.sbuf_tensor(name, list(shape), dtype)
        h = cm.__enter__()
        self._ctx.append(cm)
        self.sb[name] = h
        return h

    def PSUM(self, name, shape=(128, 512), dtype=F32):
        cm = self.nc.psum_tensor(name, list(shape), dtype)
        h = cm.__enter__()
        self._ctx.append(cm)
        return h

    def sem(self, *names):
        self.semnames.extend(names)

    def marker(self):
        return len(self._ctx)

    def free_to(self, mark):
        while len(self._ctx) > mark:
            cm = self._ctx.pop()
            cm.__exit__(None, None, None)

    def barrier(self, name="ph"):
        for s in ("te", "act", "dve", "gp", "tp", "st", "ld", "gw"):
            if self.cnt(s):
                self.wait("sync", s, self.cnt(s))
        v = self.op("sync", lambda e: e.nop(), name)
        for eng in ("tensor", "vector", "scalar", "gpsimd"):
            self.wait(eng, name, v)
        return v

    def op(self, eng, f, inc=None, n=1):
        if inc is None:
            self.q[eng].append(lambda E, _f=f: _f(E[eng]))
            return None
        if n == 16:
            prev = self.counts.get(inc, 0)
            if prev:
                self.wait(eng, inc, prev)
        self.counts[inc] = self.counts.get(inc, 0) + n
        v = self.counts[inc]
        self.q[eng].append(
            lambda E, _f=f, _s=inc, _n=n: _f(E[eng]).then_inc(E["S"][_s], _n))
        return v

    def cnt(self, sem):
        return self.counts.get(sem, 0)

    def wait(self, eng, sem, val):
        assert val <= self.cnt(sem), (sem, val, self.cnt(sem))
        self.q[eng].append(
            lambda E, _s=sem, _v=val: E[eng].wait_ge(E["S"][_s], _v))

    def drain(self, eng):
        self.q[eng].append(lambda E: E[eng].drain())

    def finalize(self):
        nc = self.nc
        semctx = []
        S = {}
        for n in self.semnames:
            cm = nc.semaphore(n)
            S[n] = cm.__enter__()
            semctx.append(cm)
        blk = nc.Block()
        block = blk.__enter__()
        E = {"S": S}

        def runner(engname):
            def _run(eng):
                E[engname] = eng
                for f in self.q[engname]:
                    f(E)
            return _run

        block.sync(runner("sync"))
        block.tensor(runner("tensor"))
        block.vector(runner("vector"))
        block.scalar(runner("scalar"))
        block.gpsimd(runner("gpsimd"))
        blk.__exit__(None, None, None)
        for cm in reversed(semctx):
            cm.__exit__(None, None, None)
        for cm in reversed(self._ctx):
            cm.__exit__(None, None, None)
        return nc



def tfm(tensor, c, t0, n, nchunk=8):
    i0 = t0 // 128
    return bass.AP(tensor, (i0 * nchunk + c) * 128,
                   [[1, 128], [nchunk * 128, n // 128], [1, 128]])



def emit_quant(kb, x_ap, out_ap, fac_ap, w, inc=None):
    nb = w // 32
    q = kb.sb["qtmp"]
    qs = kb.sb["qsc"]
    sm = kb.sb["smalls"]
    A, B, C = q[:, 0:w], q[:, w:2 * w], q[:, 2 * w:3 * w]
    s_amax = qs[:, 0:nb]
    s_pw2 = qs[:, nb:2 * nb]
    s_scale = qs[:, 2 * nb:3 * nb]
    s_rs = qs[:, 3 * nb:4 * nb]
    x3 = x_ap.rearrange("p (nb bs) -> p nb bs", bs=32)

    def o(f, inc_=None):
        r = kb.op("vector", f, inc_)
        kb.drain("vector")
        return r

    o(lambda e: e.tensor_reduce(s_amax, x3, AX.X, ALU.max,
                                apply_absolute_value=True))
    o(lambda e: e.tensor_single_scalar(s_pw2.bitcast(U32), s_amax.bitcast(U32),
                                       sm[:, 0:1].bitcast(U32), ALU.bitwise_and))
    o(lambda e: e.tensor_tensor(s_scale, s_pw2, fac_ap, ALU.mult))
    o(lambda e: e.tensor_scalar(out=s_scale, in0=s_scale, scalar1=0.25,
                                scalar2=1e-30, op0=ALU.mult, op1=ALU.max))
    o(lambda e: e.reciprocal(s_rs, s_scale))
    rs3 = s_rs.unsqueeze(2).broadcast_to([128, nb, 32])
    o(lambda e: e.tensor_tensor(A.rearrange("p (nb bs) -> p nb bs", bs=32),
                                x3, rs3, ALU.mult))
    o(lambda e: e.tensor_scalar(out=A, in0=A, scalar1=6.0, scalar2=-6.0,
                                op0=ALU.min, op1=ALU.max))
    o(lambda e: e.scalar_tensor_tensor(out=B, in0=A, scalar=-1.0, in1=A,
                                       op0=ALU.mult, op1=ALU.max))
    o(lambda e: e.tensor_scalar(out=C, in0=B, scalar1=2.0, scalar2=-1.0,
                                op0=ALU.is_ge, op1=ALU.mult))
    o(lambda e: e.tensor_scalar(out=B, in0=B, scalar1=4.0, scalar2=-0.5,
                                op0=ALU.is_ge, op1=ALU.mult))
    o(lambda e: e.scalar_tensor_tensor(out=C, in0=C, scalar=2.0, in1=B,
                                       op0=ALU.add, op1=ALU.add))
    o(lambda e: e.tensor_tensor(A, A, C, ALU.mult))
    o(lambda e: e.tensor_scalar(out=A, in0=A, scalar1=MAGIC, scalar2=MAGIC,
                                op0=ALU.add, op1=ALU.subtract))
    o(lambda e: e.tensor_scalar(out=B, in0=C, scalar1=1.0, scalar2=0.5,
                                op0=ALU.is_le, op1=ALU.mult))
    o(lambda e: e.tensor_scalar(out=C, in0=C, scalar1=0.5, scalar2=1.0,
                                op0=ALU.is_le, op1=ALU.mult))
    o(lambda e: e.scalar_tensor_tensor(out=C, in0=B, scalar=0.5, in1=C,
                                       op0=ALU.add, op1=ALU.add))
    o(lambda e: e.tensor_tensor(A, A, C, ALU.mult))
    sc3 = s_scale.unsqueeze(2).broadcast_to([128, nb, 32])
    return o(lambda e: e.tensor_tensor(
        out_ap.rearrange("p (nb bs) -> p nb bs", bs=32),
        A.rearrange("p (nb bs) -> p nb bs", bs=32), sc3, ALU.mult), inc)


def emit_quant_pair(kb, xs, outs, facs, w):
    nb = w // 32
    sm = kb.sb["smalls"]
    Q = [kb.sb["qtmp"], kb.sb["qtmpB"]]
    S = [kb.sb["qsc"], kb.sb["qscB"]]
    A = [Q[j][:, 0:w] for j in range(2)]
    B = [Q[j][:, w:2 * w] for j in range(2)]
    C = [Q[j][:, 2 * w:3 * w] for j in range(2)]
    amx = [S[j][:, 0:nb] for j in range(2)]
    pw2 = [S[j][:, nb:2 * nb] for j in range(2)]
    scl = [S[j][:, 2 * nb:3 * nb] for j in range(2)]
    rs = [S[j][:, 3 * nb:4 * nb] for j in range(2)]
    x3 = [xs[j].rearrange("p (nb bs) -> p nb bs", bs=32) for j in range(2)]

    def step(mk):
        g = None
        for j in range(2):
            g = kb.op("vector", mk(j), "dve" if j == 1 else None)
        kb.wait("vector", "dve", g)
        return g

    step(lambda j: lambda e: e.tensor_reduce(
        amx[j], x3[j], AX.X, ALU.max, apply_absolute_value=True))
    step(lambda j: lambda e: e.tensor_single_scalar(
        pw2[j].bitcast(U32), amx[j].bitcast(U32), sm[:, 0:1].bitcast(U32),
        ALU.bitwise_and))
    step(lambda j: lambda e: e.tensor_tensor(scl[j], pw2[j], facs[j], ALU.mult))
    step(lambda j: lambda e: e.tensor_scalar(
        out=scl[j], in0=scl[j], scalar1=0.25, scalar2=1e-30,
        op0=ALU.mult, op1=ALU.max))
    step(lambda j: lambda e: e.reciprocal(rs[j], scl[j]))
    rs3 = [rs[j].unsqueeze(2).broadcast_to([128, nb, 32]) for j in range(2)]
    step(lambda j: lambda e: e.tensor_tensor(
        A[j].rearrange("p (nb bs) -> p nb bs", bs=32), x3[j], rs3[j], ALU.mult))
    step(lambda j: lambda e: e.tensor_scalar(
        out=A[j], in0=A[j], scalar1=6.0, scalar2=-6.0,
        op0=ALU.min, op1=ALU.max))
    step(lambda j: lambda e: e.scalar_tensor_tensor(
        out=B[j], in0=A[j], scalar=-1.0, in1=A[j],
        op0=ALU.mult, op1=ALU.max))
    step(lambda j: lambda e: e.tensor_scalar(
        out=C[j], in0=B[j], scalar1=2.0, scalar2=-1.0,
        op0=ALU.is_ge, op1=ALU.mult))
    step(lambda j: lambda e: e.tensor_scalar(
        out=B[j], in0=B[j], scalar1=4.0, scalar2=-0.5,
        op0=ALU.is_ge, op1=ALU.mult))
    step(lambda j: lambda e: e.scalar_tensor_tensor(
        out=C[j], in0=C[j], scalar=2.0, in1=B[j],
        op0=ALU.add, op1=ALU.add))
    step(lambda j: lambda e: e.tensor_tensor(A[j], A[j], C[j], ALU.mult))
    step(lambda j: lambda e: e.tensor_scalar(
        out=A[j], in0=A[j], scalar1=MAGIC, scalar2=MAGIC,
        op0=ALU.add, op1=ALU.subtract))
    step(lambda j: lambda e: e.tensor_scalar(
        out=B[j], in0=C[j], scalar1=1.0, scalar2=0.5,
        op0=ALU.is_le, op1=ALU.mult))
    step(lambda j: lambda e: e.tensor_scalar(
        out=C[j], in0=C[j], scalar1=0.5, scalar2=1.0,
        op0=ALU.is_le, op1=ALU.mult))
    step(lambda j: lambda e: e.scalar_tensor_tensor(
        out=C[j], in0=B[j], scalar=0.5, in1=C[j],
        op0=ALU.add, op1=ALU.add))
    step(lambda j: lambda e: e.tensor_tensor(A[j], A[j], C[j], ALU.mult))
    sc3 = [scl[j].unsqueeze(2).broadcast_to([128, nb, 32]) for j in range(2)]
    return step(lambda j: lambda e: e.tensor_tensor(
        outs[j].rearrange("p (nb bs) -> p nb bs", bs=32),
        A[j].rearrange("p (nb bs) -> p nb bs", bs=32), sc3[j], ALU.mult))


def emit_xbar_tm2fm(kb, src, dst, n_src_tiles, n_chunks, dst_tile_stride,
                    inc="tp"):
    vals = []
    for i in range(n_src_tiles):
        src_ap = bass.AP(src.tensor if isinstance(src, bass.AP) else src, 0,
                         [[1, 128], [128, n_chunks], [1, 128]])
        src_ap = src[:, i * n_chunks * 128:(i + 1) * n_chunks * 128] \
            .rearrange("p (c q) -> p c q", q=128)
        dst_ap = bass.AP(dst.tensor if hasattr(dst, "tensor") else dst,
                         i * 128, [[1, 128], [dst_tile_stride, n_chunks],
                                   [1, 128]])
        v = kb.op("sync", lambda e, _s=src_ap, _d=dst_ap:
                  e.dma_start_transpose(_d, _s), inc, 16)
        vals.append(v)
    return vals[-1]



def host_consts():
    c = {}
    c["c_identity"] = np.eye(128, dtype=BF16NP)
    pos = np.arange(T, dtype=np.float32)
    dist = np.maximum(T - 1.0 - pos, 0.0)
    tw = np.zeros((32, T), np.float32)
    tw[:28] = -dist[None, :]
    tw[28:] = -pos[None, :]
    c["c_negtime"] = tw
    kex = np.zeros((32, 8, 128), BF16NP)
    for cc in range(8):
        for r in range(128):
            kex[4 * cc + r // 32, cc, r] = 1
    c["c_kexp"] = kex.reshape(32, 8 * 128)
    psm = np.zeros((128, 8, 32), BF16NP)
    for cc in range(8):
        for r in range(128):
            psm[r, cc, 4 * cc + r // 32] = 1
    c["c_psum32"] = psm.reshape(128, 8 * 32)
    return c


def host_l0_inputs(inputs):
    h = {}
    th = inputs["L0_attn_theta"].reshape(32, 32, 4)
    th_r = np.zeros((128, 32), np.float32)
    for r in range(128):
        for cc in range(8):
            for m in range(4):
                th_r[r, cc * 4 + m] = th[4 * cc + r // 32, r % 32, m]
    h["l0_theta_r"] = th_r
    wc = inputs["L0_attn_w_conv"].reshape(4, 2080)
    wcr = np.zeros((128, 64), np.float32)
    for fc in range(16):
        for tap in range(4):
            wcr[:, fc * 4 + tap] = wc[tap, fc * 128:(fc + 1) * 128]
    h["l0_wconv_r"] = wcr
    h["l0_wconv_sr"] = np.ascontiguousarray(wc[:, 2048:2080].T)
    h["l0_bconv_r"] = np.ascontiguousarray(
        inputs["L0_attn_b_conv"][:2048].reshape(16, 128).T)
    h["l0_bconv_sr"] = np.ascontiguousarray(
        inputs["L0_attn_b_conv"][2048:2080].reshape(32, 1))
    h["l0_slopes"] = np.ascontiguousarray(np.concatenate(
        [inputs["L0_attn_decay_slopes"], inputs["L0_attn_anchor_slopes"]]
    ).reshape(32, 1))
    h["l0_score"] = np.ascontiguousarray(
        inputs["L0_attn_score_scale"].reshape(32, 1))
    h["l0_deriv"] = np.ascontiguousarray(
        inputs["L0_attn_deriv_logits"].reshape(1, 3))
    gidx = np.array([[m2 * 32 + h2 for h2 in range(32)] for m2 in range(4)])
    src_idx = np.zeros(128, np.int64)
    for m2 in range(4):
        for h2 in range(32):
            src_idx[m2 * 32 + h2] = h2 * 4 + m2
    ns = inputs["L0_attn_norm_scale"]
    h["l0_norm2"] = np.ascontiguousarray(
        np.stack([ns[src_idx], ns[128 + src_idx]], 1))
    h["l0_wre"] = np.ascontiguousarray(inputs["L0_attn_W_re"][src_idx])
    h["l0_wim"] = np.ascontiguousarray(inputs["L0_attn_W_im"][src_idx])
    h["l0_win"] = np.ascontiguousarray(inputs["L0_attn_W_in"])
    h["l0_win2"] = np.ascontiguousarray(
        inputs["L0_attn_W_in"][:, 0:2048].reshape(8, 128, 16, 128)
        .transpose(2, 1, 0, 3).reshape(16, 128, 1024))
    h["l0_wout2"] = np.ascontiguousarray(
        inputs["L0_attn_W_out"].reshape(8, 128, 8, 128)
        .transpose(2, 1, 0, 3).reshape(8, 128, 1024))
    return h




def build(consts_np, want_dbg=False):
    kb = KB()
    kb.din("x0", (T, D))
    kb.din("L0_ffn_weight", (D, D))
    kb.din("L0_ffn_bias_r", (128, 8))
    kb.din("L0_ffn_delta_w_r", (128, 8 * 32))
    kb.din("c_dact", (128, 1))
    kb.din("l0_theta_r", (128, 32))
    kb.din("l0_wconv_r", (128, 64))
    kb.din("l0_wconv_sr", (32, 4))
    kb.din("l0_bconv_r", (128, 16))
    kb.din("l0_bconv_sr", (32, 1))
    kb.din("l0_slopes", (32, 1))
    kb.din("l0_score", (32, 1))
    kb.din("l0_deriv", (1, 3))
    kb.din("l0_norm2", (128, 2))
    kb.din("l0_wre", (128, 32))
    kb.din("l0_wim", (128, 32))
    kb.din("l0_win", (D, 2080))
    kb.din("l0_win2", (16, 128, 1024))
    kb.din("l0_wout2", (8, 128, 1024))
    for k, v in consts_np.items():
        kb.din(k, v.shape, BF16 if v.dtype == BF16NP else F32)
    kb.dtmp("x1", (T, D))
    if STAGE >= 3:
        kb.din("l1_wqkvT", (32, 128, 1024))
        kb.din("l1_bqkv", (128, 32))
        kb.din("l1_wqaT", (8, 128, 1024))
        kb.din("l1_wkaT", (8, 128, 1024))
        kb.din("l1_wdT", (8, 128, 1024))
        kb.din("l1_bqa", (128, 8))
        kb.din("l1_bka", (128, 8))
        kb.din("l1_bd", (128, 8))
        kb.din("l1_gwT", (1024, 8))
        kb.din("l1_wg", (8, 128, 2048))
        kb.din("l1_wu", (8, 128, 2048))
        kb.din("l1_wd", (8, 128, 2048))
        if want_dbg:
            kb.dout("xn1", (8, 128, T), BF16)
            kb.dout("g1", (8, 128, T), BF16)
            kb.dout("dbg_A", (128, 8 * T))
            kb.dout("dbg_out2", (128, NT * D))
        else:
            kb.dtmp("xn1", (8, 128, T), BF16)
            kb.dtmp("g1", (8, 128, T), BF16)
        if DIAG3:
            kb.dout("dbg_A3", (128, 8 * T), BF16)
            kb.dout("dbg_out3", (128, NT * D), BF16)
            kb.dout("dbg_qsl", (128, 4 * T), BF16)
            kb.dout("dbg_ksl", (128, 4 * T), BF16)
            kb.dout("dbg_qr", (128, 8 * T), BF16)
            kb.dout("dbg_kr", (128, 8 * T), BF16)
            kb.dout("dbg_v", (128, 8 * T), BF16)
    kb.dout("out", (T, D))
    if want_dbg:
        kb.dout("dbg_xn", (D, T))
        kb.dout("dbg_xq", (D, T))
        kb.dout("dbg_g0", (D, T))
        kb.dout("dbg_s0", (D, T))
        kb.dout("dbg_pw", (32, T))
        kb.dout("dbg_xv", (D, T))
        kb.dout("dbg_gate", (D, T))
        kb.dout("dbg_yu", (D, T))
        kb.dout("dbg_sk", (32, T))
        kb.dout("dbg_msq", (32, T))
        kb.dout("dbg_den", (32, T))

    H = kb.H
    op, wait, drain = kb.op, kb.wait, kb.drain
    kb.sem("ld", "gw", "st", "tp", "te", "dve", "act", "gp", "ev", "ph")

    ident = kb.S("ident", (128, 128), BF16)
    xn_fm = kb.S("xn_fm", (128, ND * T), BF16)
    g0 = kb.S("g0", (128, ND * T), BF16)
    smalls = kb.S("smalls", (128, 32), F32)
    red = kb.S("red", (128, 16), F32)
    sqd = kb.S("sqd", (128, D), F32)
    PS = kb.PSUM("PS", (128, 4096))
    pb = [PS[:, i * 512:(i + 1) * 512] for i in range(8)]
    kb.PSALL = PS
    MARK = kb.marker()
    xt = [kb.S(f"xt{i}", (128, D), F32) for i in range(2)]
    xnf = [kb.S(f"xnf{i}", (128, D), F32) for i in range(2)]
    xn_st = [kb.S(f"xn_st{i}", (128, D), BF16) for i in range(2)]
    xq_st = [kb.S(f"xq_st{i}", (128, D), BF16) for i in range(2)]
    wq_st = [kb.S(f"wq_st{i}", (128, D), BF16) for i in range(2)]
    xq_fm = kb.S("xq_fm", (128, ND * T), BF16)
    wqT = kb.S("wqT", (128, 8 * D), BF16)
    bias_c = kb.S("bias_c", (128, 8), F32)
    wkf = [kb.S(f"wkf{i}", (128, D), F32) for i in range(2)]
    dwf = kb.S("dwf", (128, 8 * 32), F32)
    facw = kb.S("facw", (128, 8 * 32), F32)
    qtmp = kb.S("qtmp", (128, 3 * D), F32)
    qsc = kb.S("qsc", (128, 4 * 32), F32)
    kb.S("qtmpB", (128, 3 * D), F32)
    kb.S("qscB", (128, 4 * 32), F32)

    op("sync", lambda e: e.dma_start(out=ident[:], in_=H["c_identity"][:, :]),
       "ld", 16)

    op("sync", lambda e: e.dma_start(out=bias_c[:], in_=H["L0_ffn_bias_r"][:, :]),
       "ld", 16)
    op("sync", lambda e: e.dma_start(out=dwf[:], in_=H["L0_ffn_delta_w_r"][:, :]),
       "ld", 16)
    ld_da = op("sync", lambda e: e.dma_start(out=smalls[:, 8:9], in_=H["c_dact"][:, :]),
               "ld", 16)
    ld_init = kb.cnt("ld")

    wait("vector", "ld", ld_init)
    wait("scalar", "ld", ld_init)
    op("vector", lambda e: e.memset(smalls[:, 0:1].bitcast(U32), 0x7F800000))
    drain("vector")

    def emit_tanh_exp_fac(src, dst, width, tmp0, tmp1):
        a1 = op("scalar", lambda e: e.activation(tmp0, src, AF.Exp,
                                                 bias=0.0, scale=8.0), "act")
        wait("vector", "act", a1)
        op("vector", lambda e: e.tensor_scalar(out=tmp1, in0=tmp0, scalar1=1.0,
                                               scalar2=None, op0=ALU.add))
        drain("vector")
        op("vector", lambda e: e.reciprocal(tmp1, tmp1))
        drain("vector")
        op("vector", lambda e: e.tensor_scalar(out=tmp0, in0=tmp0, scalar1=-1.0,
                                               scalar2=None, op0=ALU.add))
        drain("vector")
        v = op("vector", lambda e: e.tensor_tensor(tmp0, tmp0, tmp1, ALU.mult),
               "dve")
        wait("scalar", "dve", v)
        return op("scalar", lambda e: e.activation(dst, tmp0, AF.Exp, bias=0.0,
                                                   scale=0.5 * LN2), "act")

    a_fac = emit_tanh_exp_fac(smalls[:, 8:9], smalls[:, 14:15], 1,
                              smalls[:, 9:10], smalls[:, 10:11])
    a_facw = emit_tanh_exp_fac(dwf[:], facw[:], 256,
                               qtmp[:, 0:256], qtmp[:, 256:512])

    xv = H["x0"].rearrange("(n p) d -> n p d", p=128)
    wait("vector", "act", a_fac)
    wait("vector", "act", a_facw)
    fac32 = smalls[:, 14:15].broadcast_to([128, 32])
    last_q = None
    tp_guard = [None, None]
    ldv_t = {0: op("sync", lambda e: e.dma_start(out=xt[0][:], in_=xv[0]),
                   "ld", 16),
             1: op("sync", lambda e: e.dma_start(out=xt[1][:], in_=xv[1]),
                   "ld", 16)}
    for i0 in range(0, NT, 2):
        vxs = []
        for i in (i0, i0 + 1):
            b = i % 2
            ldv = ldv_t[i]
            wait("vector", "ld", ldv)
            wait("scalar", "ld", ldv)
            sq_done = op("scalar", lambda e, _b=b: e.activation(
                sqd[:], xt[_b][:], AF.Square, bias=0.0, scale=1.0,
                accum_out=red[:, 1:2]), "act")
            op("vector", lambda e, _b=b: e.tensor_reduce(red[:, 0:1], xt[_b][:],
                                                         AX.X, ALU.add))
            drain("vector")
            op("vector", lambda e: e.tensor_scalar(
                out=red[:, 2:3], in0=red[:, 0:1],
                scalar1=1.0 / D, scalar2=None, op0=ALU.mult))
            drain("vector")
            op("vector", lambda e: e.tensor_tensor(red[:, 3:4], red[:, 2:3],
                                                   red[:, 2:3], ALU.mult))
            wait("vector", "act", sq_done)
            drain("vector")
            op("vector", lambda e: e.scalar_tensor_tensor(
                out=red[:, 4:5], in0=red[:, 1:2], scalar=1.0 / D,
                in1=red[:, 3:4], op0=ALU.mult, op1=ALU.subtract))
            drain("vector")
            v_var = op("vector", lambda e: e.tensor_scalar(
                out=red[:, 4:5], in0=red[:, 4:5], scalar1=1e-5, scalar2=None,
                op0=ALU.add), "dve")
            wait("scalar", "dve", v_var)
            op("scalar", lambda e: e.activation(red[:, 5:6], red[:, 4:5],
                                                AF.Ln, bias=0.0, scale=1.0))
            kb.drain("scalar")
            a_r = op("scalar", lambda e: e.activation(
                red[:, 6:7], red[:, 5:6], AF.Exp, bias=0.0, scale=-0.5), "act")
            op("vector", lambda e: e.tensor_scalar(
                out=red[:, 7:8], in0=red[:, 2:3], scalar1=-1.0, scalar2=None,
                op0=ALU.mult))
            wait("vector", "act", a_r)
            drain("vector")
            op("vector", lambda e, _b=b: e.tensor_scalar(
                out=xnf[_b][:], in0=xt[_b][:], scalar1=red[:, 7:8],
                scalar2=red[:, 6:7], op0=ALU.add, op1=ALU.mult))
            drain("vector")
            if i >= 2 and tp_guard[b] is not None:
                wait("vector", "tp", tp_guard[b])
            v_xn = op("vector", lambda e, _b=b: e.tensor_copy(
                xn_st[_b][:], xnf[_b][:]), "dve")
            vxs.append(v_xn)
            if i + 2 < NT:
                wait("sync", "dve", v_xn)
                ldv_t[i + 2] = op("sync", lambda e, _j=i + 2, _b=b: e.dma_start(
                    out=xt[_b][:], in_=xv[_j]), "ld", 16)
        wait("vector", "dve", vxs[1])
        last_q = emit_quant_pair(kb, [xnf[0][:], xnf[1][:]],
                                 [xq_st[0][:], xq_st[1][:]],
                                 [fac32, fac32], D)
        for k, i in enumerate((i0, i0 + 1)):
            wait("sync", "dve", vxs[k])
            da_n = bass.AP(xn_fm, i * 128, [[NT * D, 128], [T, ND], [1, 128]])
            sa_n = xn_st[i % 2][:].rearrange("p (c q) -> p c q", q=128)
            op("sync", lambda e, _s=sa_n, _d=da_n: e.dma_start_transpose(_d, _s),
               "tp", 16)
        wait("sync", "dve", last_q)
        for i in (i0, i0 + 1):
            da_q = bass.AP(xq_fm, i * 128, [[NT * D, 128], [T, ND], [1, 128]])
            sa_q = xq_st[i % 2][:].rearrange("p (c q) -> p c q", q=128)
            tp_guard[i % 2] = op("sync", lambda e, _s=sa_q, _d=da_q:
                                 e.dma_start_transpose(_d, _s), "tp", 16)
    xq_done = last_q

    wv = H["L0_ffn_weight"].rearrange("(c p) d -> c p d", p=128)
    wait("vector", "act", a_facw)
    wtp_guard = [None, None]
    lw_t = {0: op("sync", lambda e: e.dma_start(out=wkf[0][:], in_=wv[0]),
                  "ld", 16),
            1: op("sync", lambda e: e.dma_start(out=wkf[1][:], in_=wv[1]),
                  "ld", 16)}
    for c0 in range(0, 8, 2):
        for c in (c0, c0 + 1):
            wait("vector", "ld", lw_t[c])
            if c >= 2 and wtp_guard[c % 2] is not None:
                wait("vector", "tp", wtp_guard[c % 2])
        last_q = emit_quant_pair(
            kb, [wkf[0][:], wkf[1][:]], [wq_st[0][:], wq_st[1][:]],
            [facw[:, c0 * 32:(c0 + 1) * 32],
             facw[:, (c0 + 1) * 32:(c0 + 2) * 32]], D)
        wait("sync", "dve", last_q)
        for c in (c0, c0 + 1):
            if c + 2 < 8:
                lw_t[c + 2] = op("sync", lambda e, _j=c + 2, _b=c % 2:
                                 e.dma_start(
                    out=wkf[_b][:], in_=wv[_j]), "ld", 16)
            da_w = bass.AP(wqT, c * 128, [[8 * D, 128], [D, 8], [1, 128]])
            sa_w = wq_st[c % 2][:].rearrange("p (c q) -> p c q", q=128)
            wtp_guard[c % 2] = op("sync", lambda e, _s=sa_w, _d=da_w:
                                  e.dma_start_transpose(_d, _s), "tp", 16)
    wq_done = last_q

    tp_done = kb.cnt("tp")

    wait("tensor", "tp", tp_done)
    bank_guard = {}
    g0_acts = []
    for tt in range(4):
        for oc in range(8):
            bi = (tt * 8 + oc) % 4
            bank = pb[bi]
            if bi in bank_guard:
                wait("tensor", "act", bank_guard[bi])
            mmv = None
            for ic in range(8):
                mov = kb.sb["xq_fm"][:, ic * T + tt * 512:ic * T + tt * 512 + 512]
                sta = kb.sb["wqT"][:, ic * D + oc * 128:ic * D + oc * 128 + 128]
                mmv = op("tensor",
                         lambda e, _b=bank, _s=sta, _m=mov, _i=ic: e.matmul(
                             _b[:], _s, _m,
                             start=(_i == 0), stop=(_i == 7)),
                         "te" if ic == 7 else None)
            wait("scalar", "te", mmv)
            a = op("scalar", lambda e, _b=bank, _o=oc, _t=tt: e.activation(
                kb.sb["g0"][:, _o * T + _t * 512:_o * T + _t * 512 + 512],
                _b[:], AF.Silu, bias=kb.sb["bias_c"][:, _o:_o + 1], scale=1.0),
                "act")
            bank_guard[bi] = a
            g0_acts.append(a)

    if want_dbg:
        wait("sync", "tp", tp_done)
        for c in range(ND):
            kb.wait("gpsimd", "tp", tp_done)
            op("gpsimd", lambda e, _c=c: e.dma_start(
                out=H["dbg_xn"].rearrange("(c p) t -> c p t", p=128)[_c],
                in_=xn_fm[:, _c * T:(_c + 1) * T]), "st", 16)
            op("gpsimd", lambda e, _c=c: e.dma_start(
                out=H["dbg_xq"].rearrange("(c p) t -> c p t", p=128)[_c],
                in_=xq_fm[:, _c * T:(_c + 1) * T]), "st", 16)
        kb.wait("gpsimd", "act", g0_acts[-1])
        for c in range(ND):
            op("gpsimd", lambda e, _c=c: e.dma_start(
                out=H["dbg_g0"].rearrange("(c p) t -> c p t", p=128)[_c],
                in_=g0[:, _c * T:(_c + 1) * T]), "st", 16)

    if STAGE >= 2:
        kb.barrier()
        kb.free_to(MARK)
        M1b = kb.marker()
        kb.S("rp32", (32, 2 * T), F32)
        kb.S("rpbf", (32, T), BF16)
        kb.S("kexp", (32, 8 * 128), BF16)
        kb.S("psum32", (128, 8 * 32), BF16)
        xt_l0 = [kb.S(f"xt{i}_l0", (128, D), F32) for i in range(2)]
        xnf_l0 = [kb.S(f"xnf{i}_l0", (128, D), F32) for i in range(2)]
        kb.sb["xt0"], kb.sb["xt1"] = xt_l0
        kb.sb["xnf0"], kb.sb["xnf1"] = xnf_l0
        op("sync", lambda e: e.dma_start(out=kb.sb["rp32"][0:32, T:2 * T],
                                         in_=H["c_negtime"][:, :]), "ld", 16)
        op("sync", lambda e: e.dma_start(out=kb.sb["kexp"][:, :],
                                         in_=H["c_kexp"][:, :]), "ld", 16)
        op("sync", lambda e: e.dma_start(out=kb.sb["psum32"][:, :],
                                         in_=H["c_psum32"][:, :]), "ld", 16)
        M2 = kb.marker()
        emit_l0_sraw(kb, pb)
        kb.barrier()
        kb.free_to(M2)
        emit_l0_attn(kb, pb, want_dbg=want_dbg)
        kb.wait("sync", "st", kb.cnt("st"))
        if STAGE == 2:
            op("sync", lambda e: e.dma_start(out=H["out"][:, :],
                                             in_=H["x1"][:, :]), "st", 16)
            kb.wait("sync", "st", kb.cnt("st"))
        if STAGE >= 3:
            kb.barrier()
            kb.free_to(M1b)
            emit_l1(kb, want_dbg=want_dbg)
    else:
        wait("sync", "act", g0_acts[-1])
        for i in range(NT):
            op("sync", lambda e, _i=i: e.dma_start(
                out=H["out"].rearrange("(n p) d -> n p d", p=128)[_i],
                in_=xt[0][:]), "st", 16)
        wait("sync", "st", kb.cnt("st"))

    return kb



def prep_core_inputs(inputs, consts):
    base = {}
    base.update({k: np.ascontiguousarray(v) for k, v in consts.items()})
    base["L0_ffn_weight"] = np.ascontiguousarray(inputs["L0_ffn_weight"])
    base["L0_ffn_bias_r"] = np.ascontiguousarray(
        inputs["L0_ffn_bias"].reshape(8, 128).T)
    base["L0_ffn_delta_w_r"] = np.ascontiguousarray(
        inputs["L0_ffn_delta_w"].reshape(8, 128, 32).transpose(1, 0, 2)
        .reshape(128, 8 * 32))
    base["c_dact"] = np.ascontiguousarray(
        np.broadcast_to(inputs["L0_ffn_delta_act"].reshape(1, 1), (128, 1)))
    base.update(host_l0_inputs(inputs))
    return base


def kernel(**inputs):
    consts = host_consts()
    if STAGE >= 3:
        consts.update(host_l1_consts())
    base = prep_core_inputs(inputs, consts)
    if STAGE >= 3:
        base.update(host_l1_inputs(inputs))
    x = np.ascontiguousarray(inputs["L0_attn_x"])
    in_maps = []
    for c in range(8):
        m = dict(base)
        m["x0"] = np.ascontiguousarray(x[c])
        in_maps.append(m)
    kb = build(consts, want_dbg=DEBUG)
    nc = kb.finalize()
    res = run_bass_kernel_spmd(nc, in_maps, core_ids=list(range(8)))
    out = np.stack([res.results[c]["out"] for c in range(8)], 0)
    if DEBUG:
        kernel.dbg = res.results
    return out



def emit_l0_attn(kb, pb, want_dbg=False):
    H, op, wait, drain = kb.H, kb.op, kb.wait, kb.drain
    sb = kb.sb
    TC = 512
    NTC = T // TC

    xn_fm, g0 = sb["xn_fm"], sb["g0"]
    smalls = sb["smalls"]
    rp32 = sb["rp32"]
    pw_all = rp32[0:32, 0:T]
    pw_bf = sb["rpbf"][0:32, :]

    thdiv = kb.S("thdiv", (128, 32), F32)
    wconv = kb.S("wconv", (128, 64), F32)
    bconv = kb.S("bconv", (128, 16), F32)
    norm2 = kb.S("norm2", (128, 2), F32)
    wre = kb.S("wre", (128, 32), F32)
    wim = kb.S("wim", (128, 32), F32)
    wy = kb.S("wy", (128, 8 * 128), BF16)
    diag4 = kb.S("diag4", (128, 4 * 128), BF16)
    zr = [kb.S(f"zr{i}", (128, TC + 3), BF16) for i in range(2)]
    ztail = kb.S("ztail", (128, 16 * 3), BF16)
    xv = kb.S("xv", (128, 8 * TC), F32)
    gate = kb.S("gate", (128, 8 * TC), BF16)
    y_u = kb.S("y_u", (128, 8 * TC), BF16)
    sg_tm = kb.S("sg_tm", (128, 4 * D), BF16)
    u_t = [kb.S(f"u_t{p}", (128, TC), F32) for p in range(2)]
    k_r = [kb.S(f"k_r{p}", (128, TC), F32) for p in range(2)]
    frac = [kb.S(f"frac{p}", (128, TC), F32) for p in range(4)]
    afr = [kb.S(f"afr{p}", (128, TC), F32) for p in range(2)]
    sinb = [kb.S(f"sinb{p}", (128, TC), BF16) for p in range(4)]
    cosb = [kb.S(f"cosb{p}", (128, TC), BF16) for p in range(4)]
    pwx = kb.S("pwx", (128, TC), BF16)
    pol1 = kb.S("pol1", (128, TC), F32)
    pol2 = pol1
    pp = kb.S("pp", (128, TC), BF16)
    rein = [kb.S(f"rein{p}", (128, TC), BF16) for p in range(2)]
    imin = [kb.S(f"imin{p}", (128, TC), BF16) for p in range(2)]
    csre = [kb.S(f"csre{p}", (128, TC), BF16) for p in range(2)]
    csim = [kb.S(f"csim{p}", (128, TC), BF16) for p in range(2)]
    sqt = [rein[0], imin[0], rein[1], imin[1]]
    carr = kb.S("carr", (128, 66), F32)
    l32 = kb.S("l32", (32, (5 if want_dbg else 4) * TC), F32)
    skbf = kb.S("skbf", (32, TC), BF16)
    wfc = [kb.S(f"wfc{i}", (128, 8 * 128), BF16) for i in range(2)]
    sgadd = [sb["xnf0"], sb["xnf1"]]
    sg = gate

    msq = l32[0:32, 0:TC]
    dent = l32[0:32, TC:2 * TC]
    skt = l32[0:32, 2 * TC:3 * TC]
    rtmp = l32[0:32, 3 * TC:4 * TC]
    rtmp2 = l32[0:32, 4 * TC:5 * TC] if want_dbg else None

    for (nm, tile) in (("l0_theta_r", thdiv), ("l0_wconv_r", wconv),
                       ("l0_bconv_r", bconv), ("l0_norm2", norm2),
                       ("l0_wre", wre), ("l0_wim", wim)):
        op("sync", lambda e, _n=nm, _t=tile: e.dma_start(out=_t[:], in_=H[_n][:, :]),
           "ld", 16)
    ld_l0 = kb.cnt("ld")
    wait("vector", "ld", ld_l0)
    op("vector", lambda e: e.tensor_scalar(out=thdiv[:], in0=thdiv[:],
                                           scalar1=float(1.0 / (2 * np.pi)),
                                           scalar2=None, op0=ALU.mult))
    op("vector", lambda e: e.memset(smalls[:, 1:2].bitcast(U32), 1))
    op("vector", lambda e: e.memset(smalls[:, 4:5], float(np.pi / 2)))
    op("vector", lambda e: e.memset(carr[:], 0.0))
    op("vector", lambda e: e.memset(ztail[:], 0.0))
    op("vector", lambda e: e.memset(wy[:], 0.0))
    drain("vector")
    op("vector", lambda e: e.tensor_scalar(out=wre[:], in0=wre[:],
                                           scalar1=norm2[:, 0:1], scalar2=None,
                                           op0=ALU.mult))
    op("vector", lambda e: e.tensor_scalar(out=wim[:], in0=wim[:],
                                           scalar1=norm2[:, 1:2], scalar2=None,
                                           op0=ALU.mult))
    drain("vector")
    WYP = 8 * 128
    for ri, wsrc in ((0, wre), (1, wim)):
        for m in range(4):
            j = ri * 4 + m
            for kk in range(4):
                src_ap = wsrc[m * 32:(m + 1) * 32, :]
                dst_ap = bass.AP(wy, kk * 32 * WYP + j * 128 + kk * 32,
                                 [[WYP, 32], [1, 32]])
                op("vector", lambda e, _s=src_ap, _d=dst_ap: e.tensor_copy(_d, _s))
    drain("vector")
    v_init = op("vector", lambda e: e.memset(smalls[:, 2:3], 0.0), "dve")

    wfc_guard = [0, 0]
    sgadd_guard = [0, 0]
    prev_cos_c = [None]
    for tc in range(NTC):
        t0 = tc * TC
        first = tc == 0
        for fc in range(16):
            bz = fc % 2
            in_ap = bass.AP(H["l0_win2"].tensor, fc * 131072,
                            [[1024, 128], [1, 1024]])
            if wfc_guard[bz]:
                kb.wait("gpsimd", "te", wfc_guard[bz])
            lw = op("gpsimd", lambda e, _i=in_ap, _b=bz: e.dma_start(
                out=wfc[_b][:], in_=_i), "gw", 16)
            wait("tensor", "gw", lw)
            bank = pb[2 + bz]
            mmv = None
            for dc in range(8):
                mov = xn_fm[:, dc * T + t0:dc * T + t0 + TC]
                sta = wfc[bz][:, dc * 128:(dc + 1) * 128]
                mmv = op("tensor", lambda e, _b=bank, _s=sta, _m=mov, _d=dc:
                         e.matmul(_b[:], _s, _m, start=(_d == 0), stop=(_d == 7)),
                         "te" if dc == 7 else None)
            wfc_guard[bz] = mmv
            wait("vector", "te", mmv)
            op("vector", lambda e, _b=bz, _f=fc: e.tensor_copy(
                zr[_b][:, 0:3], ztail[:, _f * 3:_f * 3 + 3]))
            ev = op("vector", lambda e, _b=bank, _z=bz: e.tensor_copy(
                zr[_z][:, 3:3 + TC], _b[:]), "dve")
            kb.drain("vector")
            op("vector", lambda e, _b=bz, _f=fc: e.tensor_copy(
                ztail[:, _f * 3:_f * 3 + 3], zr[_b][:, TC:TC + 3]))
            wait("tensor", "dve", ev)
            wait("vector", "te", wfc_guard[bz])
            for tap in range(4):
                op("vector", lambda e, _f=fc, _t=tap: e.tensor_scalar(
                    out=diag4[:, _t * 128:(_t + 1) * 128], in0=sb["ident"][:],
                    scalar1=wconv[:, (_f * 4 + _t):(_f * 4 + _t) + 1],
                    scalar2=None, op0=ALU.mult))
            v_diag = op("vector", lambda e: e.memset(smalls[:, 3:4], 0.0), "dve")
            wait("tensor", "dve", v_diag)
            bank2 = pb[4 + bz]
            mmv2 = None
            for tap in range(4):
                mov = zr[bz][:, tap:tap + TC]
                sta = diag4[:, tap * 128:(tap + 1) * 128]
                mmv2 = op("tensor", lambda e, _b=bank2, _s=sta, _m=mov, _t=tap:
                          e.matmul(_b[:], _s, _m, start=(_t == 0), stop=(_t == 3)),
                          "te" if tap == 3 else None)
            wait("scalar", "te", mmv2)
            if fc < 8:
                a = op("scalar", lambda e, _b=bank2, _f=fc: e.activation(
                    xv[:, _f * TC:(_f + 1) * TC], _b[:], AF.Identity,
                    bias=bconv[:, _f:_f + 1], scale=1.0), "act")
            else:
                a = op("scalar", lambda e, _b=bank2, _f=fc: e.activation(
                    gate[:, (_f - 8) * TC:(_f - 7) * TC], _b[:], AF.Silu,
                    bias=bconv[:, _f:_f + 1], scale=1.0), "act")
            wait("tensor", "act", a)
            wait("vector", "te", mmv2)
        a_convdone = kb.cnt("act")

        if want_dbg and "dbg_xv" in H:
            wait("sync", "act", a_convdone)
            kb.wait("gpsimd", "act", a_convdone)
            for fcc in range(8):
                op("sync", lambda e, _f=fcc, _t0=t0: e.dma_start(
                    out=H["dbg_xv"].rearrange("(c p) t -> c p t", p=128)
                    [_f][:, _t0:_t0 + TC],
                    in_=xv[:, _f * TC:(_f + 1) * TC]), "st", 16)
                op("gpsimd", lambda e, _f=fcc, _t0=t0: e.dma_start(
                    out=H["dbg_gate"].rearrange("(c p) t -> c p t", p=128)
                    [_f][:, _t0:_t0 + TC],
                    in_=gate[:, _f * TC:(_f + 1) * TC]), "st", 16)
            kb.wait("gpsimd", "st", kb.cnt("st"))
            kb.wait("vector", "st", kb.cnt("st"))
            kb.wait("scalar", "st", kb.cnt("st"))
        if "d" not in L0_PARTS:
            continue
        op("vector", lambda e, _t0=t0: e.tensor_tensor_scan(
            dent, pw_all[:, _t0:_t0 + TC], pw_all[:, _t0:_t0 + TC],
            carr[0:32, 64:65], ALU.add, ALU.bypass))
        drain("vector")
        op("vector", lambda e: e.tensor_copy(carr[0:32, 64:65],
                                             dent[:, TC - 1:TC]))
        kb.drain("vector")
        op("vector", lambda e: e.tensor_single_scalar(rtmp, dent, 1e-4,
                                                      ALU.max))
        drain("vector")
        op("vector", lambda e: e.reciprocal(rtmp, rtmp))
        drain("vector")

        if "B" not in L0_PARTS:
            continue
        wait("tensor", "act", a_convdone)
        msq_started = False
        for c in range(8):
            mm = op("tensor", lambda e, _c=c, _t0=t0: e.matmul(
                pb[1][:], sb["kexp"][0:32, _c * 128:(_c + 1) * 128],
                pw_bf[:, _t0:_t0 + TC], start=True, stop=True), "te")
            wait("vector", "te", mm)
            ev = op("vector", lambda e: e.tensor_copy(pwx[:], pb[1][:]), "dve")
            wait("tensor", "dve", ev)
            xvc = xv[:, c * TC:(c + 1) * TC]
            op("vector", lambda e, _x=xvc: e.tensor_scalar(
                out=pol1[:], in0=_x, scalar1=smalls[:, 22:23],
                scalar2=smalls[:, 21:22], op0=ALU.mult, op1=ALU.add))
            drain("vector")
            op("vector", lambda e, _x=xvc: e.scalar_tensor_tensor(
                out=pol2[:], in0=_x, scalar=1.0, in1=pol1[:], op0=ALU.mult,
                op1=ALU.mult))
            drain("vector")
            op("vector", lambda e: e.scalar_tensor_tensor(
                out=pp[:], in0=pol2[:], scalar=smalls[:, 20:21], in1=pwx[:],
                op0=ALU.add, op1=ALU.mult))
            drain("vector")
            def front(mp, prev_cos=None):
                ms = [2 * mp, 2 * mp + 1]
                g = None
                for p, m in enumerate(ms):
                    thc = thdiv[:, c * 4 + m:c * 4 + m + 1]
                    g = op("vector", lambda e, _x=xvc, _th=thc, _p=p:
                           e.tensor_scalar(
                        out=u_t[_p][:], in0=_x, scalar1=_th, scalar2=MAGIC,
                        op0=ALU.mult, op1=ALU.add),
                        "dve" if p == 1 else None)
                wait("vector", "dve", g)
                for p in range(2):
                    g = op("vector", lambda e, _p=p: e.tensor_scalar(
                        out=k_r[_p][:], in0=u_t[_p][:], scalar1=MAGIC,
                        scalar2=None, op0=ALU.subtract),
                        "dve" if p == 1 else None)
                wait("vector", "dve", g)
                for p, m in enumerate(ms):
                    thc = thdiv[:, c * 4 + m:c * 4 + m + 1]
                    g = op("vector", lambda e, _x=xvc, _th=thc, _p=p,
                           _q=2 * mp + p: e.scalar_tensor_tensor(
                        out=frac[_q][:], in0=_x, scalar=_th, in1=k_r[_p][:],
                        op0=ALU.mult, op1=ALU.subtract),
                        "dve" if p == 1 else None)
                wait("scalar", "dve", g)
                ga = None
                for p in range(2):
                    op("scalar", lambda e, _q=2 * mp + p: e.activation(
                        sinb[_q][:], frac[_q][:], AF.Sin, bias=0.0,
                        scale=float(2 * np.pi)))
                if prev_cos is not None:
                    kb.wait("scalar", "act", prev_cos)
                for p in range(2):
                    ga = op("scalar", lambda e, _p=p, _q=2 * mp + p:
                            e.activation(
                        afr[_p][:], frac[_q][:], AF.Abs, bias=0.0, scale=1.0),
                        "act" if p == 1 else None)
                kb.wait("scalar", "act", ga)
                a_cos = None
                for p in range(2):
                    a_cos = op("scalar", lambda e, _p=p, _q=2 * mp + p:
                               e.activation(
                        cosb[_q][:], afr[_p][:], AF.Sin, bias=smalls[:, 4:5],
                        scale=float(-2 * np.pi)), "act" if p == 1 else None)
                return a_cos

            def tail(mp, a_cos, lastmm):
                ms = [2 * mp, 2 * mp + 1]
                wait("vector", "act", a_cos)
                if lastmm is not None:
                    wait("vector", "te", lastmm)
                for p in range(2):
                    op("vector", lambda e, _p=p, _q=2 * mp + p: e.tensor_tensor(
                        rein[_p][:], pp[:], cosb[_q][:], ALU.mult))
                    g = op("vector", lambda e, _p=p, _q=2 * mp + p:
                           e.tensor_tensor(
                        imin[_p][:], pp[:], sinb[_q][:], ALU.mult),
                        "dve" if p == 1 else None)
                wait("vector", "dve", g)
                for p, m in enumerate(ms):
                    ci = (c * 4 + m) * 2
                    op("vector", lambda e, _ci=ci, _p=p: e.tensor_tensor_scan(
                        csre[_p][:], rein[_p][:], rein[_p][:],
                        carr[:, _ci:_ci + 1], ALU.add, ALU.bypass))
                    g = op("vector", lambda e, _ci=ci, _p=p: e.tensor_tensor_scan(
                        csim[_p][:], imin[_p][:], imin[_p][:],
                        carr[:, _ci + 1:_ci + 2], ALU.add, ALU.bypass),
                        "dve" if p == 1 else None)
                wait("vector", "dve", g)
                for p, m in enumerate(ms):
                    ci = (c * 4 + m) * 2
                    op("vector", lambda e, _ci=ci, _p=p: e.tensor_copy(
                        carr[:, _ci:_ci + 1], csre[_p][:, TC - 1:TC]))
                    op("vector", lambda e, _ci=ci, _p=p: e.tensor_copy(
                        carr[:, _ci + 1:_ci + 2], csim[_p][:, TC - 1:TC]))
                    op("vector", lambda e, _p=p: e.tensor_tensor(
                        sqt[2 * _p][:], csre[_p][:], csre[_p][:], ALU.mult))
                    g = op("vector", lambda e, _p=p: e.tensor_tensor(
                        sqt[2 * _p + 1][:], csim[_p][:], csim[_p][:],
                        ALU.mult), "dve" if p == 1 else None)
                wait("tensor", "dve", g)
                for p, m in enumerate(ms):
                    op("tensor", lambda e, _c=c, _p=p, _st=msq_st[0]:
                       e.matmul(pb[6][0:32, :],
                                sb["psum32"][:, _c * 32:(_c + 1) * 32],
                                sqt[2 * _p][:], start=_st, stop=False), None)
                    msq_st[0] = False
                    op("tensor", lambda e, _m=m, _p=p, _first=(m == 0):
                       e.matmul(pb[0][:], wy[:, _m * 128:(_m + 1) * 128],
                                csre[_p][:], start=_first, stop=False), None)
                    is_last = (c == 7 and m == 3)
                    op("tensor", lambda e, _c=c, _p=p, _L=is_last:
                       e.matmul(pb[6][0:32, :],
                                sb["psum32"][:, _c * 32:(_c + 1) * 32],
                                sqt[2 * _p + 1][:], start=False, stop=_L), None)
                    lastmm = op("tensor", lambda e, _m=m, _p=p:
                                e.matmul(pb[0][:],
                                         wy[:, (4 + _m) * 128:(5 + _m) * 128],
                                         csim[_p][:], start=False,
                                         stop=(_m == 3)), "te")
                return lastmm

            msq_st = [not msq_started]
            msq_started = True
            lastmm = None
            ac0 = front(0, prev_cos=prev_cos_c[0])
            ac1 = front(1, prev_cos=ac0)
            prev_cos_c[0] = ac1
            lastmm = tail(0, ac0, lastmm)
            lastmm = tail(1, ac1, lastmm)
            wait("vector", "te", lastmm)
            ev = op("vector", lambda e, _c=c: e.tensor_copy(
                y_u[:, _c * TC:(_c + 1) * TC], pb[0][:]), "dve")
            wait("tensor", "dve", ev)
        if "C" not in L0_PARTS:
            continue
        wait("vector", "te", kb.cnt("te"))
        op("vector", lambda e: e.tensor_scalar(
            out=msq, in0=pb[6][0:32, :], scalar1=float(1.0 / 256.0),
            scalar2=None, op0=ALU.mult))
        drain("vector")
        op("vector", lambda e: e.tensor_tensor(msq, msq, rtmp, ALU.mult))
        drain("vector")
        op("vector", lambda e: e.tensor_tensor(msq, msq, rtmp, ALU.mult))
        drain("vector")
        v_msq = op("vector", lambda e: e.tensor_scalar(
            out=msq, in0=msq, scalar1=1e-5, scalar2=None, op0=ALU.add), "dve")
        if want_dbg and "dbg_msq" in H:
            v_m = op("vector", lambda e: e.tensor_copy(rtmp2, msq), "dve")
            kb.drain("vector")
            wait("sync", "dve", v_m)
            op("sync", lambda e, _t0=t0: e.dma_start(
                out=H["dbg_msq"][:, _t0:_t0 + TC], in_=rtmp2), "st", 16)
            op("sync", lambda e, _t0=t0: e.dma_start(
                out=H["dbg_den"][:, _t0:_t0 + TC], in_=dent), "st", 16)
            wait("vector", "st", kb.cnt("st"))
        wait("scalar", "dve", v_msq)
        a_rs = op("scalar", lambda e: e.activation(
            skt, msq, AF.Sqrt, bias=0.0, scale=1.0), "act")
        wait("vector", "act", a_rs)
        op("vector", lambda e: e.reciprocal(skt, skt))
        drain("vector")
        op("vector", lambda e: e.tensor_tensor(skt, skt, rtmp, ALU.mult))
        drain("vector")
        v_sk = op("vector", lambda e: e.tensor_copy(skbf[:], skt), "dve")
        if want_dbg and "dbg_yu" in H:
            wait("gpsimd", "dve", kb.cnt("dve"))
            wait("sync", "dve", kb.cnt("dve"))
            for fcc in range(8):
                op("gpsimd", lambda e, _f=fcc, _t0=t0: e.dma_start(
                    out=H["dbg_yu"].rearrange("(c p) t -> c p t", p=128)
                    [_f][:, _t0:_t0 + TC],
                    in_=y_u[:, _f * TC:(_f + 1) * TC]), "st", 16)
            op("sync", lambda e, _t0=t0: e.dma_start(
                out=H["dbg_sk"][:, _t0:_t0 + TC], in_=skt), "st", 16)
            kb.wait("gpsimd", "st", kb.cnt("st"))
            kb.wait("vector", "st", kb.cnt("st"))
        if "D" not in L0_PARTS:
            continue
        wait("tensor", "dve", v_sk)
        for c in range(8):
            mm = op("tensor", lambda e, _c=c: e.matmul(
                pb[1][:], sb["kexp"][0:32, _c * 128:(_c + 1) * 128],
                skbf[:], start=True, stop=True), "te")
            wait("vector", "te", mm)
            off = c * TC
            op("vector", lambda e, _o=off: e.scalar_tensor_tensor(
                out=y_u[:, _o:_o + TC], in0=y_u[:, _o:_o + TC], scalar=1.0,
                in1=pb[1][:], op0=ALU.mult, op1=ALU.mult))
            kb.drain("vector")
            ev = op("vector", lambda e, _o=off: e.tensor_tensor(
                y_u[:, _o:_o + TC], y_u[:, _o:_o + TC], gate[:, _o:_o + TC],
                ALU.mult), "dve")
            wait("tensor", "dve", ev)
        for fo in range(8):
            bz = fo % 2
            in_ap = bass.AP(H["l0_wout2"].tensor, fo * 131072,
                            [[1024, 128], [1, 1024]])
            if wfc_guard[bz]:
                kb.wait("gpsimd", "te", wfc_guard[bz])
            lw = op("gpsimd", lambda e, _i=in_ap, _b=bz: e.dma_start(
                out=wfc[_b][:], in_=_i), "gw", 16)
            wait("tensor", "gw", lw)
            bank = pb[2 + bz]
            mmv = None
            for c in range(8):
                mov = y_u[:, c * TC:(c + 1) * TC]
                sta = wfc[bz][:, c * 128:(c + 1) * 128]
                mmv = op("tensor", lambda e, _b=bank, _s=sta, _m=mov, _c=c:
                         e.matmul(_b[:], _s, _m, start=(_c == 0), stop=(_c == 7)),
                         "te" if c == 7 else None)
            wfc_guard[bz] = mmv
            wait("vector", "te", mmv)
            ev = op("vector", lambda e, _b=bank, _f=fo, _t0=t0:
                    e.scalar_tensor_tensor(
                        out=sg[:, _f * TC:(_f + 1) * TC],
                        in0=g0[:, _f * T + _t0:_f * T + _t0 + TC],
                        scalar=1.0, in1=_b[:], op0=ALU.mult, op1=ALU.mult),
                    "dve")
            wait("tensor", "dve", ev)
        sg_done = kb.cnt("dve")
        wait("sync", "dve", sg_done)
        tpv = None
        for fo in range(8):
            sa = sg[:, fo * TC:(fo + 1) * TC].rearrange("p (i q) -> p i q", q=128)
            da = bass.AP(sg_tm, fo * 128, [[4 * D, 128], [D, 4], [1, 128]])
            tpv = op("sync", lambda e, _s=sa, _d=da: e.dma_start_transpose(_d, _s),
                     "tp", 16)
        xv_h = H["x0"].rearrange("(n p) d -> n p d", p=128)
        x1_h = H["x1"].rearrange("(n p) d -> n p d", p=128)
        xtb_l0 = [sb["xt0"], sb["xt1"]]
        for ii in range(TC // 128):
            i = t0 // 128 + ii
            bx = ii % 2
            ldv = op("sync", lambda e, _i=i, _x=xtb_l0[bx]: e.dma_start(
                out=_x[:], in_=xv_h[_i]), "ld", 16)
            wait("vector", "ld", ldv)
            wait("vector", "tp", tpv)
            if sgadd_guard[bx]:
                wait("vector", "st", sgadd_guard[bx])
            v1 = op("vector", lambda e, _b=bx, _x=xtb_l0[bx], _ii=ii:
                    e.tensor_tensor(
                sgadd[_b][:], _x[:],
                sg_tm[:, _ii * D:(_ii + 1) * D], ALU.add), "dve")
            wait("sync", "dve", v1)
            st = op("sync", lambda e, _i=i, _b=bx: e.dma_start(
                out=x1_h[_i], in_=sgadd[_b][:]), "st", 16)
            sgadd_guard[bx] = st
        if want_dbg and "dbg_s0" in H:
            wait("gpsimd", "dve", sg_done)
            for fo in range(8):
                gv = op("gpsimd", lambda e, _f=fo, _t0=t0: e.dma_start(
                    out=H["dbg_s0"].rearrange("(c p) t -> c p t", p=128)
                    [_f][:, _t0:_t0 + TC],
                    in_=sg[:, _f * TC:(_f + 1) * TC]), "st", 16)
            wait("scalar", "st", kb.cnt("st"))
            wait("vector", "st", kb.cnt("st"))
    return


def emit_l0_sraw(kb, pb):
    H, op, wait, drain = kb.H, kb.op, kb.wait, kb.drain
    sb = kb.sb
    smalls = sb["smalls"]
    sr_w = kb.S("sr_w", (128, 8 * 32), BF16)
    srpack = kb.S("srpack", (32, (T + 3) + T), F32)
    sr_z = srpack[0:32, 0:T + 3]
    sr_c = srpack[0:32, T + 3:2 * T + 3]
    sr_t = kb.sb["rp32"][0:32, T:2 * T]
    svec = kb.S("svec", (32, 8), F32)
    dv3 = kb.S("dv3", (1, 4), F32)
    dv128 = kb.S("dv128", (128, 3), F32)
    ones1 = kb.S("ones1", (1, 128), F32)

    in_ap = bass.AP(H["l0_win"].tensor, 2048,
                    [[2080, 128], [128 * 2080, 8], [1, 32]])
    l1 = op("gpsimd", lambda e, _i=in_ap: e.dma_start(out=sr_w[:], in_=_i),
            "gw", 16)
    op("sync", lambda e: e.dma_start(out=svec[:, 0:4], in_=H["l0_wconv_sr"][:, :]),
       "ld", 16)
    op("sync", lambda e: e.dma_start(out=svec[:, 4:5], in_=H["l0_bconv_sr"][:, :]),
       "ld", 16)
    op("sync", lambda e: e.dma_start(out=svec[:, 5:6], in_=H["l0_slopes"][:, :]),
       "ld", 16)
    op("sync", lambda e: e.dma_start(out=svec[:, 6:7], in_=H["l0_score"][:, :]),
       "ld", 16)
    op("sync", lambda e: e.dma_start(out=dv3[:, 0:3], in_=H["l0_deriv"][:, :]),
       "ld", 16)
    wait("scalar", "ld", kb.cnt("ld"))
    wait("vector", "ld", kb.cnt("ld"))
    wait("tensor", "gw", l1)

    a_e = op("scalar", lambda e: e.activation(dv3[:1, 0:3], dv3[:1, 0:3],
                                              AF.Exp, bias=0.0, scale=1.0),
             "act")
    wait("vector", "act", a_e)
    op("vector", lambda e: e.tensor_reduce(dv3[:1, 3:4], dv3[:1, 0:3],
                                           AX.X, ALU.add))
    drain("vector")
    op("vector", lambda e: e.reciprocal(dv3[:1, 3:4], dv3[:1, 3:4]))
    drain("vector")
    op("vector", lambda e: e.tensor_scalar(
        out=dv3[:1, 0:3], in0=dv3[:1, 0:3], scalar1=dv3[:1, 3:4],
        scalar2=None, op0=ALU.mult))
    v_w = op("vector", lambda e: e.memset(ones1[:], 1.0), "dve")
    kb.drain("vector")
    wait("tensor", "dve", v_w)
    mmb = op("tensor", lambda e: e.matmul(pb[1][:, 0:3], ones1[:],
                                          dv3[:1, 0:3], start=True, stop=True),
             "te")
    wait("vector", "te", mmb)
    op("vector", lambda e: e.tensor_copy(dv128[:], pb[1][:, 0:3]))
    kb.drain("vector")
    op("vector", lambda e: e.tensor_copy(smalls[:, 20:21], dv128[:, 0:1]))
    op("vector", lambda e: e.tensor_copy(smalls[:, 21:22], dv128[:, 1:2]))
    op("vector", lambda e: e.tensor_scalar(out=smalls[:, 22:23],
                                           in0=dv128[:, 2:3], scalar1=-1.0,
                                           scalar2=None, op0=ALU.mult))
    drain("vector")

    a_sp1 = op("scalar", lambda e: e.activation(svec[:32, 7:8], svec[:32, 5:6],
                                                AF.Exp, bias=0.0, scale=1.0),
               "act")
    kb.drain("scalar")
    a_sp2 = op("scalar", lambda e: e.activation(svec[:32, 7:8], svec[:32, 7:8],
                                                AF.Ln, bias=1.0, scale=1.0),
               "act")

    op("vector", lambda e: e.memset(srpack[0:32, 0:3], 0.0))
    drain("vector")
    for tt in range(4):
        bank = pb[0]
        mmv = None
        for dc in range(8):
            mov = sb["xn_fm"][:, dc * T + tt * 512:dc * T + tt * 512 + 512]
            sta = sr_w[:, dc * 32:dc * 32 + 32]
            mmv = op("tensor", lambda e, _b=bank, _s=sta, _m=mov, _d=dc:
                     e.matmul(_b[:32, :], _s, _m, start=(_d == 0),
                              stop=(_d == 7)), "te" if dc == 7 else None)
        wait("vector", "te", mmv)
        ev = op("vector", lambda e, _b=bank, _t=tt: e.tensor_copy(
            srpack[0:32, 3 + _t * 512:3 + _t * 512 + 512], _b[0:32, :]), "dve")
        kb.drain("vector")
        wait("tensor", "dve", ev)
    op("vector", lambda e: e.tensor_scalar(
        out=sr_c, in0=srpack[0:32, 0:T], scalar1=svec[:32, 0:1], scalar2=None,
        op0=ALU.mult))
    drain("vector")
    for tap in range(1, 4):
        op("vector", lambda e, _t=tap: e.scalar_tensor_tensor(
            out=sr_c, in0=srpack[0:32, _t:T + _t], scalar=svec[:32, _t:_t + 1],
            in1=sr_c, op0=ALU.mult, op1=ALU.add))
        drain("vector")
    op("vector", lambda e: e.tensor_tensor(svec[:32, 4:5], svec[:32, 4:5],
                                           svec[:32, 6:7], ALU.mult))
    drain("vector")
    op("vector", lambda e: e.tensor_scalar(
        out=sr_c, in0=sr_c, scalar1=svec[:32, 6:7],
        scalar2=svec[:32, 4:5], op0=ALU.mult, op1=ALU.add))
    drain("vector")
    op("vector", lambda e: e.tensor_scalar(
        out=sr_c, in0=sr_c, scalar1=20.0, scalar2=-20.0,
        op0=ALU.min, op1=ALU.max))
    drain("vector")
    wait("vector", "act", a_sp2)
    op("vector", lambda e: e.tensor_scalar(
        out=sr_t, in0=sr_t, scalar1=svec[:32, 7:8],
        scalar2=None, op0=ALU.mult))
    drain("vector")
    v_u = op("vector", lambda e: e.tensor_tensor(sr_c, sr_c, sr_t,
                                                 ALU.add), "dve")
    wait("scalar", "dve", v_u)
    a_pw = op("scalar", lambda e: e.activation(sb["rp32"][0:32, 0:T], sr_c,
                                               AF.Exp, bias=0.0, scale=1.0),
              "act")
    wait("vector", "act", a_pw)
    v_pwb = op("vector", lambda e: e.tensor_copy(sb["rpbf"][0:32, :],
                                                 sb["rp32"][0:32, 0:T]), "dve")
    drain("vector")
    if "dbg_pw" in H:
        wait("sync", "dve", v_pwb)
        op("sync", lambda e: e.dma_start(out=H["dbg_pw"][:, :],
                                         in_=sb["rp32"][0:32, 0:T]), "st", 16)
    return v_pwb



def host_l1_inputs(inputs):
    h = {}
    Wqkv = inputs["L1_attn_Wqkv"]
    bqkv = inputs["L1_attn_bqkv"]
    cols = np.zeros((1024, 32 * 128), np.float32)
    bias = np.zeros(32 * 128, np.float32)

    def put(col, src_row):
        cols[:, col] = Wqkv[src_row]
        bias[col] = bqkv[src_row]

    for hh in range(16):
        for d in range(16):
            put((hh // 4) * 128 + (hh % 4) * 32 + d, hh * 64 + d)
            put(512 + (hh // 4) * 128 + (hh % 4) * 32 + d, 1024 + hh * 64 + d)
        for d in range(16, 64):
            put(1024 + hh * 64 + (d - 16), hh * 64 + d)
            put(2048 + hh * 64 + (d - 16), 1024 + hh * 64 + d)
        for d in range(64):
            put(3072 + (hh // 2) * 128 + (hh % 2) * 64 + d, 2048 + hh * 64 + d)
    h["l1_wqkvT"] = np.ascontiguousarray(
        cols.reshape(8, 128, 32, 128).transpose(2, 1, 0, 3)
        .reshape(32, 128, 1024))
    h["l1_bqkv"] = np.ascontiguousarray(bias.reshape(32, 128).T)

    def oc_tiles(wT):
        return np.ascontiguousarray(
            wT.reshape(8, 128, 8, 128).transpose(2, 1, 0, 3)
            .reshape(8, 128, 1024))

    h["l1_wqaT"] = oc_tiles(inputs["L1_attn_Wqa"].T)
    h["l1_wkaT"] = oc_tiles(inputs["L1_attn_Wka"].T)
    h["l1_wdT"] = oc_tiles(inputs["L1_attn_Wd"].T)
    h["l1_bqa"] = np.ascontiguousarray(inputs["L1_attn_bqa"].reshape(8, 128).T)
    h["l1_bka"] = np.ascontiguousarray(inputs["L1_attn_bka"].reshape(8, 128).T)
    h["l1_bd"] = np.ascontiguousarray(inputs["L1_attn_bd"].reshape(8, 128).T)
    h["l1_gwT"] = np.ascontiguousarray(inputs["L1_ffn_gate_w"].T)
    h["l1_wg"] = np.ascontiguousarray(
        inputs["L1_ffn_wg"].reshape(8, 8, 128, 2, 128)
        .transpose(0, 2, 1, 3, 4).reshape(8, 128, 2048))
    h["l1_wu"] = np.ascontiguousarray(
        inputs["L1_ffn_wu"].reshape(8, 8, 128, 2, 128)
        .transpose(0, 2, 1, 3, 4).reshape(8, 128, 2048))
    h["l1_wd"] = np.ascontiguousarray(
        inputs["L1_ffn_wd"].reshape(8, 2, 128, 8, 128)
        .transpose(3, 2, 0, 1, 4).reshape(8, 128, 2048))
    return h


def host_l1_consts():
    c = {}
    P = np.zeros((128, 128), BF16NP)
    for hh in range(4):
        for d in range(8):
            P[hh * 32 + d + 8, hh * 32 + d] = -1
            P[hh * 32 + d, hh * 32 + d + 8] = 1
    c["c_rot"] = P
    inv_freq = (1.0 / 10000.0 ** (np.arange(0, 16, 2) / 16.0)).astype(np.float64)
    t = np.arange(T, dtype=np.float64)
    ang = t[None, :] * inv_freq[:, None]
    cosf = np.zeros((128, T), np.float32)
    sinf = np.zeros((128, T), np.float32)
    for hh in range(4):
        for d in range(16):
            cosf[hh * 32 + d] = np.cos(ang[d % 8])
            sinf[hh * 32 + d] = np.sin(ang[d % 8])
    c["c_cosf"] = cosf
    c["c_sinf"] = sinf
    m = np.zeros((128, 256), BF16NP)
    for j in range(128):
        for q in range(128):
            if 0 <= q - j <= 15:
                m[j, q] = 1
            if 0 <= (q + 128) - j <= 15:
                m[j, 128 + q] = 1
    c["c_wmask"] = m
    c["c_cmask"] = np.triu(np.ones((128, 128), np.float32)).astype(BF16NP)
    c["c_ones1b"] = np.ones((128, 1), BF16NP)
    ob = np.zeros((128, 12 * 16), BF16NP)
    for j in range(4):
        for r in range(128):
            ob[r, j * 16 + 4 * j + r // 32] = 1
    for p in range(8):
        for r in range(128):
            ob[r, (4 + p) * 16 + 2 * p + r // 64] = 1
    c["c_oblk"] = ob
    return c


def emit_l1(kb, want_dbg=False):
    raise NotImplementedError

def _emit_ln_tile(kb, src_tile, dst_tile):
    op, wait, drain = kb.op, kb.wait, kb.drain
    sb = kb.sb
    sq_done = op("scalar", lambda e: e.activation(
        sb["sqd"][:], src_tile, AF.Square, bias=0.0, scale=1.0,
        accum_out=sb["red"][:, 1:2]), "act")
    op("vector", lambda e: e.tensor_reduce(sb["red"][:, 0:1], src_tile,
                                           AX.X, ALU.add))
    drain("vector")
    op("vector", lambda e: e.tensor_scalar(
        out=sb["red"][:, 2:3], in0=sb["red"][:, 0:1], scalar1=1.0 / D,
        scalar2=None, op0=ALU.mult))
    drain("vector")
    op("vector", lambda e: e.tensor_tensor(
        sb["red"][:, 3:4], sb["red"][:, 2:3], sb["red"][:, 2:3], ALU.mult))
    wait("vector", "act", sq_done)
    drain("vector")
    op("vector", lambda e: e.scalar_tensor_tensor(
        out=sb["red"][:, 4:5], in0=sb["red"][:, 1:2], scalar=1.0 / D,
        in1=sb["red"][:, 3:4], op0=ALU.mult, op1=ALU.subtract))
    drain("vector")
    v_var = op("vector", lambda e: e.tensor_scalar(
        out=sb["red"][:, 4:5], in0=sb["red"][:, 4:5], scalar1=1e-5,
        scalar2=None, op0=ALU.add), "dve")
    wait("scalar", "dve", v_var)
    op("scalar", lambda e: e.activation(sb["red"][:, 5:6], sb["red"][:, 4:5],
                                        AF.Ln, bias=0.0, scale=1.0))
    kb.drain("scalar")
    a_r = op("scalar", lambda e: e.activation(
        sb["red"][:, 6:7], sb["red"][:, 5:6], AF.Exp, bias=0.0, scale=-0.5),
        "act")
    op("vector", lambda e: e.tensor_scalar(
        out=sb["red"][:, 7:8], in0=sb["red"][:, 2:3], scalar1=-1.0,
        scalar2=None, op0=ALU.mult))
    wait("vector", "act", a_r)
    drain("vector")
    return op("vector", lambda e: e.tensor_scalar(
        out=dst_tile, in0=src_tile, scalar1=kb.sb["red"][:, 7:8],
        scalar2=kb.sb["red"][:, 6:7], op0=ALU.add, op1=ALU.mult), "dve")


def emit_l1(kb, want_dbg=False):
    H, op, wait, drain = kb.H, kb.op, kb.wait, kb.drain
    sb = kb.sb
    PS = kb.PSALL
    pb = [PS[:, i * 512:(i + 1) * 512] for i in range(8)]
    smalls = sb["smalls"]
    SCALE = 0.125

    arenaA = sb["xn_fm"]
    arenaB = sb["g0"]

    ones1b = kb.S("ones1b", (128, 1), BF16)
    onesr1 = kb.S("onesr1", (1, 128), BF16)
    ones16 = kb.S("ones16", (16, 128), F32)
    oblk = kb.S("oblk", (128, 12 * 16), BF16)
    bqkv_t = kb.S("bqkv_t", (128, 32), F32)
    bqa_t = kb.S("bqa_t", (128, 8), F32)
    bka_t = kb.S("bka_t", (128, 8), F32)
    bd_t = kb.S("bd_t", (128, 8), F32)
    gw_t = kb.S("gw_t", (128, 8 * 8), BF16)
    qtmp2 = kb.S("qtmp2", (128, 16), F32)
    diag16 = kb.S("diag16", (16, 16), F32)
    identf = kb.S("identf", (128, 128), F32)
    rdc = kb.S("rdc", (128, 2), F32)
    nrm = kb.S("nrm", (16, 32), F32)
    negmb1 = kb.S("negmb1", (128, 16), F32)
    negmb2 = kb.S("negmb2", (128, 16), F32)
    dentm = kb.S("dentm", (128, 256), F32)
    relub = kb.S("relub", (128, 512), BF16)
    ubuf = kb.S("ubuf", (128, 512), BF16)
    g1st = [kb.S(f"g1st{i}", (128, 512), BF16) for i in range(2)]
    xn1st = [kb.S(f"xn1st{i}", (128, D), BF16) for i in range(2)]
    xn1ft = [kb.S(f"xn1ft{i}", (128, D), BF16) for i in range(2)]
    kb.M_MOE = kb.marker()
    masksf = kb.S("masksf", (8, T), BF16)
    maskb = kb.S("maskb", (128, 8 * 512), BF16)
    onehot = kb.S("onehot", (128, NT * 8), BF16)
    xnb = kb.S("xnb", (128, 8 * 512), BF16)
    wgu = [kb.S(f"wgu{i}", (128, 32 * 128), BF16) for i in range(2)]
    inter = kb.S("inter", (128, 16 * 512), BF16)
    wst = [kb.S(f"wst{i}", (128, 16 * 128), BF16) for i in range(2)]
    xt_m = [kb.S(f"xt{i}_l1m", (128, D), F32) for i in range(2)]
    mrow = kb.S("mrow", (1, 512), BF16)
    kb.sb["xt0"], kb.sb["xt1"] = xt_m
    qrest = arenaA
    v_tm = arenaB

    OB = PS[0:64, 1024:3072]
    DENROW = PS[64:65, 1024:3072]

    for nm, tile in (("c_ones1b", ones1b), ("c_oblk", oblk),
                     ("l1_bqkv", bqkv_t), ("l1_bqa", bqa_t),
                     ("l1_bka", bka_t), ("l1_bd", bd_t)):
        op("sync", lambda e, _n=nm, _t=tile: e.dma_start(out=_t[:], in_=H[_n][:, :]),
           "ld", 16)
    op("gpsimd", lambda e: e.dma_start(
        out=gw_t[:], in_=bass.AP(H["l1_gwT"].tensor, 0,
                                 [[8, 128], [1024, 8], [1, 8]])), "gw", 16)
    ld0 = kb.cnt("ld")
    lw0 = kb.cnt("gw")
    wait("vector", "ld", ld0)
    wait("vector", "gw", lw0)
    op("vector", lambda e: e.memset(ones16[:], 1.0))
    op("vector", lambda e: e.memset(onesr1[:], 1.0))
    op("vector", lambda e: e.memset(smalls[:, 5:6], float(np.log(0.125))))
    op("vector", lambda e: e.tensor_copy(identf[:], sb["ident"][:]))
    drain("vector")

    x1v = H["x1"].rearrange("(n p) d -> n p d", p=128)
    tp_guard = [None, None]
    ft_guard = [0, 0]
    st_guard = [None, None]
    oh_done = None
    xtb_m = [sb["xt0"], sb["xt1"]]
    for i in range(NT):
        b = i % 2
        ldv = op("sync", lambda e, _i=i, _x=xtb_m[b]: e.dma_start(
            out=_x[:], in_=x1v[_i]), "ld", 16)
        wait("vector", "ld", ldv)
        wait("scalar", "ld", ldv)
        if i >= 2 and tp_guard[b] is not None:
            wait("vector", "tp", tp_guard[b])
        v_xn = _emit_ln_tile(kb, xtb_m[b][:], xn1st[b][:])
        wait("sync", "dve", v_xn)
        if i >= 2 and ft_guard[b]:
            wait("sync", "te", ft_guard[b])
        if i >= 2 and st_guard[b]:
            wait("sync", "st", st_guard[b])
        sa = xn1st[b][:].rearrange("p (c q) -> p c q", q=128)
        da = xn1ft[b][:].rearrange("p (c q) -> p c q", q=128)
        tpv = op("sync", lambda e, _s=sa, _d=da: e.dma_start_transpose(_d, _s),
                 "tp", 16)
        tp_guard[b] = tpv
        out_ap = bass.AP(H["xn1"].tensor, i * 128,
                         [[2048, 128], [128 * 2048, 8], [1, 128]])
        wait("sync", "tp", tpv)
        st1 = op("sync", lambda e, _o=out_ap, _b=b: e.dma_start(
            out=_o, in_=xn1ft[_b][:].rearrange("p (c q) -> p c q", q=128)),
            "st", 16)
        st_guard[b] = st1
        wait("tensor", "tp", tpv)
        wait("tensor", "gw", lw0)
        mmv = None
        for dc in range(8):
            mmv = op("tensor", lambda e, _b=b, _d=dc: e.matmul(
                pb[7][:, 0:8], xn1ft[_b][:, _d * 128:(_d + 1) * 128],
                gw_t[:, _d * 8:(_d + 1) * 8], start=(_d == 0), stop=(_d == 7)),
                "te" if dc == 7 else None)
        ft_guard[b] = mmv
        wait("vector", "te", mmv)
        wait("vector", "st", st1)
        op("vector", lambda e: e.tensor_reduce(sb["red"][:, 8:9], pb[7][:, 0:8],
                                               AX.X, ALU.max))
        drain("vector")
        op("vector", lambda e: e.tensor_scalar(
            out=qtmp2[:, 0:8], in0=pb[7][:, 0:8], scalar1=sb["red"][:, 8:9],
            scalar2=None, op0=ALU.is_ge))
        drain("vector")
        op("vector", lambda e: e.tensor_tensor_scan(
            qtmp2[:, 8:16], qtmp2[:, 0:8], qtmp2[:, 0:8], 0.0,
            ALU.add, ALU.bypass))
        drain("vector")
        op("vector", lambda e: e.tensor_scalar(
            out=qtmp2[:, 8:16], in0=qtmp2[:, 8:16], scalar1=1.0,
            scalar2=None, op0=ALU.is_le))
        drain("vector")
        oh_done = op("vector", lambda e, _i=i: e.tensor_tensor(
            onehot[:, _i * 8:(_i + 1) * 8], qtmp2[:, 0:8], qtmp2[:, 8:16],
            ALU.mult), "dve")
        kb.drain("vector")
        wait("tensor", "dve", oh_done)
    wait("tensor", "dve", oh_done)
    mtp_out = pb[7][0:8, 128:192].bitcast(BF16)
    for i in range(NT):
        mt = op("tensor", lambda e, _i=i: e.transpose(
            mtp_out, onehot[:, _i * 8:(_i + 1) * 8], sb["ident"][:]), "te")
        wait("vector", "te", mt)
        ev = op("vector", lambda e, _i=i: e.tensor_copy(
            masksf[:, _i * 128:(_i + 1) * 128], mtp_out), "dve")
        kb.drain("vector")
        wait("tensor", "dve", ev)

    wgu_guard = [0, 0]
    wd_guard = [0, 0]
    g1_guard = [None, None]
    for tt in range(4):
        wait("sync", "te", kb.cnt("te"))
        wait("vector", "te", kb.cnt("te"))
        wait("sync", "dve", kb.cnt("dve"))
        for e_ in range(8):
            op("sync", lambda e, _e=e_, _t=tt: e.dma_start(
                out=maskb[:, _e * 512:(_e + 1) * 512],
                in_=bass.AP(masksf, _e * T + _t * 512,
                            [[T, 1], [0, 128], [1, 512]])), "ld", 16)
        lx = op("sync", lambda e, _t=tt: e.dma_start(
            out=xnb[:], in_=bass.AP(H["xn1"].tensor, _t * 512,
                                    [[2048, 128], [128 * 2048, 8], [1, 512]])),
            "ld", 16)
        wait("tensor", "ld", lx)
        wait("vector", "ld", lx)
        epi_hist = []
        for e_ in range(8):
            bwf = e_ % 2
            bb = 4 * (e_ % 2)
            if wgu_guard[bwf]:
                kb.wait("gpsimd", "te", wgu_guard[bwf])
            in_g = bass.AP(H["l1_wg"].tensor, e_ * 262144,
                           [[2048, 128], [1, 2048]])
            in_u = bass.AP(H["l1_wu"].tensor, e_ * 262144,
                           [[2048, 128], [1, 2048]])
            op("gpsimd", lambda e, _i=in_g, _b=bwf: e.dma_start(
                out=wgu[_b][:, 0:2048], in_=_i), "gw", 16)
            lwg = op("gpsimd", lambda e, _i=in_u, _b=bwf: e.dma_start(
                out=wgu[_b][:, 2048:4096], in_=_i), "gw", 16)
            wait("tensor", "gw", lwg)
            if len(epi_hist) >= 2:
                wait("tensor", "act", epi_hist[-2])
            mmg = [None, None]
            for gu in range(2):
                for dc in range(2):
                    bank = pb[bb + gu * 2 + dc]
                    mm2 = None
                    for hc in range(8):
                        sta = wgu[bwf][:, gu * 2048 + (hc * 2 + dc) * 128:
                                       gu * 2048 + (hc * 2 + dc) * 128 + 128]
                        mov = xnb[:, hc * 512:(hc + 1) * 512]
                        mm2 = op("tensor", lambda e, _b=bank, _s=sta, _m=mov,
                                 _h=hc: e.matmul(_b, _s, _m, start=(_h == 0),
                                                 stop=(_h == 7)),
                                 "te" if hc == 7 else None)
                    mmg[gu] = mm2
            wgu_guard[bwf] = mmg[1]
            wait("scalar", "te", mmg[1])
            a_last = None
            for dc in range(2):
                if kb.cnt("dve"):
                    wait("scalar", "dve", kb.cnt("dve"))
                a_rel = op("scalar", lambda e, _d=bb + dc: e.activation(
                    relub[:], pb[_d], AF.Relu, bias=0.0, scale=1.0), "act")
                a_ue = op("scalar", lambda e, _d=bb + 2 + dc: e.activation(
                    ubuf[:], pb[_d], AF.Identity, bias=0.0, scale=1.0),
                    "act")
                a_last = a_ue
                wait("vector", "act", a_ue)
                op("vector", lambda e: e.tensor_tensor(relub[:], relub[:],
                                                       relub[:], ALU.mult))
                drain("vector")
                op("vector", lambda e: e.tensor_tensor(relub[:], relub[:],
                                                       ubuf[:], ALU.mult))
                drain("vector")
                evi = op("vector", lambda e, _e=e_, _d=dc: e.scalar_tensor_tensor(
                    out=inter[:, (_e * 2 + _d) * 512:(_e * 2 + _d + 1) * 512],
                    in0=relub[:], scalar=1.0,
                    in1=maskb[:, _e * 512:(_e + 1) * 512],
                    op0=ALU.mult, op1=ALU.mult), "dve")
            epi_hist.append(a_last)
        wait("tensor", "dve", kb.cnt("dve"))
        wait("tensor", "act", kb.cnt("act"))
        for fo in range(8):
            bwf = fo % 2
            if wd_guard[bwf]:
                kb.wait("gpsimd", "te", wd_guard[bwf])
            in_d = bass.AP(H["l1_wd"].tensor, fo * 262144,
                           [[2048, 128], [1, 2048]])
            lwd = op("gpsimd", lambda e, _i=in_d, _b=bwf: e.dma_start(
                out=wst[_b][:], in_=_i), "gw", 16)
            wait("tensor", "gw", lwd)
            bank = pb[5 + (fo % 2)]
            mmd = None
            for kc in range(16):
                sta = wst[bwf][:, kc * 128:(kc + 1) * 128]
                mov = inter[:, kc * 512:(kc + 1) * 512]
                mmd = op("tensor", lambda e, _b=bank, _s=sta, _m=mov, _k=kc:
                         e.matmul(_b, _s, _m, start=(_k == 0), stop=(_k == 15)),
                         "te" if kc == 15 else None)
            wd_guard[bwf] = mmd
            wait("scalar", "te", mmd)
            bg = fo % 2
            if g1_guard[bg] is not None:
                wait("scalar", "st", g1_guard[bg])
            a_g1 = op("scalar", lambda e, _b=bank, _g=bg: e.activation(
                g1st[_g][:], _b, AF.Silu, bias=0.0, scale=1.0), "act")
            wait("sync", "act", a_g1)
            g1ap = bass.AP(H["g1"].tensor, fo * 128 * 2048 + tt * 512,
                           [[2048, 128], [1, 512]])
            g1_guard[bg] = op("sync", lambda e, _o=g1ap, _g=bg: e.dma_start(
                out=_o, in_=g1st[_g][:]), "st", 16)
            wait("tensor", "act", a_g1)
    kb.barrier()
    kb.free_to(kb.M_MOE)
    kb.S("A_tm", (128, 8 * T), BF16)
    kb.M_K = kb.marker()
    kb.S("krest", (128, 8 * T), BF16)
    kb.S("qsl", (128, 4 * T), BF16)
    kb.S("ksl", (128, 4 * T), BF16)
    emit_l1_attn(kb, want_dbg)
    return


def emit_l1_attn(kb, want_dbg=False):
    H, op, wait, drain = kb.H, kb.op, kb.wait, kb.drain
    sb = kb.sb
    PS = kb.PSALL
    pb = [PS[:, i * 512:(i + 1) * 512] for i in range(8)]
    smalls = sb["smalls"]
    qsl, ksl = sb["qsl"], sb["ksl"]
    krest = sb["krest"]
    qrest = sb["xn_fm"]
    v_tm = sb["g0"]
    A_tm = sb["A_tm"]
    nrm = sb["nrm"]
    negmb1 = sb["negmb1"]
    oblk = sb["oblk"]
    ones1b = sb["ones1b"]
    relub, ubuf = sb["relub"], sb["ubuf"]
    M_R = kb.marker()
    xnb = kb.S("xnb_q", (128, 8 * 512), BF16)
    wst = [kb.S(f"wstq{i}", (128, 8 * 128), BF16) for i in range(2)]
    rot = kb.S("rot", (128, 128), BF16)
    cosl = kb.S("cosl", (128, 512), BF16)
    sinl = kb.S("sinl", (128, 512), BF16)
    op("sync", lambda e: e.dma_start(out=rot[:], in_=H["c_rot"][:, :]),
       "ld", 16)
    wait("tensor", "ld", kb.cnt("ld"))

    wst_guard = [0, 0]
    evac_hist = []
    vtp_hist = []
    for tt in range(4):
        wait("sync", "te", kb.cnt("te"))
        lx = op("sync", lambda e, _t=tt: e.dma_start(
            out=xnb[:], in_=bass.AP(H["xn1"].tensor, _t * 512,
                                    [[2048, 128], [128 * 2048, 8], [1, 512]])),
            "ld", 16)
        wait("tensor", "ld", lx)
        for oc in range(32):
            bws = oc % 2
            if wst_guard[bws]:
                kb.wait("gpsimd", "te", wst_guard[bws])
            in_w = bass.AP(H["l1_wqkvT"].tensor, oc * 131072,
                           [[1024, 128], [1, 1024]])
            lww = op("gpsimd", lambda e, _i=in_w, _b=bws: e.dma_start(
                out=wst[_b][:], in_=_i), "gw", 16)
            wait("tensor", "gw", lww)
            if len(evac_hist) >= 4:
                wait("tensor", "act", evac_hist[-4])
            bank = pb[oc % 4]
            mmv = None
            for dc in range(8):
                mmv = op("tensor", lambda e, _b=bank, _s=wst[bws][:, dc * 128:(dc + 1) * 128],
                         _m=xnb[:, dc * 512:(dc + 1) * 512], _d=dc:
                         e.matmul(_b, _s, _m, start=(_d == 0), stop=(_d == 7)),
                         "te" if dc == 7 else None)
            wst_guard[bws] = mmv
            wait("scalar", "te", mmv)
            if oc < 4:
                dst = qsl[:, oc * T + tt * 512:oc * T + tt * 512 + 512]
            elif oc < 8:
                dst = ksl[:, (oc - 4) * T + tt * 512:(oc - 4) * T + tt * 512 + 512]
            elif oc < 16:
                dst = qrest[:, (oc - 8) * T + tt * 512:(oc - 8) * T + tt * 512 + 512]
            elif oc < 24:
                dst = krest[:, (oc - 16) * T + tt * 512:(oc - 16) * T + tt * 512 + 512]
            else:
                dst = v_tm[:, (oc - 24) * T + tt * 512:(oc - 24) * T + tt * 512 + 512]
            if oc < 24:
                a_e = op("scalar", lambda e, _b=bank, _d=dst, _o=oc: e.activation(
                    _d, _b, AF.Identity, bias=sb["bqkv_t"][:, _o:_o + 1],
                    scale=1.0), "act")
                evac_hist.append(a_e)
            else:
                vstage = sb["relub"] if (oc % 2 == 0) else sb["ubuf"]
                if len(vtp_hist) >= 2:
                    wait("scalar", "tp", vtp_hist[-2])
                a_e = op("scalar", lambda e, _b=bank, _o=oc, _v=vstage:
                         e.activation(_v[:], _b, AF.Identity,
                                      bias=sb["bqkv_t"][:, _o:_o + 1],
                                      scale=1.0), "act")
                evac_hist.append(a_e)
                wait("sync", "act", a_e)
                pair = oc - 24
                sa = vstage[:].rearrange("p (i q) -> p i q", q=128)
                da = bass.AP(v_tm, pair * T + tt * 512,
                             [[8 * T, 128], [128, 4], [1, 128]])
                tpv = op("sync", lambda e, _s=sa, _d=da:
                         e.dma_start_transpose(_d, _s), "tp", 16)
                vtp_hist.append(tpv)
    wait("tensor", "act", kb.cnt("act"))
    wait("vector", "tp", kb.cnt("tp"))
    for tt in range(4):
        wait("gpsimd", "dve", kb.cnt("dve"))
        op("gpsimd", lambda e, _t=tt: e.dma_start(
            out=cosl[:], in_=H["c_cosf"][:, _t * 512:(_t + 1) * 512]),
            "gw", 16)
        lcs = op("gpsimd", lambda e, _t=tt: e.dma_start(
            out=sinl[:], in_=H["c_sinf"][:, _t * 512:(_t + 1) * 512]),
            "gw", 16)
        wait("vector", "gw", lcs)
        for sl in range(8):
            tens = qsl if sl < 4 else ksl
            j = sl % 4
            base = j * T + tt * 512
            mmr = op("tensor", lambda e, _t=tens, _b=base: e.matmul(
                pb[4], rot[:], _t[:, _b:_b + 512], start=True, stop=True),
                "te")
            wait("vector", "te", mmr)
            op("vector", lambda e, _t=tens, _b=base: e.tensor_tensor(
                _t[:, _b:_b + 512], _t[:, _b:_b + 512], cosl[:], ALU.mult))
            op("vector", lambda e: e.tensor_tensor(ubuf[:], pb[4], sinl[:],
                                                   ALU.mult))
            drain("vector")
            ev = op("vector", lambda e, _t=tens, _b=base: e.tensor_tensor(
                _t[:, _b:_b + 512], _t[:, _b:_b + 512], ubuf[:], ALU.add),
                "dve")
            wait("tensor", "dve", ev)
    if DIAG3:
        wait("sync", "dve", kb.cnt("dve"))
        wait("sync", "act", kb.cnt("act"))
        wait("sync", "tp", kb.cnt("tp"))
        for nm, tens, nch in (("dbg_qsl", qsl, 4), ("dbg_ksl", ksl, 4),
                              ("dbg_qr", qrest, 8), ("dbg_kr", krest, 8),
                              ("dbg_v", v_tm, 8)):
            for p in range(nch):
                op("sync", lambda e, _n=nm, _t=tens, _p=p: e.dma_start(
                    out=H[_n][:, _p * T:(_p + 1) * T],
                    in_=_t[:, _p * T:(_p + 1) * T]), "st", 16)
    kb.barrier()
    kb.free_to(M_R)
    ptw = [kb.S(f"ptw{i}", (128, 256), BF16) for i in range(2)]
    wmask = kb.S("wmask", (128, 256), BF16)
    cmask = kb.S("cmask", (128, 128), BF16)
    sqb = kb.S("sqb_w", (128, T), BF16)
    kb.sb["sqb"] = sqb
    for nm, tile in (("c_wmask", wmask), ("c_cmask", cmask)):
        op("sync", lambda e, _n=nm, _t=tile: e.dma_start(
            out=_t[:], in_=H[_n][:, :]), "ld", 16)
    wait("tensor", "ld", kb.cnt("ld"))
    wait("scalar", "ld", kb.cnt("ld"))
    wait("vector", "ld", kb.cnt("ld"))

    def emit_norms(groups, out_col0, negout):
        ng = len(groups)
        for g, (tens, coff, obs) in enumerate(groups):
            a_sq = op("scalar", lambda e, _t=tens, _c=coff: e.activation(
                sqb[:], _t[:, _c:_c + T], AF.Square, bias=0.0, scale=1.0),
                "act")
            wait("tensor", "act", a_sq)
            gmm = None
            for tt in range(4):
                gmm = op("tensor", lambda e, _tt=tt, _o=obs, _g=g, _ng=ng:
                         e.matmul(pb[_tt][0:16, :],
                                  oblk[:, _o * 16:(_o + 1) * 16],
                                  sqb[:, _tt * 512:(_tt + 1) * 512],
                                  start=(_g == 0), stop=(_g == _ng - 1)),
                         "te" if tt == 3 else None)
            wait("scalar", "te", gmm)
        mmn = kb.cnt("te")
        wait("vector", "te", mmn)
        for tt in range(4):
            op("vector", lambda e, _tt=tt, _c=out_col0: e.tensor_reduce(
                nrm[:, _c + _tt:_c + _tt + 1], pb[_tt][0:16, :], AX.X,
                ALU.max))
        drain("vector")
        v_nr = op("vector", lambda e, _c=out_col0: e.tensor_reduce(
            nrm[:, _c + 4:_c + 5], nrm[:, _c:_c + 4], AX.X, ALU.max), "dve")
        wait("tensor", "dve", v_nr)
        return mmn

    qgroups = [(qsl, j * T, j) for j in range(4)] + \
              [(qrest, p * T, 4 + p) for p in range(8)]
    kgroups = [(ksl, j * T, j) for j in range(4)] + \
              [(krest, p * T, 4 + p) for p in range(8)]
    emit_norms(qgroups, 0, None)
    emit_norms(kgroups, 8, None)
    emit_mb_tail(kb, nrm, negmb1)
    emit_l1_wattn(kb)
    return


def emit_mb_broadcast(kb, nrm, negout):
    op, wait, drain = kb.op, kb.wait, kb.drain
    sb = kb.sb
    PS = kb.PSALL
    pb5 = PS[:, 5 * 512:6 * 512]
    v_dg = op("vector", lambda e: e.tensor_scalar(
        out=sb["diag16"][0:16, 0:16], in0=sb["ident"][0:16, 0:16],
        scalar1=nrm[:, 19:20], scalar2=None, op0=ALU.mult),
        "dve")
    kb.drain("vector")
    wait("tensor", "dve", v_dg)
    mmb = op("tensor", lambda e: e.matmul(pb5[:, 0:16], sb["ones16"][:],
                                          sb["diag16"][0:16, 0:16],
                                          start=True, stop=True), "te")
    wait("vector", "te", mmb)
    v_mb = op("vector", lambda e: e.tensor_copy(negout[:], pb5[:, 0:16]),
              "dve")
    kb.drain("vector")
    wait("tensor", "dve", v_mb)
    wait("scalar", "dve", v_mb)


def emit_l1_wattn(kb, want_dbg=False):
    H, op, wait, drain = kb.H, kb.op, kb.wait, kb.drain
    sb = kb.sb
    PS = kb.PSALL
    pb = [PS[:, i * 512:(i + 1) * 512] for i in range(8)]
    qsl, ksl = sb["qsl"], sb["ksl"]
    krest, qrest = sb["krest"], sb["xn_fm"]
    v_tm, A_tm = sb["g0"], sb["A_tm"]
    ptw = [sb["ptw0"], sb["ptw1"]]
    negmb1 = sb["negmb1"]
    rdc = sb["rdc"]
    ones1b = sb["ones1b"]
    wmask = sb["wmask"]

    ap_hist, aA_hist, vrd_hist = [], [], []
    for h in range(16):
        slab_c, srow = h // 4, (h % 4) * 32
        rest_c, rrow = h // 2, (h % 2) * 64
        pair, half = h // 2, h % 2
        for jt in range(16):
            bs = jt % 2
            Sb = pb[bs]
            ncols = 256 if jt < 15 else 128
            if len(ap_hist) >= 2:
                wait("tensor", "act", ap_hist[-2])
            ksliceA = bass.AP(ksl, srow * (4 * T) + slab_c * T + jt * 128,
                              [[4 * T, 32], [1, 128]])
            qslA = bass.AP(qsl, srow * (4 * T) + slab_c * T + jt * 128,
                           [[4 * T, 32], [1, ncols]])
            op("tensor", lambda e, _S=Sb, _k=ksliceA, _q=qslA, _n=ncols,
               _r=srow: e.matmul(_S[:, 0:_n], _k, _q, start=True, stop=False,
                                 tile_position=(_r, 0)))
            kB = bass.AP(krest, rrow * (8 * T) + rest_c * T + jt * 128,
                         [[8 * T, 64], [1, 128]])
            qB = bass.AP(qrest, rrow * (8 * T) + rest_c * T + jt * 128,
                         [[8 * T, 64], [1, ncols]])
            mmS = op("tensor", lambda e, _S=Sb, _k=kB, _q=qB, _n=ncols,
                     _r=rrow: e.matmul(_S[:, 0:_n], _k, _q, start=False,
                                       stop=True, tile_position=(_r, 0)),
                     "te")
            wait("scalar", "te", mmS)
            a_p = op("scalar", lambda e, _S=Sb, _b=bs, _n=ncols, _h=h:
                     e.activation(ptw[_b][:, 0:_n], _S[:, 0:_n], AF.Exp,
                                  bias=negmb1[:, _h:_h + 1], scale=0.125),
                     "act")
            ap_hist.append(a_p)
            wait("vector", "act", a_p)
            v_p = op("vector", lambda e, _b=bs, _n=ncols: e.tensor_tensor(
                ptw[_b][:, 0:_n], ptw[_b][:, 0:_n], wmask[:, 0:_n], ALU.mult),
                "dve")
            wait("tensor", "dve", v_p)
            if len(aA_hist) >= 2:
                wait("tensor", "act", aA_hist[-2])
                wait("tensor", "dve", vrd_hist[-2])
            ob = pb[2 + bs]
            vsl = v_tm[:, pair * T + jt * 128 + half * 64:
                       pair * T + jt * 128 + half * 64 + 64]
            op("tensor", lambda e, _o=ob, _p=ptw[bs][:, 0:128], _v=vsl:
               e.matmul(_o[:, 0:64], _p, _v, start=True, stop=False))
            if jt > 0:
                op("tensor", lambda e, _o=ob, _p=ptw[bs][:, 0:128]:
                   e.matmul(_o[:, 64:65], _p, ones1b[:], start=False,
                            stop=False))
                vsl2 = v_tm[:, pair * T + (jt - 1) * 128 + half * 64:
                            pair * T + (jt - 1) * 128 + half * 64 + 64]
                pprev = ptw[(jt - 1) % 2][:, 128:256]
                op("tensor", lambda e, _o=ob, _p=pprev, _v=vsl2:
                   e.matmul(_o[:, 0:64], _p, _v, start=False, stop=False))
                mmP = op("tensor", lambda e, _o=ob, _p=pprev:
                         e.matmul(_o[:, 64:65], _p, ones1b[:], start=False,
                                  stop=True), "te")
            else:
                mmP = op("tensor", lambda e, _o=ob, _p=ptw[bs][:, 0:128]:
                         e.matmul(
                    _o[:, 64:65], _p, ones1b[:],
                    start=False, stop=True), "te")
            wait("vector", "te", mmP)
            v_rd = op("vector", lambda e, _o=ob: e.reciprocal(
                rdc[:, 0:1], _o[:, 64:65]), "dve")
            vrd_hist.append(v_rd)
            wait("scalar", "dve", v_rd)
            adst = A_tm[:, pair * T + jt * 128 + half * 64:
                        pair * T + jt * 128 + half * 64 + 64]
            a_A = op("scalar", lambda e, _o=ob, _d=adst: e.activation(
                _d, _o[:, 0:64], AF.Identity, bias=0.0, scale=rdc[:, 0:1]),
                "act")
            aA_hist.append(a_A)
            wait("vector", "act", a_A)
    if DIAG3:
        wait("sync", "act", kb.cnt("act"))
        for p in range(8):
            op("sync", lambda e, _p=p: e.dma_start(
                out=H["dbg_A3"][:, _p * T:(_p + 1) * T],
                in_=A_tm[:, _p * T:(_p + 1) * T]), "st", 16)
    kb.barrier()
    kb.free_to(kb.M_K)
    emit_l1_tail(kb, want_dbg)


def emit_l1_tail(kb, want_dbg=False):
    H, op, wait, drain = kb.H, kb.op, kb.wait, kb.drain
    sb = kb.sb
    PS = kb.PSALL
    pb = [PS[:, i * 512:(i + 1) * 512] for i in range(8)]
    smalls = sb["smalls"]
    A_tm = sb["A_tm"]
    A_fm = kb.S("A_fm", (128, 8 * T), BF16)
    den_fm = kb.S("den_fm2", (16, T), F32)
    den_stage = kb.S("den_stage", (1, T), F32)
    kb.sb["den_fm"] = den_fm
    wst = [kb.S(f"wstd{i}", (128, 8 * 128), BF16) for i in range(2)]
    ptcol = kb.S("ptcol", (128, 2048), BF16)
    outsb = kb.S("outsb", (64, 2 * T), BF16)
    cmask = kb.S("cmask2", (128, 128), BF16)
    xt_t = [kb.S(f"xt{i}_l1t", (128, D), F32) for i in range(2)]
    xnf_t = [kb.S(f"xnf{i}_l1t", (128, D), F32) for i in range(2)]
    kb.sb["xt0"], kb.sb["xt1"] = xt_t
    kb.sb["xnf0"], kb.sb["xnf1"] = xnf_t
    kb.sb["sqb"] = ptcol
    op("sync", lambda e: e.dma_start(out=cmask[:], in_=H["c_cmask"][:, :]),
       "ld", 16)
    wait("vector", "ld", kb.cnt("ld"))
    qa = sb["xn_fm"]
    ka = sb["g0"]
    nrm = sb["nrm"]
    oblk = sb["oblk"]
    sqb = ptcol
    negmb2 = sb["negmb2"]
    dentm = sb["dentm"]
    ones1b = sb["ones1b"]

    kb.barrier("ph")
    for p in range(8):
        sa = A_tm[:, p * T:(p + 1) * T].rearrange("p (i q) -> p i q", q=128)
        da = A_fm[:, p * T:(p + 1) * T].rearrange("p (i q) -> p i q", q=128)
        op("sync", lambda e, _s=sa, _d=da: e.dma_start_transpose(_d, _s),
           "tp", 16)
    tpA = kb.cnt("tp")
    wait("tensor", "tp", tpA)

    wst_guard = [0, 0]
    evac_hist = []
    for which, wname, bname, dstt in ((0, "l1_wqaT", "bqa_t", qa),
                                      (1, "l1_wkaT", "bka_t", ka)):
        for oc in range(8):
            bws = oc % 2
            if wst_guard[bws]:
                kb.wait("gpsimd", "te", wst_guard[bws])
            in_w = bass.AP(H[wname].tensor, oc * 131072,
                           [[1024, 128], [1, 1024]])
            lww = op("gpsimd", lambda e, _i=in_w, _b=bws: e.dma_start(
                out=wst[_b][:], in_=_i), "gw", 16)
            wait("tensor", "gw", lww)
            for tt in range(4):
                bank = pb[(oc * 4 + tt) % 4]
                if len(evac_hist) >= 4:
                    wait("tensor", "act", evac_hist[-4])
                mmv = None
                for dc in range(8):
                    mmv = op("tensor", lambda e, _b=bank,
                             _s=wst[bws][:, dc * 128:(dc + 1) * 128],
                             _m=A_fm[:, dc * T + tt * 512:dc * T + tt * 512 + 512],
                             _d=dc: e.matmul(_b, _s, _m, start=(_d == 0),
                                             stop=(_d == 7)),
                             "te" if dc == 7 else None)
                wst_guard[bws] = mmv
                wait("scalar", "te", mmv)
                a_e = op("scalar", lambda e, _b=bank, _o=oc, _t=tt, _d=dstt,
                         _bn=bname: e.activation(
                             _d[:, _o * T + _t * 512:_o * T + _t * 512 + 512],
                             _b, AF.Identity, bias=sb[_bn][:, _o:_o + 1],
                             scale=1.0), "act")
                evac_hist.append(a_e)
    wait("tensor", "act", kb.cnt("act"))

    qgroups = [(qa, p * T, 4 + p) for p in range(8)]
    kgroups = [(ka, p * T, 4 + p) for p in range(8)]

    def norms2(groups, col0):
        for g, (tens, coff, obs) in enumerate(groups):
            a_sq = op("scalar", lambda e, _t=tens, _c=coff: e.activation(
                sqb[:], _t[:, _c:_c + T], AF.Square, bias=0.0, scale=1.0),
                "act")
            wait("tensor", "act", a_sq)
            gmm = None
            for tt in range(4):
                gmm = op("tensor", lambda e, _tt=tt, _o=obs, _g=g:
                         e.matmul(pb[_tt][0:16, :],
                                  oblk[:, _o * 16:(_o + 1) * 16],
                                  sqb[:, _tt * 512:(_tt + 1) * 512],
                                  start=(_g == 0), stop=(_g == 7)),
                         "te" if tt == 3 else None)
            wait("scalar", "te", gmm)
        mmn = kb.cnt("te")
        wait("vector", "te", mmn)
        for tt in range(4):
            op("vector", lambda e, _tt=tt, _c=col0: e.tensor_reduce(
                nrm[:, _c + _tt:_c + _tt + 1], pb[_tt][0:16, :], AX.X,
                ALU.max))
        drain("vector")
        v_nr = op("vector", lambda e, _c=col0: e.tensor_reduce(
            nrm[:, _c + 4:_c + 5], nrm[:, _c:_c + 4], AX.X, ALU.max), "dve")
        kb.drain("vector")
        wait("tensor", "dve", v_nr)

    norms2(qgroups, 0)
    norms2(kgroups, 8)
    emit_mb_tail(kb, nrm, negmb2)

    OB0 = 2048
    out_tm = A_fm
    a65 = [kb.S(f"a65_{i}", (128, 65), BF16) for i in range(2)]
    op("vector", lambda e: e.memset(a65[0][:, 64:65], 1.0))
    op("vector", lambda e: e.memset(a65[1][:, 64:65], 1.0))
    ap_prev = [0, 0, 0, 0]
    for h in range(16):
        pair, half = h // 2, h % 2
        pv_last = None
        for kt in range(16):
            nq = 16 - kt
            npc = (nq * 128 + 511) // 512
            b65 = kt % 2
            if pv_last is not None:
                wait("vector", "te", pv_last)
            asl = A_tm[:, pair * T + kt * 128 + half * 64:
                       pair * T + kt * 128 + half * 64 + 64]
            v_a = op("vector", lambda e, _b=b65, _a=asl: e.tensor_copy(
                a65[_b][:, 0:64], _a), "dve")
            kasl = bass.AP(ka, half * 64 * (8 * T) + pair * T + kt * 128,
                           [[8 * T, 64], [1, 128]])
            mmS_v = []
            for subp in range(npc):
                w = min(512, nq * 128 - subp * 512)
                qasl = bass.AP(qa, half * 64 * (8 * T) + pair * T
                               + kt * 128 + subp * 512, [[8 * T, 64], [1, w]])
                if ap_prev[subp]:
                    wait("tensor", "act", ap_prev[subp])
                mv = op("tensor", lambda e, _o=subp * 512, _k=kasl, _q=qasl,
                        _w=w, _r=half * 64: e.matmul(
                            PS[:, _o:_o + _w], _k, _q, start=True, stop=True,
                            tile_position=(_r, 0)), "te")
                mmS_v.append(mv)
            if pv_last is not None:
                wait("scalar", "te", pv_last)
            ap_now = []
            for subp in range(npc):
                w = min(512, nq * 128 - subp * 512)
                wait("scalar", "te", mmS_v[subp])
                a_p = op("scalar", lambda e, _o=subp * 512, _w=w, _h=h:
                         e.activation(ptcol[:, _o:_o + _w], PS[:, _o:_o + _w],
                                      AF.Exp, bias=negmb2[:, _h:_h + 1],
                                      scale=0.125), "act")
                ap_now.append(a_p)
                ap_prev[subp] = a_p
            wait("vector", "act", ap_now[0])
            v_m = op("vector", lambda e: e.tensor_tensor(
                ptcol[:, 0:128], ptcol[:, 0:128], cmask[:], ALU.mult), "dve")
            wait("tensor", "dve", v_m)
            wait("tensor", "dve", v_a)
            for b in range(kt // 4, 4):
                qt0 = max(kt, 4 * b)
                wdt = (4 * b + 3 - qt0 + 1) * 128
                st_flag, sp_flag = kt == 0, kt == 4 * b + 3
                hi_piece = ((qt0 - kt) * 128 + wdt - 1) // 512
                wait("tensor", "act", ap_now[hi_piece])
                pv_last = op("tensor", lambda e, _q=qt0, _w=wdt, _b=b65,
                             _p=ptcol[:, (qt0 - kt) * 128:
                                      (qt0 - kt) * 128 + wdt],
                             _st=st_flag, _sp=sp_flag:
                             e.matmul(PS[0:65, OB0 + _q * 128:OB0 + _q * 128 + _w],
                                      a65[_b][:, 0:65], _p,
                                      start=_st, stop=_sp), "te")
        wait("vector", "te", kb.cnt("te"))
        if h > 0:
            wait("vector", "st", kb.cnt("st"))
        vd = op("vector", lambda e: e.tensor_copy(
            den_stage[0:1, :],
            PS[64:65, OB0:4096].rearrange("p (i q) -> p i q", q=128)), "dve")
        wait("sync", "dve", vd)
        op("sync", lambda e, _h=h: e.dma_start(
            out=den_fm[_h:_h + 1, :], in_=den_stage[0:1, :]), "st", 16)
        wait("scalar", "dve", vd)
        a_o = op("scalar", lambda e: e.activation(
            outsb[:, 0:T], PS[0:64, OB0:4096],
            AF.Identity, bias=0.0, scale=1.0), "act")
        wait("sync", "act", a_o)
        sa = outsb[:, 0:T].rearrange("p (i q) -> p i q", q=128)
        da = bass.AP(out_tm, h * 64, [[8 * T, 128], [D, 16], [1, 64]])
        tpo = op("sync", lambda e, _s=sa, _d=da: e.dma_start_transpose(_d, _s),
                 "tp", 16)
        wait("scalar", "tp", tpo)
        wait("tensor", "tp", tpo)
    emit_l1_fin(kb, out_tm, want_dbg)
    return



def emit_l1_fin(kb, out_tm, want_dbg=False):
    H, op, wait, drain = kb.H, kb.op, kb.wait, kb.drain
    sb = kb.sb
    PS = kb.PSALL
    pb = [PS[:, i * 512:(i + 1) * 512] for i in range(8)]
    den_fm = sb["den_fm"]
    dentm = sb["dentm"]
    out_fm = sb["A_tm"]
    sg1 = sb["xn_fm"]
    sg_tm = sb["g0"]
    wst = [sb["wstd0"], sb["wstd1"]]
    g1b = [sb["g1st0"], sb["g1st1"]]

    kb.barrier("ph")
    op("vector", lambda e: e.reciprocal(den_fm[:, :], den_fm[:, :]))
    drain("vector")
    v_rd = op("vector", lambda e: e.tensor_copy(den_fm[:, 0:1], den_fm[:, 0:1]),
              "dve")
    wait("tensor", "dve", v_rd)
    for i in range(NT):
        mt = op("tensor", lambda e, _i=i: e.transpose(
            pb[0][:, 0:16], den_fm[:, _i * 128:(_i + 1) * 128],
            sb["identf"][0:16, 0:16]), "te")
        wait("vector", "te", mt)
        ev = op("vector", lambda e, _i=i: e.tensor_copy(
            dentm[:, _i * 16:(_i + 1) * 16], pb[0][:, 0:16]), "dve")
        kb.drain("vector")
        wait("tensor", "dve", ev)
    for i in range(NT):
        v_n = op("vector", lambda e, _i=i: e.tensor_tensor(
            out_tm[:, _i * D:(_i + 1) * D].rearrange("p (h d) -> p h d", d=64),
            out_tm[:, _i * D:(_i + 1) * D].rearrange("p (h d) -> p h d", d=64),
            dentm[:, _i * 16:(_i + 1) * 16].unsqueeze(2)
            .broadcast_to([128, 16, 64]), ALU.mult), "dve")
        kb.drain("vector")
    v_norm = kb.cnt("dve")
    wait("sync", "dve", v_norm)
    if DIAG3:
        for i in range(NT):
            op("sync", lambda e, _i=i: e.dma_start(
                out=H["dbg_out3"][:, _i * D:(_i + 1) * D],
                in_=out_tm[:, _i * D:(_i + 1) * D]), "st", 16)
    for i in range(NT):
        sa = out_tm[:, i * D:(i + 1) * D].rearrange("p (c q) -> p c q", q=128)
        da = bass.AP(out_fm, i * 128, [[8 * T, 128], [T, 8], [1, 128]])
        op("sync", lambda e, _s=sa, _d=da: e.dma_start_transpose(_d, _s),
           "tp", 16)
    tpf = kb.cnt("tp")
    wait("tensor", "tp", tpf)
    wst_guard = [0, 0]
    g1_guard = [None, None]
    for fo in range(8):
        bws = fo % 2
        if wst_guard[bws]:
            kb.wait("gpsimd", "te", wst_guard[bws])
        in_w = bass.AP(H["l1_wdT"].tensor, fo * 131072,
                       [[1024, 128], [1, 1024]])
        lww = op("gpsimd", lambda e, _i=in_w, _b=bws: e.dma_start(
            out=wst[_b][:], in_=_i), "gw", 16)
        wait("tensor", "gw", lww)
        for tt in range(4):
            bank = pb[(fo * 4 + tt) % 4]
            mmv = None
            for dc in range(8):
                mmv = op("tensor", lambda e, _b=bank,
                         _s=wst[bws][:, dc * 128:(dc + 1) * 128],
                         _m=out_fm[:, dc * T + tt * 512:dc * T + tt * 512 + 512],
                         _d=dc: e.matmul(_b, _s, _m, start=(_d == 0),
                                         stop=(_d == 7)),
                         "te" if dc == 7 else None)
            wst_guard[bws] = mmv
            bg = tt % 2
            if g1_guard[bg] is not None:
                wait("sync", "dve", g1_guard[bg])
            lg = op("sync", lambda e, _f=fo, _t=tt, _g=bg: e.dma_start(
                out=g1b[_g][:], in_=bass.AP(H["g1"].tensor,
                                            _f * 128 * 2048 + _t * 512,
                                            [[2048, 128], [1, 512]])),
                "ld", 16)
            wait("scalar", "te", mmv)
            a_b = op("scalar", lambda e, _b=bank, _f=fo: e.activation(
                ubuf_ap(kb), _b, AF.Identity, bias=sb["bd_t"][:, _f:_f + 1],
                scale=1.0), "act")
            wait("vector", "act", a_b)
            wait("vector", "ld", lg)
            ev = op("vector", lambda e, _f=fo, _t=tt, _g=bg: e.tensor_tensor(
                sg1[:, _f * T + _t * 512:_f * T + _t * 512 + 512],
                ubuf_ap(kb), g1b[_g][:], ALU.mult), "dve")
            wait("tensor", "dve", ev)
            wait("scalar", "dve", ev)
            g1_guard[bg] = ev
    sg_done = kb.cnt("dve")
    wait("sync", "dve", sg_done)
    for fo in range(8):
        sa = sg1[:, fo * T:(fo + 1) * T].rearrange("p (i q) -> p i q", q=128)
        da = bass.AP(sg_tm, fo * 128, [[8 * T, 128], [D, 16], [1, 128]])
        op("sync", lambda e, _s=sa, _d=da: e.dma_start_transpose(_d, _s),
           "tp", 16)
    tps = kb.cnt("tp")
    x1v = H["x1"].rearrange("(n p) d -> n p d", p=128)
    outv = H["out"].rearrange("(n p) d -> n p d", p=128)
    res_guard = [None, None]
    xtb_t = [sb["xt0"], sb["xt1"]]
    xnfb_t = [sb["xnf0"], sb["xnf1"]]
    for i in range(NT):
        b = i % 2
        ldv = op("sync", lambda e, _i=i, _x=xtb_t[b]: e.dma_start(
            out=_x[:], in_=x1v[_i]), "ld", 16)
        wait("vector", "ld", ldv)
        wait("vector", "tp", tps)
        if res_guard[b] is not None:
            wait("vector", "st", res_guard[b])
        v1 = op("vector", lambda e, _o=xnfb_t[b], _x=xtb_t[b], _i=i:
                e.tensor_tensor(
            _o[:], _x[:],
            sg_tm[:, _i * D:(_i + 1) * D], ALU.add), "dve")
        wait("sync", "dve", v1)
        res_guard[b] = op("sync", lambda e, _i=i, _o=xnfb_t[b]: e.dma_start(
            out=outv[_i], in_=_o[:]), "st", 16)
    kb.wait("sync", "st", kb.cnt("st"))


def ubuf_ap(kb):
    return kb.sb["ubuf"][:]


def emit_mb_tail(kb, nrm, negout):
    op, wait, drain = kb.op, kb.wait, kb.drain
    smalls = kb.sb["smalls"]
    wait("vector", "dve", kb.cnt("dve"))
    v_gm = op("vector", lambda e: e.tensor_tensor(
        nrm[:, 16:17], nrm[:, 4:5], nrm[:, 12:13], ALU.mult), "dve")
    wait("scalar", "dve", v_gm)
    op("scalar", lambda e: e.activation(nrm[:, 17:18], nrm[:, 16:17], AF.Ln,
                                        bias=0.0, scale=1.0))
    kb.drain("scalar")
    a_mb = op("scalar", lambda e: e.activation(
        nrm[:, 18:19], nrm[:, 17:18], AF.Exp, bias=smalls[0:16, 5:6],
        scale=0.5), "act")
    wait("vector", "act", a_mb)
    op("vector", lambda e: e.tensor_scalar(
        out=nrm[:, 19:20], in0=nrm[:, 18:19], scalar1=-1.0, scalar2=None,
        op0=ALU.mult))
    drain("vector")
    emit_mb_broadcast(kb, nrm, negout)


# version 11
# speedup vs baseline: 1.0037x; 1.0037x over previous
import numpy as np
import ml_dtypes
import concourse.bass as bass
import concourse.mybir as mybir
from concourse.bass_utils import run_bass_kernel_spmd

F32 = mybir.dt.float32
BF16 = mybir.dt.bfloat16
U32 = mybir.dt.uint32
AF = mybir.ActivationFunctionType
ALU = mybir.AluOpType
AX = mybir.AxisListType
BF16NP = ml_dtypes.bfloat16

T = 2048
D = 1024
NT = T // 128
ND = D // 128
MAGIC = float(np.float32(1.5 * 2 ** 23))
LN2 = float(np.log(2.0))

DEBUG = False
DIAG3 = False
STAGE = 3
L0_PARTS = "AdBCDr"

ENGINES = ("sync", "tensor", "vector", "scalar", "gpsimd")


class KB:

    def __init__(self):
        self.nc = bass.Bass("TRN2", target_bir_lowering=False)
        self.H = {}
        self.q = {e: [] for e in ENGINES}
        self.counts = {}
        self.sb = {}
        self._ctx = []
        self.semnames = []

    def din(self, name, shape, dt=F32):
        self.H[name] = self.nc.dram_tensor(name, list(shape), dt,
                                           kind="ExternalInput").ap()
        return self.H[name]

    def dout(self, name, shape, dt=F32):
        self.H[name] = self.nc.dram_tensor(name, list(shape), dt,
                                           kind="ExternalOutput").ap()
        return self.H[name]

    def dtmp(self, name, shape, dt=F32):
        self.H[name] = self.nc.dram_tensor(name, list(shape), dt,
                                           kind="Internal").ap()
        return self.H[name]

    def S(self, name, shape, dtype):
        cm = self.nc.sbuf_tensor(name, list(shape), dtype)
        h = cm.__enter__()
        self._ctx.append(cm)
        self.sb[name] = h
        return h

    def PSUM(self, name, shape=(128, 512), dtype=F32):
        cm = self.nc.psum_tensor(name, list(shape), dtype)
        h = cm.__enter__()
        self._ctx.append(cm)
        return h

    def sem(self, *names):
        self.semnames.extend(names)

    def marker(self):
        return len(self._ctx)

    def free_to(self, mark):
        while len(self._ctx) > mark:
            cm = self._ctx.pop()
            cm.__exit__(None, None, None)

    def barrier(self, name="ph"):
        for s in ("te", "act", "dve", "gp", "tp", "st", "ld", "gw"):
            if self.cnt(s):
                self.wait("sync", s, self.cnt(s))
        v = self.op("sync", lambda e: e.nop(), name)
        for eng in ("tensor", "vector", "scalar", "gpsimd"):
            self.wait(eng, name, v)
        return v

    def op(self, eng, f, inc=None, n=1):
        if inc is None:
            self.q[eng].append(lambda E, _f=f: _f(E[eng]))
            return None
        if n == 16:
            prev = self.counts.get(inc, 0)
            if prev:
                self.wait(eng, inc, prev)
        self.counts[inc] = self.counts.get(inc, 0) + n
        v = self.counts[inc]
        self.q[eng].append(
            lambda E, _f=f, _s=inc, _n=n: _f(E[eng]).then_inc(E["S"][_s], _n))
        return v

    def cnt(self, sem):
        return self.counts.get(sem, 0)

    def wait(self, eng, sem, val):
        assert val <= self.cnt(sem), (sem, val, self.cnt(sem))
        self.q[eng].append(
            lambda E, _s=sem, _v=val: E[eng].wait_ge(E["S"][_s], _v))

    def drain(self, eng):
        self.q[eng].append(lambda E: E[eng].drain())

    def finalize(self):
        nc = self.nc
        semctx = []
        S = {}
        for n in self.semnames:
            cm = nc.semaphore(n)
            S[n] = cm.__enter__()
            semctx.append(cm)
        blk = nc.Block()
        block = blk.__enter__()
        E = {"S": S}

        def runner(engname):
            def _run(eng):
                E[engname] = eng
                for f in self.q[engname]:
                    f(E)
            return _run

        block.sync(runner("sync"))
        block.tensor(runner("tensor"))
        block.vector(runner("vector"))
        block.scalar(runner("scalar"))
        block.gpsimd(runner("gpsimd"))
        blk.__exit__(None, None, None)
        for cm in reversed(semctx):
            cm.__exit__(None, None, None)
        for cm in reversed(self._ctx):
            cm.__exit__(None, None, None)
        return nc



def tfm(tensor, c, t0, n, nchunk=8):
    i0 = t0 // 128
    return bass.AP(tensor, (i0 * nchunk + c) * 128,
                   [[1, 128], [nchunk * 128, n // 128], [1, 128]])



def emit_quant(kb, x_ap, out_ap, fac_ap, w, inc=None):
    nb = w // 32
    q = kb.sb["qtmp"]
    qs = kb.sb["qsc"]
    sm = kb.sb["smalls"]
    A, B, C = q[:, 0:w], q[:, w:2 * w], q[:, 2 * w:3 * w]
    s_amax = qs[:, 0:nb]
    s_pw2 = qs[:, nb:2 * nb]
    s_scale = qs[:, 2 * nb:3 * nb]
    s_rs = qs[:, 3 * nb:4 * nb]
    x3 = x_ap.rearrange("p (nb bs) -> p nb bs", bs=32)

    def o(f, inc_=None):
        r = kb.op("vector", f, inc_)
        kb.drain("vector")
        return r

    o(lambda e: e.tensor_reduce(s_amax, x3, AX.X, ALU.max,
                                apply_absolute_value=True))
    o(lambda e: e.tensor_single_scalar(s_pw2.bitcast(U32), s_amax.bitcast(U32),
                                       sm[:, 0:1].bitcast(U32), ALU.bitwise_and))
    o(lambda e: e.tensor_tensor(s_scale, s_pw2, fac_ap, ALU.mult))
    o(lambda e: e.tensor_scalar(out=s_scale, in0=s_scale, scalar1=0.25,
                                scalar2=1e-30, op0=ALU.mult, op1=ALU.max))
    o(lambda e: e.reciprocal(s_rs, s_scale))
    rs3 = s_rs.unsqueeze(2).broadcast_to([128, nb, 32])
    o(lambda e: e.tensor_tensor(A.rearrange("p (nb bs) -> p nb bs", bs=32),
                                x3, rs3, ALU.mult))
    o(lambda e: e.tensor_scalar(out=A, in0=A, scalar1=6.0, scalar2=-6.0,
                                op0=ALU.min, op1=ALU.max))
    o(lambda e: e.scalar_tensor_tensor(out=B, in0=A, scalar=-1.0, in1=A,
                                       op0=ALU.mult, op1=ALU.max))
    o(lambda e: e.tensor_scalar(out=C, in0=B, scalar1=2.0, scalar2=-1.0,
                                op0=ALU.is_ge, op1=ALU.mult))
    o(lambda e: e.tensor_scalar(out=B, in0=B, scalar1=4.0, scalar2=-0.5,
                                op0=ALU.is_ge, op1=ALU.mult))
    o(lambda e: e.scalar_tensor_tensor(out=C, in0=C, scalar=2.0, in1=B,
                                       op0=ALU.add, op1=ALU.add))
    o(lambda e: e.tensor_tensor(A, A, C, ALU.mult))
    o(lambda e: e.tensor_scalar(out=A, in0=A, scalar1=MAGIC, scalar2=MAGIC,
                                op0=ALU.add, op1=ALU.subtract))
    o(lambda e: e.tensor_scalar(out=B, in0=C, scalar1=1.0, scalar2=0.5,
                                op0=ALU.is_le, op1=ALU.mult))
    o(lambda e: e.tensor_scalar(out=C, in0=C, scalar1=0.5, scalar2=1.0,
                                op0=ALU.is_le, op1=ALU.mult))
    o(lambda e: e.scalar_tensor_tensor(out=C, in0=B, scalar=0.5, in1=C,
                                       op0=ALU.add, op1=ALU.add))
    o(lambda e: e.tensor_tensor(A, A, C, ALU.mult))
    sc3 = s_scale.unsqueeze(2).broadcast_to([128, nb, 32])
    return o(lambda e: e.tensor_tensor(
        out_ap.rearrange("p (nb bs) -> p nb bs", bs=32),
        A.rearrange("p (nb bs) -> p nb bs", bs=32), sc3, ALU.mult), inc)


def emit_xbar_tm2fm(kb, src, dst, n_src_tiles, n_chunks, dst_tile_stride,
                    inc="tp"):
    vals = []
    for i in range(n_src_tiles):
        src_ap = bass.AP(src.tensor if isinstance(src, bass.AP) else src, 0,
                         [[1, 128], [128, n_chunks], [1, 128]])
        src_ap = src[:, i * n_chunks * 128:(i + 1) * n_chunks * 128] \
            .rearrange("p (c q) -> p c q", q=128)
        dst_ap = bass.AP(dst.tensor if hasattr(dst, "tensor") else dst,
                         i * 128, [[1, 128], [dst_tile_stride, n_chunks],
                                   [1, 128]])
        v = kb.op("sync", lambda e, _s=src_ap, _d=dst_ap:
                  e.dma_start_transpose(_d, _s), inc, 16)
        vals.append(v)
    return vals[-1]



def host_consts():
    c = {}
    c["c_identity"] = np.eye(128, dtype=BF16NP)
    pos = np.arange(T, dtype=np.float32)
    dist = np.maximum(T - 1.0 - pos, 0.0)
    tw = np.zeros((32, T), np.float32)
    tw[:28] = -dist[None, :]
    tw[28:] = -pos[None, :]
    c["c_negtime"] = tw
    kex = np.zeros((32, 8, 128), BF16NP)
    for cc in range(8):
        for r in range(128):
            kex[4 * cc + r // 32, cc, r] = 1
    c["c_kexp"] = kex.reshape(32, 8 * 128)
    psm = np.zeros((128, 8, 32), BF16NP)
    for cc in range(8):
        for r in range(128):
            psm[r, cc, 4 * cc + r // 32] = 1
    c["c_psum32"] = psm.reshape(128, 8 * 32)
    return c


def host_l0_inputs(inputs):
    h = {}
    th = inputs["L0_attn_theta"].reshape(32, 32, 4)
    th_r = np.zeros((128, 32), np.float32)
    for r in range(128):
        for cc in range(8):
            for m in range(4):
                th_r[r, cc * 4 + m] = th[4 * cc + r // 32, r % 32, m]
    h["l0_theta_r"] = th_r
    wc = inputs["L0_attn_w_conv"].reshape(4, 2080)
    wcr = np.zeros((128, 64), np.float32)
    for fc in range(16):
        for tap in range(4):
            wcr[:, fc * 4 + tap] = wc[tap, fc * 128:(fc + 1) * 128]
    h["l0_wconv_r"] = wcr
    h["l0_wconv_sr"] = np.ascontiguousarray(wc[:, 2048:2080].T)
    h["l0_bconv_r"] = np.ascontiguousarray(
        inputs["L0_attn_b_conv"][:2048].reshape(16, 128).T)
    h["l0_bconv_sr"] = np.ascontiguousarray(
        inputs["L0_attn_b_conv"][2048:2080].reshape(32, 1))
    h["l0_slopes"] = np.ascontiguousarray(np.concatenate(
        [inputs["L0_attn_decay_slopes"], inputs["L0_attn_anchor_slopes"]]
    ).reshape(32, 1))
    h["l0_score"] = np.ascontiguousarray(
        inputs["L0_attn_score_scale"].reshape(32, 1))
    h["l0_deriv"] = np.ascontiguousarray(
        inputs["L0_attn_deriv_logits"].reshape(1, 3))
    gidx = np.array([[m2 * 32 + h2 for h2 in range(32)] for m2 in range(4)])
    src_idx = np.zeros(128, np.int64)
    for m2 in range(4):
        for h2 in range(32):
            src_idx[m2 * 32 + h2] = h2 * 4 + m2
    ns = inputs["L0_attn_norm_scale"]
    h["l0_norm2"] = np.ascontiguousarray(
        np.stack([ns[src_idx], ns[128 + src_idx]], 1))
    h["l0_wre"] = np.ascontiguousarray(inputs["L0_attn_W_re"][src_idx])
    h["l0_wim"] = np.ascontiguousarray(inputs["L0_attn_W_im"][src_idx])
    h["l0_win"] = np.ascontiguousarray(inputs["L0_attn_W_in"])
    h["l0_win2"] = np.ascontiguousarray(
        inputs["L0_attn_W_in"][:, 0:2048].reshape(8, 128, 16, 128)
        .transpose(2, 1, 0, 3).reshape(16, 128, 1024))
    h["l0_wout2"] = np.ascontiguousarray(
        inputs["L0_attn_W_out"].reshape(8, 128, 8, 128)
        .transpose(2, 1, 0, 3).reshape(8, 128, 1024))
    return h




def build(consts_np, want_dbg=False):
    kb = KB()
    kb.din("x0", (T, D))
    kb.din("L0_ffn_weight", (D, D))
    kb.din("L0_ffn_bias_r", (128, 8))
    kb.din("L0_ffn_delta_w_r", (128, 8 * 32))
    kb.din("c_dact", (128, 1))
    kb.din("l0_theta_r", (128, 32))
    kb.din("l0_wconv_r", (128, 64))
    kb.din("l0_wconv_sr", (32, 4))
    kb.din("l0_bconv_r", (128, 16))
    kb.din("l0_bconv_sr", (32, 1))
    kb.din("l0_slopes", (32, 1))
    kb.din("l0_score", (32, 1))
    kb.din("l0_deriv", (1, 3))
    kb.din("l0_norm2", (128, 2))
    kb.din("l0_wre", (128, 32))
    kb.din("l0_wim", (128, 32))
    kb.din("l0_win", (D, 2080))
    kb.din("l0_win2", (16, 128, 1024))
    kb.din("l0_wout2", (8, 128, 1024))
    for k, v in consts_np.items():
        kb.din(k, v.shape, BF16 if v.dtype == BF16NP else F32)
    kb.dtmp("x1", (T, D))
    if STAGE >= 3:
        kb.din("l1_wqkvT", (32, 128, 1024))
        kb.din("l1_bqkv", (128, 32))
        kb.din("l1_wqaT", (8, 128, 1024))
        kb.din("l1_wkaT", (8, 128, 1024))
        kb.din("l1_wdT", (8, 128, 1024))
        kb.din("l1_bqa", (128, 8))
        kb.din("l1_bka", (128, 8))
        kb.din("l1_bd", (128, 8))
        kb.din("l1_gwT", (1024, 8))
        kb.din("l1_wg", (8, 128, 2048))
        kb.din("l1_wu", (8, 128, 2048))
        kb.din("l1_wd", (8, 128, 2048))
        if want_dbg:
            kb.dout("xn1", (8, 128, T), BF16)
            kb.dout("g1", (8, 128, T), BF16)
            kb.dout("dbg_A", (128, 8 * T))
            kb.dout("dbg_out2", (128, NT * D))
        else:
            kb.dtmp("xn1", (8, 128, T), BF16)
            kb.dtmp("g1", (8, 128, T), BF16)
        if DIAG3:
            kb.dout("dbg_A3", (128, 8 * T), BF16)
            kb.dout("dbg_out3", (128, NT * D), BF16)
            kb.dout("dbg_qsl", (128, 4 * T), BF16)
            kb.dout("dbg_ksl", (128, 4 * T), BF16)
            kb.dout("dbg_qr", (128, 8 * T), BF16)
            kb.dout("dbg_kr", (128, 8 * T), BF16)
            kb.dout("dbg_v", (128, 8 * T), BF16)
    kb.dout("out", (T, D))
    if want_dbg:
        kb.dout("dbg_xn", (D, T))
        kb.dout("dbg_xq", (D, T))
        kb.dout("dbg_g0", (D, T))
        kb.dout("dbg_s0", (D, T))
        kb.dout("dbg_pw", (32, T))
        kb.dout("dbg_xv", (D, T))
        kb.dout("dbg_gate", (D, T))
        kb.dout("dbg_yu", (D, T))
        kb.dout("dbg_sk", (32, T))
        kb.dout("dbg_msq", (32, T))
        kb.dout("dbg_den", (32, T))

    H = kb.H
    op, wait, drain = kb.op, kb.wait, kb.drain
    kb.sem("ld", "gw", "st", "tp", "te", "dve", "act", "gp", "ev", "ph")

    ident = kb.S("ident", (128, 128), BF16)
    xn_fm = kb.S("xn_fm", (128, ND * T), BF16)
    g0 = kb.S("g0", (128, ND * T), BF16)
    smalls = kb.S("smalls", (128, 32), F32)
    red = kb.S("red", (128, 16), F32)
    sqd = kb.S("sqd", (128, D), F32)
    PS = kb.PSUM("PS", (128, 4096))
    pb = [PS[:, i * 512:(i + 1) * 512] for i in range(8)]
    kb.PSALL = PS
    MARK = kb.marker()
    xt = [kb.S(f"xt{i}", (128, D), F32) for i in range(2)]
    xnf = [kb.S(f"xnf{i}", (128, D), F32) for i in range(2)]
    xn_st = [kb.S(f"xn_st{i}", (128, D), BF16) for i in range(2)]
    xq_st = [kb.S(f"xq_st{i}", (128, D), BF16) for i in range(2)]
    wq_st = [kb.S(f"wq_st{i}", (128, D), BF16) for i in range(2)]
    xq_fm = kb.S("xq_fm", (128, ND * T), BF16)
    wqT = kb.S("wqT", (128, 8 * D), BF16)
    bias_c = kb.S("bias_c", (128, 8), F32)
    wkf = [kb.S(f"wkf{i}", (128, D), F32) for i in range(2)]
    dwf = kb.S("dwf", (128, 8 * 32), F32)
    facw = kb.S("facw", (128, 8 * 32), F32)
    qtmp = kb.S("qtmp", (128, 3 * D), F32)
    qsc = kb.S("qsc", (128, 4 * 32), F32)

    op("sync", lambda e: e.dma_start(out=ident[:], in_=H["c_identity"][:, :]),
       "ld", 16)

    op("sync", lambda e: e.dma_start(out=bias_c[:], in_=H["L0_ffn_bias_r"][:, :]),
       "ld", 16)
    op("sync", lambda e: e.dma_start(out=dwf[:], in_=H["L0_ffn_delta_w_r"][:, :]),
       "ld", 16)
    ld_da = op("sync", lambda e: e.dma_start(out=smalls[:, 8:9], in_=H["c_dact"][:, :]),
               "ld", 16)
    ld_init = kb.cnt("ld")

    wait("vector", "ld", ld_init)
    wait("scalar", "ld", ld_init)
    op("vector", lambda e: e.memset(smalls[:, 0:1].bitcast(U32), 0x7F800000))
    drain("vector")

    def emit_tanh_exp_fac(src, dst, width, tmp0, tmp1):
        a1 = op("scalar", lambda e: e.activation(tmp0, src, AF.Exp,
                                                 bias=0.0, scale=8.0), "act")
        wait("vector", "act", a1)
        op("vector", lambda e: e.tensor_scalar(out=tmp1, in0=tmp0, scalar1=1.0,
                                               scalar2=None, op0=ALU.add))
        drain("vector")
        op("vector", lambda e: e.reciprocal(tmp1, tmp1))
        drain("vector")
        op("vector", lambda e: e.tensor_scalar(out=tmp0, in0=tmp0, scalar1=-1.0,
                                               scalar2=None, op0=ALU.add))
        drain("vector")
        v = op("vector", lambda e: e.tensor_tensor(tmp0, tmp0, tmp1, ALU.mult),
               "dve")
        wait("scalar", "dve", v)
        return op("scalar", lambda e: e.activation(dst, tmp0, AF.Exp, bias=0.0,
                                                   scale=0.5 * LN2), "act")

    a_fac = emit_tanh_exp_fac(smalls[:, 8:9], smalls[:, 14:15], 1,
                              smalls[:, 9:10], smalls[:, 10:11])
    a_facw = emit_tanh_exp_fac(dwf[:], facw[:], 256,
                               qtmp[:, 0:256], qtmp[:, 256:512])

    xv = H["x0"].rearrange("(n p) d -> n p d", p=128)
    wait("vector", "act", a_fac)
    wait("vector", "act", a_facw)
    fac32 = smalls[:, 14:15].broadcast_to([128, 32])
    last_q = None
    tp_guard = [None, None]
    ldv_t = {0: op("sync", lambda e: e.dma_start(out=xt[0][:], in_=xv[0]),
                   "ld", 16),
             1: op("sync", lambda e: e.dma_start(out=xt[1][:], in_=xv[1]),
                   "ld", 16)}
    vxn_hist = []
    for i in range(NT):
        b = i % 2
        if i + 2 < NT:
            pass
        ldv = ldv_t[i]
        wait("vector", "ld", ldv)
        wait("scalar", "ld", ldv)
        sq_done = op("scalar", lambda e, _b=b: e.activation(
            sqd[:], xt[_b][:], AF.Square, bias=0.0, scale=1.0,
            accum_out=red[:, 1:2]), "act")
        op("vector", lambda e, _b=b: e.tensor_reduce(red[:, 0:1], xt[_b][:],
                                                     AX.X, ALU.add))
        drain("vector")
        op("vector", lambda e: e.tensor_scalar(out=red[:, 2:3], in0=red[:, 0:1],
                                               scalar1=1.0 / D, scalar2=None,
                                               op0=ALU.mult))
        drain("vector")
        op("vector", lambda e: e.tensor_tensor(red[:, 3:4], red[:, 2:3],
                                               red[:, 2:3], ALU.mult))
        wait("vector", "act", sq_done)
        drain("vector")
        op("vector", lambda e: e.scalar_tensor_tensor(
            out=red[:, 4:5], in0=red[:, 1:2], scalar=1.0 / D, in1=red[:, 3:4],
            op0=ALU.mult, op1=ALU.subtract))
        drain("vector")
        v_var = op("vector", lambda e: e.tensor_scalar(
            out=red[:, 4:5], in0=red[:, 4:5], scalar1=1e-5, scalar2=None,
            op0=ALU.add), "dve")
        wait("scalar", "dve", v_var)
        op("scalar", lambda e: e.activation(red[:, 5:6], red[:, 4:5], AF.Ln,
                                            bias=0.0, scale=1.0))
        kb.drain("scalar")
        a_r = op("scalar", lambda e: e.activation(red[:, 6:7], red[:, 5:6],
                                                  AF.Exp, bias=0.0, scale=-0.5),
                 "act")
        op("vector", lambda e: e.tensor_scalar(out=red[:, 7:8], in0=red[:, 2:3],
                                               scalar1=-1.0, scalar2=None,
                                               op0=ALU.mult))
        wait("vector", "act", a_r)
        drain("vector")
        op("vector", lambda e, _b=b: e.tensor_scalar(
            out=xnf[_b][:], in0=xt[_b][:], scalar1=red[:, 7:8],
            scalar2=red[:, 6:7], op0=ALU.add, op1=ALU.mult))
        drain("vector")
        if i >= 2 and tp_guard[b] is not None:
            wait("vector", "tp", tp_guard[b])
        v_xn = op("vector", lambda e, _b=b: e.tensor_copy(
            xn_st[_b][:], xnf[_b][:]), "dve")
        drain("vector")
        vxn_hist.append(v_xn)
        if i + 2 < NT:
            wait("sync", "dve", v_xn)
            ldv_t[i + 2] = op("sync", lambda e, _j=i + 2, _b=b: e.dma_start(
                out=xt[_b][:], in_=xv[_j]), "ld", 16)
        last_q = emit_quant(kb, xnf[b][:], xq_st[b][:], fac32, D, inc="dve")
        wait("sync", "dve", v_xn)
        da_n = bass.AP(xn_fm, i * 128, [[NT * D, 128], [T, ND], [1, 128]])
        sa_n = xn_st[b][:].rearrange("p (c q) -> p c q", q=128)
        op("sync", lambda e, _s=sa_n, _d=da_n: e.dma_start_transpose(_d, _s),
           "tp", 16)
        wait("sync", "dve", last_q)
        da_q = bass.AP(xq_fm, i * 128, [[NT * D, 128], [T, ND], [1, 128]])
        sa_q = xq_st[b][:].rearrange("p (c q) -> p c q", q=128)
        tp_guard[b] = op("sync", lambda e, _s=sa_q, _d=da_q:
                         e.dma_start_transpose(_d, _s), "tp", 16)
    xq_done = last_q

    wv = H["L0_ffn_weight"].rearrange("(c p) d -> c p d", p=128)
    wait("vector", "act", a_facw)
    wtp_guard = [None, None]
    lw_t = {0: op("sync", lambda e: e.dma_start(out=wkf[0][:], in_=wv[0]),
                  "ld", 16),
            1: op("sync", lambda e: e.dma_start(out=wkf[1][:], in_=wv[1]),
                  "ld", 16)}
    for c in range(8):
        b = c % 2
        wait("vector", "ld", lw_t[c])
        if c >= 2 and wtp_guard[b] is not None:
            wait("vector", "tp", wtp_guard[b])
        last_q = emit_quant(kb, wkf[b][:], wq_st[b][:],
                            facw[:, c * 32:(c + 1) * 32], D, inc="dve")
        wait("sync", "dve", last_q)
        if c + 2 < 8:
            lw_t[c + 2] = op("sync", lambda e, _j=c + 2, _b=b: e.dma_start(
                out=wkf[_b][:], in_=wv[_j]), "ld", 16)
        da_w = bass.AP(wqT, c * 128, [[8 * D, 128], [D, 8], [1, 128]])
        sa_w = wq_st[b][:].rearrange("p (c q) -> p c q", q=128)
        wtp_guard[b] = op("sync", lambda e, _s=sa_w, _d=da_w:
                          e.dma_start_transpose(_d, _s), "tp", 16)
    wq_done = last_q

    tp_done = kb.cnt("tp")

    wait("tensor", "tp", tp_done)
    bank_guard = {}
    g0_acts = []
    for tt in range(4):
        for oc in range(8):
            bi = (tt * 8 + oc) % 4
            bank = pb[bi]
            if bi in bank_guard:
                wait("tensor", "act", bank_guard[bi])
            mmv = None
            for ic in range(8):
                mov = kb.sb["xq_fm"][:, ic * T + tt * 512:ic * T + tt * 512 + 512]
                sta = kb.sb["wqT"][:, ic * D + oc * 128:ic * D + oc * 128 + 128]
                mmv = op("tensor",
                         lambda e, _b=bank, _s=sta, _m=mov, _i=ic: e.matmul(
                             _b[:], _s, _m,
                             start=(_i == 0), stop=(_i == 7)),
                         "te" if ic == 7 else None)
            wait("scalar", "te", mmv)
            a = op("scalar", lambda e, _b=bank, _o=oc, _t=tt: e.activation(
                kb.sb["g0"][:, _o * T + _t * 512:_o * T + _t * 512 + 512],
                _b[:], AF.Silu, bias=kb.sb["bias_c"][:, _o:_o + 1], scale=1.0),
                "act")
            bank_guard[bi] = a
            g0_acts.append(a)

    if want_dbg:
        wait("sync", "tp", tp_done)
        for c in range(ND):
            kb.wait("gpsimd", "tp", tp_done)
            op("gpsimd", lambda e, _c=c: e.dma_start(
                out=H["dbg_xn"].rearrange("(c p) t -> c p t", p=128)[_c],
                in_=xn_fm[:, _c * T:(_c + 1) * T]), "st", 16)
            op("gpsimd", lambda e, _c=c: e.dma_start(
                out=H["dbg_xq"].rearrange("(c p) t -> c p t", p=128)[_c],
                in_=xq_fm[:, _c * T:(_c + 1) * T]), "st", 16)
        kb.wait("gpsimd", "act", g0_acts[-1])
        for c in range(ND):
            op("gpsimd", lambda e, _c=c: e.dma_start(
                out=H["dbg_g0"].rearrange("(c p) t -> c p t", p=128)[_c],
                in_=g0[:, _c * T:(_c + 1) * T]), "st", 16)

    if STAGE >= 2:
        kb.barrier()
        kb.free_to(MARK)
        M1b = kb.marker()
        kb.S("rp32", (32, 2 * T), F32)
        kb.S("rpbf", (32, T), BF16)
        kb.S("kexp", (32, 8 * 128), BF16)
        kb.S("psum32", (128, 8 * 32), BF16)
        xt_l0 = [kb.S(f"xt{i}_l0", (128, D), F32) for i in range(2)]
        xnf_l0 = [kb.S(f"xnf{i}_l0", (128, D), F32) for i in range(2)]
        kb.sb["xt0"], kb.sb["xt1"] = xt_l0
        kb.sb["xnf0"], kb.sb["xnf1"] = xnf_l0
        op("sync", lambda e: e.dma_start(out=kb.sb["rp32"][0:32, T:2 * T],
                                         in_=H["c_negtime"][:, :]), "ld", 16)
        op("sync", lambda e: e.dma_start(out=kb.sb["kexp"][:, :],
                                         in_=H["c_kexp"][:, :]), "ld", 16)
        op("sync", lambda e: e.dma_start(out=kb.sb["psum32"][:, :],
                                         in_=H["c_psum32"][:, :]), "ld", 16)
        M2 = kb.marker()
        emit_l0_sraw(kb, pb)
        kb.barrier()
        kb.free_to(M2)
        emit_l0_attn(kb, pb, want_dbg=want_dbg)
        kb.wait("sync", "st", kb.cnt("st"))
        if STAGE == 2:
            op("sync", lambda e: e.dma_start(out=H["out"][:, :],
                                             in_=H["x1"][:, :]), "st", 16)
            kb.wait("sync", "st", kb.cnt("st"))
        if STAGE >= 3:
            kb.barrier()
            kb.free_to(M1b)
            emit_l1(kb, want_dbg=want_dbg)
    else:
        wait("sync", "act", g0_acts[-1])
        for i in range(NT):
            op("sync", lambda e, _i=i: e.dma_start(
                out=H["out"].rearrange("(n p) d -> n p d", p=128)[_i],
                in_=xt[0][:]), "st", 16)
        wait("sync", "st", kb.cnt("st"))

    return kb



def prep_core_inputs(inputs, consts):
    base = {}
    base.update({k: np.ascontiguousarray(v) for k, v in consts.items()})
    base["L0_ffn_weight"] = np.ascontiguousarray(inputs["L0_ffn_weight"])
    base["L0_ffn_bias_r"] = np.ascontiguousarray(
        inputs["L0_ffn_bias"].reshape(8, 128).T)
    base["L0_ffn_delta_w_r"] = np.ascontiguousarray(
        inputs["L0_ffn_delta_w"].reshape(8, 128, 32).transpose(1, 0, 2)
        .reshape(128, 8 * 32))
    base["c_dact"] = np.ascontiguousarray(
        np.broadcast_to(inputs["L0_ffn_delta_act"].reshape(1, 1), (128, 1)))
    base.update(host_l0_inputs(inputs))
    return base


def kernel(**inputs):
    consts = host_consts()
    if STAGE >= 3:
        consts.update(host_l1_consts())
    base = prep_core_inputs(inputs, consts)
    if STAGE >= 3:
        base.update(host_l1_inputs(inputs))
    x = np.ascontiguousarray(inputs["L0_attn_x"])
    in_maps = []
    for c in range(8):
        m = dict(base)
        m["x0"] = np.ascontiguousarray(x[c])
        in_maps.append(m)
    kb = build(consts, want_dbg=DEBUG)
    nc = kb.finalize()
    res = run_bass_kernel_spmd(nc, in_maps, core_ids=list(range(8)))
    out = np.stack([res.results[c]["out"] for c in range(8)], 0)
    if DEBUG:
        kernel.dbg = res.results
    return out



def emit_l0_attn(kb, pb, want_dbg=False):
    H, op, wait, drain = kb.H, kb.op, kb.wait, kb.drain
    sb = kb.sb
    TC = 512
    NTC = T // TC

    xn_fm, g0 = sb["xn_fm"], sb["g0"]
    smalls = sb["smalls"]
    rp32 = sb["rp32"]
    pw_all = rp32[0:32, 0:T]
    pw_bf = sb["rpbf"][0:32, :]

    thdiv = kb.S("thdiv", (128, 32), F32)
    wconv = kb.S("wconv", (128, 64), F32)
    bconv = kb.S("bconv", (128, 16), F32)
    norm2 = kb.S("norm2", (128, 2), F32)
    wre = kb.S("wre", (128, 32), F32)
    wim = kb.S("wim", (128, 32), F32)
    wy = kb.S("wy", (128, 8 * 128), BF16)
    diag4 = kb.S("diag4", (128, 4 * 128), BF16)
    zr = [kb.S(f"zr{i}", (128, TC + 3), BF16) for i in range(2)]
    ztail = kb.S("ztail", (128, 16 * 3), BF16)
    xv = kb.S("xv", (128, 8 * TC), F32)
    gate = kb.S("gate", (128, 8 * TC), BF16)
    y_u = kb.S("y_u", (128, 8 * TC), BF16)
    sg_tm = kb.S("sg_tm", (128, 4 * D), BF16)
    u_t = [kb.S(f"u_t{p}", (128, TC), F32) for p in range(2)]
    k_r = [kb.S(f"k_r{p}", (128, TC), F32) for p in range(2)]
    frac = [kb.S(f"frac{p}", (128, TC), F32) for p in range(4)]
    afr = [kb.S(f"afr{p}", (128, TC), F32) for p in range(2)]
    sinb = [kb.S(f"sinb{p}", (128, TC), BF16) for p in range(4)]
    cosb = [kb.S(f"cosb{p}", (128, TC), BF16) for p in range(4)]
    pwx = kb.S("pwx", (128, TC), BF16)
    pol1 = kb.S("pol1", (128, TC), F32)
    pol2 = pol1
    pp = kb.S("pp", (128, TC), BF16)
    rein = [kb.S(f"rein{p}", (128, TC), BF16) for p in range(2)]
    imin = [kb.S(f"imin{p}", (128, TC), BF16) for p in range(2)]
    csre = [kb.S(f"csre{p}", (128, TC), BF16) for p in range(2)]
    csim = [kb.S(f"csim{p}", (128, TC), BF16) for p in range(2)]
    sqt = [rein[0], imin[0], rein[1], imin[1]]
    carr = kb.S("carr", (128, 66), F32)
    l32 = kb.S("l32", (32, (5 if want_dbg else 4) * TC), F32)
    skbf = kb.S("skbf", (32, TC), BF16)
    wfc = [kb.S(f"wfc{i}", (128, 8 * 128), BF16) for i in range(2)]
    sgadd = [sb["xnf0"], sb["xnf1"]]
    sg = gate

    msq = l32[0:32, 0:TC]
    dent = l32[0:32, TC:2 * TC]
    skt = l32[0:32, 2 * TC:3 * TC]
    rtmp = l32[0:32, 3 * TC:4 * TC]
    rtmp2 = l32[0:32, 4 * TC:5 * TC] if want_dbg else None

    for (nm, tile) in (("l0_theta_r", thdiv), ("l0_wconv_r", wconv),
                       ("l0_bconv_r", bconv), ("l0_norm2", norm2),
                       ("l0_wre", wre), ("l0_wim", wim)):
        op("sync", lambda e, _n=nm, _t=tile: e.dma_start(out=_t[:], in_=H[_n][:, :]),
           "ld", 16)
    ld_l0 = kb.cnt("ld")
    wait("vector", "ld", ld_l0)
    op("vector", lambda e: e.tensor_scalar(out=thdiv[:], in0=thdiv[:],
                                           scalar1=float(1.0 / (2 * np.pi)),
                                           scalar2=None, op0=ALU.mult))
    op("vector", lambda e: e.memset(smalls[:, 1:2].bitcast(U32), 1))
    op("vector", lambda e: e.memset(smalls[:, 4:5], float(np.pi / 2)))
    op("vector", lambda e: e.memset(carr[:], 0.0))
    op("vector", lambda e: e.memset(ztail[:], 0.0))
    op("vector", lambda e: e.memset(wy[:], 0.0))
    drain("vector")
    op("vector", lambda e: e.tensor_scalar(out=wre[:], in0=wre[:],
                                           scalar1=norm2[:, 0:1], scalar2=None,
                                           op0=ALU.mult))
    op("vector", lambda e: e.tensor_scalar(out=wim[:], in0=wim[:],
                                           scalar1=norm2[:, 1:2], scalar2=None,
                                           op0=ALU.mult))
    drain("vector")
    WYP = 8 * 128
    for ri, wsrc in ((0, wre), (1, wim)):
        for m in range(4):
            j = ri * 4 + m
            for kk in range(4):
                src_ap = wsrc[m * 32:(m + 1) * 32, :]
                dst_ap = bass.AP(wy, kk * 32 * WYP + j * 128 + kk * 32,
                                 [[WYP, 32], [1, 32]])
                op("vector", lambda e, _s=src_ap, _d=dst_ap: e.tensor_copy(_d, _s))
    drain("vector")
    v_init = op("vector", lambda e: e.memset(smalls[:, 2:3], 0.0), "dve")

    wfc_guard = [0, 0]
    sgadd_guard = [0, 0]
    prev_cos_c = [None]
    for tc in range(NTC):
        t0 = tc * TC
        first = tc == 0
        for fc in range(16):
            bz = fc % 2
            in_ap = bass.AP(H["l0_win2"].tensor, fc * 131072,
                            [[1024, 128], [1, 1024]])
            if wfc_guard[bz]:
                kb.wait("gpsimd", "te", wfc_guard[bz])
            lw = op("gpsimd", lambda e, _i=in_ap, _b=bz: e.dma_start(
                out=wfc[_b][:], in_=_i), "gw", 16)
            wait("tensor", "gw", lw)
            bank = pb[2 + bz]
            mmv = None
            for dc in range(8):
                mov = xn_fm[:, dc * T + t0:dc * T + t0 + TC]
                sta = wfc[bz][:, dc * 128:(dc + 1) * 128]
                mmv = op("tensor", lambda e, _b=bank, _s=sta, _m=mov, _d=dc:
                         e.matmul(_b[:], _s, _m, start=(_d == 0), stop=(_d == 7)),
                         "te" if dc == 7 else None)
            wfc_guard[bz] = mmv
            wait("vector", "te", mmv)
            op("vector", lambda e, _b=bz, _f=fc: e.tensor_copy(
                zr[_b][:, 0:3], ztail[:, _f * 3:_f * 3 + 3]))
            ev = op("vector", lambda e, _b=bank, _z=bz: e.tensor_copy(
                zr[_z][:, 3:3 + TC], _b[:]), "dve")
            kb.drain("vector")
            op("vector", lambda e, _b=bz, _f=fc: e.tensor_copy(
                ztail[:, _f * 3:_f * 3 + 3], zr[_b][:, TC:TC + 3]))
            wait("tensor", "dve", ev)
            wait("vector", "te", wfc_guard[bz])
            for tap in range(4):
                op("vector", lambda e, _f=fc, _t=tap: e.tensor_scalar(
                    out=diag4[:, _t * 128:(_t + 1) * 128], in0=sb["ident"][:],
                    scalar1=wconv[:, (_f * 4 + _t):(_f * 4 + _t) + 1],
                    scalar2=None, op0=ALU.mult))
            v_diag = op("vector", lambda e: e.memset(smalls[:, 3:4], 0.0), "dve")
            wait("tensor", "dve", v_diag)
            bank2 = pb[4 + bz]
            mmv2 = None
            for tap in range(4):
                mov = zr[bz][:, tap:tap + TC]
                sta = diag4[:, tap * 128:(tap + 1) * 128]
                mmv2 = op("tensor", lambda e, _b=bank2, _s=sta, _m=mov, _t=tap:
                          e.matmul(_b[:], _s, _m, start=(_t == 0), stop=(_t == 3)),
                          "te" if tap == 3 else None)
            wait("scalar", "te", mmv2)
            if fc < 8:
                a = op("scalar", lambda e, _b=bank2, _f=fc: e.activation(
                    xv[:, _f * TC:(_f + 1) * TC], _b[:], AF.Identity,
                    bias=bconv[:, _f:_f + 1], scale=1.0), "act")
            else:
                a = op("scalar", lambda e, _b=bank2, _f=fc: e.activation(
                    gate[:, (_f - 8) * TC:(_f - 7) * TC], _b[:], AF.Silu,
                    bias=bconv[:, _f:_f + 1], scale=1.0), "act")
            wait("tensor", "act", a)
            wait("vector", "te", mmv2)
        a_convdone = kb.cnt("act")

        if want_dbg and "dbg_xv" in H:
            wait("sync", "act", a_convdone)
            kb.wait("gpsimd", "act", a_convdone)
            for fcc in range(8):
                op("sync", lambda e, _f=fcc, _t0=t0: e.dma_start(
                    out=H["dbg_xv"].rearrange("(c p) t -> c p t", p=128)
                    [_f][:, _t0:_t0 + TC],
                    in_=xv[:, _f * TC:(_f + 1) * TC]), "st", 16)
                op("gpsimd", lambda e, _f=fcc, _t0=t0: e.dma_start(
                    out=H["dbg_gate"].rearrange("(c p) t -> c p t", p=128)
                    [_f][:, _t0:_t0 + TC],
                    in_=gate[:, _f * TC:(_f + 1) * TC]), "st", 16)
            kb.wait("gpsimd", "st", kb.cnt("st"))
            kb.wait("vector", "st", kb.cnt("st"))
            kb.wait("scalar", "st", kb.cnt("st"))
        if "d" not in L0_PARTS:
            continue
        op("vector", lambda e, _t0=t0: e.tensor_tensor_scan(
            dent, pw_all[:, _t0:_t0 + TC], pw_all[:, _t0:_t0 + TC],
            carr[0:32, 64:65], ALU.add, ALU.bypass))
        drain("vector")
        op("vector", lambda e: e.tensor_copy(carr[0:32, 64:65],
                                             dent[:, TC - 1:TC]))
        kb.drain("vector")
        op("vector", lambda e: e.tensor_single_scalar(rtmp, dent, 1e-4,
                                                      ALU.max))
        drain("vector")
        op("vector", lambda e: e.reciprocal(rtmp, rtmp))
        drain("vector")

        if "B" not in L0_PARTS:
            continue
        wait("tensor", "act", a_convdone)
        msq_started = False
        for c in range(8):
            mm = op("tensor", lambda e, _c=c, _t0=t0: e.matmul(
                pb[1][:], sb["kexp"][0:32, _c * 128:(_c + 1) * 128],
                pw_bf[:, _t0:_t0 + TC], start=True, stop=True), "te")
            wait("vector", "te", mm)
            ev = op("vector", lambda e: e.tensor_copy(pwx[:], pb[1][:]), "dve")
            wait("tensor", "dve", ev)
            xvc = xv[:, c * TC:(c + 1) * TC]
            op("vector", lambda e, _x=xvc: e.tensor_scalar(
                out=pol1[:], in0=_x, scalar1=smalls[:, 22:23],
                scalar2=smalls[:, 21:22], op0=ALU.mult, op1=ALU.add))
            drain("vector")
            op("vector", lambda e, _x=xvc: e.scalar_tensor_tensor(
                out=pol2[:], in0=_x, scalar=1.0, in1=pol1[:], op0=ALU.mult,
                op1=ALU.mult))
            drain("vector")
            op("vector", lambda e: e.scalar_tensor_tensor(
                out=pp[:], in0=pol2[:], scalar=smalls[:, 20:21], in1=pwx[:],
                op0=ALU.add, op1=ALU.mult))
            drain("vector")
            def front(mp, prev_cos=None):
                ms = [2 * mp, 2 * mp + 1]
                g = None
                for p, m in enumerate(ms):
                    thc = thdiv[:, c * 4 + m:c * 4 + m + 1]
                    g = op("vector", lambda e, _x=xvc, _th=thc, _p=p:
                           e.tensor_scalar(
                        out=u_t[_p][:], in0=_x, scalar1=_th, scalar2=MAGIC,
                        op0=ALU.mult, op1=ALU.add),
                        "dve" if p == 1 else None)
                wait("vector", "dve", g)
                for p in range(2):
                    g = op("vector", lambda e, _p=p: e.tensor_scalar(
                        out=k_r[_p][:], in0=u_t[_p][:], scalar1=MAGIC,
                        scalar2=None, op0=ALU.subtract),
                        "dve" if p == 1 else None)
                wait("vector", "dve", g)
                for p, m in enumerate(ms):
                    thc = thdiv[:, c * 4 + m:c * 4 + m + 1]
                    g = op("vector", lambda e, _x=xvc, _th=thc, _p=p,
                           _q=2 * mp + p: e.scalar_tensor_tensor(
                        out=frac[_q][:], in0=_x, scalar=_th, in1=k_r[_p][:],
                        op0=ALU.mult, op1=ALU.subtract),
                        "dve" if p == 1 else None)
                wait("scalar", "dve", g)
                ga = None
                for p in range(2):
                    op("scalar", lambda e, _q=2 * mp + p: e.activation(
                        sinb[_q][:], frac[_q][:], AF.Sin, bias=0.0,
                        scale=float(2 * np.pi)))
                if prev_cos is not None:
                    kb.wait("scalar", "act", prev_cos)
                for p in range(2):
                    ga = op("scalar", lambda e, _p=p, _q=2 * mp + p:
                            e.activation(
                        afr[_p][:], frac[_q][:], AF.Abs, bias=0.0, scale=1.0),
                        "act" if p == 1 else None)
                kb.wait("scalar", "act", ga)
                a_cos = None
                for p in range(2):
                    a_cos = op("scalar", lambda e, _p=p, _q=2 * mp + p:
                               e.activation(
                        cosb[_q][:], afr[_p][:], AF.Sin, bias=smalls[:, 4:5],
                        scale=float(-2 * np.pi)), "act" if p == 1 else None)
                return a_cos

            def tail(mp, a_cos, lastmm):
                ms = [2 * mp, 2 * mp + 1]
                wait("vector", "act", a_cos)
                if lastmm is not None:
                    wait("vector", "te", lastmm)
                for p in range(2):
                    op("vector", lambda e, _p=p, _q=2 * mp + p: e.tensor_tensor(
                        rein[_p][:], pp[:], cosb[_q][:], ALU.mult))
                    g = op("vector", lambda e, _p=p, _q=2 * mp + p:
                           e.tensor_tensor(
                        imin[_p][:], pp[:], sinb[_q][:], ALU.mult),
                        "dve" if p == 1 else None)
                wait("vector", "dve", g)
                for p, m in enumerate(ms):
                    ci = (c * 4 + m) * 2
                    op("vector", lambda e, _ci=ci, _p=p: e.tensor_tensor_scan(
                        csre[_p][:], rein[_p][:], rein[_p][:],
                        carr[:, _ci:_ci + 1], ALU.add, ALU.bypass))
                    g = op("vector", lambda e, _ci=ci, _p=p: e.tensor_tensor_scan(
                        csim[_p][:], imin[_p][:], imin[_p][:],
                        carr[:, _ci + 1:_ci + 2], ALU.add, ALU.bypass),
                        "dve" if p == 1 else None)
                wait("vector", "dve", g)
                for p, m in enumerate(ms):
                    ci = (c * 4 + m) * 2
                    op("vector", lambda e, _ci=ci, _p=p: e.tensor_copy(
                        carr[:, _ci:_ci + 1], csre[_p][:, TC - 1:TC]))
                    op("vector", lambda e, _ci=ci, _p=p: e.tensor_copy(
                        carr[:, _ci + 1:_ci + 2], csim[_p][:, TC - 1:TC]))
                    op("vector", lambda e, _p=p: e.tensor_tensor(
                        sqt[2 * _p][:], csre[_p][:], csre[_p][:], ALU.mult))
                    g = op("vector", lambda e, _p=p: e.tensor_tensor(
                        sqt[2 * _p + 1][:], csim[_p][:], csim[_p][:],
                        ALU.mult), "dve" if p == 1 else None)
                wait("tensor", "dve", g)
                for p, m in enumerate(ms):
                    op("tensor", lambda e, _c=c, _p=p, _st=msq_st[0]:
                       e.matmul(pb[6][0:32, :],
                                sb["psum32"][:, _c * 32:(_c + 1) * 32],
                                sqt[2 * _p][:], start=_st, stop=False), None)
                    msq_st[0] = False
                    op("tensor", lambda e, _m=m, _p=p, _first=(m == 0):
                       e.matmul(pb[0][:], wy[:, _m * 128:(_m + 1) * 128],
                                csre[_p][:], start=_first, stop=False), None)
                    is_last = (c == 7 and m == 3)
                    op("tensor", lambda e, _c=c, _p=p, _L=is_last:
                       e.matmul(pb[6][0:32, :],
                                sb["psum32"][:, _c * 32:(_c + 1) * 32],
                                sqt[2 * _p + 1][:], start=False, stop=_L), None)
                    lastmm = op("tensor", lambda e, _m=m, _p=p:
                                e.matmul(pb[0][:],
                                         wy[:, (4 + _m) * 128:(5 + _m) * 128],
                                         csim[_p][:], start=False,
                                         stop=(_m == 3)), "te")
                return lastmm

            msq_st = [not msq_started]
            msq_started = True
            lastmm = None
            ac0 = front(0, prev_cos=prev_cos_c[0])
            ac1 = front(1, prev_cos=ac0)
            prev_cos_c[0] = ac1
            lastmm = tail(0, ac0, lastmm)
            lastmm = tail(1, ac1, lastmm)
            wait("vector", "te", lastmm)
            ev = op("vector", lambda e, _c=c: e.tensor_copy(
                y_u[:, _c * TC:(_c + 1) * TC], pb[0][:]), "dve")
            wait("tensor", "dve", ev)
        if "C" not in L0_PARTS:
            continue
        wait("vector", "te", kb.cnt("te"))
        op("vector", lambda e: e.tensor_scalar(
            out=msq, in0=pb[6][0:32, :], scalar1=float(1.0 / 256.0),
            scalar2=None, op0=ALU.mult))
        drain("vector")
        op("vector", lambda e: e.tensor_tensor(msq, msq, rtmp, ALU.mult))
        drain("vector")
        op("vector", lambda e: e.tensor_tensor(msq, msq, rtmp, ALU.mult))
        drain("vector")
        v_msq = op("vector", lambda e: e.tensor_scalar(
            out=msq, in0=msq, scalar1=1e-5, scalar2=None, op0=ALU.add), "dve")
        if want_dbg and "dbg_msq" in H:
            v_m = op("vector", lambda e: e.tensor_copy(rtmp2, msq), "dve")
            kb.drain("vector")
            wait("sync", "dve", v_m)
            op("sync", lambda e, _t0=t0: e.dma_start(
                out=H["dbg_msq"][:, _t0:_t0 + TC], in_=rtmp2), "st", 16)
            op("sync", lambda e, _t0=t0: e.dma_start(
                out=H["dbg_den"][:, _t0:_t0 + TC], in_=dent), "st", 16)
            wait("vector", "st", kb.cnt("st"))
        wait("scalar", "dve", v_msq)
        a_rs = op("scalar", lambda e: e.activation(
            skt, msq, AF.Sqrt, bias=0.0, scale=1.0), "act")
        wait("vector", "act", a_rs)
        op("vector", lambda e: e.reciprocal(skt, skt))
        drain("vector")
        op("vector", lambda e: e.tensor_tensor(skt, skt, rtmp, ALU.mult))
        drain("vector")
        v_sk = op("vector", lambda e: e.tensor_copy(skbf[:], skt), "dve")
        if want_dbg and "dbg_yu" in H:
            wait("gpsimd", "dve", kb.cnt("dve"))
            wait("sync", "dve", kb.cnt("dve"))
            for fcc in range(8):
                op("gpsimd", lambda e, _f=fcc, _t0=t0: e.dma_start(
                    out=H["dbg_yu"].rearrange("(c p) t -> c p t", p=128)
                    [_f][:, _t0:_t0 + TC],
                    in_=y_u[:, _f * TC:(_f + 1) * TC]), "st", 16)
            op("sync", lambda e, _t0=t0: e.dma_start(
                out=H["dbg_sk"][:, _t0:_t0 + TC], in_=skt), "st", 16)
            kb.wait("gpsimd", "st", kb.cnt("st"))
            kb.wait("vector", "st", kb.cnt("st"))
        if "D" not in L0_PARTS:
            continue
        wait("tensor", "dve", v_sk)
        for c in range(8):
            mm = op("tensor", lambda e, _c=c: e.matmul(
                pb[1][:], sb["kexp"][0:32, _c * 128:(_c + 1) * 128],
                skbf[:], start=True, stop=True), "te")
            wait("vector", "te", mm)
            off = c * TC
            op("vector", lambda e, _o=off: e.scalar_tensor_tensor(
                out=y_u[:, _o:_o + TC], in0=y_u[:, _o:_o + TC], scalar=1.0,
                in1=pb[1][:], op0=ALU.mult, op1=ALU.mult))
            kb.drain("vector")
            ev = op("vector", lambda e, _o=off: e.tensor_tensor(
                y_u[:, _o:_o + TC], y_u[:, _o:_o + TC], gate[:, _o:_o + TC],
                ALU.mult), "dve")
            wait("tensor", "dve", ev)
        for fo in range(8):
            bz = fo % 2
            in_ap = bass.AP(H["l0_wout2"].tensor, fo * 131072,
                            [[1024, 128], [1, 1024]])
            if wfc_guard[bz]:
                kb.wait("gpsimd", "te", wfc_guard[bz])
            lw = op("gpsimd", lambda e, _i=in_ap, _b=bz: e.dma_start(
                out=wfc[_b][:], in_=_i), "gw", 16)
            wait("tensor", "gw", lw)
            bank = pb[2 + bz]
            mmv = None
            for c in range(8):
                mov = y_u[:, c * TC:(c + 1) * TC]
                sta = wfc[bz][:, c * 128:(c + 1) * 128]
                mmv = op("tensor", lambda e, _b=bank, _s=sta, _m=mov, _c=c:
                         e.matmul(_b[:], _s, _m, start=(_c == 0), stop=(_c == 7)),
                         "te" if c == 7 else None)
            wfc_guard[bz] = mmv
            wait("vector", "te", mmv)
            ev = op("vector", lambda e, _b=bank, _f=fo, _t0=t0:
                    e.scalar_tensor_tensor(
                        out=sg[:, _f * TC:(_f + 1) * TC],
                        in0=g0[:, _f * T + _t0:_f * T + _t0 + TC],
                        scalar=1.0, in1=_b[:], op0=ALU.mult, op1=ALU.mult),
                    "dve")
            wait("tensor", "dve", ev)
        sg_done = kb.cnt("dve")
        wait("sync", "dve", sg_done)
        tpv = None
        for fo in range(8):
            sa = sg[:, fo * TC:(fo + 1) * TC].rearrange("p (i q) -> p i q", q=128)
            da = bass.AP(sg_tm, fo * 128, [[4 * D, 128], [D, 4], [1, 128]])
            tpv = op("sync", lambda e, _s=sa, _d=da: e.dma_start_transpose(_d, _s),
                     "tp", 16)
        xv_h = H["x0"].rearrange("(n p) d -> n p d", p=128)
        x1_h = H["x1"].rearrange("(n p) d -> n p d", p=128)
        xtb_l0 = [sb["xt0"], sb["xt1"]]
        for ii in range(TC // 128):
            i = t0 // 128 + ii
            bx = ii % 2
            ldv = op("sync", lambda e, _i=i, _x=xtb_l0[bx]: e.dma_start(
                out=_x[:], in_=xv_h[_i]), "ld", 16)
            wait("vector", "ld", ldv)
            wait("vector", "tp", tpv)
            if sgadd_guard[bx]:
                wait("vector", "st", sgadd_guard[bx])
            v1 = op("vector", lambda e, _b=bx, _x=xtb_l0[bx], _ii=ii:
                    e.tensor_tensor(
                sgadd[_b][:], _x[:],
                sg_tm[:, _ii * D:(_ii + 1) * D], ALU.add), "dve")
            wait("sync", "dve", v1)
            st = op("sync", lambda e, _i=i, _b=bx: e.dma_start(
                out=x1_h[_i], in_=sgadd[_b][:]), "st", 16)
            sgadd_guard[bx] = st
        if want_dbg and "dbg_s0" in H:
            wait("gpsimd", "dve", sg_done)
            for fo in range(8):
                gv = op("gpsimd", lambda e, _f=fo, _t0=t0: e.dma_start(
                    out=H["dbg_s0"].rearrange("(c p) t -> c p t", p=128)
                    [_f][:, _t0:_t0 + TC],
                    in_=sg[:, _f * TC:(_f + 1) * TC]), "st", 16)
            wait("scalar", "st", kb.cnt("st"))
            wait("vector", "st", kb.cnt("st"))
    return


def emit_l0_sraw(kb, pb):
    H, op, wait, drain = kb.H, kb.op, kb.wait, kb.drain
    sb = kb.sb
    smalls = sb["smalls"]
    sr_w = kb.S("sr_w", (128, 8 * 32), BF16)
    srpack = kb.S("srpack", (32, (T + 3) + T), F32)
    sr_z = srpack[0:32, 0:T + 3]
    sr_c = srpack[0:32, T + 3:2 * T + 3]
    sr_t = kb.sb["rp32"][0:32, T:2 * T]
    svec = kb.S("svec", (32, 8), F32)
    dv3 = kb.S("dv3", (1, 4), F32)
    dv128 = kb.S("dv128", (128, 3), F32)
    ones1 = kb.S("ones1", (1, 128), F32)

    in_ap = bass.AP(H["l0_win"].tensor, 2048,
                    [[2080, 128], [128 * 2080, 8], [1, 32]])
    l1 = op("gpsimd", lambda e, _i=in_ap: e.dma_start(out=sr_w[:], in_=_i),
            "gw", 16)
    op("sync", lambda e: e.dma_start(out=svec[:, 0:4], in_=H["l0_wconv_sr"][:, :]),
       "ld", 16)
    op("sync", lambda e: e.dma_start(out=svec[:, 4:5], in_=H["l0_bconv_sr"][:, :]),
       "ld", 16)
    op("sync", lambda e: e.dma_start(out=svec[:, 5:6], in_=H["l0_slopes"][:, :]),
       "ld", 16)
    op("sync", lambda e: e.dma_start(out=svec[:, 6:7], in_=H["l0_score"][:, :]),
       "ld", 16)
    op("sync", lambda e: e.dma_start(out=dv3[:, 0:3], in_=H["l0_deriv"][:, :]),
       "ld", 16)
    wait("scalar", "ld", kb.cnt("ld"))
    wait("vector", "ld", kb.cnt("ld"))
    wait("tensor", "gw", l1)

    a_e = op("scalar", lambda e: e.activation(dv3[:1, 0:3], dv3[:1, 0:3],
                                              AF.Exp, bias=0.0, scale=1.0),
             "act")
    wait("vector", "act", a_e)
    op("vector", lambda e: e.tensor_reduce(dv3[:1, 3:4], dv3[:1, 0:3],
                                           AX.X, ALU.add))
    drain("vector")
    op("vector", lambda e: e.reciprocal(dv3[:1, 3:4], dv3[:1, 3:4]))
    drain("vector")
    op("vector", lambda e: e.tensor_scalar(
        out=dv3[:1, 0:3], in0=dv3[:1, 0:3], scalar1=dv3[:1, 3:4],
        scalar2=None, op0=ALU.mult))
    v_w = op("vector", lambda e: e.memset(ones1[:], 1.0), "dve")
    kb.drain("vector")
    wait("tensor", "dve", v_w)
    mmb = op("tensor", lambda e: e.matmul(pb[1][:, 0:3], ones1[:],
                                          dv3[:1, 0:3], start=True, stop=True),
             "te")
    wait("vector", "te", mmb)
    op("vector", lambda e: e.tensor_copy(dv128[:], pb[1][:, 0:3]))
    kb.drain("vector")
    op("vector", lambda e: e.tensor_copy(smalls[:, 20:21], dv128[:, 0:1]))
    op("vector", lambda e: e.tensor_copy(smalls[:, 21:22], dv128[:, 1:2]))
    op("vector", lambda e: e.tensor_scalar(out=smalls[:, 22:23],
                                           in0=dv128[:, 2:3], scalar1=-1.0,
                                           scalar2=None, op0=ALU.mult))
    drain("vector")

    a_sp1 = op("scalar", lambda e: e.activation(svec[:32, 7:8], svec[:32, 5:6],
                                                AF.Exp, bias=0.0, scale=1.0),
               "act")
    kb.drain("scalar")
    a_sp2 = op("scalar", lambda e: e.activation(svec[:32, 7:8], svec[:32, 7:8],
                                                AF.Ln, bias=1.0, scale=1.0),
               "act")

    op("vector", lambda e: e.memset(srpack[0:32, 0:3], 0.0))
    drain("vector")
    for tt in range(4):
        bank = pb[0]
        mmv = None
        for dc in range(8):
            mov = sb["xn_fm"][:, dc * T + tt * 512:dc * T + tt * 512 + 512]
            sta = sr_w[:, dc * 32:dc * 32 + 32]
            mmv = op("tensor", lambda e, _b=bank, _s=sta, _m=mov, _d=dc:
                     e.matmul(_b[:32, :], _s, _m, start=(_d == 0),
                              stop=(_d == 7)), "te" if dc == 7 else None)
        wait("vector", "te", mmv)
        ev = op("vector", lambda e, _b=bank, _t=tt: e.tensor_copy(
            srpack[0:32, 3 + _t * 512:3 + _t * 512 + 512], _b[0:32, :]), "dve")
        kb.drain("vector")
        wait("tensor", "dve", ev)
    op("vector", lambda e: e.tensor_scalar(
        out=sr_c, in0=srpack[0:32, 0:T], scalar1=svec[:32, 0:1], scalar2=None,
        op0=ALU.mult))
    drain("vector")
    for tap in range(1, 4):
        op("vector", lambda e, _t=tap: e.scalar_tensor_tensor(
            out=sr_c, in0=srpack[0:32, _t:T + _t], scalar=svec[:32, _t:_t + 1],
            in1=sr_c, op0=ALU.mult, op1=ALU.add))
        drain("vector")
    op("vector", lambda e: e.tensor_tensor(svec[:32, 4:5], svec[:32, 4:5],
                                           svec[:32, 6:7], ALU.mult))
    drain("vector")
    op("vector", lambda e: e.tensor_scalar(
        out=sr_c, in0=sr_c, scalar1=svec[:32, 6:7],
        scalar2=svec[:32, 4:5], op0=ALU.mult, op1=ALU.add))
    drain("vector")
    op("vector", lambda e: e.tensor_scalar(
        out=sr_c, in0=sr_c, scalar1=20.0, scalar2=-20.0,
        op0=ALU.min, op1=ALU.max))
    drain("vector")
    wait("vector", "act", a_sp2)
    op("vector", lambda e: e.tensor_scalar(
        out=sr_t, in0=sr_t, scalar1=svec[:32, 7:8],
        scalar2=None, op0=ALU.mult))
    drain("vector")
    v_u = op("vector", lambda e: e.tensor_tensor(sr_c, sr_c, sr_t,
                                                 ALU.add), "dve")
    wait("scalar", "dve", v_u)
    a_pw = op("scalar", lambda e: e.activation(sb["rp32"][0:32, 0:T], sr_c,
                                               AF.Exp, bias=0.0, scale=1.0),
              "act")
    wait("vector", "act", a_pw)
    v_pwb = op("vector", lambda e: e.tensor_copy(sb["rpbf"][0:32, :],
                                                 sb["rp32"][0:32, 0:T]), "dve")
    drain("vector")
    if "dbg_pw" in H:
        wait("sync", "dve", v_pwb)
        op("sync", lambda e: e.dma_start(out=H["dbg_pw"][:, :],
                                         in_=sb["rp32"][0:32, 0:T]), "st", 16)
    return v_pwb



def host_l1_inputs(inputs):
    h = {}
    Wqkv = inputs["L1_attn_Wqkv"]
    bqkv = inputs["L1_attn_bqkv"]
    cols = np.zeros((1024, 32 * 128), np.float32)
    bias = np.zeros(32 * 128, np.float32)

    def put(col, src_row):
        cols[:, col] = Wqkv[src_row]
        bias[col] = bqkv[src_row]

    for hh in range(16):
        for d in range(16):
            put((hh // 4) * 128 + (hh % 4) * 32 + d, hh * 64 + d)
            put(512 + (hh // 4) * 128 + (hh % 4) * 32 + d, 1024 + hh * 64 + d)
        for d in range(16, 64):
            put(1024 + hh * 64 + (d - 16), hh * 64 + d)
            put(2048 + hh * 64 + (d - 16), 1024 + hh * 64 + d)
        for d in range(64):
            put(3072 + (hh // 2) * 128 + (hh % 2) * 64 + d, 2048 + hh * 64 + d)
    h["l1_wqkvT"] = np.ascontiguousarray(
        cols.reshape(8, 128, 32, 128).transpose(2, 1, 0, 3)
        .reshape(32, 128, 1024))
    h["l1_bqkv"] = np.ascontiguousarray(bias.reshape(32, 128).T)

    def oc_tiles(wT):
        return np.ascontiguousarray(
            wT.reshape(8, 128, 8, 128).transpose(2, 1, 0, 3)
            .reshape(8, 128, 1024))

    h["l1_wqaT"] = oc_tiles(inputs["L1_attn_Wqa"].T)
    h["l1_wkaT"] = oc_tiles(inputs["L1_attn_Wka"].T)
    h["l1_wdT"] = oc_tiles(inputs["L1_attn_Wd"].T)
    h["l1_bqa"] = np.ascontiguousarray(inputs["L1_attn_bqa"].reshape(8, 128).T)
    h["l1_bka"] = np.ascontiguousarray(inputs["L1_attn_bka"].reshape(8, 128).T)
    h["l1_bd"] = np.ascontiguousarray(inputs["L1_attn_bd"].reshape(8, 128).T)
    h["l1_gwT"] = np.ascontiguousarray(inputs["L1_ffn_gate_w"].T)
    h["l1_wg"] = np.ascontiguousarray(
        inputs["L1_ffn_wg"].reshape(8, 8, 128, 2, 128)
        .transpose(0, 2, 1, 3, 4).reshape(8, 128, 2048))
    h["l1_wu"] = np.ascontiguousarray(
        inputs["L1_ffn_wu"].reshape(8, 8, 128, 2, 128)
        .transpose(0, 2, 1, 3, 4).reshape(8, 128, 2048))
    h["l1_wd"] = np.ascontiguousarray(
        inputs["L1_ffn_wd"].reshape(8, 2, 128, 8, 128)
        .transpose(3, 2, 0, 1, 4).reshape(8, 128, 2048))
    return h


def host_l1_consts():
    c = {}
    P = np.zeros((128, 128), BF16NP)
    for hh in range(4):
        for d in range(8):
            P[hh * 32 + d + 8, hh * 32 + d] = -1
            P[hh * 32 + d, hh * 32 + d + 8] = 1
    c["c_rot"] = P
    inv_freq = (1.0 / 10000.0 ** (np.arange(0, 16, 2) / 16.0)).astype(np.float64)
    t = np.arange(T, dtype=np.float64)
    ang = t[None, :] * inv_freq[:, None]
    cosf = np.zeros((128, T), np.float32)
    sinf = np.zeros((128, T), np.float32)
    for hh in range(4):
        for d in range(16):
            cosf[hh * 32 + d] = np.cos(ang[d % 8])
            sinf[hh * 32 + d] = np.sin(ang[d % 8])
    c["c_cosf"] = cosf
    c["c_sinf"] = sinf
    m = np.zeros((128, 256), BF16NP)
    for j in range(128):
        for q in range(128):
            if 0 <= q - j <= 15:
                m[j, q] = 1
            if 0 <= (q + 128) - j <= 15:
                m[j, 128 + q] = 1
    c["c_wmask"] = m
    c["c_cmask"] = np.triu(np.ones((128, 128), np.float32)).astype(BF16NP)
    c["c_ones1b"] = np.ones((128, 1), BF16NP)
    ob = np.zeros((128, 12 * 16), BF16NP)
    for j in range(4):
        for r in range(128):
            ob[r, j * 16 + 4 * j + r // 32] = 1
    for p in range(8):
        for r in range(128):
            ob[r, (4 + p) * 16 + 2 * p + r // 64] = 1
    c["c_oblk"] = ob
    return c


def emit_l1(kb, want_dbg=False):
    raise NotImplementedError

def _emit_ln_tile(kb, src_tile, dst_tile):
    op, wait, drain = kb.op, kb.wait, kb.drain
    sb = kb.sb
    sq_done = op("scalar", lambda e: e.activation(
        sb["sqd"][:], src_tile, AF.Square, bias=0.0, scale=1.0,
        accum_out=sb["red"][:, 1:2]), "act")
    op("vector", lambda e: e.tensor_reduce(sb["red"][:, 0:1], src_tile,
                                           AX.X, ALU.add))
    drain("vector")
    op("vector", lambda e: e.tensor_scalar(
        out=sb["red"][:, 2:3], in0=sb["red"][:, 0:1], scalar1=1.0 / D,
        scalar2=None, op0=ALU.mult))
    drain("vector")
    op("vector", lambda e: e.tensor_tensor(
        sb["red"][:, 3:4], sb["red"][:, 2:3], sb["red"][:, 2:3], ALU.mult))
    wait("vector", "act", sq_done)
    drain("vector")
    op("vector", lambda e: e.scalar_tensor_tensor(
        out=sb["red"][:, 4:5], in0=sb["red"][:, 1:2], scalar=1.0 / D,
        in1=sb["red"][:, 3:4], op0=ALU.mult, op1=ALU.subtract))
    drain("vector")
    v_var = op("vector", lambda e: e.tensor_scalar(
        out=sb["red"][:, 4:5], in0=sb["red"][:, 4:5], scalar1=1e-5,
        scalar2=None, op0=ALU.add), "dve")
    wait("scalar", "dve", v_var)
    op("scalar", lambda e: e.activation(sb["red"][:, 5:6], sb["red"][:, 4:5],
                                        AF.Ln, bias=0.0, scale=1.0))
    kb.drain("scalar")
    a_r = op("scalar", lambda e: e.activation(
        sb["red"][:, 6:7], sb["red"][:, 5:6], AF.Exp, bias=0.0, scale=-0.5),
        "act")
    op("vector", lambda e: e.tensor_scalar(
        out=sb["red"][:, 7:8], in0=sb["red"][:, 2:3], scalar1=-1.0,
        scalar2=None, op0=ALU.mult))
    wait("vector", "act", a_r)
    drain("vector")
    return op("vector", lambda e: e.tensor_scalar(
        out=dst_tile, in0=src_tile, scalar1=kb.sb["red"][:, 7:8],
        scalar2=kb.sb["red"][:, 6:7], op0=ALU.add, op1=ALU.mult), "dve")


def emit_l1(kb, want_dbg=False):
    H, op, wait, drain = kb.H, kb.op, kb.wait, kb.drain
    sb = kb.sb
    PS = kb.PSALL
    pb = [PS[:, i * 512:(i + 1) * 512] for i in range(8)]
    smalls = sb["smalls"]
    SCALE = 0.125

    arenaA = sb["xn_fm"]
    arenaB = sb["g0"]

    ones1b = kb.S("ones1b", (128, 1), BF16)
    onesr1 = kb.S("onesr1", (1, 128), BF16)
    ones16 = kb.S("ones16", (16, 128), F32)
    oblk = kb.S("oblk", (128, 12 * 16), BF16)
    bqkv_t = kb.S("bqkv_t", (128, 32), F32)
    bqa_t = kb.S("bqa_t", (128, 8), F32)
    bka_t = kb.S("bka_t", (128, 8), F32)
    bd_t = kb.S("bd_t", (128, 8), F32)
    gw_t = kb.S("gw_t", (128, 8 * 8), BF16)
    qtmp2 = kb.S("qtmp2", (128, 16), F32)
    diag16 = kb.S("diag16", (16, 16), F32)
    identf = kb.S("identf", (128, 128), F32)
    rdc = kb.S("rdc", (128, 2), F32)
    nrm = kb.S("nrm", (16, 32), F32)
    negmb1 = kb.S("negmb1", (128, 16), F32)
    negmb2 = kb.S("negmb2", (128, 16), F32)
    dentm = kb.S("dentm", (128, 256), F32)
    relub = kb.S("relub", (128, 512), BF16)
    ubuf = kb.S("ubuf", (128, 512), BF16)
    g1st = [kb.S(f"g1st{i}", (128, 512), BF16) for i in range(2)]
    xn1st = [kb.S(f"xn1st{i}", (128, D), BF16) for i in range(2)]
    xn1ft = [kb.S(f"xn1ft{i}", (128, D), BF16) for i in range(2)]
    kb.M_MOE = kb.marker()
    masksf = kb.S("masksf", (8, T), BF16)
    maskb = kb.S("maskb", (128, 8 * 512), BF16)
    onehot = kb.S("onehot", (128, NT * 8), BF16)
    xnb = kb.S("xnb", (128, 8 * 512), BF16)
    wgu = [kb.S(f"wgu{i}", (128, 32 * 128), BF16) for i in range(2)]
    inter = kb.S("inter", (128, 16 * 512), BF16)
    wst = [kb.S(f"wst{i}", (128, 16 * 128), BF16) for i in range(2)]
    xt_m = [kb.S(f"xt{i}_l1m", (128, D), F32) for i in range(2)]
    mrow = kb.S("mrow", (1, 512), BF16)
    kb.sb["xt0"], kb.sb["xt1"] = xt_m
    qrest = arenaA
    v_tm = arenaB

    OB = PS[0:64, 1024:3072]
    DENROW = PS[64:65, 1024:3072]

    for nm, tile in (("c_ones1b", ones1b), ("c_oblk", oblk),
                     ("l1_bqkv", bqkv_t), ("l1_bqa", bqa_t),
                     ("l1_bka", bka_t), ("l1_bd", bd_t)):
        op("sync", lambda e, _n=nm, _t=tile: e.dma_start(out=_t[:], in_=H[_n][:, :]),
           "ld", 16)
    op("gpsimd", lambda e: e.dma_start(
        out=gw_t[:], in_=bass.AP(H["l1_gwT"].tensor, 0,
                                 [[8, 128], [1024, 8], [1, 8]])), "gw", 16)
    ld0 = kb.cnt("ld")
    lw0 = kb.cnt("gw")
    wait("vector", "ld", ld0)
    wait("vector", "gw", lw0)
    op("vector", lambda e: e.memset(ones16[:], 1.0))
    op("vector", lambda e: e.memset(onesr1[:], 1.0))
    op("vector", lambda e: e.memset(smalls[:, 5:6], float(np.log(0.125))))
    op("vector", lambda e: e.tensor_copy(identf[:], sb["ident"][:]))
    drain("vector")

    x1v = H["x1"].rearrange("(n p) d -> n p d", p=128)
    tp_guard = [None, None]
    ft_guard = [0, 0]
    st_guard = [None, None]
    oh_done = None
    xtb_m = [sb["xt0"], sb["xt1"]]
    for i in range(NT):
        b = i % 2
        ldv = op("sync", lambda e, _i=i, _x=xtb_m[b]: e.dma_start(
            out=_x[:], in_=x1v[_i]), "ld", 16)
        wait("vector", "ld", ldv)
        wait("scalar", "ld", ldv)
        if i >= 2 and tp_guard[b] is not None:
            wait("vector", "tp", tp_guard[b])
        v_xn = _emit_ln_tile(kb, xtb_m[b][:], xn1st[b][:])
        wait("sync", "dve", v_xn)
        if i >= 2 and ft_guard[b]:
            wait("sync", "te", ft_guard[b])
        if i >= 2 and st_guard[b]:
            wait("sync", "st", st_guard[b])
        sa = xn1st[b][:].rearrange("p (c q) -> p c q", q=128)
        da = xn1ft[b][:].rearrange("p (c q) -> p c q", q=128)
        tpv = op("sync", lambda e, _s=sa, _d=da: e.dma_start_transpose(_d, _s),
                 "tp", 16)
        tp_guard[b] = tpv
        out_ap = bass.AP(H["xn1"].tensor, i * 128,
                         [[2048, 128], [128 * 2048, 8], [1, 128]])
        wait("sync", "tp", tpv)
        st1 = op("sync", lambda e, _o=out_ap, _b=b: e.dma_start(
            out=_o, in_=xn1ft[_b][:].rearrange("p (c q) -> p c q", q=128)),
            "st", 16)
        st_guard[b] = st1
        wait("tensor", "tp", tpv)
        wait("tensor", "gw", lw0)
        mmv = None
        for dc in range(8):
            mmv = op("tensor", lambda e, _b=b, _d=dc: e.matmul(
                pb[7][:, 0:8], xn1ft[_b][:, _d * 128:(_d + 1) * 128],
                gw_t[:, _d * 8:(_d + 1) * 8], start=(_d == 0), stop=(_d == 7)),
                "te" if dc == 7 else None)
        ft_guard[b] = mmv
        wait("vector", "te", mmv)
        wait("vector", "st", st1)
        op("vector", lambda e: e.tensor_reduce(sb["red"][:, 8:9], pb[7][:, 0:8],
                                               AX.X, ALU.max))
        drain("vector")
        op("vector", lambda e: e.tensor_scalar(
            out=qtmp2[:, 0:8], in0=pb[7][:, 0:8], scalar1=sb["red"][:, 8:9],
            scalar2=None, op0=ALU.is_ge))
        drain("vector")
        op("vector", lambda e: e.tensor_tensor_scan(
            qtmp2[:, 8:16], qtmp2[:, 0:8], qtmp2[:, 0:8], 0.0,
            ALU.add, ALU.bypass))
        drain("vector")
        op("vector", lambda e: e.tensor_scalar(
            out=qtmp2[:, 8:16], in0=qtmp2[:, 8:16], scalar1=1.0,
            scalar2=None, op0=ALU.is_le))
        drain("vector")
        oh_done = op("vector", lambda e, _i=i: e.tensor_tensor(
            onehot[:, _i * 8:(_i + 1) * 8], qtmp2[:, 0:8], qtmp2[:, 8:16],
            ALU.mult), "dve")
        kb.drain("vector")
        wait("tensor", "dve", oh_done)
    wait("tensor", "dve", oh_done)
    mtp_out = pb[7][0:8, 128:192].bitcast(BF16)
    for i in range(NT):
        mt = op("tensor", lambda e, _i=i: e.transpose(
            mtp_out, onehot[:, _i * 8:(_i + 1) * 8], sb["ident"][:]), "te")
        wait("vector", "te", mt)
        ev = op("vector", lambda e, _i=i: e.tensor_copy(
            masksf[:, _i * 128:(_i + 1) * 128], mtp_out), "dve")
        kb.drain("vector")
        wait("tensor", "dve", ev)

    wgu_guard = [0, 0]
    wd_guard = [0, 0]
    g1_guard = [None, None]
    for tt in range(4):
        wait("sync", "te", kb.cnt("te"))
        wait("vector", "te", kb.cnt("te"))
        wait("sync", "dve", kb.cnt("dve"))
        for e_ in range(8):
            op("sync", lambda e, _e=e_, _t=tt: e.dma_start(
                out=maskb[:, _e * 512:(_e + 1) * 512],
                in_=bass.AP(masksf, _e * T + _t * 512,
                            [[T, 1], [0, 128], [1, 512]])), "ld", 16)
        lx = op("sync", lambda e, _t=tt: e.dma_start(
            out=xnb[:], in_=bass.AP(H["xn1"].tensor, _t * 512,
                                    [[2048, 128], [128 * 2048, 8], [1, 512]])),
            "ld", 16)
        wait("tensor", "ld", lx)
        wait("vector", "ld", lx)
        epi_hist = []
        for e_ in range(8):
            bwf = e_ % 2
            bb = 4 * (e_ % 2)
            if wgu_guard[bwf]:
                kb.wait("gpsimd", "te", wgu_guard[bwf])
            in_g = bass.AP(H["l1_wg"].tensor, e_ * 262144,
                           [[2048, 128], [1, 2048]])
            in_u = bass.AP(H["l1_wu"].tensor, e_ * 262144,
                           [[2048, 128], [1, 2048]])
            op("gpsimd", lambda e, _i=in_g, _b=bwf: e.dma_start(
                out=wgu[_b][:, 0:2048], in_=_i), "gw", 16)
            lwg = op("gpsimd", lambda e, _i=in_u, _b=bwf: e.dma_start(
                out=wgu[_b][:, 2048:4096], in_=_i), "gw", 16)
            wait("tensor", "gw", lwg)
            if len(epi_hist) >= 2:
                wait("tensor", "act", epi_hist[-2])
            mmg = [None, None]
            for gu in range(2):
                for dc in range(2):
                    bank = pb[bb + gu * 2 + dc]
                    mm2 = None
                    for hc in range(8):
                        sta = wgu[bwf][:, gu * 2048 + (hc * 2 + dc) * 128:
                                       gu * 2048 + (hc * 2 + dc) * 128 + 128]
                        mov = xnb[:, hc * 512:(hc + 1) * 512]
                        mm2 = op("tensor", lambda e, _b=bank, _s=sta, _m=mov,
                                 _h=hc: e.matmul(_b, _s, _m, start=(_h == 0),
                                                 stop=(_h == 7)),
                                 "te" if hc == 7 else None)
                    mmg[gu] = mm2
            wgu_guard[bwf] = mmg[1]
            wait("scalar", "te", mmg[1])
            a_last = None
            for dc in range(2):
                if kb.cnt("dve"):
                    wait("scalar", "dve", kb.cnt("dve"))
                a_rel = op("scalar", lambda e, _d=bb + dc: e.activation(
                    relub[:], pb[_d], AF.Relu, bias=0.0, scale=1.0), "act")
                a_ue = op("scalar", lambda e, _d=bb + 2 + dc: e.activation(
                    ubuf[:], pb[_d], AF.Identity, bias=0.0, scale=1.0),
                    "act")
                a_last = a_ue
                wait("vector", "act", a_ue)
                op("vector", lambda e: e.tensor_tensor(relub[:], relub[:],
                                                       relub[:], ALU.mult))
                drain("vector")
                op("vector", lambda e: e.tensor_tensor(relub[:], relub[:],
                                                       ubuf[:], ALU.mult))
                drain("vector")
                evi = op("vector", lambda e, _e=e_, _d=dc: e.scalar_tensor_tensor(
                    out=inter[:, (_e * 2 + _d) * 512:(_e * 2 + _d + 1) * 512],
                    in0=relub[:], scalar=1.0,
                    in1=maskb[:, _e * 512:(_e + 1) * 512],
                    op0=ALU.mult, op1=ALU.mult), "dve")
            epi_hist.append(a_last)
        wait("tensor", "dve", kb.cnt("dve"))
        wait("tensor", "act", kb.cnt("act"))
        for fo in range(8):
            bwf = fo % 2
            if wd_guard[bwf]:
                kb.wait("gpsimd", "te", wd_guard[bwf])
            in_d = bass.AP(H["l1_wd"].tensor, fo * 262144,
                           [[2048, 128], [1, 2048]])
            lwd = op("gpsimd", lambda e, _i=in_d, _b=bwf: e.dma_start(
                out=wst[_b][:], in_=_i), "gw", 16)
            wait("tensor", "gw", lwd)
            bank = pb[5 + (fo % 2)]
            mmd = None
            for kc in range(16):
                sta = wst[bwf][:, kc * 128:(kc + 1) * 128]
                mov = inter[:, kc * 512:(kc + 1) * 512]
                mmd = op("tensor", lambda e, _b=bank, _s=sta, _m=mov, _k=kc:
                         e.matmul(_b, _s, _m, start=(_k == 0), stop=(_k == 15)),
                         "te" if kc == 15 else None)
            wd_guard[bwf] = mmd
            wait("scalar", "te", mmd)
            bg = fo % 2
            if g1_guard[bg] is not None:
                wait("scalar", "st", g1_guard[bg])
            a_g1 = op("scalar", lambda e, _b=bank, _g=bg: e.activation(
                g1st[_g][:], _b, AF.Silu, bias=0.0, scale=1.0), "act")
            wait("sync", "act", a_g1)
            g1ap = bass.AP(H["g1"].tensor, fo * 128 * 2048 + tt * 512,
                           [[2048, 128], [1, 512]])
            g1_guard[bg] = op("sync", lambda e, _o=g1ap, _g=bg: e.dma_start(
                out=_o, in_=g1st[_g][:]), "st", 16)
            wait("tensor", "act", a_g1)
    kb.barrier()
    kb.free_to(kb.M_MOE)
    kb.S("A_tm", (128, 8 * T), BF16)
    kb.M_K = kb.marker()
    kb.S("krest", (128, 8 * T), BF16)
    kb.S("qsl", (128, 4 * T), BF16)
    kb.S("ksl", (128, 4 * T), BF16)
    emit_l1_attn(kb, want_dbg)
    return


def emit_l1_attn(kb, want_dbg=False):
    H, op, wait, drain = kb.H, kb.op, kb.wait, kb.drain
    sb = kb.sb
    PS = kb.PSALL
    pb = [PS[:, i * 512:(i + 1) * 512] for i in range(8)]
    smalls = sb["smalls"]
    qsl, ksl = sb["qsl"], sb["ksl"]
    krest = sb["krest"]
    qrest = sb["xn_fm"]
    v_tm = sb["g0"]
    A_tm = sb["A_tm"]
    nrm = sb["nrm"]
    negmb1 = sb["negmb1"]
    oblk = sb["oblk"]
    ones1b = sb["ones1b"]
    relub, ubuf = sb["relub"], sb["ubuf"]
    M_R = kb.marker()
    xnb2 = [kb.S(f"xnb_q{i}", (128, 8 * 512), BF16) for i in range(2)]
    wst = [kb.S(f"wstq{i}", (128, 8 * 128), BF16) for i in range(2)]
    rot = kb.S("rot", (128, 128), BF16)
    cosl = kb.S("cosl", (128, 512), BF16)
    sinl = kb.S("sinl", (128, 512), BF16)
    op("sync", lambda e: e.dma_start(out=rot[:], in_=H["c_rot"][:, :]),
       "ld", 16)
    wait("tensor", "ld", kb.cnt("ld"))

    wst_guard = [0, 0]
    evac_hist = []
    vtp_hist = []
    mm_tt = [0, 0]
    lx_t = {0: op("sync", lambda e: e.dma_start(
        out=xnb2[0][:], in_=bass.AP(H["xn1"].tensor, 0,
                                    [[2048, 128], [128 * 2048, 8], [1, 512]])),
        "ld", 16)}
    for tt in range(4):
        xnb = xnb2[tt % 2]
        if tt + 1 < 4:
            if mm_tt[(tt + 1) % 2]:
                wait("sync", "te", mm_tt[(tt + 1) % 2])
            lx_t[tt + 1] = op("sync", lambda e, _t=tt + 1,
                              _o=xnb2[(tt + 1) % 2]: e.dma_start(
                out=_o[:],
                in_=bass.AP(H["xn1"].tensor, _t * 512,
                            [[2048, 128], [128 * 2048, 8], [1, 512]])),
                "ld", 16)
        lx = lx_t[tt]
        wait("tensor", "ld", lx)
        for oc in range(32):
            bws = oc % 2
            if wst_guard[bws]:
                kb.wait("gpsimd", "te", wst_guard[bws])
            in_w = bass.AP(H["l1_wqkvT"].tensor, oc * 131072,
                           [[1024, 128], [1, 1024]])
            lww = op("gpsimd", lambda e, _i=in_w, _b=bws: e.dma_start(
                out=wst[_b][:], in_=_i), "gw", 16)
            wait("tensor", "gw", lww)
            if len(evac_hist) >= 4:
                wait("tensor", "act", evac_hist[-4])
            bank = pb[oc % 4]
            mmv = None
            for dc in range(8):
                mmv = op("tensor", lambda e, _b=bank, _s=wst[bws][:, dc * 128:(dc + 1) * 128],
                         _m=xnb[:, dc * 512:(dc + 1) * 512], _d=dc:
                         e.matmul(_b, _s, _m, start=(_d == 0), stop=(_d == 7)),
                         "te" if dc == 7 else None)
            wst_guard[bws] = mmv
            mm_tt[tt % 2] = mmv
            wait("scalar", "te", mmv)
            if oc < 4:
                dst = qsl[:, oc * T + tt * 512:oc * T + tt * 512 + 512]
            elif oc < 8:
                dst = ksl[:, (oc - 4) * T + tt * 512:(oc - 4) * T + tt * 512 + 512]
            elif oc < 16:
                dst = qrest[:, (oc - 8) * T + tt * 512:(oc - 8) * T + tt * 512 + 512]
            elif oc < 24:
                dst = krest[:, (oc - 16) * T + tt * 512:(oc - 16) * T + tt * 512 + 512]
            else:
                dst = v_tm[:, (oc - 24) * T + tt * 512:(oc - 24) * T + tt * 512 + 512]
            if oc < 24:
                a_e = op("scalar", lambda e, _b=bank, _d=dst, _o=oc: e.activation(
                    _d, _b, AF.Identity, bias=sb["bqkv_t"][:, _o:_o + 1],
                    scale=1.0), "act")
                evac_hist.append(a_e)
            else:
                vstage = sb["relub"] if (oc % 2 == 0) else sb["ubuf"]
                if len(vtp_hist) >= 2:
                    wait("scalar", "tp", vtp_hist[-2])
                a_e = op("scalar", lambda e, _b=bank, _o=oc, _v=vstage:
                         e.activation(_v[:], _b, AF.Identity,
                                      bias=sb["bqkv_t"][:, _o:_o + 1],
                                      scale=1.0), "act")
                evac_hist.append(a_e)
                wait("sync", "act", a_e)
                pair = oc - 24
                sa = vstage[:].rearrange("p (i q) -> p i q", q=128)
                da = bass.AP(v_tm, pair * T + tt * 512,
                             [[8 * T, 128], [128, 4], [1, 128]])
                tpv = op("sync", lambda e, _s=sa, _d=da:
                         e.dma_start_transpose(_d, _s), "tp", 16)
                vtp_hist.append(tpv)
    wait("tensor", "act", kb.cnt("act"))
    wait("vector", "tp", kb.cnt("tp"))
    for tt in range(4):
        wait("gpsimd", "dve", kb.cnt("dve"))
        op("gpsimd", lambda e, _t=tt: e.dma_start(
            out=cosl[:], in_=H["c_cosf"][:, _t * 512:(_t + 1) * 512]),
            "gw", 16)
        lcs = op("gpsimd", lambda e, _t=tt: e.dma_start(
            out=sinl[:], in_=H["c_sinf"][:, _t * 512:(_t + 1) * 512]),
            "gw", 16)
        wait("vector", "gw", lcs)
        for sl in range(8):
            tens = qsl if sl < 4 else ksl
            j = sl % 4
            base = j * T + tt * 512
            mmr = op("tensor", lambda e, _t=tens, _b=base: e.matmul(
                pb[4], rot[:], _t[:, _b:_b + 512], start=True, stop=True),
                "te")
            wait("vector", "te", mmr)
            op("vector", lambda e, _t=tens, _b=base: e.tensor_tensor(
                _t[:, _b:_b + 512], _t[:, _b:_b + 512], cosl[:], ALU.mult))
            op("vector", lambda e: e.tensor_tensor(ubuf[:], pb[4], sinl[:],
                                                   ALU.mult))
            drain("vector")
            ev = op("vector", lambda e, _t=tens, _b=base: e.tensor_tensor(
                _t[:, _b:_b + 512], _t[:, _b:_b + 512], ubuf[:], ALU.add),
                "dve")
            wait("tensor", "dve", ev)
    if DIAG3:
        wait("sync", "dve", kb.cnt("dve"))
        wait("sync", "act", kb.cnt("act"))
        wait("sync", "tp", kb.cnt("tp"))
        for nm, tens, nch in (("dbg_qsl", qsl, 4), ("dbg_ksl", ksl, 4),
                              ("dbg_qr", qrest, 8), ("dbg_kr", krest, 8),
                              ("dbg_v", v_tm, 8)):
            for p in range(nch):
                op("sync", lambda e, _n=nm, _t=tens, _p=p: e.dma_start(
                    out=H[_n][:, _p * T:(_p + 1) * T],
                    in_=_t[:, _p * T:(_p + 1) * T]), "st", 16)
    kb.barrier()
    kb.free_to(M_R)
    ptw = [kb.S(f"ptw{i}", (128, 256), BF16) for i in range(2)]
    wmask = kb.S("wmask", (128, 256), BF16)
    cmask = kb.S("cmask", (128, 128), BF16)
    sqb = kb.S("sqb_w", (128, T), BF16)
    kb.sb["sqb"] = sqb
    for nm, tile in (("c_wmask", wmask), ("c_cmask", cmask)):
        op("sync", lambda e, _n=nm, _t=tile: e.dma_start(
            out=_t[:], in_=H[_n][:, :]), "ld", 16)
    wait("tensor", "ld", kb.cnt("ld"))
    wait("scalar", "ld", kb.cnt("ld"))
    wait("vector", "ld", kb.cnt("ld"))

    def emit_norms(groups, out_col0, negout):
        ng = len(groups)
        for g, (tens, coff, obs) in enumerate(groups):
            a_sq = op("scalar", lambda e, _t=tens, _c=coff: e.activation(
                sqb[:], _t[:, _c:_c + T], AF.Square, bias=0.0, scale=1.0),
                "act")
            wait("tensor", "act", a_sq)
            gmm = None
            for tt in range(4):
                gmm = op("tensor", lambda e, _tt=tt, _o=obs, _g=g, _ng=ng:
                         e.matmul(pb[_tt][0:16, :],
                                  oblk[:, _o * 16:(_o + 1) * 16],
                                  sqb[:, _tt * 512:(_tt + 1) * 512],
                                  start=(_g == 0), stop=(_g == _ng - 1)),
                         "te" if tt == 3 else None)
            wait("scalar", "te", gmm)
        mmn = kb.cnt("te")
        wait("vector", "te", mmn)
        for tt in range(4):
            op("vector", lambda e, _tt=tt, _c=out_col0: e.tensor_reduce(
                nrm[:, _c + _tt:_c + _tt + 1], pb[_tt][0:16, :], AX.X,
                ALU.max))
        drain("vector")
        v_nr = op("vector", lambda e, _c=out_col0: e.tensor_reduce(
            nrm[:, _c + 4:_c + 5], nrm[:, _c:_c + 4], AX.X, ALU.max), "dve")
        wait("tensor", "dve", v_nr)
        return mmn

    qgroups = [(qsl, j * T, j) for j in range(4)] + \
              [(qrest, p * T, 4 + p) for p in range(8)]
    kgroups = [(ksl, j * T, j) for j in range(4)] + \
              [(krest, p * T, 4 + p) for p in range(8)]
    emit_norms(qgroups, 0, None)
    emit_norms(kgroups, 8, None)
    emit_mb_tail(kb, nrm, negmb1)
    emit_l1_wattn(kb)
    return


def emit_mb_broadcast(kb, nrm, negout):
    op, wait, drain = kb.op, kb.wait, kb.drain
    sb = kb.sb
    PS = kb.PSALL
    pb5 = PS[:, 5 * 512:6 * 512]
    v_dg = op("vector", lambda e: e.tensor_scalar(
        out=sb["diag16"][0:16, 0:16], in0=sb["ident"][0:16, 0:16],
        scalar1=nrm[:, 19:20], scalar2=None, op0=ALU.mult),
        "dve")
    kb.drain("vector")
    wait("tensor", "dve", v_dg)
    mmb = op("tensor", lambda e: e.matmul(pb5[:, 0:16], sb["ones16"][:],
                                          sb["diag16"][0:16, 0:16],
                                          start=True, stop=True), "te")
    wait("vector", "te", mmb)
    v_mb = op("vector", lambda e: e.tensor_copy(negout[:], pb5[:, 0:16]),
              "dve")
    kb.drain("vector")
    wait("tensor", "dve", v_mb)
    wait("scalar", "dve", v_mb)


def emit_l1_wattn(kb, want_dbg=False):
    H, op, wait, drain = kb.H, kb.op, kb.wait, kb.drain
    sb = kb.sb
    PS = kb.PSALL
    pb = [PS[:, i * 512:(i + 1) * 512] for i in range(8)]
    qsl, ksl = sb["qsl"], sb["ksl"]
    krest, qrest = sb["krest"], sb["xn_fm"]
    v_tm, A_tm = sb["g0"], sb["A_tm"]
    ptw = [sb["ptw0"], sb["ptw1"]]
    negmb1 = sb["negmb1"]
    rdc = sb["rdc"]
    ones1b = sb["ones1b"]
    wmask = sb["wmask"]

    ap_hist, aA_hist, vrd_hist = [], [], []
    for h in range(16):
        slab_c, srow = h // 4, (h % 4) * 32
        rest_c, rrow = h // 2, (h % 2) * 64
        pair, half = h // 2, h % 2
        for jt in range(16):
            bs = jt % 2
            Sb = pb[bs]
            ncols = 256 if jt < 15 else 128
            if len(ap_hist) >= 2:
                wait("tensor", "act", ap_hist[-2])
            ksliceA = bass.AP(ksl, srow * (4 * T) + slab_c * T + jt * 128,
                              [[4 * T, 32], [1, 128]])
            qslA = bass.AP(qsl, srow * (4 * T) + slab_c * T + jt * 128,
                           [[4 * T, 32], [1, ncols]])
            op("tensor", lambda e, _S=Sb, _k=ksliceA, _q=qslA, _n=ncols,
               _r=srow: e.matmul(_S[:, 0:_n], _k, _q, start=True, stop=False,
                                 tile_position=(_r, 0)))
            kB = bass.AP(krest, rrow * (8 * T) + rest_c * T + jt * 128,
                         [[8 * T, 64], [1, 128]])
            qB = bass.AP(qrest, rrow * (8 * T) + rest_c * T + jt * 128,
                         [[8 * T, 64], [1, ncols]])
            mmS = op("tensor", lambda e, _S=Sb, _k=kB, _q=qB, _n=ncols,
                     _r=rrow: e.matmul(_S[:, 0:_n], _k, _q, start=False,
                                       stop=True, tile_position=(_r, 0)),
                     "te")
            wait("scalar", "te", mmS)
            a_p = op("scalar", lambda e, _S=Sb, _b=bs, _n=ncols, _h=h:
                     e.activation(ptw[_b][:, 0:_n], _S[:, 0:_n], AF.Exp,
                                  bias=negmb1[:, _h:_h + 1], scale=0.125),
                     "act")
            ap_hist.append(a_p)
            wait("vector", "act", a_p)
            v_p = op("vector", lambda e, _b=bs, _n=ncols: e.tensor_tensor(
                ptw[_b][:, 0:_n], ptw[_b][:, 0:_n], wmask[:, 0:_n], ALU.mult),
                "dve")
            wait("tensor", "dve", v_p)
            if len(aA_hist) >= 2:
                wait("tensor", "act", aA_hist[-2])
                wait("tensor", "dve", vrd_hist[-2])
            ob = pb[2 + bs]
            vsl = v_tm[:, pair * T + jt * 128 + half * 64:
                       pair * T + jt * 128 + half * 64 + 64]
            op("tensor", lambda e, _o=ob, _p=ptw[bs][:, 0:128], _v=vsl:
               e.matmul(_o[:, 0:64], _p, _v, start=True, stop=False))
            if jt > 0:
                op("tensor", lambda e, _o=ob, _p=ptw[bs][:, 0:128]:
                   e.matmul(_o[:, 64:65], _p, ones1b[:], start=False,
                            stop=False))
                vsl2 = v_tm[:, pair * T + (jt - 1) * 128 + half * 64:
                            pair * T + (jt - 1) * 128 + half * 64 + 64]
                pprev = ptw[(jt - 1) % 2][:, 128:256]
                op("tensor", lambda e, _o=ob, _p=pprev, _v=vsl2:
                   e.matmul(_o[:, 0:64], _p, _v, start=False, stop=False))
                mmP = op("tensor", lambda e, _o=ob, _p=pprev:
                         e.matmul(_o[:, 64:65], _p, ones1b[:], start=False,
                                  stop=True), "te")
            else:
                mmP = op("tensor", lambda e, _o=ob, _p=ptw[bs][:, 0:128]:
                         e.matmul(
                    _o[:, 64:65], _p, ones1b[:],
                    start=False, stop=True), "te")
            wait("vector", "te", mmP)
            v_rd = op("vector", lambda e, _o=ob: e.reciprocal(
                rdc[:, 0:1], _o[:, 64:65]), "dve")
            vrd_hist.append(v_rd)
            wait("scalar", "dve", v_rd)
            adst = A_tm[:, pair * T + jt * 128 + half * 64:
                        pair * T + jt * 128 + half * 64 + 64]
            a_A = op("scalar", lambda e, _o=ob, _d=adst: e.activation(
                _d, _o[:, 0:64], AF.Identity, bias=0.0, scale=rdc[:, 0:1]),
                "act")
            aA_hist.append(a_A)
            wait("vector", "act", a_A)
    if DIAG3:
        wait("sync", "act", kb.cnt("act"))
        for p in range(8):
            op("sync", lambda e, _p=p: e.dma_start(
                out=H["dbg_A3"][:, _p * T:(_p + 1) * T],
                in_=A_tm[:, _p * T:(_p + 1) * T]), "st", 16)
    kb.barrier()
    kb.free_to(kb.M_K)
    emit_l1_tail(kb, want_dbg)


def emit_l1_tail(kb, want_dbg=False):
    H, op, wait, drain = kb.H, kb.op, kb.wait, kb.drain
    sb = kb.sb
    PS = kb.PSALL
    pb = [PS[:, i * 512:(i + 1) * 512] for i in range(8)]
    smalls = sb["smalls"]
    A_tm = sb["A_tm"]
    A_fm = kb.S("A_fm", (128, 8 * T), BF16)
    den_fm = kb.S("den_fm2", (16, T), F32)
    den_stage = kb.S("den_stage", (1, T), F32)
    kb.sb["den_fm"] = den_fm
    wst = [kb.S(f"wstd{i}", (128, 8 * 128), BF16) for i in range(2)]
    ptcol = kb.S("ptcol", (128, 2048), BF16)
    outsb = kb.S("outsb", (64, 2 * T), BF16)
    cmask = kb.S("cmask2", (128, 128), BF16)
    xt_t = [kb.S(f"xt{i}_l1t", (128, D), F32) for i in range(2)]
    xnf_t = [kb.S(f"xnf{i}_l1t", (128, D), F32) for i in range(2)]
    kb.sb["xt0"], kb.sb["xt1"] = xt_t
    kb.sb["xnf0"], kb.sb["xnf1"] = xnf_t
    kb.sb["sqb"] = ptcol
    op("sync", lambda e: e.dma_start(out=cmask[:], in_=H["c_cmask"][:, :]),
       "ld", 16)
    wait("vector", "ld", kb.cnt("ld"))
    qa = sb["xn_fm"]
    ka = sb["g0"]
    nrm = sb["nrm"]
    oblk = sb["oblk"]
    sqb = ptcol
    negmb2 = sb["negmb2"]
    dentm = sb["dentm"]
    ones1b = sb["ones1b"]

    kb.barrier("ph")
    for p in range(8):
        sa = A_tm[:, p * T:(p + 1) * T].rearrange("p (i q) -> p i q", q=128)
        da = A_fm[:, p * T:(p + 1) * T].rearrange("p (i q) -> p i q", q=128)
        op("sync", lambda e, _s=sa, _d=da: e.dma_start_transpose(_d, _s),
           "tp", 16)
    tpA = kb.cnt("tp")
    wait("tensor", "tp", tpA)

    wst_guard = [0, 0]
    evac_hist = []
    for which, wname, bname, dstt in ((0, "l1_wqaT", "bqa_t", qa),
                                      (1, "l1_wkaT", "bka_t", ka)):
        for oc in range(8):
            bws = oc % 2
            if wst_guard[bws]:
                kb.wait("gpsimd", "te", wst_guard[bws])
            in_w = bass.AP(H[wname].tensor, oc * 131072,
                           [[1024, 128], [1, 1024]])
            lww = op("gpsimd", lambda e, _i=in_w, _b=bws: e.dma_start(
                out=wst[_b][:], in_=_i), "gw", 16)
            wait("tensor", "gw", lww)
            for tt in range(4):
                bank = pb[(oc * 4 + tt) % 4]
                if len(evac_hist) >= 4:
                    wait("tensor", "act", evac_hist[-4])
                mmv = None
                for dc in range(8):
                    mmv = op("tensor", lambda e, _b=bank,
                             _s=wst[bws][:, dc * 128:(dc + 1) * 128],
                             _m=A_fm[:, dc * T + tt * 512:dc * T + tt * 512 + 512],
                             _d=dc: e.matmul(_b, _s, _m, start=(_d == 0),
                                             stop=(_d == 7)),
                             "te" if dc == 7 else None)
                wst_guard[bws] = mmv
                wait("scalar", "te", mmv)
                a_e = op("scalar", lambda e, _b=bank, _o=oc, _t=tt, _d=dstt,
                         _bn=bname: e.activation(
                             _d[:, _o * T + _t * 512:_o * T + _t * 512 + 512],
                             _b, AF.Identity, bias=sb[_bn][:, _o:_o + 1],
                             scale=1.0), "act")
                evac_hist.append(a_e)
    wait("tensor", "act", kb.cnt("act"))

    qgroups = [(qa, p * T, 4 + p) for p in range(8)]
    kgroups = [(ka, p * T, 4 + p) for p in range(8)]

    def norms2(groups, col0):
        for g, (tens, coff, obs) in enumerate(groups):
            a_sq = op("scalar", lambda e, _t=tens, _c=coff: e.activation(
                sqb[:], _t[:, _c:_c + T], AF.Square, bias=0.0, scale=1.0),
                "act")
            wait("tensor", "act", a_sq)
            gmm = None
            for tt in range(4):
                gmm = op("tensor", lambda e, _tt=tt, _o=obs, _g=g:
                         e.matmul(pb[_tt][0:16, :],
                                  oblk[:, _o * 16:(_o + 1) * 16],
                                  sqb[:, _tt * 512:(_tt + 1) * 512],
                                  start=(_g == 0), stop=(_g == 7)),
                         "te" if tt == 3 else None)
            wait("scalar", "te", gmm)
        mmn = kb.cnt("te")
        wait("vector", "te", mmn)
        for tt in range(4):
            op("vector", lambda e, _tt=tt, _c=col0: e.tensor_reduce(
                nrm[:, _c + _tt:_c + _tt + 1], pb[_tt][0:16, :], AX.X,
                ALU.max))
        drain("vector")
        v_nr = op("vector", lambda e, _c=col0: e.tensor_reduce(
            nrm[:, _c + 4:_c + 5], nrm[:, _c:_c + 4], AX.X, ALU.max), "dve")
        kb.drain("vector")
        wait("tensor", "dve", v_nr)

    norms2(qgroups, 0)
    norms2(kgroups, 8)
    emit_mb_tail(kb, nrm, negmb2)

    OB0 = 2048
    out_tm = A_fm
    a65 = [kb.S(f"a65_{i}", (128, 65), BF16) for i in range(2)]
    op("vector", lambda e: e.memset(a65[0][:, 64:65], 1.0))
    op("vector", lambda e: e.memset(a65[1][:, 64:65], 1.0))
    ap_prev = [0, 0, 0, 0]
    for h in range(16):
        pair, half = h // 2, h % 2
        pv_last = None
        for kt in range(16):
            nq = 16 - kt
            npc = (nq * 128 + 511) // 512
            b65 = kt % 2
            if pv_last is not None:
                wait("vector", "te", pv_last)
            asl = A_tm[:, pair * T + kt * 128 + half * 64:
                       pair * T + kt * 128 + half * 64 + 64]
            v_a = op("vector", lambda e, _b=b65, _a=asl: e.tensor_copy(
                a65[_b][:, 0:64], _a), "dve")
            kasl = bass.AP(ka, half * 64 * (8 * T) + pair * T + kt * 128,
                           [[8 * T, 64], [1, 128]])
            mmS_v = []
            for subp in range(npc):
                w = min(512, nq * 128 - subp * 512)
                qasl = bass.AP(qa, half * 64 * (8 * T) + pair * T
                               + kt * 128 + subp * 512, [[8 * T, 64], [1, w]])
                if ap_prev[subp]:
                    wait("tensor", "act", ap_prev[subp])
                mv = op("tensor", lambda e, _o=subp * 512, _k=kasl, _q=qasl,
                        _w=w, _r=half * 64: e.matmul(
                            PS[:, _o:_o + _w], _k, _q, start=True, stop=True,
                            tile_position=(_r, 0)), "te")
                mmS_v.append(mv)
            if pv_last is not None:
                wait("scalar", "te", pv_last)
            ap_now = []
            for subp in range(npc):
                w = min(512, nq * 128 - subp * 512)
                wait("scalar", "te", mmS_v[subp])
                a_p = op("scalar", lambda e, _o=subp * 512, _w=w, _h=h:
                         e.activation(ptcol[:, _o:_o + _w], PS[:, _o:_o + _w],
                                      AF.Exp, bias=negmb2[:, _h:_h + 1],
                                      scale=0.125), "act")
                ap_now.append(a_p)
                ap_prev[subp] = a_p
            wait("vector", "act", ap_now[0])
            v_m = op("vector", lambda e: e.tensor_tensor(
                ptcol[:, 0:128], ptcol[:, 0:128], cmask[:], ALU.mult), "dve")
            wait("tensor", "dve", v_m)
            wait("tensor", "dve", v_a)
            for b in range(kt // 4, 4):
                qt0 = max(kt, 4 * b)
                wdt = (4 * b + 3 - qt0 + 1) * 128
                st_flag, sp_flag = kt == 0, kt == 4 * b + 3
                hi_piece = ((qt0 - kt) * 128 + wdt - 1) // 512
                wait("tensor", "act", ap_now[hi_piece])
                pv_last = op("tensor", lambda e, _q=qt0, _w=wdt, _b=b65,
                             _p=ptcol[:, (qt0 - kt) * 128:
                                      (qt0 - kt) * 128 + wdt],
                             _st=st_flag, _sp=sp_flag:
                             e.matmul(PS[0:65, OB0 + _q * 128:OB0 + _q * 128 + _w],
                                      a65[_b][:, 0:65], _p,
                                      start=_st, stop=_sp), "te")
        wait("vector", "te", kb.cnt("te"))
        if h > 0:
            wait("vector", "st", kb.cnt("st"))
        vd = op("vector", lambda e: e.tensor_copy(
            den_stage[0:1, :],
            PS[64:65, OB0:4096].rearrange("p (i q) -> p i q", q=128)), "dve")
        wait("sync", "dve", vd)
        op("sync", lambda e, _h=h: e.dma_start(
            out=den_fm[_h:_h + 1, :], in_=den_stage[0:1, :]), "st", 16)
        wait("scalar", "dve", vd)
        a_o = op("scalar", lambda e: e.activation(
            outsb[:, 0:T], PS[0:64, OB0:4096],
            AF.Identity, bias=0.0, scale=1.0), "act")
        wait("sync", "act", a_o)
        sa = outsb[:, 0:T].rearrange("p (i q) -> p i q", q=128)
        da = bass.AP(out_tm, h * 64, [[8 * T, 128], [D, 16], [1, 64]])
        tpo = op("sync", lambda e, _s=sa, _d=da: e.dma_start_transpose(_d, _s),
                 "tp", 16)
        wait("scalar", "tp", tpo)
        wait("tensor", "tp", tpo)
    emit_l1_fin(kb, out_tm, want_dbg)
    return



def emit_l1_fin(kb, out_tm, want_dbg=False):
    H, op, wait, drain = kb.H, kb.op, kb.wait, kb.drain
    sb = kb.sb
    PS = kb.PSALL
    pb = [PS[:, i * 512:(i + 1) * 512] for i in range(8)]
    den_fm = sb["den_fm"]
    dentm = sb["dentm"]
    out_fm = sb["A_tm"]
    sg1 = sb["xn_fm"]
    sg_tm = sb["g0"]
    wst = [sb["wstd0"], sb["wstd1"]]
    g1b = [sb["g1st0"], sb["g1st1"]]

    kb.barrier("ph")
    op("vector", lambda e: e.reciprocal(den_fm[:, :], den_fm[:, :]))
    drain("vector")
    v_rd = op("vector", lambda e: e.tensor_copy(den_fm[:, 0:1], den_fm[:, 0:1]),
              "dve")
    wait("tensor", "dve", v_rd)
    for i in range(NT):
        mt = op("tensor", lambda e, _i=i: e.transpose(
            pb[0][:, 0:16], den_fm[:, _i * 128:(_i + 1) * 128],
            sb["identf"][0:16, 0:16]), "te")
        wait("vector", "te", mt)
        ev = op("vector", lambda e, _i=i: e.tensor_copy(
            dentm[:, _i * 16:(_i + 1) * 16], pb[0][:, 0:16]), "dve")
        kb.drain("vector")
        wait("tensor", "dve", ev)
    for i in range(NT):
        v_n = op("vector", lambda e, _i=i: e.tensor_tensor(
            out_tm[:, _i * D:(_i + 1) * D].rearrange("p (h d) -> p h d", d=64),
            out_tm[:, _i * D:(_i + 1) * D].rearrange("p (h d) -> p h d", d=64),
            dentm[:, _i * 16:(_i + 1) * 16].unsqueeze(2)
            .broadcast_to([128, 16, 64]), ALU.mult), "dve")
        kb.drain("vector")
    v_norm = kb.cnt("dve")
    wait("sync", "dve", v_norm)
    if DIAG3:
        for i in range(NT):
            op("sync", lambda e, _i=i: e.dma_start(
                out=H["dbg_out3"][:, _i * D:(_i + 1) * D],
                in_=out_tm[:, _i * D:(_i + 1) * D]), "st", 16)
    for i in range(NT):
        sa = out_tm[:, i * D:(i + 1) * D].rearrange("p (c q) -> p c q", q=128)
        da = bass.AP(out_fm, i * 128, [[8 * T, 128], [T, 8], [1, 128]])
        op("sync", lambda e, _s=sa, _d=da: e.dma_start_transpose(_d, _s),
           "tp", 16)
    tpf = kb.cnt("tp")
    wait("tensor", "tp", tpf)
    wst_guard = [0, 0]
    g1_guard = [None, None]
    for fo in range(8):
        bws = fo % 2
        if wst_guard[bws]:
            kb.wait("gpsimd", "te", wst_guard[bws])
        in_w = bass.AP(H["l1_wdT"].tensor, fo * 131072,
                       [[1024, 128], [1, 1024]])
        lww = op("gpsimd", lambda e, _i=in_w, _b=bws: e.dma_start(
            out=wst[_b][:], in_=_i), "gw", 16)
        wait("tensor", "gw", lww)
        for tt in range(4):
            bank = pb[(fo * 4 + tt) % 4]
            mmv = None
            for dc in range(8):
                mmv = op("tensor", lambda e, _b=bank,
                         _s=wst[bws][:, dc * 128:(dc + 1) * 128],
                         _m=out_fm[:, dc * T + tt * 512:dc * T + tt * 512 + 512],
                         _d=dc: e.matmul(_b, _s, _m, start=(_d == 0),
                                         stop=(_d == 7)),
                         "te" if dc == 7 else None)
            wst_guard[bws] = mmv
            bg = tt % 2
            if g1_guard[bg] is not None:
                wait("sync", "dve", g1_guard[bg])
            lg = op("sync", lambda e, _f=fo, _t=tt, _g=bg: e.dma_start(
                out=g1b[_g][:], in_=bass.AP(H["g1"].tensor,
                                            _f * 128 * 2048 + _t * 512,
                                            [[2048, 128], [1, 512]])),
                "ld", 16)
            wait("scalar", "te", mmv)
            a_b = op("scalar", lambda e, _b=bank, _f=fo: e.activation(
                ubuf_ap(kb), _b, AF.Identity, bias=sb["bd_t"][:, _f:_f + 1],
                scale=1.0), "act")
            wait("vector", "act", a_b)
            wait("vector", "ld", lg)
            ev = op("vector", lambda e, _f=fo, _t=tt, _g=bg: e.tensor_tensor(
                sg1[:, _f * T + _t * 512:_f * T + _t * 512 + 512],
                ubuf_ap(kb), g1b[_g][:], ALU.mult), "dve")
            wait("tensor", "dve", ev)
            wait("scalar", "dve", ev)
            g1_guard[bg] = ev
    sg_done = kb.cnt("dve")
    wait("sync", "dve", sg_done)
    for fo in range(8):
        sa = sg1[:, fo * T:(fo + 1) * T].rearrange("p (i q) -> p i q", q=128)
        da = bass.AP(sg_tm, fo * 128, [[8 * T, 128], [D, 16], [1, 128]])
        op("sync", lambda e, _s=sa, _d=da: e.dma_start_transpose(_d, _s),
           "tp", 16)
    tps = kb.cnt("tp")
    x1v = H["x1"].rearrange("(n p) d -> n p d", p=128)
    outv = H["out"].rearrange("(n p) d -> n p d", p=128)
    res_guard = [None, None]
    xtb_t = [sb["xt0"], sb["xt1"]]
    xnfb_t = [sb["xnf0"], sb["xnf1"]]
    for i in range(NT):
        b = i % 2
        ldv = op("sync", lambda e, _i=i, _x=xtb_t[b]: e.dma_start(
            out=_x[:], in_=x1v[_i]), "ld", 16)
        wait("vector", "ld", ldv)
        wait("vector", "tp", tps)
        if res_guard[b] is not None:
            wait("vector", "st", res_guard[b])
        v1 = op("vector", lambda e, _o=xnfb_t[b], _x=xtb_t[b], _i=i:
                e.tensor_tensor(
            _o[:], _x[:],
            sg_tm[:, _i * D:(_i + 1) * D], ALU.add), "dve")
        wait("sync", "dve", v1)
        res_guard[b] = op("sync", lambda e, _i=i, _o=xnfb_t[b]: e.dma_start(
            out=outv[_i], in_=_o[:]), "st", 16)
    kb.wait("sync", "st", kb.cnt("st"))


def ubuf_ap(kb):
    return kb.sb["ubuf"][:]


def emit_mb_tail(kb, nrm, negout):
    op, wait, drain = kb.op, kb.wait, kb.drain
    smalls = kb.sb["smalls"]
    wait("vector", "dve", kb.cnt("dve"))
    v_gm = op("vector", lambda e: e.tensor_tensor(
        nrm[:, 16:17], nrm[:, 4:5], nrm[:, 12:13], ALU.mult), "dve")
    wait("scalar", "dve", v_gm)
    op("scalar", lambda e: e.activation(nrm[:, 17:18], nrm[:, 16:17], AF.Ln,
                                        bias=0.0, scale=1.0))
    kb.drain("scalar")
    a_mb = op("scalar", lambda e: e.activation(
        nrm[:, 18:19], nrm[:, 17:18], AF.Exp, bias=smalls[0:16, 5:6],
        scale=0.5), "act")
    wait("vector", "act", a_mb)
    op("vector", lambda e: e.tensor_scalar(
        out=nrm[:, 19:20], in0=nrm[:, 18:19], scalar1=-1.0, scalar2=None,
        op0=ALU.mult))
    drain("vector")
    emit_mb_broadcast(kb, nrm, negout)


# version 12 (final)
# speedup vs baseline: 1.0254x; 1.0216x over previous
import numpy as np
import ml_dtypes
import concourse.bass as bass
import concourse.mybir as mybir
from concourse.bass_utils import run_bass_kernel_spmd

F32 = mybir.dt.float32
BF16 = mybir.dt.bfloat16
U32 = mybir.dt.uint32
AF = mybir.ActivationFunctionType
ALU = mybir.AluOpType
AX = mybir.AxisListType
BF16NP = ml_dtypes.bfloat16

T = 2048
D = 1024
NT = T // 128
ND = D // 128
MAGIC = float(np.float32(1.5 * 2 ** 23))
LN2 = float(np.log(2.0))

DEBUG = False
DIAG3 = False
STAGE = 3
L0_PARTS = "AdBCDr"

ENGINES = ("sync", "tensor", "vector", "scalar", "gpsimd")


class KB:

    def __init__(self):
        self.nc = bass.Bass("TRN2", target_bir_lowering=False)
        self.H = {}
        self.q = {e: [] for e in ENGINES}
        self.counts = {}
        self.sb = {}
        self._ctx = []
        self.semnames = []

    def din(self, name, shape, dt=F32):
        self.H[name] = self.nc.dram_tensor(name, list(shape), dt,
                                           kind="ExternalInput").ap()
        return self.H[name]

    def dout(self, name, shape, dt=F32):
        self.H[name] = self.nc.dram_tensor(name, list(shape), dt,
                                           kind="ExternalOutput").ap()
        return self.H[name]

    def dtmp(self, name, shape, dt=F32):
        self.H[name] = self.nc.dram_tensor(name, list(shape), dt,
                                           kind="Internal").ap()
        return self.H[name]

    def S(self, name, shape, dtype):
        cm = self.nc.sbuf_tensor(name, list(shape), dtype)
        h = cm.__enter__()
        self._ctx.append(cm)
        self.sb[name] = h
        return h

    def PSUM(self, name, shape=(128, 512), dtype=F32):
        cm = self.nc.psum_tensor(name, list(shape), dtype)
        h = cm.__enter__()
        self._ctx.append(cm)
        return h

    def sem(self, *names):
        self.semnames.extend(names)

    def marker(self):
        return len(self._ctx)

    def free_to(self, mark):
        while len(self._ctx) > mark:
            cm = self._ctx.pop()
            cm.__exit__(None, None, None)

    def barrier(self, name="ph"):
        for s in ("te", "act", "dve", "gp", "tp", "st", "ld", "gw"):
            if self.cnt(s):
                self.wait("sync", s, self.cnt(s))
        v = self.op("sync", lambda e: e.nop(), name)
        for eng in ("tensor", "vector", "scalar", "gpsimd"):
            self.wait(eng, name, v)
        return v

    def op(self, eng, f, inc=None, n=1):
        if inc is None:
            self.q[eng].append(lambda E, _f=f: _f(E[eng]))
            return None
        if n == 16:
            prev = self.counts.get(inc, 0)
            if prev:
                self.wait(eng, inc, prev)
        self.counts[inc] = self.counts.get(inc, 0) + n
        v = self.counts[inc]
        self.q[eng].append(
            lambda E, _f=f, _s=inc, _n=n: _f(E[eng]).then_inc(E["S"][_s], _n))
        return v

    def cnt(self, sem):
        return self.counts.get(sem, 0)

    def wait(self, eng, sem, val):
        assert val <= self.cnt(sem), (sem, val, self.cnt(sem))
        self.q[eng].append(
            lambda E, _s=sem, _v=val: E[eng].wait_ge(E["S"][_s], _v))

    def drain(self, eng):
        self.q[eng].append(lambda E: E[eng].drain())

    def finalize(self):
        nc = self.nc
        semctx = []
        S = {}
        for n in self.semnames:
            cm = nc.semaphore(n)
            S[n] = cm.__enter__()
            semctx.append(cm)
        blk = nc.Block()
        block = blk.__enter__()
        E = {"S": S}

        def runner(engname):
            def _run(eng):
                E[engname] = eng
                for f in self.q[engname]:
                    f(E)
            return _run

        block.sync(runner("sync"))
        block.tensor(runner("tensor"))
        block.vector(runner("vector"))
        block.scalar(runner("scalar"))
        block.gpsimd(runner("gpsimd"))
        blk.__exit__(None, None, None)
        for cm in reversed(semctx):
            cm.__exit__(None, None, None)
        for cm in reversed(self._ctx):
            cm.__exit__(None, None, None)
        return nc



def tfm(tensor, c, t0, n, nchunk=8):
    i0 = t0 // 128
    return bass.AP(tensor, (i0 * nchunk + c) * 128,
                   [[1, 128], [nchunk * 128, n // 128], [1, 128]])



def emit_quant(kb, x_ap, out_ap, fac_ap, w, inc=None):
    nb = w // 32
    q = kb.sb["qtmp"]
    qs = kb.sb["qsc"]
    sm = kb.sb["smalls"]
    A, B, C = q[:, 0:w], q[:, w:2 * w], q[:, 2 * w:3 * w]
    s_amax = qs[:, 0:nb]
    s_pw2 = qs[:, nb:2 * nb]
    s_scale = qs[:, 2 * nb:3 * nb]
    s_rs = qs[:, 3 * nb:4 * nb]
    x3 = x_ap.rearrange("p (nb bs) -> p nb bs", bs=32)

    def o(f, inc_=None):
        r = kb.op("vector", f, inc_)
        kb.drain("vector")
        return r

    o(lambda e: e.tensor_reduce(s_amax, x3, AX.X, ALU.max,
                                apply_absolute_value=True))
    o(lambda e: e.tensor_single_scalar(s_pw2.bitcast(U32), s_amax.bitcast(U32),
                                       sm[:, 0:1].bitcast(U32), ALU.bitwise_and))
    o(lambda e: e.tensor_tensor(s_scale, s_pw2, fac_ap, ALU.mult))
    o(lambda e: e.tensor_scalar(out=s_scale, in0=s_scale, scalar1=0.25,
                                scalar2=1e-30, op0=ALU.mult, op1=ALU.max))
    o(lambda e: e.reciprocal(s_rs, s_scale))
    rs3 = s_rs.unsqueeze(2).broadcast_to([128, nb, 32])
    o(lambda e: e.tensor_tensor(A.rearrange("p (nb bs) -> p nb bs", bs=32),
                                x3, rs3, ALU.mult))
    o(lambda e: e.tensor_scalar(out=A, in0=A, scalar1=6.0, scalar2=-6.0,
                                op0=ALU.min, op1=ALU.max))
    o(lambda e: e.scalar_tensor_tensor(out=B, in0=A, scalar=-1.0, in1=A,
                                       op0=ALU.mult, op1=ALU.max))
    o(lambda e: e.tensor_scalar(out=C, in0=B, scalar1=2.0, scalar2=-1.0,
                                op0=ALU.is_ge, op1=ALU.mult))
    o(lambda e: e.tensor_scalar(out=B, in0=B, scalar1=4.0, scalar2=-0.5,
                                op0=ALU.is_ge, op1=ALU.mult))
    o(lambda e: e.scalar_tensor_tensor(out=C, in0=C, scalar=2.0, in1=B,
                                       op0=ALU.add, op1=ALU.add))
    o(lambda e: e.tensor_tensor(A, A, C, ALU.mult))
    o(lambda e: e.tensor_scalar(out=A, in0=A, scalar1=MAGIC, scalar2=MAGIC,
                                op0=ALU.add, op1=ALU.subtract))
    o(lambda e: e.tensor_scalar(out=B, in0=C, scalar1=1.0, scalar2=0.5,
                                op0=ALU.is_le, op1=ALU.mult))
    o(lambda e: e.tensor_scalar(out=C, in0=C, scalar1=0.5, scalar2=1.0,
                                op0=ALU.is_le, op1=ALU.mult))
    o(lambda e: e.scalar_tensor_tensor(out=C, in0=B, scalar=0.5, in1=C,
                                       op0=ALU.add, op1=ALU.add))
    o(lambda e: e.tensor_tensor(A, A, C, ALU.mult))
    sc3 = s_scale.unsqueeze(2).broadcast_to([128, nb, 32])
    return o(lambda e: e.tensor_tensor(
        out_ap.rearrange("p (nb bs) -> p nb bs", bs=32),
        A.rearrange("p (nb bs) -> p nb bs", bs=32), sc3, ALU.mult), inc)


def emit_xbar_tm2fm(kb, src, dst, n_src_tiles, n_chunks, dst_tile_stride,
                    inc="tp"):
    vals = []
    for i in range(n_src_tiles):
        src_ap = bass.AP(src.tensor if isinstance(src, bass.AP) else src, 0,
                         [[1, 128], [128, n_chunks], [1, 128]])
        src_ap = src[:, i * n_chunks * 128:(i + 1) * n_chunks * 128] \
            .rearrange("p (c q) -> p c q", q=128)
        dst_ap = bass.AP(dst.tensor if hasattr(dst, "tensor") else dst,
                         i * 128, [[1, 128], [dst_tile_stride, n_chunks],
                                   [1, 128]])
        v = kb.op("sync", lambda e, _s=src_ap, _d=dst_ap:
                  e.dma_start_transpose(_d, _s), inc, 16)
        vals.append(v)
    return vals[-1]



def host_consts():
    c = {}
    c["c_identity"] = np.eye(128, dtype=BF16NP)
    pos = np.arange(T, dtype=np.float32)
    dist = np.maximum(T - 1.0 - pos, 0.0)
    tw = np.zeros((32, T), np.float32)
    tw[:28] = -dist[None, :]
    tw[28:] = -pos[None, :]
    c["c_negtime"] = tw
    kex = np.zeros((32, 8, 128), BF16NP)
    for cc in range(8):
        for r in range(128):
            kex[4 * cc + r // 32, cc, r] = 1
    c["c_kexp"] = kex.reshape(32, 8 * 128)
    psm = np.zeros((128, 8, 32), BF16NP)
    for cc in range(8):
        for r in range(128):
            psm[r, cc, 4 * cc + r // 32] = 1
    c["c_psum32"] = psm.reshape(128, 8 * 32)
    return c


def host_l0_inputs(inputs):
    h = {}
    th = inputs["L0_attn_theta"].reshape(32, 32, 4)
    th_r = np.zeros((128, 32), np.float32)
    for r in range(128):
        for cc in range(8):
            for m in range(4):
                th_r[r, cc * 4 + m] = th[4 * cc + r // 32, r % 32, m]
    h["l0_theta_r"] = th_r
    wc = inputs["L0_attn_w_conv"].reshape(4, 2080)
    wcr = np.zeros((128, 64), np.float32)
    for fc in range(16):
        for tap in range(4):
            wcr[:, fc * 4 + tap] = wc[tap, fc * 128:(fc + 1) * 128]
    h["l0_wconv_r"] = wcr
    h["l0_wconv_sr"] = np.ascontiguousarray(wc[:, 2048:2080].T)
    h["l0_bconv_r"] = np.ascontiguousarray(
        inputs["L0_attn_b_conv"][:2048].reshape(16, 128).T)
    h["l0_bconv_sr"] = np.ascontiguousarray(
        inputs["L0_attn_b_conv"][2048:2080].reshape(32, 1))
    h["l0_slopes"] = np.ascontiguousarray(np.concatenate(
        [inputs["L0_attn_decay_slopes"], inputs["L0_attn_anchor_slopes"]]
    ).reshape(32, 1))
    h["l0_score"] = np.ascontiguousarray(
        inputs["L0_attn_score_scale"].reshape(32, 1))
    h["l0_deriv"] = np.ascontiguousarray(
        inputs["L0_attn_deriv_logits"].reshape(1, 3))
    gidx = np.array([[m2 * 32 + h2 for h2 in range(32)] for m2 in range(4)])
    src_idx = np.zeros(128, np.int64)
    for m2 in range(4):
        for h2 in range(32):
            src_idx[m2 * 32 + h2] = h2 * 4 + m2
    ns = inputs["L0_attn_norm_scale"]
    h["l0_norm2"] = np.ascontiguousarray(
        np.stack([ns[src_idx], ns[128 + src_idx]], 1))
    h["l0_wre"] = np.ascontiguousarray(inputs["L0_attn_W_re"][src_idx])
    h["l0_wim"] = np.ascontiguousarray(inputs["L0_attn_W_im"][src_idx])
    h["l0_win"] = np.ascontiguousarray(inputs["L0_attn_W_in"])
    h["l0_win2"] = np.ascontiguousarray(
        inputs["L0_attn_W_in"][:, 0:2048].reshape(8, 128, 16, 128)
        .transpose(2, 1, 0, 3).reshape(16, 128, 1024))
    h["l0_wout2"] = np.ascontiguousarray(
        inputs["L0_attn_W_out"].reshape(8, 128, 8, 128)
        .transpose(2, 1, 0, 3).reshape(8, 128, 1024))
    return h




def build(consts_np, want_dbg=False):
    kb = KB()
    kb.din("x0", (T, D))
    kb.din("L0_ffn_weight", (D, D))
    kb.din("L0_ffn_bias_r", (128, 8))
    kb.din("L0_ffn_delta_w_r", (128, 8 * 32))
    kb.din("c_dact", (128, 1))
    kb.din("l0_theta_r", (128, 32))
    kb.din("l0_wconv_r", (128, 64))
    kb.din("l0_wconv_sr", (32, 4))
    kb.din("l0_bconv_r", (128, 16))
    kb.din("l0_bconv_sr", (32, 1))
    kb.din("l0_slopes", (32, 1))
    kb.din("l0_score", (32, 1))
    kb.din("l0_deriv", (1, 3))
    kb.din("l0_norm2", (128, 2))
    kb.din("l0_wre", (128, 32))
    kb.din("l0_wim", (128, 32))
    kb.din("l0_win", (D, 2080))
    kb.din("l0_win2", (16, 128, 1024))
    kb.din("l0_wout2", (8, 128, 1024))
    for k, v in consts_np.items():
        kb.din(k, v.shape, BF16 if v.dtype == BF16NP else F32)
    kb.dtmp("x1", (T, D))
    if STAGE >= 3:
        kb.din("l1_wqkvT", (32, 128, 1024))
        kb.din("l1_bqkv", (128, 32))
        kb.din("l1_wqaT", (8, 128, 1024))
        kb.din("l1_wkaT", (8, 128, 1024))
        kb.din("l1_wdT", (8, 128, 1024))
        kb.din("l1_bqa", (128, 8))
        kb.din("l1_bka", (128, 8))
        kb.din("l1_bd", (128, 8))
        kb.din("l1_gwT", (1024, 8))
        kb.din("l1_wg", (8, 128, 2048))
        kb.din("l1_wu", (8, 128, 2048))
        kb.din("l1_wd", (8, 128, 2048))
        if want_dbg:
            kb.dout("xn1", (8, 128, T), BF16)
            kb.dout("g1", (8, 128, T), BF16)
            kb.dout("dbg_A", (128, 8 * T))
            kb.dout("dbg_out2", (128, NT * D))
        else:
            kb.dtmp("xn1", (8, 128, T), BF16)
            kb.dtmp("g1", (8, 128, T), BF16)
        if DIAG3:
            kb.dout("dbg_A3", (128, 8 * T), BF16)
            kb.dout("dbg_out3", (128, NT * D), BF16)
            kb.dout("dbg_qsl", (128, 4 * T), BF16)
            kb.dout("dbg_ksl", (128, 4 * T), BF16)
            kb.dout("dbg_qr", (128, 8 * T), BF16)
            kb.dout("dbg_kr", (128, 8 * T), BF16)
            kb.dout("dbg_v", (128, 8 * T), BF16)
    kb.dout("out", (T, D))
    if want_dbg:
        kb.dout("dbg_xn", (D, T))
        kb.dout("dbg_xq", (D, T))
        kb.dout("dbg_g0", (D, T))
        kb.dout("dbg_s0", (D, T))
        kb.dout("dbg_pw", (32, T))
        kb.dout("dbg_xv", (D, T))
        kb.dout("dbg_gate", (D, T))
        kb.dout("dbg_yu", (D, T))
        kb.dout("dbg_sk", (32, T))
        kb.dout("dbg_msq", (32, T))
        kb.dout("dbg_den", (32, T))

    H = kb.H
    op, wait, drain = kb.op, kb.wait, kb.drain
    kb.sem("ld", "gw", "st", "tp", "te", "dve", "act", "gp", "ev", "ph")

    ident = kb.S("ident", (128, 128), BF16)
    xn_fm = kb.S("xn_fm", (128, ND * T), BF16)
    g0 = kb.S("g0", (128, ND * T), BF16)
    smalls = kb.S("smalls", (128, 32), F32)
    red = kb.S("red", (128, 16), F32)
    sqd = kb.S("sqd", (128, D), F32)
    PS = kb.PSUM("PS", (128, 4096))
    pb = [PS[:, i * 512:(i + 1) * 512] for i in range(8)]
    kb.PSALL = PS
    MARK = kb.marker()
    xt = [kb.S(f"xt{i}", (128, D), F32) for i in range(2)]
    xnf = [kb.S(f"xnf{i}", (128, D), F32) for i in range(2)]
    xn_st = [kb.S(f"xn_st{i}", (128, D), BF16) for i in range(2)]
    xq_st = [kb.S(f"xq_st{i}", (128, D), BF16) for i in range(2)]
    wq_st = [kb.S(f"wq_st{i}", (128, D), BF16) for i in range(2)]
    xq_fm = kb.S("xq_fm", (128, ND * T), BF16)
    wqT = kb.S("wqT", (128, 8 * D), BF16)
    bias_c = kb.S("bias_c", (128, 8), F32)
    wkf = [kb.S(f"wkf{i}", (128, D), F32) for i in range(2)]
    dwf = kb.S("dwf", (128, 8 * 32), F32)
    facw = kb.S("facw", (128, 8 * 32), F32)
    qtmp = kb.S("qtmp", (128, 3 * D), F32)
    qsc = kb.S("qsc", (128, 4 * 32), F32)

    op("sync", lambda e: e.dma_start(out=ident[:], in_=H["c_identity"][:, :]),
       "ld", 16)

    op("sync", lambda e: e.dma_start(out=bias_c[:], in_=H["L0_ffn_bias_r"][:, :]),
       "ld", 16)
    op("sync", lambda e: e.dma_start(out=dwf[:], in_=H["L0_ffn_delta_w_r"][:, :]),
       "ld", 16)
    ld_da = op("sync", lambda e: e.dma_start(out=smalls[:, 8:9], in_=H["c_dact"][:, :]),
               "ld", 16)
    ld_init = kb.cnt("ld")

    wait("vector", "ld", ld_init)
    wait("scalar", "ld", ld_init)
    op("vector", lambda e: e.memset(smalls[:, 0:1].bitcast(U32), 0x7F800000))
    drain("vector")

    def emit_tanh_exp_fac(src, dst, width, tmp0, tmp1):
        a1 = op("scalar", lambda e: e.activation(tmp0, src, AF.Exp,
                                                 bias=0.0, scale=8.0), "act")
        wait("vector", "act", a1)
        op("vector", lambda e: e.tensor_scalar(out=tmp1, in0=tmp0, scalar1=1.0,
                                               scalar2=None, op0=ALU.add))
        drain("vector")
        op("vector", lambda e: e.reciprocal(tmp1, tmp1))
        drain("vector")
        op("vector", lambda e: e.tensor_scalar(out=tmp0, in0=tmp0, scalar1=-1.0,
                                               scalar2=None, op0=ALU.add))
        drain("vector")
        v = op("vector", lambda e: e.tensor_tensor(tmp0, tmp0, tmp1, ALU.mult),
               "dve")
        wait("scalar", "dve", v)
        return op("scalar", lambda e: e.activation(dst, tmp0, AF.Exp, bias=0.0,
                                                   scale=0.5 * LN2), "act")

    a_fac = emit_tanh_exp_fac(smalls[:, 8:9], smalls[:, 14:15], 1,
                              smalls[:, 9:10], smalls[:, 10:11])
    a_facw = emit_tanh_exp_fac(dwf[:], facw[:], 256,
                               qtmp[:, 0:256], qtmp[:, 256:512])

    xv = H["x0"].rearrange("(n p) d -> n p d", p=128)
    wait("vector", "act", a_fac)
    wait("vector", "act", a_facw)
    fac32 = smalls[:, 14:15].broadcast_to([128, 32])
    last_q = None
    tp_guard = [None, None]
    ldv_t = {0: op("sync", lambda e: e.dma_start(out=xt[0][:], in_=xv[0]),
                   "ld", 16),
             1: op("sync", lambda e: e.dma_start(out=xt[1][:], in_=xv[1]),
                   "ld", 16)}
    vxn_hist = []
    for i in range(NT):
        b = i % 2
        if i + 2 < NT:
            pass
        ldv = ldv_t[i]
        wait("vector", "ld", ldv)
        wait("scalar", "ld", ldv)
        sq_done = op("scalar", lambda e, _b=b: e.activation(
            sqd[:], xt[_b][:], AF.Square, bias=0.0, scale=1.0,
            accum_out=red[:, 1:2]), "act")
        op("vector", lambda e, _b=b: e.tensor_reduce(red[:, 0:1], xt[_b][:],
                                                     AX.X, ALU.add))
        drain("vector")
        op("vector", lambda e: e.tensor_scalar(out=red[:, 2:3], in0=red[:, 0:1],
                                               scalar1=1.0 / D, scalar2=None,
                                               op0=ALU.mult))
        drain("vector")
        op("vector", lambda e: e.tensor_tensor(red[:, 3:4], red[:, 2:3],
                                               red[:, 2:3], ALU.mult))
        wait("vector", "act", sq_done)
        drain("vector")
        op("vector", lambda e: e.scalar_tensor_tensor(
            out=red[:, 4:5], in0=red[:, 1:2], scalar=1.0 / D, in1=red[:, 3:4],
            op0=ALU.mult, op1=ALU.subtract))
        drain("vector")
        v_var = op("vector", lambda e: e.tensor_scalar(
            out=red[:, 4:5], in0=red[:, 4:5], scalar1=1e-5, scalar2=None,
            op0=ALU.add), "dve")
        wait("scalar", "dve", v_var)
        op("scalar", lambda e: e.activation(red[:, 5:6], red[:, 4:5], AF.Ln,
                                            bias=0.0, scale=1.0))
        kb.drain("scalar")
        a_r = op("scalar", lambda e: e.activation(red[:, 6:7], red[:, 5:6],
                                                  AF.Exp, bias=0.0, scale=-0.5),
                 "act")
        op("vector", lambda e: e.tensor_scalar(out=red[:, 7:8], in0=red[:, 2:3],
                                               scalar1=-1.0, scalar2=None,
                                               op0=ALU.mult))
        wait("vector", "act", a_r)
        drain("vector")
        op("vector", lambda e, _b=b: e.tensor_scalar(
            out=xnf[_b][:], in0=xt[_b][:], scalar1=red[:, 7:8],
            scalar2=red[:, 6:7], op0=ALU.add, op1=ALU.mult))
        drain("vector")
        if i >= 2 and tp_guard[b] is not None:
            wait("vector", "tp", tp_guard[b])
        v_xn = op("vector", lambda e, _b=b: e.tensor_copy(
            xn_st[_b][:], xnf[_b][:]), "dve")
        drain("vector")
        vxn_hist.append(v_xn)
        if i + 2 < NT:
            wait("sync", "dve", v_xn)
            ldv_t[i + 2] = op("sync", lambda e, _j=i + 2, _b=b: e.dma_start(
                out=xt[_b][:], in_=xv[_j]), "ld", 16)
        last_q = emit_quant(kb, xnf[b][:], xq_st[b][:], fac32, D, inc="dve")
        wait("sync", "dve", v_xn)
        da_n = bass.AP(xn_fm, i * 128, [[NT * D, 128], [T, ND], [1, 128]])
        sa_n = xn_st[b][:].rearrange("p (c q) -> p c q", q=128)
        op("sync", lambda e, _s=sa_n, _d=da_n: e.dma_start_transpose(_d, _s),
           "tp", 16)
        wait("sync", "dve", last_q)
        da_q = bass.AP(xq_fm, i * 128, [[NT * D, 128], [T, ND], [1, 128]])
        sa_q = xq_st[b][:].rearrange("p (c q) -> p c q", q=128)
        tp_guard[b] = op("sync", lambda e, _s=sa_q, _d=da_q:
                         e.dma_start_transpose(_d, _s), "tp", 16)
    xq_done = last_q

    wv = H["L0_ffn_weight"].rearrange("(c p) d -> c p d", p=128)
    wait("vector", "act", a_facw)
    wtp_guard = [None, None]
    lw_t = {0: op("sync", lambda e: e.dma_start(out=wkf[0][:], in_=wv[0]),
                  "ld", 16),
            1: op("sync", lambda e: e.dma_start(out=wkf[1][:], in_=wv[1]),
                  "ld", 16)}
    for c in range(8):
        b = c % 2
        wait("vector", "ld", lw_t[c])
        if c >= 2 and wtp_guard[b] is not None:
            wait("vector", "tp", wtp_guard[b])
        last_q = emit_quant(kb, wkf[b][:], wq_st[b][:],
                            facw[:, c * 32:(c + 1) * 32], D, inc="dve")
        wait("sync", "dve", last_q)
        if c + 2 < 8:
            lw_t[c + 2] = op("sync", lambda e, _j=c + 2, _b=b: e.dma_start(
                out=wkf[_b][:], in_=wv[_j]), "ld", 16)
        da_w = bass.AP(wqT, c * 128, [[8 * D, 128], [D, 8], [1, 128]])
        sa_w = wq_st[b][:].rearrange("p (c q) -> p c q", q=128)
        wtp_guard[b] = op("sync", lambda e, _s=sa_w, _d=da_w:
                          e.dma_start_transpose(_d, _s), "tp", 16)
    wq_done = last_q

    tp_done = kb.cnt("tp")

    wait("tensor", "tp", tp_done)
    bank_guard = {}
    g0_acts = []
    for tt in range(4):
        for oc in range(8):
            bi = (tt * 8 + oc) % 4
            bank = pb[bi]
            if bi in bank_guard:
                wait("tensor", "act", bank_guard[bi])
            mmv = None
            for ic in range(8):
                mov = kb.sb["xq_fm"][:, ic * T + tt * 512:ic * T + tt * 512 + 512]
                sta = kb.sb["wqT"][:, ic * D + oc * 128:ic * D + oc * 128 + 128]
                mmv = op("tensor",
                         lambda e, _b=bank, _s=sta, _m=mov, _i=ic: e.matmul(
                             _b[:], _s, _m,
                             start=(_i == 0), stop=(_i == 7)),
                         "te" if ic == 7 else None)
            wait("scalar", "te", mmv)
            a = op("scalar", lambda e, _b=bank, _o=oc, _t=tt: e.activation(
                kb.sb["g0"][:, _o * T + _t * 512:_o * T + _t * 512 + 512],
                _b[:], AF.Silu, bias=kb.sb["bias_c"][:, _o:_o + 1], scale=1.0),
                "act")
            bank_guard[bi] = a
            g0_acts.append(a)

    if want_dbg:
        wait("sync", "tp", tp_done)
        for c in range(ND):
            kb.wait("gpsimd", "tp", tp_done)
            op("gpsimd", lambda e, _c=c: e.dma_start(
                out=H["dbg_xn"].rearrange("(c p) t -> c p t", p=128)[_c],
                in_=xn_fm[:, _c * T:(_c + 1) * T]), "st", 16)
            op("gpsimd", lambda e, _c=c: e.dma_start(
                out=H["dbg_xq"].rearrange("(c p) t -> c p t", p=128)[_c],
                in_=xq_fm[:, _c * T:(_c + 1) * T]), "st", 16)
        kb.wait("gpsimd", "act", g0_acts[-1])
        for c in range(ND):
            op("gpsimd", lambda e, _c=c: e.dma_start(
                out=H["dbg_g0"].rearrange("(c p) t -> c p t", p=128)[_c],
                in_=g0[:, _c * T:(_c + 1) * T]), "st", 16)

    if STAGE >= 2:
        kb.barrier()
        kb.free_to(MARK)
        M1b = kb.marker()
        kb.S("rp32", (32, 2 * T), F32)
        kb.S("rpbf", (32, T), BF16)
        kb.S("kexp", (32, 8 * 128), BF16)
        kb.S("psum32", (128, 8 * 32), BF16)
        xt_l0 = [kb.S(f"xt{i}_l0", (128, D), F32) for i in range(2)]
        xnf_l0 = [kb.S(f"xnf{i}_l0", (128, D), F32) for i in range(2)]
        kb.sb["xt0"], kb.sb["xt1"] = xt_l0
        kb.sb["xnf0"], kb.sb["xnf1"] = xnf_l0
        op("sync", lambda e: e.dma_start(out=kb.sb["rp32"][0:32, T:2 * T],
                                         in_=H["c_negtime"][:, :]), "ld", 16)
        op("sync", lambda e: e.dma_start(out=kb.sb["kexp"][:, :],
                                         in_=H["c_kexp"][:, :]), "ld", 16)
        op("sync", lambda e: e.dma_start(out=kb.sb["psum32"][:, :],
                                         in_=H["c_psum32"][:, :]), "ld", 16)
        M2 = kb.marker()
        emit_l0_sraw(kb, pb)
        kb.barrier()
        kb.free_to(M2)
        emit_l0_attn(kb, pb, want_dbg=want_dbg)
        kb.wait("sync", "st", kb.cnt("st"))
        if STAGE == 2:
            op("sync", lambda e: e.dma_start(out=H["out"][:, :],
                                             in_=H["x1"][:, :]), "st", 16)
            kb.wait("sync", "st", kb.cnt("st"))
        if STAGE >= 3:
            kb.barrier()
            kb.free_to(M1b)
            emit_l1(kb, want_dbg=want_dbg)
    else:
        wait("sync", "act", g0_acts[-1])
        for i in range(NT):
            op("sync", lambda e, _i=i: e.dma_start(
                out=H["out"].rearrange("(n p) d -> n p d", p=128)[_i],
                in_=xt[0][:]), "st", 16)
        wait("sync", "st", kb.cnt("st"))

    return kb



def prep_core_inputs(inputs, consts):
    base = {}
    base.update({k: np.ascontiguousarray(v) for k, v in consts.items()})
    base["L0_ffn_weight"] = np.ascontiguousarray(inputs["L0_ffn_weight"])
    base["L0_ffn_bias_r"] = np.ascontiguousarray(
        inputs["L0_ffn_bias"].reshape(8, 128).T)
    base["L0_ffn_delta_w_r"] = np.ascontiguousarray(
        inputs["L0_ffn_delta_w"].reshape(8, 128, 32).transpose(1, 0, 2)
        .reshape(128, 8 * 32))
    base["c_dact"] = np.ascontiguousarray(
        np.broadcast_to(inputs["L0_ffn_delta_act"].reshape(1, 1), (128, 1)))
    base.update(host_l0_inputs(inputs))
    return base


def kernel(**inputs):
    consts = host_consts()
    if STAGE >= 3:
        consts.update(host_l1_consts())
    base = prep_core_inputs(inputs, consts)
    if STAGE >= 3:
        base.update(host_l1_inputs(inputs))
    x = np.ascontiguousarray(inputs["L0_attn_x"])
    in_maps = []
    for c in range(8):
        m = dict(base)
        m["x0"] = np.ascontiguousarray(x[c])
        in_maps.append(m)
    kb = build(consts, want_dbg=DEBUG)
    nc = kb.finalize()
    res = run_bass_kernel_spmd(nc, in_maps, core_ids=list(range(8)))
    out = np.stack([res.results[c]["out"] for c in range(8)], 0)
    if DEBUG:
        kernel.dbg = res.results
    return out



def emit_l0_attn(kb, pb, want_dbg=False):
    H, op, wait, drain = kb.H, kb.op, kb.wait, kb.drain
    sb = kb.sb
    TC = 512
    NTC = T // TC

    xn_fm, g0 = sb["xn_fm"], sb["g0"]
    smalls = sb["smalls"]
    rp32 = sb["rp32"]
    pw_all = rp32[0:32, 0:T]
    pw_bf = sb["rpbf"][0:32, :]

    thdiv = kb.S("thdiv", (128, 32), F32)
    wconv = kb.S("wconv", (128, 64), F32)
    bconv = kb.S("bconv", (128, 16), F32)
    norm2 = kb.S("norm2", (128, 2), F32)
    wre = kb.S("wre", (128, 32), F32)
    wim = kb.S("wim", (128, 32), F32)
    wy = kb.S("wy", (128, 8 * 128), BF16)
    diag4 = [kb.S(f"diag4_{i}", (128, 4 * 128), BF16) for i in range(2)]
    zr = [kb.S(f"zr{i}", (128, TC + 3), BF16) for i in range(2)]
    ztail = kb.S("ztail", (128, 16 * 3), BF16)
    xv = kb.S("xv", (128, 8 * TC), F32)
    gate = kb.S("gate", (128, 8 * TC), BF16)
    y_u = kb.S("y_u", (128, 8 * TC), BF16)
    sg_tm = kb.S("sg_tm", (128, 4 * D), BF16)
    u_t = [kb.S(f"u_t{p}", (128, TC), F32) for p in range(2)]
    k_r = [kb.S(f"k_r{p}", (128, TC), F32) for p in range(2)]
    frac = [kb.S(f"frac{p}", (128, TC), F32) for p in range(4)]
    afr = [kb.S(f"afr{p}", (128, TC), F32) for p in range(2)]
    sinb = [kb.S(f"sinb{p}", (128, TC), BF16) for p in range(4)]
    cosb = [kb.S(f"cosb{p}", (128, TC), BF16) for p in range(4)]
    pwx = kb.S("pwx", (128, TC), BF16)
    pol1 = kb.S("pol1", (128, TC), F32)
    pol2 = pol1
    pp = kb.S("pp", (128, TC), BF16)
    rein = [kb.S(f"rein{p}", (128, TC), BF16) for p in range(2)]
    imin = [kb.S(f"imin{p}", (128, TC), BF16) for p in range(2)]
    csre = [kb.S(f"csre{p}", (128, TC), BF16) for p in range(2)]
    csim = [kb.S(f"csim{p}", (128, TC), BF16) for p in range(2)]
    sqt = [rein[0], imin[0], rein[1], imin[1]]
    carr = kb.S("carr", (128, 66), F32)
    l32 = kb.S("l32", (32, (5 if want_dbg else 4) * TC), F32)
    skbf = kb.S("skbf", (32, TC), BF16)
    wfc = [kb.S(f"wfc{i}", (128, 8 * 128), BF16) for i in range(2)]
    sgadd = [sb["xnf0"], sb["xnf1"]]
    sg = gate

    msq = l32[0:32, 0:TC]
    dent = l32[0:32, TC:2 * TC]
    skt = l32[0:32, 2 * TC:3 * TC]
    rtmp = l32[0:32, 3 * TC:4 * TC]
    rtmp2 = l32[0:32, 4 * TC:5 * TC] if want_dbg else None

    for (nm, tile) in (("l0_theta_r", thdiv), ("l0_wconv_r", wconv),
                       ("l0_bconv_r", bconv), ("l0_norm2", norm2),
                       ("l0_wre", wre), ("l0_wim", wim)):
        op("sync", lambda e, _n=nm, _t=tile: e.dma_start(out=_t[:], in_=H[_n][:, :]),
           "ld", 16)
    ld_l0 = kb.cnt("ld")
    wait("vector", "ld", ld_l0)
    op("vector", lambda e: e.tensor_scalar(out=thdiv[:], in0=thdiv[:],
                                           scalar1=float(1.0 / (2 * np.pi)),
                                           scalar2=None, op0=ALU.mult))
    op("vector", lambda e: e.memset(smalls[:, 1:2].bitcast(U32), 1))
    op("vector", lambda e: e.memset(smalls[:, 4:5], float(np.pi / 2)))
    op("vector", lambda e: e.memset(carr[:], 0.0))
    op("vector", lambda e: e.memset(ztail[:], 0.0))
    op("vector", lambda e: e.memset(wy[:], 0.0))
    drain("vector")
    op("vector", lambda e: e.tensor_scalar(out=wre[:], in0=wre[:],
                                           scalar1=norm2[:, 0:1], scalar2=None,
                                           op0=ALU.mult))
    op("vector", lambda e: e.tensor_scalar(out=wim[:], in0=wim[:],
                                           scalar1=norm2[:, 1:2], scalar2=None,
                                           op0=ALU.mult))
    drain("vector")
    WYP = 8 * 128
    for ri, wsrc in ((0, wre), (1, wim)):
        for m in range(4):
            j = ri * 4 + m
            for kk in range(4):
                src_ap = wsrc[m * 32:(m + 1) * 32, :]
                dst_ap = bass.AP(wy, kk * 32 * WYP + j * 128 + kk * 32,
                                 [[WYP, 32], [1, 32]])
                op("vector", lambda e, _s=src_ap, _d=dst_ap: e.tensor_copy(_d, _s))
    drain("vector")
    v_init = op("vector", lambda e: e.memset(smalls[:, 2:3], 0.0), "dve")

    wfc_guard = [0, 0]
    sgadd_guard = [0, 0]
    prev_cos_c = [None]
    for tc in range(NTC):
        t0 = tc * TC
        first = tc == 0
        def build_diag(fcx):
            for tap in range(4):
                g = op("vector", lambda e, _f=fcx, _t=tap, _d=diag4[fcx % 2]:
                       e.tensor_scalar(
                    out=_d[:, _t * 128:(_t + 1) * 128], in0=sb["ident"][:],
                    scalar1=wconv[:, (_f * 4 + _t):(_f * 4 + _t) + 1],
                    scalar2=None, op0=ALU.mult), "dve" if tap == 3 else None)
            return g

        wait("vector", "te", kb.cnt("te"))
        diag_v = {0: build_diag(0)}
        act_hist = []
        for fc in range(16):
            bz = fc % 2
            in_ap = bass.AP(H["l0_win2"].tensor, fc * 131072,
                            [[1024, 128], [1, 1024]])
            if wfc_guard[bz]:
                kb.wait("gpsimd", "te", wfc_guard[bz])
            lw = op("gpsimd", lambda e, _i=in_ap, _b=bz: e.dma_start(
                out=wfc[_b][:], in_=_i), "gw", 16)
            wait("tensor", "gw", lw)
            bank = pb[2 + bz]
            mmv = None
            for dc in range(8):
                mov = xn_fm[:, dc * T + t0:dc * T + t0 + TC]
                sta = wfc[bz][:, dc * 128:(dc + 1) * 128]
                mmv = op("tensor", lambda e, _b=bank, _s=sta, _m=mov, _d=dc:
                         e.matmul(_b[:], _s, _m, start=(_d == 0), stop=(_d == 7)),
                         "te" if dc == 7 else None)
            wfc_guard[bz] = mmv
            wait("vector", "te", mmv)
            op("vector", lambda e, _b=bz, _f=fc: e.tensor_copy(
                zr[_b][:, 0:3], ztail[:, _f * 3:_f * 3 + 3]))
            ev = op("vector", lambda e, _b=bank, _z=bz: e.tensor_copy(
                zr[_z][:, 3:3 + TC], _b[:]), "dve")
            kb.drain("vector")
            op("vector", lambda e, _b=bz, _f=fc: e.tensor_copy(
                ztail[:, _f * 3:_f * 3 + 3], zr[_b][:, TC:TC + 3]))
            if fc + 1 < 16:
                diag_v[fc + 1] = build_diag(fc + 1)
            wait("tensor", "dve", ev)
            wait("tensor", "dve", diag_v[fc])
            if len(act_hist) >= 2:
                wait("tensor", "act", act_hist[-2])
            bank2 = pb[4 + bz]
            mmv2 = None
            for tap in range(4):
                mov = zr[bz][:, tap:tap + TC]
                sta = diag4[fc % 2][:, tap * 128:(tap + 1) * 128]
                mmv2 = op("tensor", lambda e, _b=bank2, _s=sta, _m=mov, _t=tap:
                          e.matmul(_b[:], _s, _m, start=(_t == 0), stop=(_t == 3)),
                          "te" if tap == 3 else None)
            wait("scalar", "te", mmv2)
            if fc < 8:
                a = op("scalar", lambda e, _b=bank2, _f=fc: e.activation(
                    xv[:, _f * TC:(_f + 1) * TC], _b[:], AF.Identity,
                    bias=bconv[:, _f:_f + 1], scale=1.0), "act")
            else:
                a = op("scalar", lambda e, _b=bank2, _f=fc: e.activation(
                    gate[:, (_f - 8) * TC:(_f - 7) * TC], _b[:], AF.Silu,
                    bias=bconv[:, _f:_f + 1], scale=1.0), "act")
            act_hist.append(a)
        wait("tensor", "act", kb.cnt("act"))
        a_convdone = kb.cnt("act")

        if want_dbg and "dbg_xv" in H:
            wait("sync", "act", a_convdone)
            kb.wait("gpsimd", "act", a_convdone)
            for fcc in range(8):
                op("sync", lambda e, _f=fcc, _t0=t0: e.dma_start(
                    out=H["dbg_xv"].rearrange("(c p) t -> c p t", p=128)
                    [_f][:, _t0:_t0 + TC],
                    in_=xv[:, _f * TC:(_f + 1) * TC]), "st", 16)
                op("gpsimd", lambda e, _f=fcc, _t0=t0: e.dma_start(
                    out=H["dbg_gate"].rearrange("(c p) t -> c p t", p=128)
                    [_f][:, _t0:_t0 + TC],
                    in_=gate[:, _f * TC:(_f + 1) * TC]), "st", 16)
            kb.wait("gpsimd", "st", kb.cnt("st"))
            kb.wait("vector", "st", kb.cnt("st"))
            kb.wait("scalar", "st", kb.cnt("st"))
        if "d" not in L0_PARTS:
            continue
        op("vector", lambda e, _t0=t0: e.tensor_tensor_scan(
            dent, pw_all[:, _t0:_t0 + TC], pw_all[:, _t0:_t0 + TC],
            carr[0:32, 64:65], ALU.add, ALU.bypass))
        drain("vector")
        op("vector", lambda e: e.tensor_copy(carr[0:32, 64:65],
                                             dent[:, TC - 1:TC]))
        kb.drain("vector")
        op("vector", lambda e: e.tensor_single_scalar(rtmp, dent, 1e-4,
                                                      ALU.max))
        drain("vector")
        op("vector", lambda e: e.reciprocal(rtmp, rtmp))
        drain("vector")

        if "B" not in L0_PARTS:
            continue
        wait("tensor", "act", a_convdone)
        msq_started = False
        for c in range(8):
            mm = op("tensor", lambda e, _c=c, _t0=t0: e.matmul(
                pb[1][:], sb["kexp"][0:32, _c * 128:(_c + 1) * 128],
                pw_bf[:, _t0:_t0 + TC], start=True, stop=True), "te")
            wait("vector", "te", mm)
            ev = op("vector", lambda e: e.tensor_copy(pwx[:], pb[1][:]), "dve")
            wait("tensor", "dve", ev)
            xvc = xv[:, c * TC:(c + 1) * TC]
            op("vector", lambda e, _x=xvc: e.tensor_scalar(
                out=pol1[:], in0=_x, scalar1=smalls[:, 22:23],
                scalar2=smalls[:, 21:22], op0=ALU.mult, op1=ALU.add))
            drain("vector")
            op("vector", lambda e, _x=xvc: e.scalar_tensor_tensor(
                out=pol2[:], in0=_x, scalar=1.0, in1=pol1[:], op0=ALU.mult,
                op1=ALU.mult))
            drain("vector")
            op("vector", lambda e: e.scalar_tensor_tensor(
                out=pp[:], in0=pol2[:], scalar=smalls[:, 20:21], in1=pwx[:],
                op0=ALU.add, op1=ALU.mult))
            drain("vector")
            def front(mp, prev_cos=None):
                ms = [2 * mp, 2 * mp + 1]
                g = None
                for p, m in enumerate(ms):
                    thc = thdiv[:, c * 4 + m:c * 4 + m + 1]
                    g = op("vector", lambda e, _x=xvc, _th=thc, _p=p:
                           e.tensor_scalar(
                        out=u_t[_p][:], in0=_x, scalar1=_th, scalar2=MAGIC,
                        op0=ALU.mult, op1=ALU.add),
                        "dve" if p == 1 else None)
                wait("vector", "dve", g)
                for p in range(2):
                    g = op("vector", lambda e, _p=p: e.tensor_scalar(
                        out=k_r[_p][:], in0=u_t[_p][:], scalar1=MAGIC,
                        scalar2=None, op0=ALU.subtract),
                        "dve" if p == 1 else None)
                wait("vector", "dve", g)
                for p, m in enumerate(ms):
                    thc = thdiv[:, c * 4 + m:c * 4 + m + 1]
                    g = op("vector", lambda e, _x=xvc, _th=thc, _p=p,
                           _q=2 * mp + p: e.scalar_tensor_tensor(
                        out=frac[_q][:], in0=_x, scalar=_th, in1=k_r[_p][:],
                        op0=ALU.mult, op1=ALU.subtract),
                        "dve" if p == 1 else None)
                wait("scalar", "dve", g)
                ga = None
                for p in range(2):
                    op("scalar", lambda e, _q=2 * mp + p: e.activation(
                        sinb[_q][:], frac[_q][:], AF.Sin, bias=0.0,
                        scale=float(2 * np.pi)))
                if prev_cos is not None:
                    kb.wait("scalar", "act", prev_cos)
                for p in range(2):
                    ga = op("scalar", lambda e, _p=p, _q=2 * mp + p:
                            e.activation(
                        afr[_p][:], frac[_q][:], AF.Abs, bias=0.0, scale=1.0),
                        "act" if p == 1 else None)
                kb.wait("scalar", "act", ga)
                a_cos = None
                for p in range(2):
                    a_cos = op("scalar", lambda e, _p=p, _q=2 * mp + p:
                               e.activation(
                        cosb[_q][:], afr[_p][:], AF.Sin, bias=smalls[:, 4:5],
                        scale=float(-2 * np.pi)), "act" if p == 1 else None)
                return a_cos

            def tail(mp, a_cos, lastmm):
                ms = [2 * mp, 2 * mp + 1]
                wait("vector", "act", a_cos)
                if lastmm is not None:
                    wait("vector", "te", lastmm)
                for p in range(2):
                    op("vector", lambda e, _p=p, _q=2 * mp + p: e.tensor_tensor(
                        rein[_p][:], pp[:], cosb[_q][:], ALU.mult))
                    g = op("vector", lambda e, _p=p, _q=2 * mp + p:
                           e.tensor_tensor(
                        imin[_p][:], pp[:], sinb[_q][:], ALU.mult),
                        "dve" if p == 1 else None)
                wait("vector", "dve", g)
                for p, m in enumerate(ms):
                    ci = (c * 4 + m) * 2
                    op("vector", lambda e, _ci=ci, _p=p: e.tensor_tensor_scan(
                        csre[_p][:], rein[_p][:], rein[_p][:],
                        carr[:, _ci:_ci + 1], ALU.add, ALU.bypass))
                    g = op("vector", lambda e, _ci=ci, _p=p: e.tensor_tensor_scan(
                        csim[_p][:], imin[_p][:], imin[_p][:],
                        carr[:, _ci + 1:_ci + 2], ALU.add, ALU.bypass),
                        "dve" if p == 1 else None)
                wait("vector", "dve", g)
                for p, m in enumerate(ms):
                    ci = (c * 4 + m) * 2
                    op("vector", lambda e, _ci=ci, _p=p: e.tensor_copy(
                        carr[:, _ci:_ci + 1], csre[_p][:, TC - 1:TC]))
                    op("vector", lambda e, _ci=ci, _p=p: e.tensor_copy(
                        carr[:, _ci + 1:_ci + 2], csim[_p][:, TC - 1:TC]))
                    op("vector", lambda e, _p=p: e.tensor_tensor(
                        sqt[2 * _p][:], csre[_p][:], csre[_p][:], ALU.mult))
                    g = op("vector", lambda e, _p=p: e.tensor_tensor(
                        sqt[2 * _p + 1][:], csim[_p][:], csim[_p][:],
                        ALU.mult), "dve" if p == 1 else None)
                wait("tensor", "dve", g)
                for p, m in enumerate(ms):
                    op("tensor", lambda e, _c=c, _p=p, _st=msq_st[0]:
                       e.matmul(pb[6][0:32, :],
                                sb["psum32"][:, _c * 32:(_c + 1) * 32],
                                sqt[2 * _p][:], start=_st, stop=False), None)
                    msq_st[0] = False
                    op("tensor", lambda e, _m=m, _p=p, _first=(m == 0):
                       e.matmul(pb[0][:], wy[:, _m * 128:(_m + 1) * 128],
                                csre[_p][:], start=_first, stop=False), None)
                    is_last = (c == 7 and m == 3)
                    op("tensor", lambda e, _c=c, _p=p, _L=is_last:
                       e.matmul(pb[6][0:32, :],
                                sb["psum32"][:, _c * 32:(_c + 1) * 32],
                                sqt[2 * _p + 1][:], start=False, stop=_L), None)
                    lastmm = op("tensor", lambda e, _m=m, _p=p:
                                e.matmul(pb[0][:],
                                         wy[:, (4 + _m) * 128:(5 + _m) * 128],
                                         csim[_p][:], start=False,
                                         stop=(_m == 3)), "te")
                return lastmm

            msq_st = [not msq_started]
            msq_started = True
            lastmm = None
            ac0 = front(0, prev_cos=prev_cos_c[0])
            ac1 = front(1, prev_cos=ac0)
            prev_cos_c[0] = ac1
            lastmm = tail(0, ac0, lastmm)
            lastmm = tail(1, ac1, lastmm)
            wait("vector", "te", lastmm)
            ev = op("vector", lambda e, _c=c: e.tensor_copy(
                y_u[:, _c * TC:(_c + 1) * TC], pb[0][:]), "dve")
            wait("tensor", "dve", ev)
        if "C" not in L0_PARTS:
            continue
        wait("vector", "te", kb.cnt("te"))
        op("vector", lambda e: e.tensor_scalar(
            out=msq, in0=pb[6][0:32, :], scalar1=float(1.0 / 256.0),
            scalar2=None, op0=ALU.mult))
        drain("vector")
        op("vector", lambda e: e.tensor_tensor(msq, msq, rtmp, ALU.mult))
        drain("vector")
        op("vector", lambda e: e.tensor_tensor(msq, msq, rtmp, ALU.mult))
        drain("vector")
        v_msq = op("vector", lambda e: e.tensor_scalar(
            out=msq, in0=msq, scalar1=1e-5, scalar2=None, op0=ALU.add), "dve")
        if want_dbg and "dbg_msq" in H:
            v_m = op("vector", lambda e: e.tensor_copy(rtmp2, msq), "dve")
            kb.drain("vector")
            wait("sync", "dve", v_m)
            op("sync", lambda e, _t0=t0: e.dma_start(
                out=H["dbg_msq"][:, _t0:_t0 + TC], in_=rtmp2), "st", 16)
            op("sync", lambda e, _t0=t0: e.dma_start(
                out=H["dbg_den"][:, _t0:_t0 + TC], in_=dent), "st", 16)
            wait("vector", "st", kb.cnt("st"))
        wait("scalar", "dve", v_msq)
        a_rs = op("scalar", lambda e: e.activation(
            skt, msq, AF.Sqrt, bias=0.0, scale=1.0), "act")
        wait("vector", "act", a_rs)
        op("vector", lambda e: e.reciprocal(skt, skt))
        drain("vector")
        op("vector", lambda e: e.tensor_tensor(skt, skt, rtmp, ALU.mult))
        drain("vector")
        v_sk = op("vector", lambda e: e.tensor_copy(skbf[:], skt), "dve")
        if want_dbg and "dbg_yu" in H:
            wait("gpsimd", "dve", kb.cnt("dve"))
            wait("sync", "dve", kb.cnt("dve"))
            for fcc in range(8):
                op("gpsimd", lambda e, _f=fcc, _t0=t0: e.dma_start(
                    out=H["dbg_yu"].rearrange("(c p) t -> c p t", p=128)
                    [_f][:, _t0:_t0 + TC],
                    in_=y_u[:, _f * TC:(_f + 1) * TC]), "st", 16)
            op("sync", lambda e, _t0=t0: e.dma_start(
                out=H["dbg_sk"][:, _t0:_t0 + TC], in_=skt), "st", 16)
            kb.wait("gpsimd", "st", kb.cnt("st"))
            kb.wait("vector", "st", kb.cnt("st"))
        if "D" not in L0_PARTS:
            continue
        wait("tensor", "dve", v_sk)
        for c in range(8):
            mm = op("tensor", lambda e, _c=c: e.matmul(
                pb[1][:], sb["kexp"][0:32, _c * 128:(_c + 1) * 128],
                skbf[:], start=True, stop=True), "te")
            wait("vector", "te", mm)
            off = c * TC
            op("vector", lambda e, _o=off: e.scalar_tensor_tensor(
                out=y_u[:, _o:_o + TC], in0=y_u[:, _o:_o + TC], scalar=1.0,
                in1=pb[1][:], op0=ALU.mult, op1=ALU.mult))
            kb.drain("vector")
            ev = op("vector", lambda e, _o=off: e.tensor_tensor(
                y_u[:, _o:_o + TC], y_u[:, _o:_o + TC], gate[:, _o:_o + TC],
                ALU.mult), "dve")
            wait("tensor", "dve", ev)
        for fo in range(8):
            bz = fo % 2
            in_ap = bass.AP(H["l0_wout2"].tensor, fo * 131072,
                            [[1024, 128], [1, 1024]])
            if wfc_guard[bz]:
                kb.wait("gpsimd", "te", wfc_guard[bz])
            lw = op("gpsimd", lambda e, _i=in_ap, _b=bz: e.dma_start(
                out=wfc[_b][:], in_=_i), "gw", 16)
            wait("tensor", "gw", lw)
            bank = pb[2 + bz]
            mmv = None
            for c in range(8):
                mov = y_u[:, c * TC:(c + 1) * TC]
                sta = wfc[bz][:, c * 128:(c + 1) * 128]
                mmv = op("tensor", lambda e, _b=bank, _s=sta, _m=mov, _c=c:
                         e.matmul(_b[:], _s, _m, start=(_c == 0), stop=(_c == 7)),
                         "te" if c == 7 else None)
            wfc_guard[bz] = mmv
            wait("vector", "te", mmv)
            ev = op("vector", lambda e, _b=bank, _f=fo, _t0=t0:
                    e.scalar_tensor_tensor(
                        out=sg[:, _f * TC:(_f + 1) * TC],
                        in0=g0[:, _f * T + _t0:_f * T + _t0 + TC],
                        scalar=1.0, in1=_b[:], op0=ALU.mult, op1=ALU.mult),
                    "dve")
            wait("tensor", "dve", ev)
        sg_done = kb.cnt("dve")
        wait("sync", "dve", sg_done)
        tpv = None
        for fo in range(8):
            sa = sg[:, fo * TC:(fo + 1) * TC].rearrange("p (i q) -> p i q", q=128)
            da = bass.AP(sg_tm, fo * 128, [[4 * D, 128], [D, 4], [1, 128]])
            tpv = op("sync", lambda e, _s=sa, _d=da: e.dma_start_transpose(_d, _s),
                     "tp", 16)
        xv_h = H["x0"].rearrange("(n p) d -> n p d", p=128)
        x1_h = H["x1"].rearrange("(n p) d -> n p d", p=128)
        xtb_l0 = [sb["xt0"], sb["xt1"]]
        for ii in range(TC // 128):
            i = t0 // 128 + ii
            bx = ii % 2
            ldv = op("sync", lambda e, _i=i, _x=xtb_l0[bx]: e.dma_start(
                out=_x[:], in_=xv_h[_i]), "ld", 16)
            wait("vector", "ld", ldv)
            wait("vector", "tp", tpv)
            if sgadd_guard[bx]:
                wait("vector", "st", sgadd_guard[bx])
            v1 = op("vector", lambda e, _b=bx, _x=xtb_l0[bx], _ii=ii:
                    e.tensor_tensor(
                sgadd[_b][:], _x[:],
                sg_tm[:, _ii * D:(_ii + 1) * D], ALU.add), "dve")
            wait("sync", "dve", v1)
            st = op("sync", lambda e, _i=i, _b=bx: e.dma_start(
                out=x1_h[_i], in_=sgadd[_b][:]), "st", 16)
            sgadd_guard[bx] = st
        if want_dbg and "dbg_s0" in H:
            wait("gpsimd", "dve", sg_done)
            for fo in range(8):
                gv = op("gpsimd", lambda e, _f=fo, _t0=t0: e.dma_start(
                    out=H["dbg_s0"].rearrange("(c p) t -> c p t", p=128)
                    [_f][:, _t0:_t0 + TC],
                    in_=sg[:, _f * TC:(_f + 1) * TC]), "st", 16)
            wait("scalar", "st", kb.cnt("st"))
            wait("vector", "st", kb.cnt("st"))
    return


def emit_l0_sraw(kb, pb):
    H, op, wait, drain = kb.H, kb.op, kb.wait, kb.drain
    sb = kb.sb
    smalls = sb["smalls"]
    sr_w = kb.S("sr_w", (128, 8 * 32), BF16)
    srpack = kb.S("srpack", (32, (T + 3) + T), F32)
    sr_z = srpack[0:32, 0:T + 3]
    sr_c = srpack[0:32, T + 3:2 * T + 3]
    sr_t = kb.sb["rp32"][0:32, T:2 * T]
    svec = kb.S("svec", (32, 8), F32)
    dv3 = kb.S("dv3", (1, 4), F32)
    dv128 = kb.S("dv128", (128, 3), F32)
    ones1 = kb.S("ones1", (1, 128), F32)

    in_ap = bass.AP(H["l0_win"].tensor, 2048,
                    [[2080, 128], [128 * 2080, 8], [1, 32]])
    l1 = op("gpsimd", lambda e, _i=in_ap: e.dma_start(out=sr_w[:], in_=_i),
            "gw", 16)
    op("sync", lambda e: e.dma_start(out=svec[:, 0:4], in_=H["l0_wconv_sr"][:, :]),
       "ld", 16)
    op("sync", lambda e: e.dma_start(out=svec[:, 4:5], in_=H["l0_bconv_sr"][:, :]),
       "ld", 16)
    op("sync", lambda e: e.dma_start(out=svec[:, 5:6], in_=H["l0_slopes"][:, :]),
       "ld", 16)
    op("sync", lambda e: e.dma_start(out=svec[:, 6:7], in_=H["l0_score"][:, :]),
       "ld", 16)
    op("sync", lambda e: e.dma_start(out=dv3[:, 0:3], in_=H["l0_deriv"][:, :]),
       "ld", 16)
    wait("scalar", "ld", kb.cnt("ld"))
    wait("vector", "ld", kb.cnt("ld"))
    wait("tensor", "gw", l1)

    a_e = op("scalar", lambda e: e.activation(dv3[:1, 0:3], dv3[:1, 0:3],
                                              AF.Exp, bias=0.0, scale=1.0),
             "act")
    wait("vector", "act", a_e)
    op("vector", lambda e: e.tensor_reduce(dv3[:1, 3:4], dv3[:1, 0:3],
                                           AX.X, ALU.add))
    drain("vector")
    op("vector", lambda e: e.reciprocal(dv3[:1, 3:4], dv3[:1, 3:4]))
    drain("vector")
    op("vector", lambda e: e.tensor_scalar(
        out=dv3[:1, 0:3], in0=dv3[:1, 0:3], scalar1=dv3[:1, 3:4],
        scalar2=None, op0=ALU.mult))
    v_w = op("vector", lambda e: e.memset(ones1[:], 1.0), "dve")
    kb.drain("vector")
    wait("tensor", "dve", v_w)
    mmb = op("tensor", lambda e: e.matmul(pb[1][:, 0:3], ones1[:],
                                          dv3[:1, 0:3], start=True, stop=True),
             "te")
    wait("vector", "te", mmb)
    op("vector", lambda e: e.tensor_copy(dv128[:], pb[1][:, 0:3]))
    kb.drain("vector")
    op("vector", lambda e: e.tensor_copy(smalls[:, 20:21], dv128[:, 0:1]))
    op("vector", lambda e: e.tensor_copy(smalls[:, 21:22], dv128[:, 1:2]))
    op("vector", lambda e: e.tensor_scalar(out=smalls[:, 22:23],
                                           in0=dv128[:, 2:3], scalar1=-1.0,
                                           scalar2=None, op0=ALU.mult))
    drain("vector")

    a_sp1 = op("scalar", lambda e: e.activation(svec[:32, 7:8], svec[:32, 5:6],
                                                AF.Exp, bias=0.0, scale=1.0),
               "act")
    kb.drain("scalar")
    a_sp2 = op("scalar", lambda e: e.activation(svec[:32, 7:8], svec[:32, 7:8],
                                                AF.Ln, bias=1.0, scale=1.0),
               "act")

    op("vector", lambda e: e.memset(srpack[0:32, 0:3], 0.0))
    drain("vector")
    for tt in range(4):
        bank = pb[0]
        mmv = None
        for dc in range(8):
            mov = sb["xn_fm"][:, dc * T + tt * 512:dc * T + tt * 512 + 512]
            sta = sr_w[:, dc * 32:dc * 32 + 32]
            mmv = op("tensor", lambda e, _b=bank, _s=sta, _m=mov, _d=dc:
                     e.matmul(_b[:32, :], _s, _m, start=(_d == 0),
                              stop=(_d == 7)), "te" if dc == 7 else None)
        wait("vector", "te", mmv)
        ev = op("vector", lambda e, _b=bank, _t=tt: e.tensor_copy(
            srpack[0:32, 3 + _t * 512:3 + _t * 512 + 512], _b[0:32, :]), "dve")
        kb.drain("vector")
        wait("tensor", "dve", ev)
    op("vector", lambda e: e.tensor_scalar(
        out=sr_c, in0=srpack[0:32, 0:T], scalar1=svec[:32, 0:1], scalar2=None,
        op0=ALU.mult))
    drain("vector")
    for tap in range(1, 4):
        op("vector", lambda e, _t=tap: e.scalar_tensor_tensor(
            out=sr_c, in0=srpack[0:32, _t:T + _t], scalar=svec[:32, _t:_t + 1],
            in1=sr_c, op0=ALU.mult, op1=ALU.add))
        drain("vector")
    op("vector", lambda e: e.tensor_tensor(svec[:32, 4:5], svec[:32, 4:5],
                                           svec[:32, 6:7], ALU.mult))
    drain("vector")
    op("vector", lambda e: e.tensor_scalar(
        out=sr_c, in0=sr_c, scalar1=svec[:32, 6:7],
        scalar2=svec[:32, 4:5], op0=ALU.mult, op1=ALU.add))
    drain("vector")
    op("vector", lambda e: e.tensor_scalar(
        out=sr_c, in0=sr_c, scalar1=20.0, scalar2=-20.0,
        op0=ALU.min, op1=ALU.max))
    drain("vector")
    wait("vector", "act", a_sp2)
    op("vector", lambda e: e.tensor_scalar(
        out=sr_t, in0=sr_t, scalar1=svec[:32, 7:8],
        scalar2=None, op0=ALU.mult))
    drain("vector")
    v_u = op("vector", lambda e: e.tensor_tensor(sr_c, sr_c, sr_t,
                                                 ALU.add), "dve")
    wait("scalar", "dve", v_u)
    a_pw = op("scalar", lambda e: e.activation(sb["rp32"][0:32, 0:T], sr_c,
                                               AF.Exp, bias=0.0, scale=1.0),
              "act")
    wait("vector", "act", a_pw)
    v_pwb = op("vector", lambda e: e.tensor_copy(sb["rpbf"][0:32, :],
                                                 sb["rp32"][0:32, 0:T]), "dve")
    drain("vector")
    if "dbg_pw" in H:
        wait("sync", "dve", v_pwb)
        op("sync", lambda e: e.dma_start(out=H["dbg_pw"][:, :],
                                         in_=sb["rp32"][0:32, 0:T]), "st", 16)
    return v_pwb



def host_l1_inputs(inputs):
    h = {}
    Wqkv = inputs["L1_attn_Wqkv"]
    bqkv = inputs["L1_attn_bqkv"]
    cols = np.zeros((1024, 32 * 128), np.float32)
    bias = np.zeros(32 * 128, np.float32)

    def put(col, src_row):
        cols[:, col] = Wqkv[src_row]
        bias[col] = bqkv[src_row]

    for hh in range(16):
        for d in range(16):
            put((hh // 4) * 128 + (hh % 4) * 32 + d, hh * 64 + d)
            put(512 + (hh // 4) * 128 + (hh % 4) * 32 + d, 1024 + hh * 64 + d)
        for d in range(16, 64):
            put(1024 + hh * 64 + (d - 16), hh * 64 + d)
            put(2048 + hh * 64 + (d - 16), 1024 + hh * 64 + d)
        for d in range(64):
            put(3072 + (hh // 2) * 128 + (hh % 2) * 64 + d, 2048 + hh * 64 + d)
    h["l1_wqkvT"] = np.ascontiguousarray(
        cols.reshape(8, 128, 32, 128).transpose(2, 1, 0, 3)
        .reshape(32, 128, 1024))
    h["l1_bqkv"] = np.ascontiguousarray(bias.reshape(32, 128).T)

    def oc_tiles(wT):
        return np.ascontiguousarray(
            wT.reshape(8, 128, 8, 128).transpose(2, 1, 0, 3)
            .reshape(8, 128, 1024))

    h["l1_wqaT"] = oc_tiles(inputs["L1_attn_Wqa"].T)
    h["l1_wkaT"] = oc_tiles(inputs["L1_attn_Wka"].T)
    h["l1_wdT"] = oc_tiles(inputs["L1_attn_Wd"].T)
    h["l1_bqa"] = np.ascontiguousarray(inputs["L1_attn_bqa"].reshape(8, 128).T)
    h["l1_bka"] = np.ascontiguousarray(inputs["L1_attn_bka"].reshape(8, 128).T)
    h["l1_bd"] = np.ascontiguousarray(inputs["L1_attn_bd"].reshape(8, 128).T)
    h["l1_gwT"] = np.ascontiguousarray(inputs["L1_ffn_gate_w"].T)
    h["l1_wg"] = np.ascontiguousarray(
        inputs["L1_ffn_wg"].reshape(8, 8, 128, 2, 128)
        .transpose(0, 2, 1, 3, 4).reshape(8, 128, 2048))
    h["l1_wu"] = np.ascontiguousarray(
        inputs["L1_ffn_wu"].reshape(8, 8, 128, 2, 128)
        .transpose(0, 2, 1, 3, 4).reshape(8, 128, 2048))
    h["l1_wd"] = np.ascontiguousarray(
        inputs["L1_ffn_wd"].reshape(8, 2, 128, 8, 128)
        .transpose(3, 2, 0, 1, 4).reshape(8, 128, 2048))
    return h


def host_l1_consts():
    c = {}
    P = np.zeros((128, 128), BF16NP)
    for hh in range(4):
        for d in range(8):
            P[hh * 32 + d + 8, hh * 32 + d] = -1
            P[hh * 32 + d, hh * 32 + d + 8] = 1
    c["c_rot"] = P
    inv_freq = (1.0 / 10000.0 ** (np.arange(0, 16, 2) / 16.0)).astype(np.float64)
    t = np.arange(T, dtype=np.float64)
    ang = t[None, :] * inv_freq[:, None]
    cosf = np.zeros((128, T), np.float32)
    sinf = np.zeros((128, T), np.float32)
    for hh in range(4):
        for d in range(16):
            cosf[hh * 32 + d] = np.cos(ang[d % 8])
            sinf[hh * 32 + d] = np.sin(ang[d % 8])
    c["c_cosf"] = cosf
    c["c_sinf"] = sinf
    m = np.zeros((128, 256), BF16NP)
    for j in range(128):
        for q in range(128):
            if 0 <= q - j <= 15:
                m[j, q] = 1
            if 0 <= (q + 128) - j <= 15:
                m[j, 128 + q] = 1
    c["c_wmask"] = m
    c["c_cmask"] = np.triu(np.ones((128, 128), np.float32)).astype(BF16NP)
    c["c_ones1b"] = np.ones((128, 1), BF16NP)
    ob = np.zeros((128, 12 * 16), BF16NP)
    for j in range(4):
        for r in range(128):
            ob[r, j * 16 + 4 * j + r // 32] = 1
    for p in range(8):
        for r in range(128):
            ob[r, (4 + p) * 16 + 2 * p + r // 64] = 1
    c["c_oblk"] = ob
    return c


def emit_l1(kb, want_dbg=False):
    raise NotImplementedError

def _emit_ln_tile(kb, src_tile, dst_tile):
    op, wait, drain = kb.op, kb.wait, kb.drain
    sb = kb.sb
    sq_done = op("scalar", lambda e: e.activation(
        sb["sqd"][:], src_tile, AF.Square, bias=0.0, scale=1.0,
        accum_out=sb["red"][:, 1:2]), "act")
    op("vector", lambda e: e.tensor_reduce(sb["red"][:, 0:1], src_tile,
                                           AX.X, ALU.add))
    drain("vector")
    op("vector", lambda e: e.tensor_scalar(
        out=sb["red"][:, 2:3], in0=sb["red"][:, 0:1], scalar1=1.0 / D,
        scalar2=None, op0=ALU.mult))
    drain("vector")
    op("vector", lambda e: e.tensor_tensor(
        sb["red"][:, 3:4], sb["red"][:, 2:3], sb["red"][:, 2:3], ALU.mult))
    wait("vector", "act", sq_done)
    drain("vector")
    op("vector", lambda e: e.scalar_tensor_tensor(
        out=sb["red"][:, 4:5], in0=sb["red"][:, 1:2], scalar=1.0 / D,
        in1=sb["red"][:, 3:4], op0=ALU.mult, op1=ALU.subtract))
    drain("vector")
    v_var = op("vector", lambda e: e.tensor_scalar(
        out=sb["red"][:, 4:5], in0=sb["red"][:, 4:5], scalar1=1e-5,
        scalar2=None, op0=ALU.add), "dve")
    wait("scalar", "dve", v_var)
    op("scalar", lambda e: e.activation(sb["red"][:, 5:6], sb["red"][:, 4:5],
                                        AF.Ln, bias=0.0, scale=1.0))
    kb.drain("scalar")
    a_r = op("scalar", lambda e: e.activation(
        sb["red"][:, 6:7], sb["red"][:, 5:6], AF.Exp, bias=0.0, scale=-0.5),
        "act")
    op("vector", lambda e: e.tensor_scalar(
        out=sb["red"][:, 7:8], in0=sb["red"][:, 2:3], scalar1=-1.0,
        scalar2=None, op0=ALU.mult))
    wait("vector", "act", a_r)
    drain("vector")
    return op("vector", lambda e: e.tensor_scalar(
        out=dst_tile, in0=src_tile, scalar1=kb.sb["red"][:, 7:8],
        scalar2=kb.sb["red"][:, 6:7], op0=ALU.add, op1=ALU.mult), "dve")


def emit_l1(kb, want_dbg=False):
    H, op, wait, drain = kb.H, kb.op, kb.wait, kb.drain
    sb = kb.sb
    PS = kb.PSALL
    pb = [PS[:, i * 512:(i + 1) * 512] for i in range(8)]
    smalls = sb["smalls"]
    SCALE = 0.125

    arenaA = sb["xn_fm"]
    arenaB = sb["g0"]

    ones1b = kb.S("ones1b", (128, 1), BF16)
    onesr1 = kb.S("onesr1", (1, 128), BF16)
    ones16 = kb.S("ones16", (16, 128), F32)
    oblk = kb.S("oblk", (128, 12 * 16), BF16)
    bqkv_t = kb.S("bqkv_t", (128, 32), F32)
    bqa_t = kb.S("bqa_t", (128, 8), F32)
    bka_t = kb.S("bka_t", (128, 8), F32)
    bd_t = kb.S("bd_t", (128, 8), F32)
    gw_t = kb.S("gw_t", (128, 8 * 8), BF16)
    qtmp2 = kb.S("qtmp2", (128, 16), F32)
    diag16 = kb.S("diag16", (16, 16), F32)
    identf = kb.S("identf", (128, 128), F32)
    rdc = kb.S("rdc", (128, 2), F32)
    nrm = kb.S("nrm", (16, 32), F32)
    negmb1 = kb.S("negmb1", (128, 16), F32)
    negmb2 = kb.S("negmb2", (128, 16), F32)
    dentm = kb.S("dentm", (128, 256), F32)
    relub = kb.S("relub", (128, 512), BF16)
    ubuf = kb.S("ubuf", (128, 512), BF16)
    g1st = [kb.S(f"g1st{i}", (128, 512), BF16) for i in range(2)]
    xn1st = [kb.S(f"xn1st{i}", (128, D), BF16) for i in range(2)]
    xn1ft = [kb.S(f"xn1ft{i}", (128, D), BF16) for i in range(2)]
    kb.M_MOE = kb.marker()
    masksf = kb.S("masksf", (8, T), BF16)
    maskb = kb.S("maskb", (128, 8 * 512), BF16)
    onehot = kb.S("onehot", (128, NT * 8), BF16)
    xnb = kb.S("xnb", (128, 8 * 512), BF16)
    wgu = [kb.S(f"wgu{i}", (128, 32 * 128), BF16) for i in range(2)]
    inter = kb.S("inter", (128, 16 * 512), BF16)
    wst = [kb.S(f"wst{i}", (128, 16 * 128), BF16) for i in range(2)]
    xt_m = [kb.S(f"xt{i}_l1m", (128, D), F32) for i in range(2)]
    mrow = kb.S("mrow", (1, 512), BF16)
    kb.sb["xt0"], kb.sb["xt1"] = xt_m
    qrest = arenaA
    v_tm = arenaB

    OB = PS[0:64, 1024:3072]
    DENROW = PS[64:65, 1024:3072]

    for nm, tile in (("c_ones1b", ones1b), ("c_oblk", oblk),
                     ("l1_bqkv", bqkv_t), ("l1_bqa", bqa_t),
                     ("l1_bka", bka_t), ("l1_bd", bd_t)):
        op("sync", lambda e, _n=nm, _t=tile: e.dma_start(out=_t[:], in_=H[_n][:, :]),
           "ld", 16)
    op("gpsimd", lambda e: e.dma_start(
        out=gw_t[:], in_=bass.AP(H["l1_gwT"].tensor, 0,
                                 [[8, 128], [1024, 8], [1, 8]])), "gw", 16)
    ld0 = kb.cnt("ld")
    lw0 = kb.cnt("gw")
    wait("vector", "ld", ld0)
    wait("vector", "gw", lw0)
    op("vector", lambda e: e.memset(ones16[:], 1.0))
    op("vector", lambda e: e.memset(onesr1[:], 1.0))
    op("vector", lambda e: e.memset(smalls[:, 5:6], float(np.log(0.125))))
    op("vector", lambda e: e.tensor_copy(identf[:], sb["ident"][:]))
    drain("vector")

    x1v = H["x1"].rearrange("(n p) d -> n p d", p=128)
    tp_guard = [None, None]
    ft_guard = [0, 0]
    st_guard = [None, None]
    oh_done = None
    xtb_m = [sb["xt0"], sb["xt1"]]
    for i in range(NT):
        b = i % 2
        ldv = op("sync", lambda e, _i=i, _x=xtb_m[b]: e.dma_start(
            out=_x[:], in_=x1v[_i]), "ld", 16)
        wait("vector", "ld", ldv)
        wait("scalar", "ld", ldv)
        if i >= 2 and tp_guard[b] is not None:
            wait("vector", "tp", tp_guard[b])
        v_xn = _emit_ln_tile(kb, xtb_m[b][:], xn1st[b][:])
        wait("sync", "dve", v_xn)
        if i >= 2 and ft_guard[b]:
            wait("sync", "te", ft_guard[b])
        if i >= 2 and st_guard[b]:
            wait("sync", "st", st_guard[b])
        sa = xn1st[b][:].rearrange("p (c q) -> p c q", q=128)
        da = xn1ft[b][:].rearrange("p (c q) -> p c q", q=128)
        tpv = op("sync", lambda e, _s=sa, _d=da: e.dma_start_transpose(_d, _s),
                 "tp", 16)
        tp_guard[b] = tpv
        out_ap = bass.AP(H["xn1"].tensor, i * 128,
                         [[2048, 128], [128 * 2048, 8], [1, 128]])
        wait("sync", "tp", tpv)
        st1 = op("sync", lambda e, _o=out_ap, _b=b: e.dma_start(
            out=_o, in_=xn1ft[_b][:].rearrange("p (c q) -> p c q", q=128)),
            "st", 16)
        st_guard[b] = st1
        wait("tensor", "tp", tpv)
        wait("tensor", "gw", lw0)
        mmv = None
        for dc in range(8):
            mmv = op("tensor", lambda e, _b=b, _d=dc: e.matmul(
                pb[7][:, 0:8], xn1ft[_b][:, _d * 128:(_d + 1) * 128],
                gw_t[:, _d * 8:(_d + 1) * 8], start=(_d == 0), stop=(_d == 7)),
                "te" if dc == 7 else None)
        ft_guard[b] = mmv
        wait("vector", "te", mmv)
        wait("vector", "st", st1)
        op("vector", lambda e: e.tensor_reduce(sb["red"][:, 8:9], pb[7][:, 0:8],
                                               AX.X, ALU.max))
        drain("vector")
        op("vector", lambda e: e.tensor_scalar(
            out=qtmp2[:, 0:8], in0=pb[7][:, 0:8], scalar1=sb["red"][:, 8:9],
            scalar2=None, op0=ALU.is_ge))
        drain("vector")
        op("vector", lambda e: e.tensor_tensor_scan(
            qtmp2[:, 8:16], qtmp2[:, 0:8], qtmp2[:, 0:8], 0.0,
            ALU.add, ALU.bypass))
        drain("vector")
        op("vector", lambda e: e.tensor_scalar(
            out=qtmp2[:, 8:16], in0=qtmp2[:, 8:16], scalar1=1.0,
            scalar2=None, op0=ALU.is_le))
        drain("vector")
        oh_done = op("vector", lambda e, _i=i: e.tensor_tensor(
            onehot[:, _i * 8:(_i + 1) * 8], qtmp2[:, 0:8], qtmp2[:, 8:16],
            ALU.mult), "dve")
        kb.drain("vector")
        wait("tensor", "dve", oh_done)
    wait("tensor", "dve", oh_done)
    mtp_out = pb[7][0:8, 128:192].bitcast(BF16)
    for i in range(NT):
        mt = op("tensor", lambda e, _i=i: e.transpose(
            mtp_out, onehot[:, _i * 8:(_i + 1) * 8], sb["ident"][:]), "te")
        wait("vector", "te", mt)
        ev = op("vector", lambda e, _i=i: e.tensor_copy(
            masksf[:, _i * 128:(_i + 1) * 128], mtp_out), "dve")
        kb.drain("vector")
        wait("tensor", "dve", ev)

    wgu_guard = [0, 0]
    wd_guard = [0, 0]
    g1_guard = [None, None]
    for tt in range(4):
        wait("sync", "te", kb.cnt("te"))
        wait("vector", "te", kb.cnt("te"))
        wait("sync", "dve", kb.cnt("dve"))
        for e_ in range(8):
            op("sync", lambda e, _e=e_, _t=tt: e.dma_start(
                out=maskb[:, _e * 512:(_e + 1) * 512],
                in_=bass.AP(masksf, _e * T + _t * 512,
                            [[T, 1], [0, 128], [1, 512]])), "ld", 16)
        lx = op("sync", lambda e, _t=tt: e.dma_start(
            out=xnb[:], in_=bass.AP(H["xn1"].tensor, _t * 512,
                                    [[2048, 128], [128 * 2048, 8], [1, 512]])),
            "ld", 16)
        wait("tensor", "ld", lx)
        wait("vector", "ld", lx)
        epi_hist = []
        for e_ in range(8):
            bwf = e_ % 2
            bb = 4 * (e_ % 2)
            if wgu_guard[bwf]:
                kb.wait("gpsimd", "te", wgu_guard[bwf])
            in_g = bass.AP(H["l1_wg"].tensor, e_ * 262144,
                           [[2048, 128], [1, 2048]])
            in_u = bass.AP(H["l1_wu"].tensor, e_ * 262144,
                           [[2048, 128], [1, 2048]])
            op("gpsimd", lambda e, _i=in_g, _b=bwf: e.dma_start(
                out=wgu[_b][:, 0:2048], in_=_i), "gw", 16)
            lwg = op("gpsimd", lambda e, _i=in_u, _b=bwf: e.dma_start(
                out=wgu[_b][:, 2048:4096], in_=_i), "gw", 16)
            wait("tensor", "gw", lwg)
            if len(epi_hist) >= 2:
                wait("tensor", "act", epi_hist[-2])
            mmg = [None, None]
            for gu in range(2):
                for dc in range(2):
                    bank = pb[bb + gu * 2 + dc]
                    mm2 = None
                    for hc in range(8):
                        sta = wgu[bwf][:, gu * 2048 + (hc * 2 + dc) * 128:
                                       gu * 2048 + (hc * 2 + dc) * 128 + 128]
                        mov = xnb[:, hc * 512:(hc + 1) * 512]
                        mm2 = op("tensor", lambda e, _b=bank, _s=sta, _m=mov,
                                 _h=hc: e.matmul(_b, _s, _m, start=(_h == 0),
                                                 stop=(_h == 7)),
                                 "te" if hc == 7 else None)
                    mmg[gu] = mm2
            wgu_guard[bwf] = mmg[1]
            wait("scalar", "te", mmg[1])
            a_last = None
            for dc in range(2):
                if kb.cnt("dve"):
                    wait("scalar", "dve", kb.cnt("dve"))
                a_rel = op("scalar", lambda e, _d=bb + dc: e.activation(
                    relub[:], pb[_d], AF.Relu, bias=0.0, scale=1.0), "act")
                a_ue = op("scalar", lambda e, _d=bb + 2 + dc: e.activation(
                    ubuf[:], pb[_d], AF.Identity, bias=0.0, scale=1.0),
                    "act")
                a_last = a_ue
                wait("vector", "act", a_ue)
                op("vector", lambda e: e.tensor_tensor(relub[:], relub[:],
                                                       relub[:], ALU.mult))
                drain("vector")
                op("vector", lambda e: e.tensor_tensor(relub[:], relub[:],
                                                       ubuf[:], ALU.mult))
                drain("vector")
                evi = op("vector", lambda e, _e=e_, _d=dc: e.scalar_tensor_tensor(
                    out=inter[:, (_e * 2 + _d) * 512:(_e * 2 + _d + 1) * 512],
                    in0=relub[:], scalar=1.0,
                    in1=maskb[:, _e * 512:(_e + 1) * 512],
                    op0=ALU.mult, op1=ALU.mult), "dve")
            epi_hist.append(a_last)
        wait("tensor", "dve", kb.cnt("dve"))
        wait("tensor", "act", kb.cnt("act"))
        for fo in range(8):
            bwf = fo % 2
            if wd_guard[bwf]:
                kb.wait("gpsimd", "te", wd_guard[bwf])
            in_d = bass.AP(H["l1_wd"].tensor, fo * 262144,
                           [[2048, 128], [1, 2048]])
            lwd = op("gpsimd", lambda e, _i=in_d, _b=bwf: e.dma_start(
                out=wst[_b][:], in_=_i), "gw", 16)
            wait("tensor", "gw", lwd)
            bank = pb[5 + (fo % 2)]
            mmd = None
            for kc in range(16):
                sta = wst[bwf][:, kc * 128:(kc + 1) * 128]
                mov = inter[:, kc * 512:(kc + 1) * 512]
                mmd = op("tensor", lambda e, _b=bank, _s=sta, _m=mov, _k=kc:
                         e.matmul(_b, _s, _m, start=(_k == 0), stop=(_k == 15)),
                         "te" if kc == 15 else None)
            wd_guard[bwf] = mmd
            wait("scalar", "te", mmd)
            bg = fo % 2
            if g1_guard[bg] is not None:
                wait("scalar", "st", g1_guard[bg])
            a_g1 = op("scalar", lambda e, _b=bank, _g=bg: e.activation(
                g1st[_g][:], _b, AF.Silu, bias=0.0, scale=1.0), "act")
            wait("sync", "act", a_g1)
            g1ap = bass.AP(H["g1"].tensor, fo * 128 * 2048 + tt * 512,
                           [[2048, 128], [1, 512]])
            g1_guard[bg] = op("sync", lambda e, _o=g1ap, _g=bg: e.dma_start(
                out=_o, in_=g1st[_g][:]), "st", 16)
            wait("tensor", "act", a_g1)
    kb.barrier()
    kb.free_to(kb.M_MOE)
    kb.S("A_tm", (128, 8 * T), BF16)
    kb.M_K = kb.marker()
    kb.S("krest", (128, 8 * T), BF16)
    kb.S("qsl", (128, 4 * T), BF16)
    kb.S("ksl", (128, 4 * T), BF16)
    emit_l1_attn(kb, want_dbg)
    return


def emit_l1_attn(kb, want_dbg=False):
    H, op, wait, drain = kb.H, kb.op, kb.wait, kb.drain
    sb = kb.sb
    PS = kb.PSALL
    pb = [PS[:, i * 512:(i + 1) * 512] for i in range(8)]
    smalls = sb["smalls"]
    qsl, ksl = sb["qsl"], sb["ksl"]
    krest = sb["krest"]
    qrest = sb["xn_fm"]
    v_tm = sb["g0"]
    A_tm = sb["A_tm"]
    nrm = sb["nrm"]
    negmb1 = sb["negmb1"]
    oblk = sb["oblk"]
    ones1b = sb["ones1b"]
    relub, ubuf = sb["relub"], sb["ubuf"]
    M_R = kb.marker()
    xnb = kb.S("xnb_q", (128, 8 * 512), BF16)
    wst = [kb.S(f"wstq{i}", (128, 8 * 128), BF16) for i in range(2)]
    rot = kb.S("rot", (128, 128), BF16)
    cosl = kb.S("cosl", (128, 512), BF16)
    sinl = kb.S("sinl", (128, 512), BF16)
    op("sync", lambda e: e.dma_start(out=rot[:], in_=H["c_rot"][:, :]),
       "ld", 16)
    wait("tensor", "ld", kb.cnt("ld"))

    wst_guard = [0, 0]
    evac_hist = []
    vtp_hist = []
    for tt in range(4):
        wait("sync", "te", kb.cnt("te"))
        lx = op("sync", lambda e, _t=tt: e.dma_start(
            out=xnb[:], in_=bass.AP(H["xn1"].tensor, _t * 512,
                                    [[2048, 128], [128 * 2048, 8], [1, 512]])),
            "ld", 16)
        wait("tensor", "ld", lx)
        for oc in range(32):
            bws = oc % 2
            if wst_guard[bws]:
                kb.wait("gpsimd", "te", wst_guard[bws])
            in_w = bass.AP(H["l1_wqkvT"].tensor, oc * 131072,
                           [[1024, 128], [1, 1024]])
            lww = op("gpsimd", lambda e, _i=in_w, _b=bws: e.dma_start(
                out=wst[_b][:], in_=_i), "gw", 16)
            wait("tensor", "gw", lww)
            if len(evac_hist) >= 4:
                wait("tensor", "act", evac_hist[-4])
            bank = pb[oc % 4]
            mmv = None
            for dc in range(8):
                mmv = op("tensor", lambda e, _b=bank, _s=wst[bws][:, dc * 128:(dc + 1) * 128],
                         _m=xnb[:, dc * 512:(dc + 1) * 512], _d=dc:
                         e.matmul(_b, _s, _m, start=(_d == 0), stop=(_d == 7)),
                         "te" if dc == 7 else None)
            wst_guard[bws] = mmv
            wait("scalar", "te", mmv)
            if oc < 4:
                dst = qsl[:, oc * T + tt * 512:oc * T + tt * 512 + 512]
            elif oc < 8:
                dst = ksl[:, (oc - 4) * T + tt * 512:(oc - 4) * T + tt * 512 + 512]
            elif oc < 16:
                dst = qrest[:, (oc - 8) * T + tt * 512:(oc - 8) * T + tt * 512 + 512]
            elif oc < 24:
                dst = krest[:, (oc - 16) * T + tt * 512:(oc - 16) * T + tt * 512 + 512]
            else:
                dst = v_tm[:, (oc - 24) * T + tt * 512:(oc - 24) * T + tt * 512 + 512]
            if oc < 24:
                a_e = op("scalar", lambda e, _b=bank, _d=dst, _o=oc: e.activation(
                    _d, _b, AF.Identity, bias=sb["bqkv_t"][:, _o:_o + 1],
                    scale=1.0), "act")
                evac_hist.append(a_e)
            else:
                vstage = sb["relub"] if (oc % 2 == 0) else sb["ubuf"]
                if len(vtp_hist) >= 2:
                    wait("scalar", "tp", vtp_hist[-2])
                a_e = op("scalar", lambda e, _b=bank, _o=oc, _v=vstage:
                         e.activation(_v[:], _b, AF.Identity,
                                      bias=sb["bqkv_t"][:, _o:_o + 1],
                                      scale=1.0), "act")
                evac_hist.append(a_e)
                wait("sync", "act", a_e)
                pair = oc - 24
                sa = vstage[:].rearrange("p (i q) -> p i q", q=128)
                da = bass.AP(v_tm, pair * T + tt * 512,
                             [[8 * T, 128], [128, 4], [1, 128]])
                tpv = op("sync", lambda e, _s=sa, _d=da:
                         e.dma_start_transpose(_d, _s), "tp", 16)
                vtp_hist.append(tpv)
    wait("tensor", "act", kb.cnt("act"))
    wait("vector", "tp", kb.cnt("tp"))
    for tt in range(4):
        wait("gpsimd", "dve", kb.cnt("dve"))
        op("gpsimd", lambda e, _t=tt: e.dma_start(
            out=cosl[:], in_=H["c_cosf"][:, _t * 512:(_t + 1) * 512]),
            "gw", 16)
        lcs = op("gpsimd", lambda e, _t=tt: e.dma_start(
            out=sinl[:], in_=H["c_sinf"][:, _t * 512:(_t + 1) * 512]),
            "gw", 16)
        wait("vector", "gw", lcs)
        for sl in range(8):
            tens = qsl if sl < 4 else ksl
            j = sl % 4
            base = j * T + tt * 512
            mmr = op("tensor", lambda e, _t=tens, _b=base: e.matmul(
                pb[4], rot[:], _t[:, _b:_b + 512], start=True, stop=True),
                "te")
            wait("vector", "te", mmr)
            op("vector", lambda e, _t=tens, _b=base: e.tensor_tensor(
                _t[:, _b:_b + 512], _t[:, _b:_b + 512], cosl[:], ALU.mult))
            op("vector", lambda e: e.tensor_tensor(ubuf[:], pb[4], sinl[:],
                                                   ALU.mult))
            drain("vector")
            ev = op("vector", lambda e, _t=tens, _b=base: e.tensor_tensor(
                _t[:, _b:_b + 512], _t[:, _b:_b + 512], ubuf[:], ALU.add),
                "dve")
            wait("tensor", "dve", ev)
    if DIAG3:
        wait("sync", "dve", kb.cnt("dve"))
        wait("sync", "act", kb.cnt("act"))
        wait("sync", "tp", kb.cnt("tp"))
        for nm, tens, nch in (("dbg_qsl", qsl, 4), ("dbg_ksl", ksl, 4),
                              ("dbg_qr", qrest, 8), ("dbg_kr", krest, 8),
                              ("dbg_v", v_tm, 8)):
            for p in range(nch):
                op("sync", lambda e, _n=nm, _t=tens, _p=p: e.dma_start(
                    out=H[_n][:, _p * T:(_p + 1) * T],
                    in_=_t[:, _p * T:(_p + 1) * T]), "st", 16)
    kb.barrier()
    kb.free_to(M_R)
    ptw = [kb.S(f"ptw{i}", (128, 256), BF16) for i in range(2)]
    wmask = kb.S("wmask", (128, 256), BF16)
    cmask = kb.S("cmask", (128, 128), BF16)
    sqb = kb.S("sqb_w", (128, T), BF16)
    kb.sb["sqb"] = sqb
    for nm, tile in (("c_wmask", wmask), ("c_cmask", cmask)):
        op("sync", lambda e, _n=nm, _t=tile: e.dma_start(
            out=_t[:], in_=H[_n][:, :]), "ld", 16)
    wait("tensor", "ld", kb.cnt("ld"))
    wait("scalar", "ld", kb.cnt("ld"))
    wait("vector", "ld", kb.cnt("ld"))

    def emit_norms(groups, out_col0, negout):
        ng = len(groups)
        for g, (tens, coff, obs) in enumerate(groups):
            a_sq = op("scalar", lambda e, _t=tens, _c=coff: e.activation(
                sqb[:], _t[:, _c:_c + T], AF.Square, bias=0.0, scale=1.0),
                "act")
            wait("tensor", "act", a_sq)
            gmm = None
            for tt in range(4):
                gmm = op("tensor", lambda e, _tt=tt, _o=obs, _g=g, _ng=ng:
                         e.matmul(pb[_tt][0:16, :],
                                  oblk[:, _o * 16:(_o + 1) * 16],
                                  sqb[:, _tt * 512:(_tt + 1) * 512],
                                  start=(_g == 0), stop=(_g == _ng - 1)),
                         "te" if tt == 3 else None)
            wait("scalar", "te", gmm)
        mmn = kb.cnt("te")
        wait("vector", "te", mmn)
        for tt in range(4):
            op("vector", lambda e, _tt=tt, _c=out_col0: e.tensor_reduce(
                nrm[:, _c + _tt:_c + _tt + 1], pb[_tt][0:16, :], AX.X,
                ALU.max))
        drain("vector")
        v_nr = op("vector", lambda e, _c=out_col0: e.tensor_reduce(
            nrm[:, _c + 4:_c + 5], nrm[:, _c:_c + 4], AX.X, ALU.max), "dve")
        wait("tensor", "dve", v_nr)
        return mmn

    qgroups = [(qsl, j * T, j) for j in range(4)] + \
              [(qrest, p * T, 4 + p) for p in range(8)]
    kgroups = [(ksl, j * T, j) for j in range(4)] + \
              [(krest, p * T, 4 + p) for p in range(8)]
    emit_norms(qgroups, 0, None)
    emit_norms(kgroups, 8, None)
    emit_mb_tail(kb, nrm, negmb1)
    emit_l1_wattn(kb)
    return


def emit_mb_broadcast(kb, nrm, negout):
    op, wait, drain = kb.op, kb.wait, kb.drain
    sb = kb.sb
    PS = kb.PSALL
    pb5 = PS[:, 5 * 512:6 * 512]
    v_dg = op("vector", lambda e: e.tensor_scalar(
        out=sb["diag16"][0:16, 0:16], in0=sb["ident"][0:16, 0:16],
        scalar1=nrm[:, 19:20], scalar2=None, op0=ALU.mult),
        "dve")
    kb.drain("vector")
    wait("tensor", "dve", v_dg)
    mmb = op("tensor", lambda e: e.matmul(pb5[:, 0:16], sb["ones16"][:],
                                          sb["diag16"][0:16, 0:16],
                                          start=True, stop=True), "te")
    wait("vector", "te", mmb)
    v_mb = op("vector", lambda e: e.tensor_copy(negout[:], pb5[:, 0:16]),
              "dve")
    kb.drain("vector")
    wait("tensor", "dve", v_mb)
    wait("scalar", "dve", v_mb)


def emit_l1_wattn(kb, want_dbg=False):
    H, op, wait, drain = kb.H, kb.op, kb.wait, kb.drain
    sb = kb.sb
    PS = kb.PSALL
    pb = [PS[:, i * 512:(i + 1) * 512] for i in range(8)]
    qsl, ksl = sb["qsl"], sb["ksl"]
    krest, qrest = sb["krest"], sb["xn_fm"]
    v_tm, A_tm = sb["g0"], sb["A_tm"]
    ptw = [sb["ptw0"], sb["ptw1"]]
    negmb1 = sb["negmb1"]
    rdc = sb["rdc"]
    ones1b = sb["ones1b"]
    wmask = sb["wmask"]

    ap_hist, aA_hist, vrd_hist = [], [], []
    for h in range(16):
        slab_c, srow = h // 4, (h % 4) * 32
        rest_c, rrow = h // 2, (h % 2) * 64
        pair, half = h // 2, h % 2
        for jt in range(16):
            bs = jt % 2
            Sb = pb[bs]
            ncols = 256 if jt < 15 else 128
            if len(ap_hist) >= 2:
                wait("tensor", "act", ap_hist[-2])
            ksliceA = bass.AP(ksl, srow * (4 * T) + slab_c * T + jt * 128,
                              [[4 * T, 32], [1, 128]])
            qslA = bass.AP(qsl, srow * (4 * T) + slab_c * T + jt * 128,
                           [[4 * T, 32], [1, ncols]])
            op("tensor", lambda e, _S=Sb, _k=ksliceA, _q=qslA, _n=ncols,
               _r=srow: e.matmul(_S[:, 0:_n], _k, _q, start=True, stop=False,
                                 tile_position=(_r, 0)))
            kB = bass.AP(krest, rrow * (8 * T) + rest_c * T + jt * 128,
                         [[8 * T, 64], [1, 128]])
            qB = bass.AP(qrest, rrow * (8 * T) + rest_c * T + jt * 128,
                         [[8 * T, 64], [1, ncols]])
            mmS = op("tensor", lambda e, _S=Sb, _k=kB, _q=qB, _n=ncols,
                     _r=rrow: e.matmul(_S[:, 0:_n], _k, _q, start=False,
                                       stop=True, tile_position=(_r, 0)),
                     "te")
            wait("scalar", "te", mmS)
            a_p = op("scalar", lambda e, _S=Sb, _b=bs, _n=ncols, _h=h:
                     e.activation(ptw[_b][:, 0:_n], _S[:, 0:_n], AF.Exp,
                                  bias=negmb1[:, _h:_h + 1], scale=0.125),
                     "act")
            ap_hist.append(a_p)
            wait("vector", "act", a_p)
            v_p = op("vector", lambda e, _b=bs, _n=ncols: e.tensor_tensor(
                ptw[_b][:, 0:_n], ptw[_b][:, 0:_n], wmask[:, 0:_n], ALU.mult),
                "dve")
            wait("tensor", "dve", v_p)
            if len(aA_hist) >= 2:
                wait("tensor", "act", aA_hist[-2])
                wait("tensor", "dve", vrd_hist[-2])
            ob = pb[2 + bs]
            vsl = v_tm[:, pair * T + jt * 128 + half * 64:
                       pair * T + jt * 128 + half * 64 + 64]
            op("tensor", lambda e, _o=ob, _p=ptw[bs][:, 0:128], _v=vsl:
               e.matmul(_o[:, 0:64], _p, _v, start=True, stop=False))
            if jt > 0:
                op("tensor", lambda e, _o=ob, _p=ptw[bs][:, 0:128]:
                   e.matmul(_o[:, 64:65], _p, ones1b[:], start=False,
                            stop=False))
                vsl2 = v_tm[:, pair * T + (jt - 1) * 128 + half * 64:
                            pair * T + (jt - 1) * 128 + half * 64 + 64]
                pprev = ptw[(jt - 1) % 2][:, 128:256]
                op("tensor", lambda e, _o=ob, _p=pprev, _v=vsl2:
                   e.matmul(_o[:, 0:64], _p, _v, start=False, stop=False))
                mmP = op("tensor", lambda e, _o=ob, _p=pprev:
                         e.matmul(_o[:, 64:65], _p, ones1b[:], start=False,
                                  stop=True), "te")
            else:
                mmP = op("tensor", lambda e, _o=ob, _p=ptw[bs][:, 0:128]:
                         e.matmul(
                    _o[:, 64:65], _p, ones1b[:],
                    start=False, stop=True), "te")
            wait("vector", "te", mmP)
            v_rd = op("vector", lambda e, _o=ob: e.reciprocal(
                rdc[:, 0:1], _o[:, 64:65]), "dve")
            vrd_hist.append(v_rd)
            wait("scalar", "dve", v_rd)
            adst = A_tm[:, pair * T + jt * 128 + half * 64:
                        pair * T + jt * 128 + half * 64 + 64]
            a_A = op("scalar", lambda e, _o=ob, _d=adst: e.activation(
                _d, _o[:, 0:64], AF.Identity, bias=0.0, scale=rdc[:, 0:1]),
                "act")
            aA_hist.append(a_A)
            wait("vector", "act", a_A)
    if DIAG3:
        wait("sync", "act", kb.cnt("act"))
        for p in range(8):
            op("sync", lambda e, _p=p: e.dma_start(
                out=H["dbg_A3"][:, _p * T:(_p + 1) * T],
                in_=A_tm[:, _p * T:(_p + 1) * T]), "st", 16)
    kb.barrier()
    kb.free_to(kb.M_K)
    emit_l1_tail(kb, want_dbg)


def emit_l1_tail(kb, want_dbg=False):
    H, op, wait, drain = kb.H, kb.op, kb.wait, kb.drain
    sb = kb.sb
    PS = kb.PSALL
    pb = [PS[:, i * 512:(i + 1) * 512] for i in range(8)]
    smalls = sb["smalls"]
    A_tm = sb["A_tm"]
    A_fm = kb.S("A_fm", (128, 8 * T), BF16)
    den_fm = kb.S("den_fm2", (16, T), F32)
    den_stage = kb.S("den_stage", (1, T), F32)
    kb.sb["den_fm"] = den_fm
    wst = [kb.S(f"wstd{i}", (128, 8 * 128), BF16) for i in range(2)]
    ptcol = kb.S("ptcol", (128, 2048), BF16)
    outsb = kb.S("outsb", (64, 2 * T), BF16)
    cmask = kb.S("cmask2", (128, 128), BF16)
    xt_t = [kb.S(f"xt{i}_l1t", (128, D), F32) for i in range(2)]
    xnf_t = [kb.S(f"xnf{i}_l1t", (128, D), F32) for i in range(2)]
    kb.sb["xt0"], kb.sb["xt1"] = xt_t
    kb.sb["xnf0"], kb.sb["xnf1"] = xnf_t
    kb.sb["sqb"] = ptcol
    op("sync", lambda e: e.dma_start(out=cmask[:], in_=H["c_cmask"][:, :]),
       "ld", 16)
    wait("vector", "ld", kb.cnt("ld"))
    qa = sb["xn_fm"]
    ka = sb["g0"]
    nrm = sb["nrm"]
    oblk = sb["oblk"]
    sqb = ptcol
    negmb2 = sb["negmb2"]
    dentm = sb["dentm"]
    ones1b = sb["ones1b"]

    kb.barrier("ph")
    for p in range(8):
        sa = A_tm[:, p * T:(p + 1) * T].rearrange("p (i q) -> p i q", q=128)
        da = A_fm[:, p * T:(p + 1) * T].rearrange("p (i q) -> p i q", q=128)
        op("sync", lambda e, _s=sa, _d=da: e.dma_start_transpose(_d, _s),
           "tp", 16)
    tpA = kb.cnt("tp")
    wait("tensor", "tp", tpA)

    wst_guard = [0, 0]
    evac_hist = []
    for which, wname, bname, dstt in ((0, "l1_wqaT", "bqa_t", qa),
                                      (1, "l1_wkaT", "bka_t", ka)):
        for oc in range(8):
            bws = oc % 2
            if wst_guard[bws]:
                kb.wait("gpsimd", "te", wst_guard[bws])
            in_w = bass.AP(H[wname].tensor, oc * 131072,
                           [[1024, 128], [1, 1024]])
            lww = op("gpsimd", lambda e, _i=in_w, _b=bws: e.dma_start(
                out=wst[_b][:], in_=_i), "gw", 16)
            wait("tensor", "gw", lww)
            for tt in range(4):
                bank = pb[(oc * 4 + tt) % 4]
                if len(evac_hist) >= 4:
                    wait("tensor", "act", evac_hist[-4])
                mmv = None
                for dc in range(8):
                    mmv = op("tensor", lambda e, _b=bank,
                             _s=wst[bws][:, dc * 128:(dc + 1) * 128],
                             _m=A_fm[:, dc * T + tt * 512:dc * T + tt * 512 + 512],
                             _d=dc: e.matmul(_b, _s, _m, start=(_d == 0),
                                             stop=(_d == 7)),
                             "te" if dc == 7 else None)
                wst_guard[bws] = mmv
                wait("scalar", "te", mmv)
                a_e = op("scalar", lambda e, _b=bank, _o=oc, _t=tt, _d=dstt,
                         _bn=bname: e.activation(
                             _d[:, _o * T + _t * 512:_o * T + _t * 512 + 512],
                             _b, AF.Identity, bias=sb[_bn][:, _o:_o + 1],
                             scale=1.0), "act")
                evac_hist.append(a_e)
    wait("tensor", "act", kb.cnt("act"))

    qgroups = [(qa, p * T, 4 + p) for p in range(8)]
    kgroups = [(ka, p * T, 4 + p) for p in range(8)]

    def norms2(groups, col0):
        for g, (tens, coff, obs) in enumerate(groups):
            a_sq = op("scalar", lambda e, _t=tens, _c=coff: e.activation(
                sqb[:], _t[:, _c:_c + T], AF.Square, bias=0.0, scale=1.0),
                "act")
            wait("tensor", "act", a_sq)
            gmm = None
            for tt in range(4):
                gmm = op("tensor", lambda e, _tt=tt, _o=obs, _g=g:
                         e.matmul(pb[_tt][0:16, :],
                                  oblk[:, _o * 16:(_o + 1) * 16],
                                  sqb[:, _tt * 512:(_tt + 1) * 512],
                                  start=(_g == 0), stop=(_g == 7)),
                         "te" if tt == 3 else None)
            wait("scalar", "te", gmm)
        mmn = kb.cnt("te")
        wait("vector", "te", mmn)
        for tt in range(4):
            op("vector", lambda e, _tt=tt, _c=col0: e.tensor_reduce(
                nrm[:, _c + _tt:_c + _tt + 1], pb[_tt][0:16, :], AX.X,
                ALU.max))
        drain("vector")
        v_nr = op("vector", lambda e, _c=col0: e.tensor_reduce(
            nrm[:, _c + 4:_c + 5], nrm[:, _c:_c + 4], AX.X, ALU.max), "dve")
        kb.drain("vector")
        wait("tensor", "dve", v_nr)

    norms2(qgroups, 0)
    norms2(kgroups, 8)
    emit_mb_tail(kb, nrm, negmb2)

    OB0 = 2048
    out_tm = A_fm
    a65 = [kb.S(f"a65_{i}", (128, 65), BF16) for i in range(2)]
    op("vector", lambda e: e.memset(a65[0][:, 64:65], 1.0))
    op("vector", lambda e: e.memset(a65[1][:, 64:65], 1.0))
    ap_prev = [0, 0, 0, 0]
    for h in range(16):
        pair, half = h // 2, h % 2
        pv_last = None
        for kt in range(16):
            nq = 16 - kt
            npc = (nq * 128 + 511) // 512
            b65 = kt % 2
            if pv_last is not None:
                wait("vector", "te", pv_last)
            asl = A_tm[:, pair * T + kt * 128 + half * 64:
                       pair * T + kt * 128 + half * 64 + 64]
            v_a = op("vector", lambda e, _b=b65, _a=asl: e.tensor_copy(
                a65[_b][:, 0:64], _a), "dve")
            kasl = bass.AP(ka, half * 64 * (8 * T) + pair * T + kt * 128,
                           [[8 * T, 64], [1, 128]])
            mmS_v = []
            for subp in range(npc):
                w = min(512, nq * 128 - subp * 512)
                qasl = bass.AP(qa, half * 64 * (8 * T) + pair * T
                               + kt * 128 + subp * 512, [[8 * T, 64], [1, w]])
                if ap_prev[subp]:
                    wait("tensor", "act", ap_prev[subp])
                mv = op("tensor", lambda e, _o=subp * 512, _k=kasl, _q=qasl,
                        _w=w, _r=half * 64: e.matmul(
                            PS[:, _o:_o + _w], _k, _q, start=True, stop=True,
                            tile_position=(_r, 0)), "te")
                mmS_v.append(mv)
            if pv_last is not None:
                wait("scalar", "te", pv_last)
            ap_now = []
            for subp in range(npc):
                w = min(512, nq * 128 - subp * 512)
                wait("scalar", "te", mmS_v[subp])
                a_p = op("scalar", lambda e, _o=subp * 512, _w=w, _h=h:
                         e.activation(ptcol[:, _o:_o + _w], PS[:, _o:_o + _w],
                                      AF.Exp, bias=negmb2[:, _h:_h + 1],
                                      scale=0.125), "act")
                ap_now.append(a_p)
                ap_prev[subp] = a_p
            wait("vector", "act", ap_now[0])
            v_m = op("vector", lambda e: e.tensor_tensor(
                ptcol[:, 0:128], ptcol[:, 0:128], cmask[:], ALU.mult), "dve")
            wait("tensor", "dve", v_m)
            wait("tensor", "dve", v_a)
            for b in range(kt // 4, 4):
                qt0 = max(kt, 4 * b)
                wdt = (4 * b + 3 - qt0 + 1) * 128
                st_flag, sp_flag = kt == 0, kt == 4 * b + 3
                hi_piece = ((qt0 - kt) * 128 + wdt - 1) // 512
                wait("tensor", "act", ap_now[hi_piece])
                pv_last = op("tensor", lambda e, _q=qt0, _w=wdt, _b=b65,
                             _p=ptcol[:, (qt0 - kt) * 128:
                                      (qt0 - kt) * 128 + wdt],
                             _st=st_flag, _sp=sp_flag:
                             e.matmul(PS[0:65, OB0 + _q * 128:OB0 + _q * 128 + _w],
                                      a65[_b][:, 0:65], _p,
                                      start=_st, stop=_sp), "te")
        wait("vector", "te", kb.cnt("te"))
        if h > 0:
            wait("vector", "st", kb.cnt("st"))
        vd = op("vector", lambda e: e.tensor_copy(
            den_stage[0:1, :],
            PS[64:65, OB0:4096].rearrange("p (i q) -> p i q", q=128)), "dve")
        wait("sync", "dve", vd)
        op("sync", lambda e, _h=h: e.dma_start(
            out=den_fm[_h:_h + 1, :], in_=den_stage[0:1, :]), "st", 16)
        wait("scalar", "dve", vd)
        a_o = op("scalar", lambda e: e.activation(
            outsb[:, 0:T], PS[0:64, OB0:4096],
            AF.Identity, bias=0.0, scale=1.0), "act")
        wait("sync", "act", a_o)
        sa = outsb[:, 0:T].rearrange("p (i q) -> p i q", q=128)
        da = bass.AP(out_tm, h * 64, [[8 * T, 128], [D, 16], [1, 64]])
        tpo = op("sync", lambda e, _s=sa, _d=da: e.dma_start_transpose(_d, _s),
                 "tp", 16)
        wait("scalar", "tp", tpo)
        wait("tensor", "tp", tpo)
    emit_l1_fin(kb, out_tm, want_dbg)
    return



def emit_l1_fin(kb, out_tm, want_dbg=False):
    H, op, wait, drain = kb.H, kb.op, kb.wait, kb.drain
    sb = kb.sb
    PS = kb.PSALL
    pb = [PS[:, i * 512:(i + 1) * 512] for i in range(8)]
    den_fm = sb["den_fm"]
    dentm = sb["dentm"]
    out_fm = sb["A_tm"]
    sg1 = sb["xn_fm"]
    sg_tm = sb["g0"]
    wst = [sb["wstd0"], sb["wstd1"]]
    g1b = [sb["g1st0"], sb["g1st1"]]

    kb.barrier("ph")
    op("vector", lambda e: e.reciprocal(den_fm[:, :], den_fm[:, :]))
    drain("vector")
    v_rd = op("vector", lambda e: e.tensor_copy(den_fm[:, 0:1], den_fm[:, 0:1]),
              "dve")
    wait("tensor", "dve", v_rd)
    for i in range(NT):
        mt = op("tensor", lambda e, _i=i: e.transpose(
            pb[0][:, 0:16], den_fm[:, _i * 128:(_i + 1) * 128],
            sb["identf"][0:16, 0:16]), "te")
        wait("vector", "te", mt)
        ev = op("vector", lambda e, _i=i: e.tensor_copy(
            dentm[:, _i * 16:(_i + 1) * 16], pb[0][:, 0:16]), "dve")
        kb.drain("vector")
        wait("tensor", "dve", ev)
    for i in range(NT):
        v_n = op("vector", lambda e, _i=i: e.tensor_tensor(
            out_tm[:, _i * D:(_i + 1) * D].rearrange("p (h d) -> p h d", d=64),
            out_tm[:, _i * D:(_i + 1) * D].rearrange("p (h d) -> p h d", d=64),
            dentm[:, _i * 16:(_i + 1) * 16].unsqueeze(2)
            .broadcast_to([128, 16, 64]), ALU.mult), "dve")
        kb.drain("vector")
    v_norm = kb.cnt("dve")
    wait("sync", "dve", v_norm)
    if DIAG3:
        for i in range(NT):
            op("sync", lambda e, _i=i: e.dma_start(
                out=H["dbg_out3"][:, _i * D:(_i + 1) * D],
                in_=out_tm[:, _i * D:(_i + 1) * D]), "st", 16)
    for i in range(NT):
        sa = out_tm[:, i * D:(i + 1) * D].rearrange("p (c q) -> p c q", q=128)
        da = bass.AP(out_fm, i * 128, [[8 * T, 128], [T, 8], [1, 128]])
        op("sync", lambda e, _s=sa, _d=da: e.dma_start_transpose(_d, _s),
           "tp", 16)
    tpf = kb.cnt("tp")
    wait("tensor", "tp", tpf)
    wst_guard = [0, 0]
    g1_guard = [None, None]
    for fo in range(8):
        bws = fo % 2
        if wst_guard[bws]:
            kb.wait("gpsimd", "te", wst_guard[bws])
        in_w = bass.AP(H["l1_wdT"].tensor, fo * 131072,
                       [[1024, 128], [1, 1024]])
        lww = op("gpsimd", lambda e, _i=in_w, _b=bws: e.dma_start(
            out=wst[_b][:], in_=_i), "gw", 16)
        wait("tensor", "gw", lww)
        for tt in range(4):
            bank = pb[(fo * 4 + tt) % 4]
            mmv = None
            for dc in range(8):
                mmv = op("tensor", lambda e, _b=bank,
                         _s=wst[bws][:, dc * 128:(dc + 1) * 128],
                         _m=out_fm[:, dc * T + tt * 512:dc * T + tt * 512 + 512],
                         _d=dc: e.matmul(_b, _s, _m, start=(_d == 0),
                                         stop=(_d == 7)),
                         "te" if dc == 7 else None)
            wst_guard[bws] = mmv
            bg = tt % 2
            if g1_guard[bg] is not None:
                wait("sync", "dve", g1_guard[bg])
            lg = op("sync", lambda e, _f=fo, _t=tt, _g=bg: e.dma_start(
                out=g1b[_g][:], in_=bass.AP(H["g1"].tensor,
                                            _f * 128 * 2048 + _t * 512,
                                            [[2048, 128], [1, 512]])),
                "ld", 16)
            wait("scalar", "te", mmv)
            a_b = op("scalar", lambda e, _b=bank, _f=fo: e.activation(
                ubuf_ap(kb), _b, AF.Identity, bias=sb["bd_t"][:, _f:_f + 1],
                scale=1.0), "act")
            wait("vector", "act", a_b)
            wait("vector", "ld", lg)
            ev = op("vector", lambda e, _f=fo, _t=tt, _g=bg: e.tensor_tensor(
                sg1[:, _f * T + _t * 512:_f * T + _t * 512 + 512],
                ubuf_ap(kb), g1b[_g][:], ALU.mult), "dve")
            wait("tensor", "dve", ev)
            wait("scalar", "dve", ev)
            g1_guard[bg] = ev
    sg_done = kb.cnt("dve")
    wait("sync", "dve", sg_done)
    for fo in range(8):
        sa = sg1[:, fo * T:(fo + 1) * T].rearrange("p (i q) -> p i q", q=128)
        da = bass.AP(sg_tm, fo * 128, [[8 * T, 128], [D, 16], [1, 128]])
        op("sync", lambda e, _s=sa, _d=da: e.dma_start_transpose(_d, _s),
           "tp", 16)
    tps = kb.cnt("tp")
    x1v = H["x1"].rearrange("(n p) d -> n p d", p=128)
    outv = H["out"].rearrange("(n p) d -> n p d", p=128)
    res_guard = [None, None]
    xtb_t = [sb["xt0"], sb["xt1"]]
    xnfb_t = [sb["xnf0"], sb["xnf1"]]
    for i in range(NT):
        b = i % 2
        ldv = op("sync", lambda e, _i=i, _x=xtb_t[b]: e.dma_start(
            out=_x[:], in_=x1v[_i]), "ld", 16)
        wait("vector", "ld", ldv)
        wait("vector", "tp", tps)
        if res_guard[b] is not None:
            wait("vector", "st", res_guard[b])
        v1 = op("vector", lambda e, _o=xnfb_t[b], _x=xtb_t[b], _i=i:
                e.tensor_tensor(
            _o[:], _x[:],
            sg_tm[:, _i * D:(_i + 1) * D], ALU.add), "dve")
        wait("sync", "dve", v1)
        res_guard[b] = op("sync", lambda e, _i=i, _o=xnfb_t[b]: e.dma_start(
            out=outv[_i], in_=_o[:]), "st", 16)
    kb.wait("sync", "st", kb.cnt("st"))


def ubuf_ap(kb):
    return kb.sb["ubuf"][:]


def emit_mb_tail(kb, nrm, negout):
    op, wait, drain = kb.op, kb.wait, kb.drain
    smalls = kb.sb["smalls"]
    wait("vector", "dve", kb.cnt("dve"))
    v_gm = op("vector", lambda e: e.tensor_tensor(
        nrm[:, 16:17], nrm[:, 4:5], nrm[:, 12:13], ALU.mult), "dve")
    wait("scalar", "dve", v_gm)
    op("scalar", lambda e: e.activation(nrm[:, 17:18], nrm[:, 16:17], AF.Ln,
                                        bias=0.0, scale=1.0))
    kb.drain("scalar")
    a_mb = op("scalar", lambda e: e.activation(
        nrm[:, 18:19], nrm[:, 17:18], AF.Exp, bias=smalls[0:16, 5:6],
        scale=0.5), "act")
    wait("vector", "act", a_mb)
    op("vector", lambda e: e.tensor_scalar(
        out=nrm[:, 19:20], in0=nrm[:, 18:19], scalar1=-1.0, scalar2=None,
        op0=ALU.mult))
    drain("vector")
    emit_mb_broadcast(kb, nrm, negout)


# version 13
# speedup vs baseline: 1.0380x; 1.0124x over previous
import numpy as np
import ml_dtypes
import concourse.bass as bass
import concourse.mybir as mybir
from concourse.bass_utils import run_bass_kernel_spmd

F32 = mybir.dt.float32
BF16 = mybir.dt.bfloat16
U32 = mybir.dt.uint32
AF = mybir.ActivationFunctionType
ALU = mybir.AluOpType
AX = mybir.AxisListType
BF16NP = ml_dtypes.bfloat16

T = 2048
D = 1024
NT = T // 128
ND = D // 128
MAGIC = float(np.float32(1.5 * 2 ** 23))
LN2 = float(np.log(2.0))

DEBUG = False
DIAG3 = False
STAGE = 3
L0_PARTS = "AdBCDr"

ENGINES = ("sync", "tensor", "vector", "scalar", "gpsimd")


class KB:

    def __init__(self):
        self.nc = bass.Bass("TRN2", target_bir_lowering=False)
        self.H = {}
        self.q = {e: [] for e in ENGINES}
        self.counts = {}
        self.sb = {}
        self._ctx = []
        self.semnames = []

    def din(self, name, shape, dt=F32):
        self.H[name] = self.nc.dram_tensor(name, list(shape), dt,
                                           kind="ExternalInput").ap()
        return self.H[name]

    def dout(self, name, shape, dt=F32):
        self.H[name] = self.nc.dram_tensor(name, list(shape), dt,
                                           kind="ExternalOutput").ap()
        return self.H[name]

    def dtmp(self, name, shape, dt=F32):
        self.H[name] = self.nc.dram_tensor(name, list(shape), dt,
                                           kind="Internal").ap()
        return self.H[name]

    def S(self, name, shape, dtype):
        cm = self.nc.sbuf_tensor(name, list(shape), dtype)
        h = cm.__enter__()
        self._ctx.append(cm)
        self.sb[name] = h
        return h

    def PSUM(self, name, shape=(128, 512), dtype=F32):
        cm = self.nc.psum_tensor(name, list(shape), dtype)
        h = cm.__enter__()
        self._ctx.append(cm)
        return h

    def sem(self, *names):
        self.semnames.extend(names)

    def marker(self):
        return len(self._ctx)

    def free_to(self, mark):
        while len(self._ctx) > mark:
            cm = self._ctx.pop()
            cm.__exit__(None, None, None)

    def barrier(self, name="ph"):
        for s in ("te", "act", "dve", "gp", "tp", "st", "ld", "gw"):
            if self.cnt(s):
                self.wait("sync", s, self.cnt(s))
        v = self.op("sync", lambda e: e.nop(), name)
        for eng in ("tensor", "vector", "scalar", "gpsimd"):
            self.wait(eng, name, v)
        return v

    def op(self, eng, f, inc=None, n=1):
        if inc is None:
            self.q[eng].append(lambda E, _f=f: _f(E[eng]))
            return None
        if n == 16:
            prev = self.counts.get(inc, 0)
            if prev:
                self.wait(eng, inc, prev)
        self.counts[inc] = self.counts.get(inc, 0) + n
        v = self.counts[inc]
        self.q[eng].append(
            lambda E, _f=f, _s=inc, _n=n: _f(E[eng]).then_inc(E["S"][_s], _n))
        return v

    def cnt(self, sem):
        return self.counts.get(sem, 0)

    def wait(self, eng, sem, val):
        assert val <= self.cnt(sem), (sem, val, self.cnt(sem))
        self.q[eng].append(
            lambda E, _s=sem, _v=val: E[eng].wait_ge(E["S"][_s], _v))

    def drain(self, eng):
        self.q[eng].append(lambda E: E[eng].drain())

    def finalize(self):
        nc = self.nc
        semctx = []
        S = {}
        for n in self.semnames:
            cm = nc.semaphore(n)
            S[n] = cm.__enter__()
            semctx.append(cm)
        blk = nc.Block()
        block = blk.__enter__()
        E = {"S": S}

        def runner(engname):
            def _run(eng):
                E[engname] = eng
                for f in self.q[engname]:
                    f(E)
            return _run

        block.sync(runner("sync"))
        block.tensor(runner("tensor"))
        block.vector(runner("vector"))
        block.scalar(runner("scalar"))
        block.gpsimd(runner("gpsimd"))
        blk.__exit__(None, None, None)
        for cm in reversed(semctx):
            cm.__exit__(None, None, None)
        for cm in reversed(self._ctx):
            cm.__exit__(None, None, None)
        return nc



def tfm(tensor, c, t0, n, nchunk=8):
    i0 = t0 // 128
    return bass.AP(tensor, (i0 * nchunk + c) * 128,
                   [[1, 128], [nchunk * 128, n // 128], [1, 128]])



def emit_quant(kb, x_ap, out_ap, fac_ap, w, inc=None):
    nb = w // 32
    q = kb.sb["qtmp"]
    qs = kb.sb["qsc"]
    sm = kb.sb["smalls"]
    A, B, C = q[:, 0:w], q[:, w:2 * w], q[:, 2 * w:3 * w]
    s_amax = qs[:, 0:nb]
    s_pw2 = qs[:, nb:2 * nb]
    s_scale = qs[:, 2 * nb:3 * nb]
    s_rs = qs[:, 3 * nb:4 * nb]
    x3 = x_ap.rearrange("p (nb bs) -> p nb bs", bs=32)

    def o(f, inc_=None):
        r = kb.op("vector", f, inc_)
        kb.drain("vector")
        return r

    o(lambda e: e.tensor_reduce(s_amax, x3, AX.X, ALU.max,
                                apply_absolute_value=True))
    o(lambda e: e.tensor_single_scalar(s_pw2.bitcast(U32), s_amax.bitcast(U32),
                                       sm[:, 0:1].bitcast(U32), ALU.bitwise_and))
    o(lambda e: e.tensor_tensor(s_scale, s_pw2, fac_ap, ALU.mult))
    o(lambda e: e.tensor_scalar(out=s_scale, in0=s_scale, scalar1=0.25,
                                scalar2=1e-30, op0=ALU.mult, op1=ALU.max))
    o(lambda e: e.reciprocal(s_rs, s_scale))
    rs3 = s_rs.unsqueeze(2).broadcast_to([128, nb, 32])
    o(lambda e: e.tensor_tensor(A.rearrange("p (nb bs) -> p nb bs", bs=32),
                                x3, rs3, ALU.mult))
    o(lambda e: e.tensor_scalar(out=A, in0=A, scalar1=6.0, scalar2=-6.0,
                                op0=ALU.min, op1=ALU.max))
    o(lambda e: e.scalar_tensor_tensor(out=B, in0=A, scalar=-1.0, in1=A,
                                       op0=ALU.mult, op1=ALU.max))
    o(lambda e: e.tensor_scalar(out=C, in0=B, scalar1=2.0, scalar2=-1.0,
                                op0=ALU.is_ge, op1=ALU.mult))
    o(lambda e: e.tensor_scalar(out=B, in0=B, scalar1=4.0, scalar2=-0.5,
                                op0=ALU.is_ge, op1=ALU.mult))
    o(lambda e: e.scalar_tensor_tensor(out=C, in0=C, scalar=2.0, in1=B,
                                       op0=ALU.add, op1=ALU.add))
    o(lambda e: e.tensor_tensor(A, A, C, ALU.mult))
    o(lambda e: e.tensor_scalar(out=A, in0=A, scalar1=MAGIC, scalar2=MAGIC,
                                op0=ALU.add, op1=ALU.subtract))
    o(lambda e: e.tensor_scalar(out=B, in0=C, scalar1=1.0, scalar2=0.5,
                                op0=ALU.is_le, op1=ALU.mult))
    o(lambda e: e.tensor_scalar(out=C, in0=C, scalar1=0.5, scalar2=1.0,
                                op0=ALU.is_le, op1=ALU.mult))
    o(lambda e: e.scalar_tensor_tensor(out=C, in0=B, scalar=0.5, in1=C,
                                       op0=ALU.add, op1=ALU.add))
    o(lambda e: e.tensor_tensor(A, A, C, ALU.mult))
    sc3 = s_scale.unsqueeze(2).broadcast_to([128, nb, 32])
    return o(lambda e: e.tensor_tensor(
        out_ap.rearrange("p (nb bs) -> p nb bs", bs=32),
        A.rearrange("p (nb bs) -> p nb bs", bs=32), sc3, ALU.mult), inc)


def emit_xbar_tm2fm(kb, src, dst, n_src_tiles, n_chunks, dst_tile_stride,
                    inc="tp"):
    vals = []
    for i in range(n_src_tiles):
        src_ap = bass.AP(src.tensor if isinstance(src, bass.AP) else src, 0,
                         [[1, 128], [128, n_chunks], [1, 128]])
        src_ap = src[:, i * n_chunks * 128:(i + 1) * n_chunks * 128] \
            .rearrange("p (c q) -> p c q", q=128)
        dst_ap = bass.AP(dst.tensor if hasattr(dst, "tensor") else dst,
                         i * 128, [[1, 128], [dst_tile_stride, n_chunks],
                                   [1, 128]])
        v = kb.op("sync", lambda e, _s=src_ap, _d=dst_ap:
                  e.dma_start_transpose(_d, _s), inc, 16)
        vals.append(v)
    return vals[-1]



def host_consts():
    c = {}
    c["c_identity"] = np.eye(128, dtype=BF16NP)
    pos = np.arange(T, dtype=np.float32)
    dist = np.maximum(T - 1.0 - pos, 0.0)
    tw = np.zeros((32, T), np.float32)
    tw[:28] = -dist[None, :]
    tw[28:] = -pos[None, :]
    c["c_negtime"] = tw
    kex = np.zeros((32, 8, 128), BF16NP)
    for cc in range(8):
        for r in range(128):
            kex[4 * cc + r // 32, cc, r] = 1
    c["c_kexp"] = kex.reshape(32, 8 * 128)
    psm = np.zeros((128, 8, 32), BF16NP)
    for cc in range(8):
        for r in range(128):
            psm[r, cc, 4 * cc + r // 32] = 1
    c["c_psum32"] = psm.reshape(128, 8 * 32)
    return c


def host_l0_inputs(inputs):
    h = {}
    th = inputs["L0_attn_theta"].reshape(32, 32, 4)
    th_r = np.zeros((128, 32), np.float32)
    for r in range(128):
        for cc in range(8):
            for m in range(4):
                th_r[r, cc * 4 + m] = th[4 * cc + r // 32, r % 32, m]
    h["l0_theta_r"] = th_r
    wc = inputs["L0_attn_w_conv"].reshape(4, 2080)
    wcr = np.zeros((128, 64), np.float32)
    for fc in range(16):
        for tap in range(4):
            wcr[:, fc * 4 + tap] = wc[tap, fc * 128:(fc + 1) * 128]
    h["l0_wconv_r"] = wcr
    h["l0_wconv_sr"] = np.ascontiguousarray(wc[:, 2048:2080].T)
    h["l0_bconv_r"] = np.ascontiguousarray(
        inputs["L0_attn_b_conv"][:2048].reshape(16, 128).T)
    h["l0_bconv_sr"] = np.ascontiguousarray(
        inputs["L0_attn_b_conv"][2048:2080].reshape(32, 1))
    h["l0_slopes"] = np.ascontiguousarray(np.concatenate(
        [inputs["L0_attn_decay_slopes"], inputs["L0_attn_anchor_slopes"]]
    ).reshape(32, 1))
    h["l0_score"] = np.ascontiguousarray(
        inputs["L0_attn_score_scale"].reshape(32, 1))
    h["l0_deriv"] = np.ascontiguousarray(
        inputs["L0_attn_deriv_logits"].reshape(1, 3))
    gidx = np.array([[m2 * 32 + h2 for h2 in range(32)] for m2 in range(4)])
    src_idx = np.zeros(128, np.int64)
    for m2 in range(4):
        for h2 in range(32):
            src_idx[m2 * 32 + h2] = h2 * 4 + m2
    ns = inputs["L0_attn_norm_scale"]
    h["l0_norm2"] = np.ascontiguousarray(
        np.stack([ns[src_idx], ns[128 + src_idx]], 1))
    h["l0_wre"] = np.ascontiguousarray(inputs["L0_attn_W_re"][src_idx])
    h["l0_wim"] = np.ascontiguousarray(inputs["L0_attn_W_im"][src_idx])
    h["l0_win"] = np.ascontiguousarray(inputs["L0_attn_W_in"])
    h["l0_win2"] = np.ascontiguousarray(
        inputs["L0_attn_W_in"][:, 0:2048].reshape(8, 128, 16, 128)
        .transpose(2, 1, 0, 3).reshape(16, 128, 1024))
    h["l0_wout2"] = np.ascontiguousarray(
        inputs["L0_attn_W_out"].reshape(8, 128, 8, 128)
        .transpose(2, 1, 0, 3).reshape(8, 128, 1024))
    return h




def build(consts_np, want_dbg=False):
    kb = KB()
    kb.din("x0", (T, D))
    kb.din("L0_ffn_weight", (D, D))
    kb.din("L0_ffn_bias_r", (128, 8))
    kb.din("L0_ffn_delta_w_r", (128, 8 * 32))
    kb.din("c_dact", (128, 1))
    kb.din("l0_theta_r", (128, 32))
    kb.din("l0_wconv_r", (128, 64))
    kb.din("l0_wconv_sr", (32, 4))
    kb.din("l0_bconv_r", (128, 16))
    kb.din("l0_bconv_sr", (32, 1))
    kb.din("l0_slopes", (32, 1))
    kb.din("l0_score", (32, 1))
    kb.din("l0_deriv", (1, 3))
    kb.din("l0_norm2", (128, 2))
    kb.din("l0_wre", (128, 32))
    kb.din("l0_wim", (128, 32))
    kb.din("l0_win", (D, 2080))
    kb.din("l0_win2", (16, 128, 1024))
    kb.din("l0_wout2", (8, 128, 1024))
    for k, v in consts_np.items():
        kb.din(k, v.shape, BF16 if v.dtype == BF16NP else F32)
    kb.dtmp("x1", (T, D))
    if STAGE >= 3:
        kb.din("l1_wqkvT", (32, 128, 1024))
        kb.din("l1_bqkv", (128, 32))
        kb.din("l1_wqaT", (8, 128, 1024))
        kb.din("l1_wkaT", (8, 128, 1024))
        kb.din("l1_wdT", (8, 128, 1024))
        kb.din("l1_bqa", (128, 8))
        kb.din("l1_bka", (128, 8))
        kb.din("l1_bd", (128, 8))
        kb.din("l1_gwT", (1024, 8))
        kb.din("l1_wg", (8, 128, 2048))
        kb.din("l1_wu", (8, 128, 2048))
        kb.din("l1_wd", (8, 128, 2048))
        if want_dbg:
            kb.dout("xn1", (8, 128, T), BF16)
            kb.dout("g1", (8, 128, T), BF16)
            kb.dout("dbg_A", (128, 8 * T))
            kb.dout("dbg_out2", (128, NT * D))
        else:
            kb.dtmp("xn1", (8, 128, T), BF16)
            kb.dtmp("g1", (8, 128, T), BF16)
        if DIAG3:
            kb.dout("dbg_A3", (128, 8 * T), BF16)
            kb.dout("dbg_out3", (128, NT * D), BF16)
            kb.dout("dbg_qsl", (128, 4 * T), BF16)
            kb.dout("dbg_ksl", (128, 4 * T), BF16)
            kb.dout("dbg_qr", (128, 8 * T), BF16)
            kb.dout("dbg_kr", (128, 8 * T), BF16)
            kb.dout("dbg_v", (128, 8 * T), BF16)
    kb.dout("out", (T, D))
    if want_dbg:
        kb.dout("dbg_xn", (D, T))
        kb.dout("dbg_xq", (D, T))
        kb.dout("dbg_g0", (D, T))
        kb.dout("dbg_s0", (D, T))
        kb.dout("dbg_pw", (32, T))
        kb.dout("dbg_xv", (D, T))
        kb.dout("dbg_gate", (D, T))
        kb.dout("dbg_yu", (D, T))
        kb.dout("dbg_sk", (32, T))
        kb.dout("dbg_msq", (32, T))
        kb.dout("dbg_den", (32, T))

    H = kb.H
    op, wait, drain = kb.op, kb.wait, kb.drain
    kb.sem("ld", "gw", "st", "tp", "te", "dve", "act", "gp", "ev", "ph")

    ident = kb.S("ident", (128, 128), BF16)
    xn_fm = kb.S("xn_fm", (128, ND * T), BF16)
    g0 = kb.S("g0", (128, ND * T), BF16)
    smalls = kb.S("smalls", (128, 32), F32)
    red = kb.S("red", (128, 16), F32)
    sqd = kb.S("sqd", (128, D), F32)
    PS = kb.PSUM("PS", (128, 4096))
    pb = [PS[:, i * 512:(i + 1) * 512] for i in range(8)]
    kb.PSALL = PS
    MARK = kb.marker()
    xt = [kb.S(f"xt{i}", (128, D), F32) for i in range(2)]
    xnf = [kb.S(f"xnf{i}", (128, D), F32) for i in range(2)]
    xn_st = [kb.S(f"xn_st{i}", (128, D), BF16) for i in range(2)]
    xq_st = [kb.S(f"xq_st{i}", (128, D), BF16) for i in range(2)]
    wq_st = [kb.S(f"wq_st{i}", (128, D), BF16) for i in range(2)]
    xq_fm = kb.S("xq_fm", (128, ND * T), BF16)
    wqT = kb.S("wqT", (128, 8 * D), BF16)
    bias_c = kb.S("bias_c", (128, 8), F32)
    wkf = [kb.S(f"wkf{i}", (128, D), F32) for i in range(2)]
    dwf = kb.S("dwf", (128, 8 * 32), F32)
    facw = kb.S("facw", (128, 8 * 32), F32)
    qtmp = kb.S("qtmp", (128, 3 * D), F32)
    qsc = kb.S("qsc", (128, 4 * 32), F32)

    op("sync", lambda e: e.dma_start(out=ident[:], in_=H["c_identity"][:, :]),
       "ld", 16)

    op("sync", lambda e: e.dma_start(out=bias_c[:], in_=H["L0_ffn_bias_r"][:, :]),
       "ld", 16)
    op("sync", lambda e: e.dma_start(out=dwf[:], in_=H["L0_ffn_delta_w_r"][:, :]),
       "ld", 16)
    ld_da = op("sync", lambda e: e.dma_start(out=smalls[:, 8:9], in_=H["c_dact"][:, :]),
               "ld", 16)
    ld_init = kb.cnt("ld")

    wait("vector", "ld", ld_init)
    wait("scalar", "ld", ld_init)
    op("vector", lambda e: e.memset(smalls[:, 0:1].bitcast(U32), 0x7F800000))
    drain("vector")

    def emit_tanh_exp_fac(src, dst, width, tmp0, tmp1):
        a1 = op("scalar", lambda e: e.activation(tmp0, src, AF.Exp,
                                                 bias=0.0, scale=8.0), "act")
        wait("vector", "act", a1)
        op("vector", lambda e: e.tensor_scalar(out=tmp1, in0=tmp0, scalar1=1.0,
                                               scalar2=None, op0=ALU.add))
        drain("vector")
        op("vector", lambda e: e.reciprocal(tmp1, tmp1))
        drain("vector")
        op("vector", lambda e: e.tensor_scalar(out=tmp0, in0=tmp0, scalar1=-1.0,
                                               scalar2=None, op0=ALU.add))
        drain("vector")
        v = op("vector", lambda e: e.tensor_tensor(tmp0, tmp0, tmp1, ALU.mult),
               "dve")
        wait("scalar", "dve", v)
        return op("scalar", lambda e: e.activation(dst, tmp0, AF.Exp, bias=0.0,
                                                   scale=0.5 * LN2), "act")

    a_fac = emit_tanh_exp_fac(smalls[:, 8:9], smalls[:, 14:15], 1,
                              smalls[:, 9:10], smalls[:, 10:11])
    a_facw = emit_tanh_exp_fac(dwf[:], facw[:], 256,
                               qtmp[:, 0:256], qtmp[:, 256:512])

    xv = H["x0"].rearrange("(n p) d -> n p d", p=128)
    wait("vector", "act", a_fac)
    wait("vector", "act", a_facw)
    fac32 = smalls[:, 14:15].broadcast_to([128, 32])
    last_q = None
    tp_guard = [None, None]
    ldv_t = {0: op("sync", lambda e: e.dma_start(out=xt[0][:], in_=xv[0]),
                   "ld", 16),
             1: op("sync", lambda e: e.dma_start(out=xt[1][:], in_=xv[1]),
                   "ld", 16)}
    vxn_hist = []
    for i in range(NT):
        b = i % 2
        if i + 2 < NT:
            pass
        ldv = ldv_t[i]
        wait("vector", "ld", ldv)
        wait("scalar", "ld", ldv)
        sq_done = op("scalar", lambda e, _b=b: e.activation(
            sqd[:], xt[_b][:], AF.Square, bias=0.0, scale=1.0,
            accum_out=red[:, 1:2]), "act")
        op("vector", lambda e, _b=b: e.tensor_reduce(red[:, 0:1], xt[_b][:],
                                                     AX.X, ALU.add))
        drain("vector")
        op("vector", lambda e: e.tensor_scalar(out=red[:, 2:3], in0=red[:, 0:1],
                                               scalar1=1.0 / D, scalar2=None,
                                               op0=ALU.mult))
        drain("vector")
        op("vector", lambda e: e.tensor_tensor(red[:, 3:4], red[:, 2:3],
                                               red[:, 2:3], ALU.mult))
        wait("vector", "act", sq_done)
        drain("vector")
        op("vector", lambda e: e.scalar_tensor_tensor(
            out=red[:, 4:5], in0=red[:, 1:2], scalar=1.0 / D, in1=red[:, 3:4],
            op0=ALU.mult, op1=ALU.subtract))
        drain("vector")
        v_var = op("vector", lambda e: e.tensor_scalar(
            out=red[:, 4:5], in0=red[:, 4:5], scalar1=1e-5, scalar2=None,
            op0=ALU.add), "dve")
        wait("scalar", "dve", v_var)
        op("scalar", lambda e: e.activation(red[:, 5:6], red[:, 4:5], AF.Ln,
                                            bias=0.0, scale=1.0))
        kb.drain("scalar")
        a_r = op("scalar", lambda e: e.activation(red[:, 6:7], red[:, 5:6],
                                                  AF.Exp, bias=0.0, scale=-0.5),
                 "act")
        op("vector", lambda e: e.tensor_scalar(out=red[:, 7:8], in0=red[:, 2:3],
                                               scalar1=-1.0, scalar2=None,
                                               op0=ALU.mult))
        wait("vector", "act", a_r)
        drain("vector")
        op("vector", lambda e, _b=b: e.tensor_scalar(
            out=xnf[_b][:], in0=xt[_b][:], scalar1=red[:, 7:8],
            scalar2=red[:, 6:7], op0=ALU.add, op1=ALU.mult))
        drain("vector")
        if i >= 2 and tp_guard[b] is not None:
            wait("vector", "tp", tp_guard[b])
        v_xn = op("vector", lambda e, _b=b: e.tensor_copy(
            xn_st[_b][:], xnf[_b][:]), "dve")
        drain("vector")
        vxn_hist.append(v_xn)
        if i + 2 < NT:
            wait("sync", "dve", v_xn)
            ldv_t[i + 2] = op("sync", lambda e, _j=i + 2, _b=b: e.dma_start(
                out=xt[_b][:], in_=xv[_j]), "ld", 16)
        last_q = emit_quant(kb, xnf[b][:], xq_st[b][:], fac32, D, inc="dve")
        wait("sync", "dve", v_xn)
        da_n = bass.AP(xn_fm, i * 128, [[NT * D, 128], [T, ND], [1, 128]])
        sa_n = xn_st[b][:].rearrange("p (c q) -> p c q", q=128)
        op("sync", lambda e, _s=sa_n, _d=da_n: e.dma_start_transpose(_d, _s),
           "tp", 16)
        wait("sync", "dve", last_q)
        da_q = bass.AP(xq_fm, i * 128, [[NT * D, 128], [T, ND], [1, 128]])
        sa_q = xq_st[b][:].rearrange("p (c q) -> p c q", q=128)
        tp_guard[b] = op("sync", lambda e, _s=sa_q, _d=da_q:
                         e.dma_start_transpose(_d, _s), "tp", 16)
    xq_done = last_q

    wv = H["L0_ffn_weight"].rearrange("(c p) d -> c p d", p=128)
    wait("vector", "act", a_facw)
    wtp_guard = [None, None]
    lw_t = {0: op("sync", lambda e: e.dma_start(out=wkf[0][:], in_=wv[0]),
                  "ld", 16),
            1: op("sync", lambda e: e.dma_start(out=wkf[1][:], in_=wv[1]),
                  "ld", 16)}
    for c in range(8):
        b = c % 2
        wait("vector", "ld", lw_t[c])
        if c >= 2 and wtp_guard[b] is not None:
            wait("vector", "tp", wtp_guard[b])
        last_q = emit_quant(kb, wkf[b][:], wq_st[b][:],
                            facw[:, c * 32:(c + 1) * 32], D, inc="dve")
        wait("sync", "dve", last_q)
        if c + 2 < 8:
            lw_t[c + 2] = op("sync", lambda e, _j=c + 2, _b=b: e.dma_start(
                out=wkf[_b][:], in_=wv[_j]), "ld", 16)
        da_w = bass.AP(wqT, c * 128, [[8 * D, 128], [D, 8], [1, 128]])
        sa_w = wq_st[b][:].rearrange("p (c q) -> p c q", q=128)
        wtp_guard[b] = op("sync", lambda e, _s=sa_w, _d=da_w:
                          e.dma_start_transpose(_d, _s), "tp", 16)
    wq_done = last_q

    tp_done = kb.cnt("tp")

    wait("tensor", "tp", tp_done)
    bank_guard = {}
    g0_acts = []
    for tt in range(4):
        for oc in range(8):
            bi = (tt * 8 + oc) % 4
            bank = pb[bi]
            if bi in bank_guard:
                wait("tensor", "act", bank_guard[bi])
            mmv = None
            for ic in range(8):
                mov = kb.sb["xq_fm"][:, ic * T + tt * 512:ic * T + tt * 512 + 512]
                sta = kb.sb["wqT"][:, ic * D + oc * 128:ic * D + oc * 128 + 128]
                mmv = op("tensor",
                         lambda e, _b=bank, _s=sta, _m=mov, _i=ic: e.matmul(
                             _b[:], _s, _m,
                             start=(_i == 0), stop=(_i == 7)),
                         "te" if ic == 7 else None)
            wait("scalar", "te", mmv)
            a = op("scalar", lambda e, _b=bank, _o=oc, _t=tt: e.activation(
                kb.sb["g0"][:, _o * T + _t * 512:_o * T + _t * 512 + 512],
                _b[:], AF.Silu, bias=kb.sb["bias_c"][:, _o:_o + 1], scale=1.0),
                "act")
            bank_guard[bi] = a
            g0_acts.append(a)

    if want_dbg:
        wait("sync", "tp", tp_done)
        for c in range(ND):
            kb.wait("gpsimd", "tp", tp_done)
            op("gpsimd", lambda e, _c=c: e.dma_start(
                out=H["dbg_xn"].rearrange("(c p) t -> c p t", p=128)[_c],
                in_=xn_fm[:, _c * T:(_c + 1) * T]), "st", 16)
            op("gpsimd", lambda e, _c=c: e.dma_start(
                out=H["dbg_xq"].rearrange("(c p) t -> c p t", p=128)[_c],
                in_=xq_fm[:, _c * T:(_c + 1) * T]), "st", 16)
        kb.wait("gpsimd", "act", g0_acts[-1])
        for c in range(ND):
            op("gpsimd", lambda e, _c=c: e.dma_start(
                out=H["dbg_g0"].rearrange("(c p) t -> c p t", p=128)[_c],
                in_=g0[:, _c * T:(_c + 1) * T]), "st", 16)

    if STAGE >= 2:
        kb.barrier()
        kb.free_to(MARK)
        M1b = kb.marker()
        kb.S("rp32", (32, 2 * T), F32)
        kb.S("rpbf", (32, T), BF16)
        kb.S("kexp", (32, 8 * 128), BF16)
        kb.S("psum32", (128, 8 * 32), BF16)
        xt_l0 = [kb.S(f"xt{i}_l0", (128, D), F32) for i in range(2)]
        xnf_l0 = [kb.S(f"xnf{i}_l0", (128, D), F32) for i in range(2)]
        kb.sb["xt0"], kb.sb["xt1"] = xt_l0
        kb.sb["xnf0"], kb.sb["xnf1"] = xnf_l0
        op("sync", lambda e: e.dma_start(out=kb.sb["rp32"][0:32, T:2 * T],
                                         in_=H["c_negtime"][:, :]), "ld", 16)
        op("sync", lambda e: e.dma_start(out=kb.sb["kexp"][:, :],
                                         in_=H["c_kexp"][:, :]), "ld", 16)
        op("sync", lambda e: e.dma_start(out=kb.sb["psum32"][:, :],
                                         in_=H["c_psum32"][:, :]), "ld", 16)
        M2 = kb.marker()
        emit_l0_sraw(kb, pb)
        kb.barrier()
        kb.free_to(M2)
        emit_l0_attn(kb, pb, want_dbg=want_dbg)
        kb.wait("sync", "st", kb.cnt("st"))
        if STAGE == 2:
            op("sync", lambda e: e.dma_start(out=H["out"][:, :],
                                             in_=H["x1"][:, :]), "st", 16)
            kb.wait("sync", "st", kb.cnt("st"))
        if STAGE >= 3:
            kb.barrier()
            kb.free_to(M1b)
            emit_l1(kb, want_dbg=want_dbg)
    else:
        wait("sync", "act", g0_acts[-1])
        for i in range(NT):
            op("sync", lambda e, _i=i: e.dma_start(
                out=H["out"].rearrange("(n p) d -> n p d", p=128)[_i],
                in_=xt[0][:]), "st", 16)
        wait("sync", "st", kb.cnt("st"))

    return kb



def prep_core_inputs(inputs, consts):
    base = {}
    base.update({k: np.ascontiguousarray(v) for k, v in consts.items()})
    base["L0_ffn_weight"] = np.ascontiguousarray(inputs["L0_ffn_weight"])
    base["L0_ffn_bias_r"] = np.ascontiguousarray(
        inputs["L0_ffn_bias"].reshape(8, 128).T)
    base["L0_ffn_delta_w_r"] = np.ascontiguousarray(
        inputs["L0_ffn_delta_w"].reshape(8, 128, 32).transpose(1, 0, 2)
        .reshape(128, 8 * 32))
    base["c_dact"] = np.ascontiguousarray(
        np.broadcast_to(inputs["L0_ffn_delta_act"].reshape(1, 1), (128, 1)))
    base.update(host_l0_inputs(inputs))
    return base


def kernel(**inputs):
    consts = host_consts()
    if STAGE >= 3:
        consts.update(host_l1_consts())
    base = prep_core_inputs(inputs, consts)
    if STAGE >= 3:
        base.update(host_l1_inputs(inputs))
    x = np.ascontiguousarray(inputs["L0_attn_x"])
    in_maps = []
    for c in range(8):
        m = dict(base)
        m["x0"] = np.ascontiguousarray(x[c])
        in_maps.append(m)
    kb = build(consts, want_dbg=DEBUG)
    nc = kb.finalize()
    res = run_bass_kernel_spmd(nc, in_maps, core_ids=list(range(8)))
    out = np.stack([res.results[c]["out"] for c in range(8)], 0)
    if DEBUG:
        kernel.dbg = res.results
    return out



def emit_l0_attn(kb, pb, want_dbg=False):
    H, op, wait, drain = kb.H, kb.op, kb.wait, kb.drain
    sb = kb.sb
    TC = 512
    NTC = T // TC

    xn_fm, g0 = sb["xn_fm"], sb["g0"]
    smalls = sb["smalls"]
    rp32 = sb["rp32"]
    pw_all = rp32[0:32, 0:T]
    pw_bf = sb["rpbf"][0:32, :]

    thdiv = kb.S("thdiv", (128, 32), F32)
    wconv = kb.S("wconv", (128, 64), F32)
    bconv = kb.S("bconv", (128, 16), F32)
    norm2 = kb.S("norm2", (128, 2), F32)
    wre = kb.S("wre", (128, 32), F32)
    wim = kb.S("wim", (128, 32), F32)
    wy = kb.S("wy", (128, 8 * 128), BF16)
    diag4 = [kb.S(f"diag4_{i}", (128, 4 * 128), BF16) for i in range(2)]
    zr = [kb.S(f"zr{i}", (128, TC + 3), BF16) for i in range(2)]
    ztail = kb.S("ztail", (128, 16 * 3), BF16)
    xv = kb.S("xv", (128, 8 * TC), F32)
    gate = kb.S("gate", (128, 8 * TC), BF16)
    y_u = kb.S("y_u", (128, 8 * TC), BF16)
    sg_tm = kb.S("sg_tm", (128, 4 * D), BF16)
    u_t = [kb.S(f"u_t{p}", (128, TC), F32) for p in range(2)]
    k_r = [kb.S(f"k_r{p}", (128, TC), F32) for p in range(2)]
    frac = [kb.S(f"frac{p}", (128, TC), F32) for p in range(4)]
    afr = [kb.S(f"afr{p}", (128, TC), F32) for p in range(2)]
    sinb = [kb.S(f"sinb{p}", (128, TC), BF16) for p in range(4)]
    cosb = [kb.S(f"cosb{p}", (128, TC), BF16) for p in range(4)]
    pwx = kb.S("pwx", (128, TC), BF16)
    pol1 = kb.S("pol1", (128, TC), F32)
    pol2 = pol1
    pp = kb.S("pp", (128, TC), BF16)
    rein = [kb.S(f"rein{p}", (128, TC), BF16) for p in range(2)]
    imin = [kb.S(f"imin{p}", (128, TC), BF16) for p in range(2)]
    csre = [kb.S(f"csre{p}", (128, TC), BF16) for p in range(2)]
    csim = [kb.S(f"csim{p}", (128, TC), BF16) for p in range(2)]
    sqt = [rein[0], imin[0], rein[1], imin[1]]
    carr = kb.S("carr", (128, 66), F32)
    l32 = kb.S("l32", (32, (5 if want_dbg else 4) * TC), F32)
    skbf = kb.S("skbf", (32, TC), BF16)
    wfc = [kb.S(f"wfc{i}", (128, 8 * 128), BF16) for i in range(2)]
    sgadd = [sb["xnf0"], sb["xnf1"]]
    sg = gate

    msq = l32[0:32, 0:TC]
    dent = l32[0:32, TC:2 * TC]
    skt = l32[0:32, 2 * TC:3 * TC]
    rtmp = l32[0:32, 3 * TC:4 * TC]
    rtmp2 = l32[0:32, 4 * TC:5 * TC] if want_dbg else None

    for (nm, tile) in (("l0_theta_r", thdiv), ("l0_wconv_r", wconv),
                       ("l0_bconv_r", bconv), ("l0_norm2", norm2),
                       ("l0_wre", wre), ("l0_wim", wim)):
        op("sync", lambda e, _n=nm, _t=tile: e.dma_start(out=_t[:], in_=H[_n][:, :]),
           "ld", 16)
    ld_l0 = kb.cnt("ld")
    wait("vector", "ld", ld_l0)
    op("vector", lambda e: e.tensor_scalar(out=thdiv[:], in0=thdiv[:],
                                           scalar1=float(1.0 / (2 * np.pi)),
                                           scalar2=None, op0=ALU.mult))
    op("vector", lambda e: e.memset(smalls[:, 1:2].bitcast(U32), 1))
    op("vector", lambda e: e.memset(smalls[:, 4:5], float(np.pi / 2)))
    op("vector", lambda e: e.memset(carr[:], 0.0))
    op("vector", lambda e: e.memset(ztail[:], 0.0))
    op("vector", lambda e: e.memset(wy[:], 0.0))
    drain("vector")
    op("vector", lambda e: e.tensor_scalar(out=wre[:], in0=wre[:],
                                           scalar1=norm2[:, 0:1], scalar2=None,
                                           op0=ALU.mult))
    op("vector", lambda e: e.tensor_scalar(out=wim[:], in0=wim[:],
                                           scalar1=norm2[:, 1:2], scalar2=None,
                                           op0=ALU.mult))
    drain("vector")
    WYP = 8 * 128
    for ri, wsrc in ((0, wre), (1, wim)):
        for m in range(4):
            j = ri * 4 + m
            for kk in range(4):
                src_ap = wsrc[m * 32:(m + 1) * 32, :]
                dst_ap = bass.AP(wy, kk * 32 * WYP + j * 128 + kk * 32,
                                 [[WYP, 32], [1, 32]])
                op("vector", lambda e, _s=src_ap, _d=dst_ap: e.tensor_copy(_d, _s))
    drain("vector")
    v_init = op("vector", lambda e: e.memset(smalls[:, 2:3], 0.0), "dve")

    wfc_guard = [0, 0]
    sgadd_guard = [0, 0]
    prev_cos_c = [None]
    for tc in range(NTC):
        t0 = tc * TC
        first = tc == 0
        def build_diag(fcx):
            for tap in range(4):
                g = op("vector", lambda e, _f=fcx, _t=tap, _d=diag4[fcx % 2]:
                       e.tensor_scalar(
                    out=_d[:, _t * 128:(_t + 1) * 128], in0=sb["ident"][:],
                    scalar1=wconv[:, (_f * 4 + _t):(_f * 4 + _t) + 1],
                    scalar2=None, op0=ALU.mult), "dve" if tap == 3 else None)
            return g

        wait("vector", "te", kb.cnt("te"))
        diag_v = {0: build_diag(0)}
        act_hist = []
        for fc in range(16):
            bz = fc % 2
            in_ap = bass.AP(H["l0_win2"].tensor, fc * 131072,
                            [[1024, 128], [1, 1024]])
            if wfc_guard[bz]:
                kb.wait("gpsimd", "te", wfc_guard[bz])
            lw = op("gpsimd", lambda e, _i=in_ap, _b=bz: e.dma_start(
                out=wfc[_b][:], in_=_i), "gw", 16)
            wait("tensor", "gw", lw)
            bank = pb[2 + bz]
            mmv = None
            for dc in range(8):
                mov = xn_fm[:, dc * T + t0:dc * T + t0 + TC]
                sta = wfc[bz][:, dc * 128:(dc + 1) * 128]
                mmv = op("tensor", lambda e, _b=bank, _s=sta, _m=mov, _d=dc:
                         e.matmul(_b[:], _s, _m, start=(_d == 0), stop=(_d == 7)),
                         "te" if dc == 7 else None)
            wfc_guard[bz] = mmv
            wait("vector", "te", mmv)
            op("vector", lambda e, _b=bz, _f=fc: e.tensor_copy(
                zr[_b][:, 0:3], ztail[:, _f * 3:_f * 3 + 3]))
            ev = op("vector", lambda e, _b=bank, _z=bz: e.tensor_copy(
                zr[_z][:, 3:3 + TC], _b[:]), "dve")
            kb.drain("vector")
            op("vector", lambda e, _b=bz, _f=fc: e.tensor_copy(
                ztail[:, _f * 3:_f * 3 + 3], zr[_b][:, TC:TC + 3]))
            if fc + 1 < 16:
                diag_v[fc + 1] = build_diag(fc + 1)
            wait("tensor", "dve", ev)
            wait("tensor", "dve", diag_v[fc])
            if len(act_hist) >= 2:
                wait("tensor", "act", act_hist[-2])
            bank2 = pb[4 + bz]
            mmv2 = None
            for tap in range(4):
                mov = zr[bz][:, tap:tap + TC]
                sta = diag4[fc % 2][:, tap * 128:(tap + 1) * 128]
                mmv2 = op("tensor", lambda e, _b=bank2, _s=sta, _m=mov, _t=tap:
                          e.matmul(_b[:], _s, _m, start=(_t == 0), stop=(_t == 3)),
                          "te" if tap == 3 else None)
            wait("scalar", "te", mmv2)
            if fc < 8:
                a = op("scalar", lambda e, _b=bank2, _f=fc: e.activation(
                    xv[:, _f * TC:(_f + 1) * TC], _b[:], AF.Identity,
                    bias=bconv[:, _f:_f + 1], scale=1.0), "act")
            else:
                a = op("scalar", lambda e, _b=bank2, _f=fc: e.activation(
                    gate[:, (_f - 8) * TC:(_f - 7) * TC], _b[:], AF.Silu,
                    bias=bconv[:, _f:_f + 1], scale=1.0), "act")
            act_hist.append(a)
        wait("tensor", "act", kb.cnt("act"))
        a_convdone = kb.cnt("act")

        if want_dbg and "dbg_xv" in H:
            wait("sync", "act", a_convdone)
            kb.wait("gpsimd", "act", a_convdone)
            for fcc in range(8):
                op("sync", lambda e, _f=fcc, _t0=t0: e.dma_start(
                    out=H["dbg_xv"].rearrange("(c p) t -> c p t", p=128)
                    [_f][:, _t0:_t0 + TC],
                    in_=xv[:, _f * TC:(_f + 1) * TC]), "st", 16)
                op("gpsimd", lambda e, _f=fcc, _t0=t0: e.dma_start(
                    out=H["dbg_gate"].rearrange("(c p) t -> c p t", p=128)
                    [_f][:, _t0:_t0 + TC],
                    in_=gate[:, _f * TC:(_f + 1) * TC]), "st", 16)
            kb.wait("gpsimd", "st", kb.cnt("st"))
            kb.wait("vector", "st", kb.cnt("st"))
            kb.wait("scalar", "st", kb.cnt("st"))
        if "d" not in L0_PARTS:
            continue
        op("vector", lambda e, _t0=t0: e.tensor_tensor_scan(
            dent, pw_all[:, _t0:_t0 + TC], pw_all[:, _t0:_t0 + TC],
            carr[0:32, 64:65], ALU.add, ALU.bypass))
        drain("vector")
        op("vector", lambda e: e.tensor_copy(carr[0:32, 64:65],
                                             dent[:, TC - 1:TC]))
        kb.drain("vector")
        op("vector", lambda e: e.tensor_single_scalar(rtmp, dent, 1e-4,
                                                      ALU.max))
        drain("vector")
        op("vector", lambda e: e.reciprocal(rtmp, rtmp))
        drain("vector")

        if "B" not in L0_PARTS:
            continue
        wait("tensor", "act", a_convdone)
        msq_started = False
        for c in range(8):
            mm = op("tensor", lambda e, _c=c, _t0=t0: e.matmul(
                pb[1][:], sb["kexp"][0:32, _c * 128:(_c + 1) * 128],
                pw_bf[:, _t0:_t0 + TC], start=True, stop=True), "te")
            wait("vector", "te", mm)
            ev = op("vector", lambda e: e.tensor_copy(pwx[:], pb[1][:]), "dve")
            wait("tensor", "dve", ev)
            xvc = xv[:, c * TC:(c + 1) * TC]
            op("vector", lambda e, _x=xvc: e.tensor_scalar(
                out=pol1[:], in0=_x, scalar1=smalls[:, 22:23],
                scalar2=smalls[:, 21:22], op0=ALU.mult, op1=ALU.add))
            drain("vector")
            op("vector", lambda e, _x=xvc: e.scalar_tensor_tensor(
                out=pol2[:], in0=_x, scalar=1.0, in1=pol1[:], op0=ALU.mult,
                op1=ALU.mult))
            drain("vector")
            op("vector", lambda e: e.scalar_tensor_tensor(
                out=pp[:], in0=pol2[:], scalar=smalls[:, 20:21], in1=pwx[:],
                op0=ALU.add, op1=ALU.mult))
            drain("vector")
            def front(mp, prev_cos=None):
                ms = [2 * mp, 2 * mp + 1]
                g = None
                for p, m in enumerate(ms):
                    thc = thdiv[:, c * 4 + m:c * 4 + m + 1]
                    g = op("vector", lambda e, _x=xvc, _th=thc, _p=p:
                           e.tensor_scalar(
                        out=u_t[_p][:], in0=_x, scalar1=_th, scalar2=MAGIC,
                        op0=ALU.mult, op1=ALU.add),
                        "dve" if p == 1 else None)
                wait("vector", "dve", g)
                for p in range(2):
                    g = op("vector", lambda e, _p=p: e.tensor_scalar(
                        out=k_r[_p][:], in0=u_t[_p][:], scalar1=MAGIC,
                        scalar2=None, op0=ALU.subtract),
                        "dve" if p == 1 else None)
                wait("vector", "dve", g)
                for p, m in enumerate(ms):
                    thc = thdiv[:, c * 4 + m:c * 4 + m + 1]
                    g = op("vector", lambda e, _x=xvc, _th=thc, _p=p,
                           _q=2 * mp + p: e.scalar_tensor_tensor(
                        out=frac[_q][:], in0=_x, scalar=_th, in1=k_r[_p][:],
                        op0=ALU.mult, op1=ALU.subtract),
                        "dve" if p == 1 else None)
                wait("scalar", "dve", g)
                ga = None
                for p in range(2):
                    op("scalar", lambda e, _q=2 * mp + p: e.activation(
                        sinb[_q][:], frac[_q][:], AF.Sin, bias=0.0,
                        scale=float(2 * np.pi)))
                if prev_cos is not None:
                    kb.wait("scalar", "act", prev_cos)
                for p in range(2):
                    ga = op("scalar", lambda e, _p=p, _q=2 * mp + p:
                            e.activation(
                        afr[_p][:], frac[_q][:], AF.Abs, bias=0.0, scale=1.0),
                        "act" if p == 1 else None)
                kb.wait("scalar", "act", ga)
                a_cos = None
                for p in range(2):
                    a_cos = op("scalar", lambda e, _p=p, _q=2 * mp + p:
                               e.activation(
                        cosb[_q][:], afr[_p][:], AF.Sin, bias=smalls[:, 4:5],
                        scale=float(-2 * np.pi)), "act" if p == 1 else None)
                return a_cos

            def tail(mp, a_cos, lastmm):
                ms = [2 * mp, 2 * mp + 1]
                wait("vector", "act", a_cos)
                if lastmm is not None:
                    wait("vector", "te", lastmm)
                for p in range(2):
                    op("vector", lambda e, _p=p, _q=2 * mp + p: e.tensor_tensor(
                        rein[_p][:], pp[:], cosb[_q][:], ALU.mult))
                    g = op("vector", lambda e, _p=p, _q=2 * mp + p:
                           e.tensor_tensor(
                        imin[_p][:], pp[:], sinb[_q][:], ALU.mult),
                        "dve" if p == 1 else None)
                wait("vector", "dve", g)
                for p, m in enumerate(ms):
                    ci = (c * 4 + m) * 2
                    op("vector", lambda e, _ci=ci, _p=p: e.tensor_tensor_scan(
                        csre[_p][:], rein[_p][:], rein[_p][:],
                        carr[:, _ci:_ci + 1], ALU.add, ALU.bypass))
                    g = op("vector", lambda e, _ci=ci, _p=p: e.tensor_tensor_scan(
                        csim[_p][:], imin[_p][:], imin[_p][:],
                        carr[:, _ci + 1:_ci + 2], ALU.add, ALU.bypass),
                        "dve" if p == 1 else None)
                wait("vector", "dve", g)
                for p, m in enumerate(ms):
                    ci = (c * 4 + m) * 2
                    op("vector", lambda e, _ci=ci, _p=p: e.tensor_copy(
                        carr[:, _ci:_ci + 1], csre[_p][:, TC - 1:TC]))
                    op("vector", lambda e, _ci=ci, _p=p: e.tensor_copy(
                        carr[:, _ci + 1:_ci + 2], csim[_p][:, TC - 1:TC]))
                    op("vector", lambda e, _p=p: e.tensor_tensor(
                        sqt[2 * _p][:], csre[_p][:], csre[_p][:], ALU.mult))
                    g = op("vector", lambda e, _p=p: e.tensor_tensor(
                        sqt[2 * _p + 1][:], csim[_p][:], csim[_p][:],
                        ALU.mult), "dve" if p == 1 else None)
                wait("tensor", "dve", g)
                for p, m in enumerate(ms):
                    op("tensor", lambda e, _c=c, _p=p, _st=msq_st[0]:
                       e.matmul(pb[6][0:32, :],
                                sb["psum32"][:, _c * 32:(_c + 1) * 32],
                                sqt[2 * _p][:], start=_st, stop=False), None)
                    msq_st[0] = False
                    op("tensor", lambda e, _m=m, _p=p, _first=(m == 0):
                       e.matmul(pb[0][:], wy[:, _m * 128:(_m + 1) * 128],
                                csre[_p][:], start=_first, stop=False), None)
                    is_last = (c == 7 and m == 3)
                    op("tensor", lambda e, _c=c, _p=p, _L=is_last:
                       e.matmul(pb[6][0:32, :],
                                sb["psum32"][:, _c * 32:(_c + 1) * 32],
                                sqt[2 * _p + 1][:], start=False, stop=_L), None)
                    lastmm = op("tensor", lambda e, _m=m, _p=p:
                                e.matmul(pb[0][:],
                                         wy[:, (4 + _m) * 128:(5 + _m) * 128],
                                         csim[_p][:], start=False,
                                         stop=(_m == 3)), "te")
                return lastmm

            msq_st = [not msq_started]
            msq_started = True
            lastmm = None
            ac0 = front(0, prev_cos=prev_cos_c[0])
            ac1 = front(1, prev_cos=ac0)
            prev_cos_c[0] = ac1
            lastmm = tail(0, ac0, lastmm)
            lastmm = tail(1, ac1, lastmm)
            wait("vector", "te", lastmm)
            ev = op("vector", lambda e, _c=c: e.tensor_copy(
                y_u[:, _c * TC:(_c + 1) * TC], pb[0][:]), "dve")
            wait("tensor", "dve", ev)
        if "C" not in L0_PARTS:
            continue
        wait("vector", "te", kb.cnt("te"))
        op("vector", lambda e: e.tensor_scalar(
            out=msq, in0=pb[6][0:32, :], scalar1=float(1.0 / 256.0),
            scalar2=None, op0=ALU.mult))
        drain("vector")
        op("vector", lambda e: e.tensor_tensor(msq, msq, rtmp, ALU.mult))
        drain("vector")
        op("vector", lambda e: e.tensor_tensor(msq, msq, rtmp, ALU.mult))
        drain("vector")
        v_msq = op("vector", lambda e: e.tensor_scalar(
            out=msq, in0=msq, scalar1=1e-5, scalar2=None, op0=ALU.add), "dve")
        if want_dbg and "dbg_msq" in H:
            v_m = op("vector", lambda e: e.tensor_copy(rtmp2, msq), "dve")
            kb.drain("vector")
            wait("sync", "dve", v_m)
            op("sync", lambda e, _t0=t0: e.dma_start(
                out=H["dbg_msq"][:, _t0:_t0 + TC], in_=rtmp2), "st", 16)
            op("sync", lambda e, _t0=t0: e.dma_start(
                out=H["dbg_den"][:, _t0:_t0 + TC], in_=dent), "st", 16)
            wait("vector", "st", kb.cnt("st"))
        wait("scalar", "dve", v_msq)
        a_rs = op("scalar", lambda e: e.activation(
            skt, msq, AF.Sqrt, bias=0.0, scale=1.0), "act")
        wait("vector", "act", a_rs)
        op("vector", lambda e: e.reciprocal(skt, skt))
        drain("vector")
        op("vector", lambda e: e.tensor_tensor(skt, skt, rtmp, ALU.mult))
        drain("vector")
        v_sk = op("vector", lambda e: e.tensor_copy(skbf[:], skt), "dve")
        if want_dbg and "dbg_yu" in H:
            wait("gpsimd", "dve", kb.cnt("dve"))
            wait("sync", "dve", kb.cnt("dve"))
            for fcc in range(8):
                op("gpsimd", lambda e, _f=fcc, _t0=t0: e.dma_start(
                    out=H["dbg_yu"].rearrange("(c p) t -> c p t", p=128)
                    [_f][:, _t0:_t0 + TC],
                    in_=y_u[:, _f * TC:(_f + 1) * TC]), "st", 16)
            op("sync", lambda e, _t0=t0: e.dma_start(
                out=H["dbg_sk"][:, _t0:_t0 + TC], in_=skt), "st", 16)
            kb.wait("gpsimd", "st", kb.cnt("st"))
            kb.wait("vector", "st", kb.cnt("st"))
        if "D" not in L0_PARTS:
            continue
        wait("tensor", "dve", v_sk)
        for c in range(8):
            mm = op("tensor", lambda e, _c=c: e.matmul(
                pb[1][:], sb["kexp"][0:32, _c * 128:(_c + 1) * 128],
                skbf[:], start=True, stop=True), "te")
            wait("vector", "te", mm)
            off = c * TC
            op("vector", lambda e, _o=off: e.scalar_tensor_tensor(
                out=y_u[:, _o:_o + TC], in0=y_u[:, _o:_o + TC], scalar=1.0,
                in1=pb[1][:], op0=ALU.mult, op1=ALU.mult))
            kb.drain("vector")
            ev = op("vector", lambda e, _o=off: e.tensor_tensor(
                y_u[:, _o:_o + TC], y_u[:, _o:_o + TC], gate[:, _o:_o + TC],
                ALU.mult), "dve")
            wait("tensor", "dve", ev)
        ev_hist = []
        for fo in range(8):
            bz = fo % 2
            in_ap = bass.AP(H["l0_wout2"].tensor, fo * 131072,
                            [[1024, 128], [1, 1024]])
            if wfc_guard[bz]:
                kb.wait("gpsimd", "te", wfc_guard[bz])
            lw = op("gpsimd", lambda e, _i=in_ap, _b=bz: e.dma_start(
                out=wfc[_b][:], in_=_i), "gw", 16)
            wait("tensor", "gw", lw)
            if len(ev_hist) >= 2:
                wait("tensor", "dve", ev_hist[-2])
            bank = pb[2 + bz]
            mmv = None
            for c in range(8):
                mov = y_u[:, c * TC:(c + 1) * TC]
                sta = wfc[bz][:, c * 128:(c + 1) * 128]
                mmv = op("tensor", lambda e, _b=bank, _s=sta, _m=mov, _c=c:
                         e.matmul(_b[:], _s, _m, start=(_c == 0), stop=(_c == 7)),
                         "te" if c == 7 else None)
            wfc_guard[bz] = mmv
            wait("vector", "te", mmv)
            ev = op("vector", lambda e, _b=bank, _f=fo, _t0=t0:
                    e.scalar_tensor_tensor(
                        out=sg[:, _f * TC:(_f + 1) * TC],
                        in0=g0[:, _f * T + _t0:_f * T + _t0 + TC],
                        scalar=1.0, in1=_b[:], op0=ALU.mult, op1=ALU.mult),
                    "dve")
            ev_hist.append(ev)
        wait("tensor", "dve", kb.cnt("dve"))
        sg_done = kb.cnt("dve")
        wait("sync", "dve", sg_done)
        tpv = None
        for fo in range(8):
            sa = sg[:, fo * TC:(fo + 1) * TC].rearrange("p (i q) -> p i q", q=128)
            da = bass.AP(sg_tm, fo * 128, [[4 * D, 128], [D, 4], [1, 128]])
            tpv = op("sync", lambda e, _s=sa, _d=da: e.dma_start_transpose(_d, _s),
                     "tp", 16)
        xv_h = H["x0"].rearrange("(n p) d -> n p d", p=128)
        x1_h = H["x1"].rearrange("(n p) d -> n p d", p=128)
        xtb_l0 = [sb["xt0"], sb["xt1"]]
        for ii in range(TC // 128):
            i = t0 // 128 + ii
            bx = ii % 2
            ldv = op("sync", lambda e, _i=i, _x=xtb_l0[bx]: e.dma_start(
                out=_x[:], in_=xv_h[_i]), "ld", 16)
            wait("vector", "ld", ldv)
            wait("vector", "tp", tpv)
            if sgadd_guard[bx]:
                wait("vector", "st", sgadd_guard[bx])
            v1 = op("vector", lambda e, _b=bx, _x=xtb_l0[bx], _ii=ii:
                    e.tensor_tensor(
                sgadd[_b][:], _x[:],
                sg_tm[:, _ii * D:(_ii + 1) * D], ALU.add), "dve")
            wait("sync", "dve", v1)
            st = op("sync", lambda e, _i=i, _b=bx: e.dma_start(
                out=x1_h[_i], in_=sgadd[_b][:]), "st", 16)
            sgadd_guard[bx] = st
        if want_dbg and "dbg_s0" in H:
            wait("gpsimd", "dve", sg_done)
            for fo in range(8):
                gv = op("gpsimd", lambda e, _f=fo, _t0=t0: e.dma_start(
                    out=H["dbg_s0"].rearrange("(c p) t -> c p t", p=128)
                    [_f][:, _t0:_t0 + TC],
                    in_=sg[:, _f * TC:(_f + 1) * TC]), "st", 16)
            wait("scalar", "st", kb.cnt("st"))
            wait("vector", "st", kb.cnt("st"))
    return


def emit_l0_sraw(kb, pb):
    H, op, wait, drain = kb.H, kb.op, kb.wait, kb.drain
    sb = kb.sb
    smalls = sb["smalls"]
    sr_w = kb.S("sr_w", (128, 8 * 32), BF16)
    srpack = kb.S("srpack", (32, (T + 3) + T), F32)
    sr_z = srpack[0:32, 0:T + 3]
    sr_c = srpack[0:32, T + 3:2 * T + 3]
    sr_t = kb.sb["rp32"][0:32, T:2 * T]
    svec = kb.S("svec", (32, 8), F32)
    dv3 = kb.S("dv3", (1, 4), F32)
    dv128 = kb.S("dv128", (128, 3), F32)
    ones1 = kb.S("ones1", (1, 128), F32)

    in_ap = bass.AP(H["l0_win"].tensor, 2048,
                    [[2080, 128], [128 * 2080, 8], [1, 32]])
    l1 = op("gpsimd", lambda e, _i=in_ap: e.dma_start(out=sr_w[:], in_=_i),
            "gw", 16)
    op("sync", lambda e: e.dma_start(out=svec[:, 0:4], in_=H["l0_wconv_sr"][:, :]),
       "ld", 16)
    op("sync", lambda e: e.dma_start(out=svec[:, 4:5], in_=H["l0_bconv_sr"][:, :]),
       "ld", 16)
    op("sync", lambda e: e.dma_start(out=svec[:, 5:6], in_=H["l0_slopes"][:, :]),
       "ld", 16)
    op("sync", lambda e: e.dma_start(out=svec[:, 6:7], in_=H["l0_score"][:, :]),
       "ld", 16)
    op("sync", lambda e: e.dma_start(out=dv3[:, 0:3], in_=H["l0_deriv"][:, :]),
       "ld", 16)
    wait("scalar", "ld", kb.cnt("ld"))
    wait("vector", "ld", kb.cnt("ld"))
    wait("tensor", "gw", l1)

    a_e = op("scalar", lambda e: e.activation(dv3[:1, 0:3], dv3[:1, 0:3],
                                              AF.Exp, bias=0.0, scale=1.0),
             "act")
    wait("vector", "act", a_e)
    op("vector", lambda e: e.tensor_reduce(dv3[:1, 3:4], dv3[:1, 0:3],
                                           AX.X, ALU.add))
    drain("vector")
    op("vector", lambda e: e.reciprocal(dv3[:1, 3:4], dv3[:1, 3:4]))
    drain("vector")
    op("vector", lambda e: e.tensor_scalar(
        out=dv3[:1, 0:3], in0=dv3[:1, 0:3], scalar1=dv3[:1, 3:4],
        scalar2=None, op0=ALU.mult))
    v_w = op("vector", lambda e: e.memset(ones1[:], 1.0), "dve")
    kb.drain("vector")
    wait("tensor", "dve", v_w)
    mmb = op("tensor", lambda e: e.matmul(pb[1][:, 0:3], ones1[:],
                                          dv3[:1, 0:3], start=True, stop=True),
             "te")
    wait("vector", "te", mmb)
    op("vector", lambda e: e.tensor_copy(dv128[:], pb[1][:, 0:3]))
    kb.drain("vector")
    op("vector", lambda e: e.tensor_copy(smalls[:, 20:21], dv128[:, 0:1]))
    op("vector", lambda e: e.tensor_copy(smalls[:, 21:22], dv128[:, 1:2]))
    op("vector", lambda e: e.tensor_scalar(out=smalls[:, 22:23],
                                           in0=dv128[:, 2:3], scalar1=-1.0,
                                           scalar2=None, op0=ALU.mult))
    drain("vector")

    a_sp1 = op("scalar", lambda e: e.activation(svec[:32, 7:8], svec[:32, 5:6],
                                                AF.Exp, bias=0.0, scale=1.0),
               "act")
    kb.drain("scalar")
    a_sp2 = op("scalar", lambda e: e.activation(svec[:32, 7:8], svec[:32, 7:8],
                                                AF.Ln, bias=1.0, scale=1.0),
               "act")

    op("vector", lambda e: e.memset(srpack[0:32, 0:3], 0.0))
    drain("vector")
    for tt in range(4):
        bank = pb[0]
        mmv = None
        for dc in range(8):
            mov = sb["xn_fm"][:, dc * T + tt * 512:dc * T + tt * 512 + 512]
            sta = sr_w[:, dc * 32:dc * 32 + 32]
            mmv = op("tensor", lambda e, _b=bank, _s=sta, _m=mov, _d=dc:
                     e.matmul(_b[:32, :], _s, _m, start=(_d == 0),
                              stop=(_d == 7)), "te" if dc == 7 else None)
        wait("vector", "te", mmv)
        ev = op("vector", lambda e, _b=bank, _t=tt: e.tensor_copy(
            srpack[0:32, 3 + _t * 512:3 + _t * 512 + 512], _b[0:32, :]), "dve")
        kb.drain("vector")
        wait("tensor", "dve", ev)
    op("vector", lambda e: e.tensor_scalar(
        out=sr_c, in0=srpack[0:32, 0:T], scalar1=svec[:32, 0:1], scalar2=None,
        op0=ALU.mult))
    drain("vector")
    for tap in range(1, 4):
        op("vector", lambda e, _t=tap: e.scalar_tensor_tensor(
            out=sr_c, in0=srpack[0:32, _t:T + _t], scalar=svec[:32, _t:_t + 1],
            in1=sr_c, op0=ALU.mult, op1=ALU.add))
        drain("vector")
    op("vector", lambda e: e.tensor_tensor(svec[:32, 4:5], svec[:32, 4:5],
                                           svec[:32, 6:7], ALU.mult))
    drain("vector")
    op("vector", lambda e: e.tensor_scalar(
        out=sr_c, in0=sr_c, scalar1=svec[:32, 6:7],
        scalar2=svec[:32, 4:5], op0=ALU.mult, op1=ALU.add))
    drain("vector")
    op("vector", lambda e: e.tensor_scalar(
        out=sr_c, in0=sr_c, scalar1=20.0, scalar2=-20.0,
        op0=ALU.min, op1=ALU.max))
    drain("vector")
    wait("vector", "act", a_sp2)
    op("vector", lambda e: e.tensor_scalar(
        out=sr_t, in0=sr_t, scalar1=svec[:32, 7:8],
        scalar2=None, op0=ALU.mult))
    drain("vector")
    v_u = op("vector", lambda e: e.tensor_tensor(sr_c, sr_c, sr_t,
                                                 ALU.add), "dve")
    wait("scalar", "dve", v_u)
    a_pw = op("scalar", lambda e: e.activation(sb["rp32"][0:32, 0:T], sr_c,
                                               AF.Exp, bias=0.0, scale=1.0),
              "act")
    wait("vector", "act", a_pw)
    v_pwb = op("vector", lambda e: e.tensor_copy(sb["rpbf"][0:32, :],
                                                 sb["rp32"][0:32, 0:T]), "dve")
    drain("vector")
    if "dbg_pw" in H:
        wait("sync", "dve", v_pwb)
        op("sync", lambda e: e.dma_start(out=H["dbg_pw"][:, :],
                                         in_=sb["rp32"][0:32, 0:T]), "st", 16)
    return v_pwb



def host_l1_inputs(inputs):
    h = {}
    Wqkv = inputs["L1_attn_Wqkv"]
    bqkv = inputs["L1_attn_bqkv"]
    cols = np.zeros((1024, 32 * 128), np.float32)
    bias = np.zeros(32 * 128, np.float32)

    def put(col, src_row):
        cols[:, col] = Wqkv[src_row]
        bias[col] = bqkv[src_row]

    for hh in range(16):
        for d in range(16):
            put((hh // 4) * 128 + (hh % 4) * 32 + d, hh * 64 + d)
            put(512 + (hh // 4) * 128 + (hh % 4) * 32 + d, 1024 + hh * 64 + d)
        for d in range(16, 64):
            put(1024 + hh * 64 + (d - 16), hh * 64 + d)
            put(2048 + hh * 64 + (d - 16), 1024 + hh * 64 + d)
        for d in range(64):
            put(3072 + (hh // 2) * 128 + (hh % 2) * 64 + d, 2048 + hh * 64 + d)
    h["l1_wqkvT"] = np.ascontiguousarray(
        cols.reshape(8, 128, 32, 128).transpose(2, 1, 0, 3)
        .reshape(32, 128, 1024))
    h["l1_bqkv"] = np.ascontiguousarray(bias.reshape(32, 128).T)

    def oc_tiles(wT):
        return np.ascontiguousarray(
            wT.reshape(8, 128, 8, 128).transpose(2, 1, 0, 3)
            .reshape(8, 128, 1024))

    h["l1_wqaT"] = oc_tiles(inputs["L1_attn_Wqa"].T)
    h["l1_wkaT"] = oc_tiles(inputs["L1_attn_Wka"].T)
    h["l1_wdT"] = oc_tiles(inputs["L1_attn_Wd"].T)
    h["l1_bqa"] = np.ascontiguousarray(inputs["L1_attn_bqa"].reshape(8, 128).T)
    h["l1_bka"] = np.ascontiguousarray(inputs["L1_attn_bka"].reshape(8, 128).T)
    h["l1_bd"] = np.ascontiguousarray(inputs["L1_attn_bd"].reshape(8, 128).T)
    h["l1_gwT"] = np.ascontiguousarray(inputs["L1_ffn_gate_w"].T)
    h["l1_wg"] = np.ascontiguousarray(
        inputs["L1_ffn_wg"].reshape(8, 8, 128, 2, 128)
        .transpose(0, 2, 1, 3, 4).reshape(8, 128, 2048))
    h["l1_wu"] = np.ascontiguousarray(
        inputs["L1_ffn_wu"].reshape(8, 8, 128, 2, 128)
        .transpose(0, 2, 1, 3, 4).reshape(8, 128, 2048))
    h["l1_wd"] = np.ascontiguousarray(
        inputs["L1_ffn_wd"].reshape(8, 2, 128, 8, 128)
        .transpose(3, 2, 0, 1, 4).reshape(8, 128, 2048))
    return h


def host_l1_consts():
    c = {}
    P = np.zeros((128, 128), BF16NP)
    for hh in range(4):
        for d in range(8):
            P[hh * 32 + d + 8, hh * 32 + d] = -1
            P[hh * 32 + d, hh * 32 + d + 8] = 1
    c["c_rot"] = P
    inv_freq = (1.0 / 10000.0 ** (np.arange(0, 16, 2) / 16.0)).astype(np.float64)
    t = np.arange(T, dtype=np.float64)
    ang = t[None, :] * inv_freq[:, None]
    cosf = np.zeros((128, T), np.float32)
    sinf = np.zeros((128, T), np.float32)
    for hh in range(4):
        for d in range(16):
            cosf[hh * 32 + d] = np.cos(ang[d % 8])
            sinf[hh * 32 + d] = np.sin(ang[d % 8])
    c["c_cosf"] = cosf
    c["c_sinf"] = sinf
    m = np.zeros((128, 256), BF16NP)
    for j in range(128):
        for q in range(128):
            if 0 <= q - j <= 15:
                m[j, q] = 1
            if 0 <= (q + 128) - j <= 15:
                m[j, 128 + q] = 1
    c["c_wmask"] = m
    c["c_cmask"] = np.triu(np.ones((128, 128), np.float32)).astype(BF16NP)
    c["c_ones1b"] = np.ones((128, 1), BF16NP)
    ob = np.zeros((128, 12 * 16), BF16NP)
    for j in range(4):
        for r in range(128):
            ob[r, j * 16 + 4 * j + r // 32] = 1
    for p in range(8):
        for r in range(128):
            ob[r, (4 + p) * 16 + 2 * p + r // 64] = 1
    c["c_oblk"] = ob
    return c


def emit_l1(kb, want_dbg=False):
    raise NotImplementedError

def _emit_ln_tile(kb, src_tile, dst_tile):
    op, wait, drain = kb.op, kb.wait, kb.drain
    sb = kb.sb
    sq_done = op("scalar", lambda e: e.activation(
        sb["sqd"][:], src_tile, AF.Square, bias=0.0, scale=1.0,
        accum_out=sb["red"][:, 1:2]), "act")
    op("vector", lambda e: e.tensor_reduce(sb["red"][:, 0:1], src_tile,
                                           AX.X, ALU.add))
    drain("vector")
    op("vector", lambda e: e.tensor_scalar(
        out=sb["red"][:, 2:3], in0=sb["red"][:, 0:1], scalar1=1.0 / D,
        scalar2=None, op0=ALU.mult))
    drain("vector")
    op("vector", lambda e: e.tensor_tensor(
        sb["red"][:, 3:4], sb["red"][:, 2:3], sb["red"][:, 2:3], ALU.mult))
    wait("vector", "act", sq_done)
    drain("vector")
    op("vector", lambda e: e.scalar_tensor_tensor(
        out=sb["red"][:, 4:5], in0=sb["red"][:, 1:2], scalar=1.0 / D,
        in1=sb["red"][:, 3:4], op0=ALU.mult, op1=ALU.subtract))
    drain("vector")
    v_var = op("vector", lambda e: e.tensor_scalar(
        out=sb["red"][:, 4:5], in0=sb["red"][:, 4:5], scalar1=1e-5,
        scalar2=None, op0=ALU.add), "dve")
    wait("scalar", "dve", v_var)
    op("scalar", lambda e: e.activation(sb["red"][:, 5:6], sb["red"][:, 4:5],
                                        AF.Ln, bias=0.0, scale=1.0))
    kb.drain("scalar")
    a_r = op("scalar", lambda e: e.activation(
        sb["red"][:, 6:7], sb["red"][:, 5:6], AF.Exp, bias=0.0, scale=-0.5),
        "act")
    op("vector", lambda e: e.tensor_scalar(
        out=sb["red"][:, 7:8], in0=sb["red"][:, 2:3], scalar1=-1.0,
        scalar2=None, op0=ALU.mult))
    wait("vector", "act", a_r)
    drain("vector")
    return op("vector", lambda e: e.tensor_scalar(
        out=dst_tile, in0=src_tile, scalar1=kb.sb["red"][:, 7:8],
        scalar2=kb.sb["red"][:, 6:7], op0=ALU.add, op1=ALU.mult), "dve")


def emit_l1(kb, want_dbg=False):
    H, op, wait, drain = kb.H, kb.op, kb.wait, kb.drain
    sb = kb.sb
    PS = kb.PSALL
    pb = [PS[:, i * 512:(i + 1) * 512] for i in range(8)]
    smalls = sb["smalls"]
    SCALE = 0.125

    arenaA = sb["xn_fm"]
    arenaB = sb["g0"]

    ones1b = kb.S("ones1b", (128, 1), BF16)
    onesr1 = kb.S("onesr1", (1, 128), BF16)
    ones16 = kb.S("ones16", (16, 128), F32)
    oblk = kb.S("oblk", (128, 12 * 16), BF16)
    bqkv_t = kb.S("bqkv_t", (128, 32), F32)
    bqa_t = kb.S("bqa_t", (128, 8), F32)
    bka_t = kb.S("bka_t", (128, 8), F32)
    bd_t = kb.S("bd_t", (128, 8), F32)
    gw_t = kb.S("gw_t", (128, 8 * 8), BF16)
    qtmp2 = kb.S("qtmp2", (128, 16), F32)
    diag16 = kb.S("diag16", (16, 16), F32)
    identf = kb.S("identf", (128, 128), F32)
    rdc = kb.S("rdc", (128, 2), F32)
    nrm = kb.S("nrm", (16, 32), F32)
    negmb1 = kb.S("negmb1", (128, 16), F32)
    negmb2 = kb.S("negmb2", (128, 16), F32)
    dentm = kb.S("dentm", (128, 256), F32)
    relub = kb.S("relub", (128, 512), BF16)
    ubuf = kb.S("ubuf", (128, 512), BF16)
    g1st = [kb.S(f"g1st{i}", (128, 512), BF16) for i in range(2)]
    xn1st = [kb.S(f"xn1st{i}", (128, D), BF16) for i in range(2)]
    xn1ft = [kb.S(f"xn1ft{i}", (128, D), BF16) for i in range(2)]
    kb.M_MOE = kb.marker()
    masksf = kb.S("masksf", (8, T), BF16)
    maskb = kb.S("maskb", (128, 8 * 512), BF16)
    onehot = kb.S("onehot", (128, NT * 8), BF16)
    xnb = kb.S("xnb", (128, 8 * 512), BF16)
    wgu = [kb.S(f"wgu{i}", (128, 32 * 128), BF16) for i in range(2)]
    inter = kb.S("inter", (128, 16 * 512), BF16)
    wst = [kb.S(f"wst{i}", (128, 16 * 128), BF16) for i in range(2)]
    xt_m = [kb.S(f"xt{i}_l1m", (128, D), F32) for i in range(2)]
    mrow = kb.S("mrow", (1, 512), BF16)
    kb.sb["xt0"], kb.sb["xt1"] = xt_m
    qrest = arenaA
    v_tm = arenaB

    OB = PS[0:64, 1024:3072]
    DENROW = PS[64:65, 1024:3072]

    for nm, tile in (("c_ones1b", ones1b), ("c_oblk", oblk),
                     ("l1_bqkv", bqkv_t), ("l1_bqa", bqa_t),
                     ("l1_bka", bka_t), ("l1_bd", bd_t)):
        op("sync", lambda e, _n=nm, _t=tile: e.dma_start(out=_t[:], in_=H[_n][:, :]),
           "ld", 16)
    op("gpsimd", lambda e: e.dma_start(
        out=gw_t[:], in_=bass.AP(H["l1_gwT"].tensor, 0,
                                 [[8, 128], [1024, 8], [1, 8]])), "gw", 16)
    ld0 = kb.cnt("ld")
    lw0 = kb.cnt("gw")
    wait("vector", "ld", ld0)
    wait("vector", "gw", lw0)
    op("vector", lambda e: e.memset(ones16[:], 1.0))
    op("vector", lambda e: e.memset(onesr1[:], 1.0))
    op("vector", lambda e: e.memset(smalls[:, 5:6], float(np.log(0.125))))
    op("vector", lambda e: e.tensor_copy(identf[:], sb["ident"][:]))
    drain("vector")

    x1v = H["x1"].rearrange("(n p) d -> n p d", p=128)
    tp_guard = [None, None]
    ft_guard = [0, 0]
    st_guard = [None, None]
    oh_done = None
    xtb_m = [sb["xt0"], sb["xt1"]]
    for i in range(NT):
        b = i % 2
        ldv = op("sync", lambda e, _i=i, _x=xtb_m[b]: e.dma_start(
            out=_x[:], in_=x1v[_i]), "ld", 16)
        wait("vector", "ld", ldv)
        wait("scalar", "ld", ldv)
        if i >= 2 and tp_guard[b] is not None:
            wait("vector", "tp", tp_guard[b])
        v_xn = _emit_ln_tile(kb, xtb_m[b][:], xn1st[b][:])
        wait("sync", "dve", v_xn)
        if i >= 2 and ft_guard[b]:
            wait("sync", "te", ft_guard[b])
        if i >= 2 and st_guard[b]:
            wait("sync", "st", st_guard[b])
        sa = xn1st[b][:].rearrange("p (c q) -> p c q", q=128)
        da = xn1ft[b][:].rearrange("p (c q) -> p c q", q=128)
        tpv = op("sync", lambda e, _s=sa, _d=da: e.dma_start_transpose(_d, _s),
                 "tp", 16)
        tp_guard[b] = tpv
        out_ap = bass.AP(H["xn1"].tensor, i * 128,
                         [[2048, 128], [128 * 2048, 8], [1, 128]])
        wait("sync", "tp", tpv)
        st1 = op("sync", lambda e, _o=out_ap, _b=b: e.dma_start(
            out=_o, in_=xn1ft[_b][:].rearrange("p (c q) -> p c q", q=128)),
            "st", 16)
        st_guard[b] = st1
        wait("tensor", "tp", tpv)
        wait("tensor", "gw", lw0)
        mmv = None
        for dc in range(8):
            mmv = op("tensor", lambda e, _b=b, _d=dc: e.matmul(
                pb[7][:, 0:8], xn1ft[_b][:, _d * 128:(_d + 1) * 128],
                gw_t[:, _d * 8:(_d + 1) * 8], start=(_d == 0), stop=(_d == 7)),
                "te" if dc == 7 else None)
        ft_guard[b] = mmv
        wait("vector", "te", mmv)
        wait("vector", "st", st1)
        op("vector", lambda e: e.tensor_reduce(sb["red"][:, 8:9], pb[7][:, 0:8],
                                               AX.X, ALU.max))
        drain("vector")
        op("vector", lambda e: e.tensor_scalar(
            out=qtmp2[:, 0:8], in0=pb[7][:, 0:8], scalar1=sb["red"][:, 8:9],
            scalar2=None, op0=ALU.is_ge))
        drain("vector")
        op("vector", lambda e: e.tensor_tensor_scan(
            qtmp2[:, 8:16], qtmp2[:, 0:8], qtmp2[:, 0:8], 0.0,
            ALU.add, ALU.bypass))
        drain("vector")
        op("vector", lambda e: e.tensor_scalar(
            out=qtmp2[:, 8:16], in0=qtmp2[:, 8:16], scalar1=1.0,
            scalar2=None, op0=ALU.is_le))
        drain("vector")
        oh_done = op("vector", lambda e, _i=i: e.tensor_tensor(
            onehot[:, _i * 8:(_i + 1) * 8], qtmp2[:, 0:8], qtmp2[:, 8:16],
            ALU.mult), "dve")
        kb.drain("vector")
        wait("tensor", "dve", oh_done)
    wait("tensor", "dve", oh_done)
    mtp_out = pb[7][0:8, 128:192].bitcast(BF16)
    for i in range(NT):
        mt = op("tensor", lambda e, _i=i: e.transpose(
            mtp_out, onehot[:, _i * 8:(_i + 1) * 8], sb["ident"][:]), "te")
        wait("vector", "te", mt)
        ev = op("vector", lambda e, _i=i: e.tensor_copy(
            masksf[:, _i * 128:(_i + 1) * 128], mtp_out), "dve")
        kb.drain("vector")
        wait("tensor", "dve", ev)

    wgu_guard = [0, 0]
    wd_guard = [0, 0]
    g1_guard = [None, None]
    for tt in range(4):
        wait("sync", "te", kb.cnt("te"))
        wait("vector", "te", kb.cnt("te"))
        wait("sync", "dve", kb.cnt("dve"))
        for e_ in range(8):
            op("sync", lambda e, _e=e_, _t=tt: e.dma_start(
                out=maskb[:, _e * 512:(_e + 1) * 512],
                in_=bass.AP(masksf, _e * T + _t * 512,
                            [[T, 1], [0, 128], [1, 512]])), "ld", 16)
        lx = op("sync", lambda e, _t=tt: e.dma_start(
            out=xnb[:], in_=bass.AP(H["xn1"].tensor, _t * 512,
                                    [[2048, 128], [128 * 2048, 8], [1, 512]])),
            "ld", 16)
        wait("tensor", "ld", lx)
        wait("vector", "ld", lx)
        epi_hist = []
        for e_ in range(8):
            bwf = e_ % 2
            bb = 4 * (e_ % 2)
            if wgu_guard[bwf]:
                kb.wait("gpsimd", "te", wgu_guard[bwf])
            in_g = bass.AP(H["l1_wg"].tensor, e_ * 262144,
                           [[2048, 128], [1, 2048]])
            in_u = bass.AP(H["l1_wu"].tensor, e_ * 262144,
                           [[2048, 128], [1, 2048]])
            op("gpsimd", lambda e, _i=in_g, _b=bwf: e.dma_start(
                out=wgu[_b][:, 0:2048], in_=_i), "gw", 16)
            lwg = op("gpsimd", lambda e, _i=in_u, _b=bwf: e.dma_start(
                out=wgu[_b][:, 2048:4096], in_=_i), "gw", 16)
            wait("tensor", "gw", lwg)
            if len(epi_hist) >= 2:
                wait("tensor", "act", epi_hist[-2])
            mmg = [None, None]
            for gu in range(2):
                for dc in range(2):
                    bank = pb[bb + gu * 2 + dc]
                    mm2 = None
                    for hc in range(8):
                        sta = wgu[bwf][:, gu * 2048 + (hc * 2 + dc) * 128:
                                       gu * 2048 + (hc * 2 + dc) * 128 + 128]
                        mov = xnb[:, hc * 512:(hc + 1) * 512]
                        mm2 = op("tensor", lambda e, _b=bank, _s=sta, _m=mov,
                                 _h=hc: e.matmul(_b, _s, _m, start=(_h == 0),
                                                 stop=(_h == 7)),
                                 "te" if hc == 7 else None)
                    mmg[gu] = mm2
            wgu_guard[bwf] = mmg[1]
            wait("scalar", "te", mmg[1])
            a_last = None
            for dc in range(2):
                if kb.cnt("dve"):
                    wait("scalar", "dve", kb.cnt("dve"))
                a_rel = op("scalar", lambda e, _d=bb + dc: e.activation(
                    relub[:], pb[_d], AF.Relu, bias=0.0, scale=1.0), "act")
                a_ue = op("scalar", lambda e, _d=bb + 2 + dc: e.activation(
                    ubuf[:], pb[_d], AF.Identity, bias=0.0, scale=1.0),
                    "act")
                a_last = a_ue
                wait("vector", "act", a_ue)
                op("vector", lambda e: e.tensor_tensor(relub[:], relub[:],
                                                       relub[:], ALU.mult))
                drain("vector")
                op("vector", lambda e: e.tensor_tensor(relub[:], relub[:],
                                                       ubuf[:], ALU.mult))
                drain("vector")
                evi = op("vector", lambda e, _e=e_, _d=dc: e.scalar_tensor_tensor(
                    out=inter[:, (_e * 2 + _d) * 512:(_e * 2 + _d + 1) * 512],
                    in0=relub[:], scalar=1.0,
                    in1=maskb[:, _e * 512:(_e + 1) * 512],
                    op0=ALU.mult, op1=ALU.mult), "dve")
            epi_hist.append(a_last)
        wait("tensor", "dve", kb.cnt("dve"))
        wait("tensor", "act", kb.cnt("act"))
        for fo in range(8):
            bwf = fo % 2
            if wd_guard[bwf]:
                kb.wait("gpsimd", "te", wd_guard[bwf])
            in_d = bass.AP(H["l1_wd"].tensor, fo * 262144,
                           [[2048, 128], [1, 2048]])
            lwd = op("gpsimd", lambda e, _i=in_d, _b=bwf: e.dma_start(
                out=wst[_b][:], in_=_i), "gw", 16)
            wait("tensor", "gw", lwd)
            bank = pb[5 + (fo % 2)]
            mmd = None
            for kc in range(16):
                sta = wst[bwf][:, kc * 128:(kc + 1) * 128]
                mov = inter[:, kc * 512:(kc + 1) * 512]
                mmd = op("tensor", lambda e, _b=bank, _s=sta, _m=mov, _k=kc:
                         e.matmul(_b, _s, _m, start=(_k == 0), stop=(_k == 15)),
                         "te" if kc == 15 else None)
            wd_guard[bwf] = mmd
            wait("scalar", "te", mmd)
            bg = fo % 2
            if g1_guard[bg] is not None:
                wait("scalar", "st", g1_guard[bg])
            a_g1 = op("scalar", lambda e, _b=bank, _g=bg: e.activation(
                g1st[_g][:], _b, AF.Silu, bias=0.0, scale=1.0), "act")
            wait("sync", "act", a_g1)
            g1ap = bass.AP(H["g1"].tensor, fo * 128 * 2048 + tt * 512,
                           [[2048, 128], [1, 512]])
            g1_guard[bg] = op("sync", lambda e, _o=g1ap, _g=bg: e.dma_start(
                out=_o, in_=g1st[_g][:]), "st", 16)
            wait("tensor", "act", a_g1)
    kb.barrier()
    kb.free_to(kb.M_MOE)
    kb.S("A_tm", (128, 8 * T), BF16)
    kb.M_K = kb.marker()
    kb.S("krest", (128, 8 * T), BF16)
    kb.S("qsl", (128, 4 * T), BF16)
    kb.S("ksl", (128, 4 * T), BF16)
    emit_l1_attn(kb, want_dbg)
    return


def emit_l1_attn(kb, want_dbg=False):
    H, op, wait, drain = kb.H, kb.op, kb.wait, kb.drain
    sb = kb.sb
    PS = kb.PSALL
    pb = [PS[:, i * 512:(i + 1) * 512] for i in range(8)]
    smalls = sb["smalls"]
    qsl, ksl = sb["qsl"], sb["ksl"]
    krest = sb["krest"]
    qrest = sb["xn_fm"]
    v_tm = sb["g0"]
    A_tm = sb["A_tm"]
    nrm = sb["nrm"]
    negmb1 = sb["negmb1"]
    oblk = sb["oblk"]
    ones1b = sb["ones1b"]
    relub, ubuf = sb["relub"], sb["ubuf"]
    M_R = kb.marker()
    xnb = kb.S("xnb_q", (128, 8 * 512), BF16)
    wst = [kb.S(f"wstq{i}", (128, 8 * 128), BF16) for i in range(2)]
    rot = kb.S("rot", (128, 128), BF16)
    cosl = kb.S("cosl", (128, 512), BF16)
    sinl = kb.S("sinl", (128, 512), BF16)
    op("sync", lambda e: e.dma_start(out=rot[:], in_=H["c_rot"][:, :]),
       "ld", 16)
    wait("tensor", "ld", kb.cnt("ld"))

    wst_guard = [0, 0]
    evac_hist = []
    vtp_hist = []
    for tt in range(4):
        wait("sync", "te", kb.cnt("te"))
        lx = op("sync", lambda e, _t=tt: e.dma_start(
            out=xnb[:], in_=bass.AP(H["xn1"].tensor, _t * 512,
                                    [[2048, 128], [128 * 2048, 8], [1, 512]])),
            "ld", 16)
        wait("tensor", "ld", lx)
        for oc in range(32):
            bws = oc % 2
            if wst_guard[bws]:
                kb.wait("gpsimd", "te", wst_guard[bws])
            in_w = bass.AP(H["l1_wqkvT"].tensor, oc * 131072,
                           [[1024, 128], [1, 1024]])
            lww = op("gpsimd", lambda e, _i=in_w, _b=bws: e.dma_start(
                out=wst[_b][:], in_=_i), "gw", 16)
            wait("tensor", "gw", lww)
            if len(evac_hist) >= 4:
                wait("tensor", "act", evac_hist[-4])
            bank = pb[oc % 4]
            mmv = None
            for dc in range(8):
                mmv = op("tensor", lambda e, _b=bank, _s=wst[bws][:, dc * 128:(dc + 1) * 128],
                         _m=xnb[:, dc * 512:(dc + 1) * 512], _d=dc:
                         e.matmul(_b, _s, _m, start=(_d == 0), stop=(_d == 7)),
                         "te" if dc == 7 else None)
            wst_guard[bws] = mmv
            wait("scalar", "te", mmv)
            if oc < 4:
                dst = qsl[:, oc * T + tt * 512:oc * T + tt * 512 + 512]
            elif oc < 8:
                dst = ksl[:, (oc - 4) * T + tt * 512:(oc - 4) * T + tt * 512 + 512]
            elif oc < 16:
                dst = qrest[:, (oc - 8) * T + tt * 512:(oc - 8) * T + tt * 512 + 512]
            elif oc < 24:
                dst = krest[:, (oc - 16) * T + tt * 512:(oc - 16) * T + tt * 512 + 512]
            else:
                dst = v_tm[:, (oc - 24) * T + tt * 512:(oc - 24) * T + tt * 512 + 512]
            if oc < 24:
                a_e = op("scalar", lambda e, _b=bank, _d=dst, _o=oc: e.activation(
                    _d, _b, AF.Identity, bias=sb["bqkv_t"][:, _o:_o + 1],
                    scale=1.0), "act")
                evac_hist.append(a_e)
            else:
                vstage = sb["relub"] if (oc % 2 == 0) else sb["ubuf"]
                if len(vtp_hist) >= 2:
                    wait("scalar", "tp", vtp_hist[-2])
                a_e = op("scalar", lambda e, _b=bank, _o=oc, _v=vstage:
                         e.activation(_v[:], _b, AF.Identity,
                                      bias=sb["bqkv_t"][:, _o:_o + 1],
                                      scale=1.0), "act")
                evac_hist.append(a_e)
                wait("sync", "act", a_e)
                pair = oc - 24
                sa = vstage[:].rearrange("p (i q) -> p i q", q=128)
                da = bass.AP(v_tm, pair * T + tt * 512,
                             [[8 * T, 128], [128, 4], [1, 128]])
                tpv = op("sync", lambda e, _s=sa, _d=da:
                         e.dma_start_transpose(_d, _s), "tp", 16)
                vtp_hist.append(tpv)
    wait("tensor", "act", kb.cnt("act"))
    wait("vector", "tp", kb.cnt("tp"))
    for tt in range(4):
        wait("gpsimd", "dve", kb.cnt("dve"))
        op("gpsimd", lambda e, _t=tt: e.dma_start(
            out=cosl[:], in_=H["c_cosf"][:, _t * 512:(_t + 1) * 512]),
            "gw", 16)
        lcs = op("gpsimd", lambda e, _t=tt: e.dma_start(
            out=sinl[:], in_=H["c_sinf"][:, _t * 512:(_t + 1) * 512]),
            "gw", 16)
        wait("vector", "gw", lcs)
        for sl in range(8):
            tens = qsl if sl < 4 else ksl
            j = sl % 4
            base = j * T + tt * 512
            mmr = op("tensor", lambda e, _t=tens, _b=base: e.matmul(
                pb[4], rot[:], _t[:, _b:_b + 512], start=True, stop=True),
                "te")
            wait("vector", "te", mmr)
            op("vector", lambda e, _t=tens, _b=base: e.tensor_tensor(
                _t[:, _b:_b + 512], _t[:, _b:_b + 512], cosl[:], ALU.mult))
            op("vector", lambda e: e.tensor_tensor(ubuf[:], pb[4], sinl[:],
                                                   ALU.mult))
            drain("vector")
            ev = op("vector", lambda e, _t=tens, _b=base: e.tensor_tensor(
                _t[:, _b:_b + 512], _t[:, _b:_b + 512], ubuf[:], ALU.add),
                "dve")
            wait("tensor", "dve", ev)
    if DIAG3:
        wait("sync", "dve", kb.cnt("dve"))
        wait("sync", "act", kb.cnt("act"))
        wait("sync", "tp", kb.cnt("tp"))
        for nm, tens, nch in (("dbg_qsl", qsl, 4), ("dbg_ksl", ksl, 4),
                              ("dbg_qr", qrest, 8), ("dbg_kr", krest, 8),
                              ("dbg_v", v_tm, 8)):
            for p in range(nch):
                op("sync", lambda e, _n=nm, _t=tens, _p=p: e.dma_start(
                    out=H[_n][:, _p * T:(_p + 1) * T],
                    in_=_t[:, _p * T:(_p + 1) * T]), "st", 16)
    kb.barrier()
    kb.free_to(M_R)
    ptw = [kb.S(f"ptw{i}", (128, 256), BF16) for i in range(2)]
    wmask = kb.S("wmask", (128, 256), BF16)
    cmask = kb.S("cmask", (128, 128), BF16)
    sqb = kb.S("sqb_w", (128, T), BF16)
    kb.sb["sqb"] = sqb
    for nm, tile in (("c_wmask", wmask), ("c_cmask", cmask)):
        op("sync", lambda e, _n=nm, _t=tile: e.dma_start(
            out=_t[:], in_=H[_n][:, :]), "ld", 16)
    wait("tensor", "ld", kb.cnt("ld"))
    wait("scalar", "ld", kb.cnt("ld"))
    wait("vector", "ld", kb.cnt("ld"))

    def emit_norms(groups, out_col0, negout):
        ng = len(groups)
        for g, (tens, coff, obs) in enumerate(groups):
            a_sq = op("scalar", lambda e, _t=tens, _c=coff: e.activation(
                sqb[:], _t[:, _c:_c + T], AF.Square, bias=0.0, scale=1.0),
                "act")
            wait("tensor", "act", a_sq)
            gmm = None
            for tt in range(4):
                gmm = op("tensor", lambda e, _tt=tt, _o=obs, _g=g, _ng=ng:
                         e.matmul(pb[_tt][0:16, :],
                                  oblk[:, _o * 16:(_o + 1) * 16],
                                  sqb[:, _tt * 512:(_tt + 1) * 512],
                                  start=(_g == 0), stop=(_g == _ng - 1)),
                         "te" if tt == 3 else None)
            wait("scalar", "te", gmm)
        mmn = kb.cnt("te")
        wait("vector", "te", mmn)
        for tt in range(4):
            op("vector", lambda e, _tt=tt, _c=out_col0: e.tensor_reduce(
                nrm[:, _c + _tt:_c + _tt + 1], pb[_tt][0:16, :], AX.X,
                ALU.max))
        drain("vector")
        v_nr = op("vector", lambda e, _c=out_col0: e.tensor_reduce(
            nrm[:, _c + 4:_c + 5], nrm[:, _c:_c + 4], AX.X, ALU.max), "dve")
        wait("tensor", "dve", v_nr)
        return mmn

    qgroups = [(qsl, j * T, j) for j in range(4)] + \
              [(qrest, p * T, 4 + p) for p in range(8)]
    kgroups = [(ksl, j * T, j) for j in range(4)] + \
              [(krest, p * T, 4 + p) for p in range(8)]
    emit_norms(qgroups, 0, None)
    emit_norms(kgroups, 8, None)
    emit_mb_tail(kb, nrm, negmb1)
    emit_l1_wattn(kb)
    return


def emit_mb_broadcast(kb, nrm, negout):
    op, wait, drain = kb.op, kb.wait, kb.drain
    sb = kb.sb
    PS = kb.PSALL
    pb5 = PS[:, 5 * 512:6 * 512]
    v_dg = op("vector", lambda e: e.tensor_scalar(
        out=sb["diag16"][0:16, 0:16], in0=sb["ident"][0:16, 0:16],
        scalar1=nrm[:, 19:20], scalar2=None, op0=ALU.mult),
        "dve")
    kb.drain("vector")
    wait("tensor", "dve", v_dg)
    mmb = op("tensor", lambda e: e.matmul(pb5[:, 0:16], sb["ones16"][:],
                                          sb["diag16"][0:16, 0:16],
                                          start=True, stop=True), "te")
    wait("vector", "te", mmb)
    v_mb = op("vector", lambda e: e.tensor_copy(negout[:], pb5[:, 0:16]),
              "dve")
    kb.drain("vector")
    wait("tensor", "dve", v_mb)
    wait("scalar", "dve", v_mb)


def emit_l1_wattn(kb, want_dbg=False):
    H, op, wait, drain = kb.H, kb.op, kb.wait, kb.drain
    sb = kb.sb
    PS = kb.PSALL
    pb = [PS[:, i * 512:(i + 1) * 512] for i in range(8)]
    qsl, ksl = sb["qsl"], sb["ksl"]
    krest, qrest = sb["krest"], sb["xn_fm"]
    v_tm, A_tm = sb["g0"], sb["A_tm"]
    ptw = [sb["ptw0"], sb["ptw1"]]
    negmb1 = sb["negmb1"]
    rdc = sb["rdc"]
    ones1b = sb["ones1b"]
    wmask = sb["wmask"]

    ap_hist, aA_hist, vrd_hist = [], [], []
    for h in range(16):
        slab_c, srow = h // 4, (h % 4) * 32
        rest_c, rrow = h // 2, (h % 2) * 64
        pair, half = h // 2, h % 2
        for jt in range(16):
            bs = jt % 2
            Sb = pb[bs]
            ncols = 256 if jt < 15 else 128
            if len(ap_hist) >= 2:
                wait("tensor", "act", ap_hist[-2])
            ksliceA = bass.AP(ksl, srow * (4 * T) + slab_c * T + jt * 128,
                              [[4 * T, 32], [1, 128]])
            qslA = bass.AP(qsl, srow * (4 * T) + slab_c * T + jt * 128,
                           [[4 * T, 32], [1, ncols]])
            op("tensor", lambda e, _S=Sb, _k=ksliceA, _q=qslA, _n=ncols,
               _r=srow: e.matmul(_S[:, 0:_n], _k, _q, start=True, stop=False,
                                 tile_position=(_r, 0)))
            kB = bass.AP(krest, rrow * (8 * T) + rest_c * T + jt * 128,
                         [[8 * T, 64], [1, 128]])
            qB = bass.AP(qrest, rrow * (8 * T) + rest_c * T + jt * 128,
                         [[8 * T, 64], [1, ncols]])
            mmS = op("tensor", lambda e, _S=Sb, _k=kB, _q=qB, _n=ncols,
                     _r=rrow: e.matmul(_S[:, 0:_n], _k, _q, start=False,
                                       stop=True, tile_position=(_r, 0)),
                     "te")
            wait("scalar", "te", mmS)
            a_p = op("scalar", lambda e, _S=Sb, _b=bs, _n=ncols, _h=h:
                     e.activation(ptw[_b][:, 0:_n], _S[:, 0:_n], AF.Exp,
                                  bias=negmb1[:, _h:_h + 1], scale=0.125),
                     "act")
            ap_hist.append(a_p)
            wait("vector", "act", a_p)
            v_p = op("vector", lambda e, _b=bs, _n=ncols: e.tensor_tensor(
                ptw[_b][:, 0:_n], ptw[_b][:, 0:_n], wmask[:, 0:_n], ALU.mult),
                "dve")
            wait("tensor", "dve", v_p)
            if len(aA_hist) >= 2:
                wait("tensor", "act", aA_hist[-2])
                wait("tensor", "dve", vrd_hist[-2])
            ob = pb[2 + bs]
            vsl = v_tm[:, pair * T + jt * 128 + half * 64:
                       pair * T + jt * 128 + half * 64 + 64]
            op("tensor", lambda e, _o=ob, _p=ptw[bs][:, 0:128], _v=vsl:
               e.matmul(_o[:, 0:64], _p, _v, start=True, stop=False))
            if jt > 0:
                op("tensor", lambda e, _o=ob, _p=ptw[bs][:, 0:128]:
                   e.matmul(_o[:, 64:65], _p, ones1b[:], start=False,
                            stop=False))
                vsl2 = v_tm[:, pair * T + (jt - 1) * 128 + half * 64:
                            pair * T + (jt - 1) * 128 + half * 64 + 64]
                pprev = ptw[(jt - 1) % 2][:, 128:256]
                op("tensor", lambda e, _o=ob, _p=pprev, _v=vsl2:
                   e.matmul(_o[:, 0:64], _p, _v, start=False, stop=False))
                mmP = op("tensor", lambda e, _o=ob, _p=pprev:
                         e.matmul(_o[:, 64:65], _p, ones1b[:], start=False,
                                  stop=True), "te")
            else:
                mmP = op("tensor", lambda e, _o=ob, _p=ptw[bs][:, 0:128]:
                         e.matmul(
                    _o[:, 64:65], _p, ones1b[:],
                    start=False, stop=True), "te")
            wait("vector", "te", mmP)
            v_rd = op("vector", lambda e, _o=ob: e.reciprocal(
                rdc[:, 0:1], _o[:, 64:65]), "dve")
            vrd_hist.append(v_rd)
            wait("scalar", "dve", v_rd)
            adst = A_tm[:, pair * T + jt * 128 + half * 64:
                        pair * T + jt * 128 + half * 64 + 64]
            a_A = op("scalar", lambda e, _o=ob, _d=adst: e.activation(
                _d, _o[:, 0:64], AF.Identity, bias=0.0, scale=rdc[:, 0:1]),
                "act")
            aA_hist.append(a_A)
            wait("vector", "act", a_A)
    if DIAG3:
        wait("sync", "act", kb.cnt("act"))
        for p in range(8):
            op("sync", lambda e, _p=p: e.dma_start(
                out=H["dbg_A3"][:, _p * T:(_p + 1) * T],
                in_=A_tm[:, _p * T:(_p + 1) * T]), "st", 16)
    kb.barrier()
    kb.free_to(kb.M_K)
    emit_l1_tail(kb, want_dbg)


def emit_l1_tail(kb, want_dbg=False):
    H, op, wait, drain = kb.H, kb.op, kb.wait, kb.drain
    sb = kb.sb
    PS = kb.PSALL
    pb = [PS[:, i * 512:(i + 1) * 512] for i in range(8)]
    smalls = sb["smalls"]
    A_tm = sb["A_tm"]
    A_fm = kb.S("A_fm", (128, 8 * T), BF16)
    den_fm = kb.S("den_fm2", (16, T), F32)
    den_stage = kb.S("den_stage", (1, T), F32)
    kb.sb["den_fm"] = den_fm
    wst = [kb.S(f"wstd{i}", (128, 8 * 128), BF16) for i in range(2)]
    ptcol = kb.S("ptcol", (128, 2048), BF16)
    outsb = kb.S("outsb", (64, 2 * T), BF16)
    cmask = kb.S("cmask2", (128, 128), BF16)
    xt_t = [kb.S(f"xt{i}_l1t", (128, D), F32) for i in range(2)]
    xnf_t = [kb.S(f"xnf{i}_l1t", (128, D), F32) for i in range(2)]
    kb.sb["xt0"], kb.sb["xt1"] = xt_t
    kb.sb["xnf0"], kb.sb["xnf1"] = xnf_t
    kb.sb["sqb"] = ptcol
    op("sync", lambda e: e.dma_start(out=cmask[:], in_=H["c_cmask"][:, :]),
       "ld", 16)
    wait("vector", "ld", kb.cnt("ld"))
    qa = sb["xn_fm"]
    ka = sb["g0"]
    nrm = sb["nrm"]
    oblk = sb["oblk"]
    sqb = ptcol
    negmb2 = sb["negmb2"]
    dentm = sb["dentm"]
    ones1b = sb["ones1b"]

    kb.barrier("ph")
    for p in range(8):
        sa = A_tm[:, p * T:(p + 1) * T].rearrange("p (i q) -> p i q", q=128)
        da = A_fm[:, p * T:(p + 1) * T].rearrange("p (i q) -> p i q", q=128)
        op("sync", lambda e, _s=sa, _d=da: e.dma_start_transpose(_d, _s),
           "tp", 16)
    tpA = kb.cnt("tp")
    wait("tensor", "tp", tpA)

    wst_guard = [0, 0]
    evac_hist = []
    for which, wname, bname, dstt in ((0, "l1_wqaT", "bqa_t", qa),
                                      (1, "l1_wkaT", "bka_t", ka)):
        for oc in range(8):
            bws = oc % 2
            if wst_guard[bws]:
                kb.wait("gpsimd", "te", wst_guard[bws])
            in_w = bass.AP(H[wname].tensor, oc * 131072,
                           [[1024, 128], [1, 1024]])
            lww = op("gpsimd", lambda e, _i=in_w, _b=bws: e.dma_start(
                out=wst[_b][:], in_=_i), "gw", 16)
            wait("tensor", "gw", lww)
            for tt in range(4):
                bank = pb[(oc * 4 + tt) % 4]
                if len(evac_hist) >= 4:
                    wait("tensor", "act", evac_hist[-4])
                mmv = None
                for dc in range(8):
                    mmv = op("tensor", lambda e, _b=bank,
                             _s=wst[bws][:, dc * 128:(dc + 1) * 128],
                             _m=A_fm[:, dc * T + tt * 512:dc * T + tt * 512 + 512],
                             _d=dc: e.matmul(_b, _s, _m, start=(_d == 0),
                                             stop=(_d == 7)),
                             "te" if dc == 7 else None)
                wst_guard[bws] = mmv
                wait("scalar", "te", mmv)
                a_e = op("scalar", lambda e, _b=bank, _o=oc, _t=tt, _d=dstt,
                         _bn=bname: e.activation(
                             _d[:, _o * T + _t * 512:_o * T + _t * 512 + 512],
                             _b, AF.Identity, bias=sb[_bn][:, _o:_o + 1],
                             scale=1.0), "act")
                evac_hist.append(a_e)
    wait("tensor", "act", kb.cnt("act"))

    qgroups = [(qa, p * T, 4 + p) for p in range(8)]
    kgroups = [(ka, p * T, 4 + p) for p in range(8)]

    def norms2(groups, col0):
        for g, (tens, coff, obs) in enumerate(groups):
            a_sq = op("scalar", lambda e, _t=tens, _c=coff: e.activation(
                sqb[:], _t[:, _c:_c + T], AF.Square, bias=0.0, scale=1.0),
                "act")
            wait("tensor", "act", a_sq)
            gmm = None
            for tt in range(4):
                gmm = op("tensor", lambda e, _tt=tt, _o=obs, _g=g:
                         e.matmul(pb[_tt][0:16, :],
                                  oblk[:, _o * 16:(_o + 1) * 16],
                                  sqb[:, _tt * 512:(_tt + 1) * 512],
                                  start=(_g == 0), stop=(_g == 7)),
                         "te" if tt == 3 else None)
            wait("scalar", "te", gmm)
        mmn = kb.cnt("te")
        wait("vector", "te", mmn)
        for tt in range(4):
            op("vector", lambda e, _tt=tt, _c=col0: e.tensor_reduce(
                nrm[:, _c + _tt:_c + _tt + 1], pb[_tt][0:16, :], AX.X,
                ALU.max))
        drain("vector")
        v_nr = op("vector", lambda e, _c=col0: e.tensor_reduce(
            nrm[:, _c + 4:_c + 5], nrm[:, _c:_c + 4], AX.X, ALU.max), "dve")
        kb.drain("vector")
        wait("tensor", "dve", v_nr)

    norms2(qgroups, 0)
    norms2(kgroups, 8)
    emit_mb_tail(kb, nrm, negmb2)

    OB0 = 2048
    out_tm = A_fm
    a65 = [kb.S(f"a65_{i}", (128, 65), BF16) for i in range(2)]
    op("vector", lambda e: e.memset(a65[0][:, 64:65], 1.0))
    op("vector", lambda e: e.memset(a65[1][:, 64:65], 1.0))
    ap_prev = [0, 0, 0, 0]
    for h in range(16):
        pair, half = h // 2, h % 2
        pv_last = None
        for kt in range(16):
            nq = 16 - kt
            npc = (nq * 128 + 511) // 512
            b65 = kt % 2
            if pv_last is not None:
                wait("vector", "te", pv_last)
            asl = A_tm[:, pair * T + kt * 128 + half * 64:
                       pair * T + kt * 128 + half * 64 + 64]
            v_a = op("vector", lambda e, _b=b65, _a=asl: e.tensor_copy(
                a65[_b][:, 0:64], _a), "dve")
            kasl = bass.AP(ka, half * 64 * (8 * T) + pair * T + kt * 128,
                           [[8 * T, 64], [1, 128]])
            mmS_v = []
            for subp in range(npc):
                w = min(512, nq * 128 - subp * 512)
                qasl = bass.AP(qa, half * 64 * (8 * T) + pair * T
                               + kt * 128 + subp * 512, [[8 * T, 64], [1, w]])
                if ap_prev[subp]:
                    wait("tensor", "act", ap_prev[subp])
                mv = op("tensor", lambda e, _o=subp * 512, _k=kasl, _q=qasl,
                        _w=w, _r=half * 64: e.matmul(
                            PS[:, _o:_o + _w], _k, _q, start=True, stop=True,
                            tile_position=(_r, 0)), "te")
                mmS_v.append(mv)
            if pv_last is not None:
                wait("scalar", "te", pv_last)
            ap_now = []
            for subp in range(npc):
                w = min(512, nq * 128 - subp * 512)
                wait("scalar", "te", mmS_v[subp])
                a_p = op("scalar", lambda e, _o=subp * 512, _w=w, _h=h:
                         e.activation(ptcol[:, _o:_o + _w], PS[:, _o:_o + _w],
                                      AF.Exp, bias=negmb2[:, _h:_h + 1],
                                      scale=0.125), "act")
                ap_now.append(a_p)
                ap_prev[subp] = a_p
            wait("vector", "act", ap_now[0])
            v_m = op("vector", lambda e: e.tensor_tensor(
                ptcol[:, 0:128], ptcol[:, 0:128], cmask[:], ALU.mult), "dve")
            wait("tensor", "dve", v_m)
            wait("tensor", "dve", v_a)
            for b in range(kt // 4, 4):
                qt0 = max(kt, 4 * b)
                wdt = (4 * b + 3 - qt0 + 1) * 128
                st_flag, sp_flag = kt == 0, kt == 4 * b + 3
                hi_piece = ((qt0 - kt) * 128 + wdt - 1) // 512
                wait("tensor", "act", ap_now[hi_piece])
                pv_last = op("tensor", lambda e, _q=qt0, _w=wdt, _b=b65,
                             _p=ptcol[:, (qt0 - kt) * 128:
                                      (qt0 - kt) * 128 + wdt],
                             _st=st_flag, _sp=sp_flag:
                             e.matmul(PS[0:65, OB0 + _q * 128:OB0 + _q * 128 + _w],
                                      a65[_b][:, 0:65], _p,
                                      start=_st, stop=_sp), "te")
        wait("vector", "te", kb.cnt("te"))
        if h > 0:
            wait("vector", "st", kb.cnt("st"))
        vd = op("vector", lambda e: e.tensor_copy(
            den_stage[0:1, :],
            PS[64:65, OB0:4096].rearrange("p (i q) -> p i q", q=128)), "dve")
        wait("sync", "dve", vd)
        op("sync", lambda e, _h=h: e.dma_start(
            out=den_fm[_h:_h + 1, :], in_=den_stage[0:1, :]), "st", 16)
        wait("scalar", "dve", vd)
        a_o = op("scalar", lambda e: e.activation(
            outsb[:, 0:T], PS[0:64, OB0:4096],
            AF.Identity, bias=0.0, scale=1.0), "act")
        wait("sync", "act", a_o)
        sa = outsb[:, 0:T].rearrange("p (i q) -> p i q", q=128)
        da = bass.AP(out_tm, h * 64, [[8 * T, 128], [D, 16], [1, 64]])
        tpo = op("sync", lambda e, _s=sa, _d=da: e.dma_start_transpose(_d, _s),
                 "tp", 16)
        wait("scalar", "tp", tpo)
        wait("tensor", "tp", tpo)
    emit_l1_fin(kb, out_tm, want_dbg)
    return



def emit_l1_fin(kb, out_tm, want_dbg=False):
    H, op, wait, drain = kb.H, kb.op, kb.wait, kb.drain
    sb = kb.sb
    PS = kb.PSALL
    pb = [PS[:, i * 512:(i + 1) * 512] for i in range(8)]
    den_fm = sb["den_fm"]
    dentm = sb["dentm"]
    out_fm = sb["A_tm"]
    sg1 = sb["xn_fm"]
    sg_tm = sb["g0"]
    wst = [sb["wstd0"], sb["wstd1"]]
    g1b = [sb["g1st0"], sb["g1st1"]]

    kb.barrier("ph")
    op("vector", lambda e: e.reciprocal(den_fm[:, :], den_fm[:, :]))
    drain("vector")
    v_rd = op("vector", lambda e: e.tensor_copy(den_fm[:, 0:1], den_fm[:, 0:1]),
              "dve")
    wait("tensor", "dve", v_rd)
    for i in range(NT):
        mt = op("tensor", lambda e, _i=i: e.transpose(
            pb[0][:, 0:16], den_fm[:, _i * 128:(_i + 1) * 128],
            sb["identf"][0:16, 0:16]), "te")
        wait("vector", "te", mt)
        ev = op("vector", lambda e, _i=i: e.tensor_copy(
            dentm[:, _i * 16:(_i + 1) * 16], pb[0][:, 0:16]), "dve")
        kb.drain("vector")
        wait("tensor", "dve", ev)
    for i in range(NT):
        v_n = op("vector", lambda e, _i=i: e.tensor_tensor(
            out_tm[:, _i * D:(_i + 1) * D].rearrange("p (h d) -> p h d", d=64),
            out_tm[:, _i * D:(_i + 1) * D].rearrange("p (h d) -> p h d", d=64),
            dentm[:, _i * 16:(_i + 1) * 16].unsqueeze(2)
            .broadcast_to([128, 16, 64]), ALU.mult), "dve")
        kb.drain("vector")
    v_norm = kb.cnt("dve")
    wait("sync", "dve", v_norm)
    if DIAG3:
        for i in range(NT):
            op("sync", lambda e, _i=i: e.dma_start(
                out=H["dbg_out3"][:, _i * D:(_i + 1) * D],
                in_=out_tm[:, _i * D:(_i + 1) * D]), "st", 16)
    for i in range(NT):
        sa = out_tm[:, i * D:(i + 1) * D].rearrange("p (c q) -> p c q", q=128)
        da = bass.AP(out_fm, i * 128, [[8 * T, 128], [T, 8], [1, 128]])
        op("sync", lambda e, _s=sa, _d=da: e.dma_start_transpose(_d, _s),
           "tp", 16)
    tpf = kb.cnt("tp")
    wait("tensor", "tp", tpf)
    wst_guard = [0, 0]
    g1_guard = [None, None]
    for fo in range(8):
        bws = fo % 2
        if wst_guard[bws]:
            kb.wait("gpsimd", "te", wst_guard[bws])
        in_w = bass.AP(H["l1_wdT"].tensor, fo * 131072,
                       [[1024, 128], [1, 1024]])
        lww = op("gpsimd", lambda e, _i=in_w, _b=bws: e.dma_start(
            out=wst[_b][:], in_=_i), "gw", 16)
        wait("tensor", "gw", lww)
        for tt in range(4):
            bank = pb[(fo * 4 + tt) % 4]
            mmv = None
            for dc in range(8):
                mmv = op("tensor", lambda e, _b=bank,
                         _s=wst[bws][:, dc * 128:(dc + 1) * 128],
                         _m=out_fm[:, dc * T + tt * 512:dc * T + tt * 512 + 512],
                         _d=dc: e.matmul(_b, _s, _m, start=(_d == 0),
                                         stop=(_d == 7)),
                         "te" if dc == 7 else None)
            wst_guard[bws] = mmv
            bg = tt % 2
            if g1_guard[bg] is not None:
                wait("sync", "dve", g1_guard[bg])
            lg = op("sync", lambda e, _f=fo, _t=tt, _g=bg: e.dma_start(
                out=g1b[_g][:], in_=bass.AP(H["g1"].tensor,
                                            _f * 128 * 2048 + _t * 512,
                                            [[2048, 128], [1, 512]])),
                "ld", 16)
            wait("scalar", "te", mmv)
            a_b = op("scalar", lambda e, _b=bank, _f=fo: e.activation(
                ubuf_ap(kb), _b, AF.Identity, bias=sb["bd_t"][:, _f:_f + 1],
                scale=1.0), "act")
            wait("vector", "act", a_b)
            wait("vector", "ld", lg)
            ev = op("vector", lambda e, _f=fo, _t=tt, _g=bg: e.tensor_tensor(
                sg1[:, _f * T + _t * 512:_f * T + _t * 512 + 512],
                ubuf_ap(kb), g1b[_g][:], ALU.mult), "dve")
            wait("tensor", "dve", ev)
            wait("scalar", "dve", ev)
            g1_guard[bg] = ev
    sg_done = kb.cnt("dve")
    wait("sync", "dve", sg_done)
    for fo in range(8):
        sa = sg1[:, fo * T:(fo + 1) * T].rearrange("p (i q) -> p i q", q=128)
        da = bass.AP(sg_tm, fo * 128, [[8 * T, 128], [D, 16], [1, 128]])
        op("sync", lambda e, _s=sa, _d=da: e.dma_start_transpose(_d, _s),
           "tp", 16)
    tps = kb.cnt("tp")
    x1v = H["x1"].rearrange("(n p) d -> n p d", p=128)
    outv = H["out"].rearrange("(n p) d -> n p d", p=128)
    res_guard = [None, None]
    xtb_t = [sb["xt0"], sb["xt1"]]
    xnfb_t = [sb["xnf0"], sb["xnf1"]]
    for i in range(NT):
        b = i % 2
        ldv = op("sync", lambda e, _i=i, _x=xtb_t[b]: e.dma_start(
            out=_x[:], in_=x1v[_i]), "ld", 16)
        wait("vector", "ld", ldv)
        wait("vector", "tp", tps)
        if res_guard[b] is not None:
            wait("vector", "st", res_guard[b])
        v1 = op("vector", lambda e, _o=xnfb_t[b], _x=xtb_t[b], _i=i:
                e.tensor_tensor(
            _o[:], _x[:],
            sg_tm[:, _i * D:(_i + 1) * D], ALU.add), "dve")
        wait("sync", "dve", v1)
        res_guard[b] = op("sync", lambda e, _i=i, _o=xnfb_t[b]: e.dma_start(
            out=outv[_i], in_=_o[:]), "st", 16)
    kb.wait("sync", "st", kb.cnt("st"))


def ubuf_ap(kb):
    return kb.sb["ubuf"][:]


def emit_mb_tail(kb, nrm, negout):
    op, wait, drain = kb.op, kb.wait, kb.drain
    smalls = kb.sb["smalls"]
    wait("vector", "dve", kb.cnt("dve"))
    v_gm = op("vector", lambda e: e.tensor_tensor(
        nrm[:, 16:17], nrm[:, 4:5], nrm[:, 12:13], ALU.mult), "dve")
    wait("scalar", "dve", v_gm)
    op("scalar", lambda e: e.activation(nrm[:, 17:18], nrm[:, 16:17], AF.Ln,
                                        bias=0.0, scale=1.0))
    kb.drain("scalar")
    a_mb = op("scalar", lambda e: e.activation(
        nrm[:, 18:19], nrm[:, 17:18], AF.Exp, bias=smalls[0:16, 5:6],
        scale=0.5), "act")
    wait("vector", "act", a_mb)
    op("vector", lambda e: e.tensor_scalar(
        out=nrm[:, 19:20], in0=nrm[:, 18:19], scalar1=-1.0, scalar2=None,
        op0=ALU.mult))
    drain("vector")
    emit_mb_broadcast(kb, nrm, negout)
